# version 30
# baseline (speedup 1.0000x reference)
import numpy as np
import ml_dtypes
from contextlib import ExitStack

import concourse.bass as bass
import concourse.mybir as mybir
from concourse.bass_utils import run_bass_kernel_spmd
from concourse.aot_env import allow_ici_all_to_all

F32 = mybir.dt.float32
BF16 = mybir.dt.bfloat16
AF = mybir.ActivationFunctionType
ALU = mybir.AluOpType
AX = mybir.AxisListType
BF = ml_dtypes.bfloat16

NCORE = 8
TOK = 2048
D = 1024
DT = D // 128
CH = 512
NCH = TOK // CH
CAP = 2048.0
LNEPS = 1e-5
QEPS = 1e-5
DFF = 4096
SEQ = 8192
ATTN_SCALE = 0.125


class Sem:
    def __init__(self, ctx, nc, name):
        self.h = ctx.enter_context(nc.semaphore(name))
        self.n = 0


class G:
    ENG = ("sync", "tensor", "vector", "scalar", "gpsimd")

    def __init__(self, nc, ctx):
        self.nc = nc
        self.ctx = ctx
        self.cur = ctx
        self.prefix = ""
        self.q = {k: [] for k in self.ENG}
        self.sem = {name: Sem(ctx, nc, "sem_" + name) for name in
                    ("qs", "qg", "qa", "pe", "ve", "se", "gp", "cc", "wa", "wb", "sc", "sd", "n0", "n1")}
        self.nblk = 0

    def sb(self, name, shape, dt):
        return self.cur.enter_context(self.nc.sbuf_tensor(self.prefix + name, shape, dt))

    def ps(self, name, shape, dt=F32):
        return self.nc.psum_tensor(self.prefix + name, shape, dt)

    def subscope(g, prefix):
        class _S:
            def __enter__(s):
                s.st = ExitStack()
                s.st.__enter__()
                s.old = (g.cur, g.prefix)
                g.cur, g.prefix = s.st, prefix
                for eng in ("sync", "gpsimd"):
                    g.w(eng, g.sem["pe"], g.sem["ve"], g.sem["se"],
                        g.sem["qs"], g.sem["qg"])
                return s
            def __exit__(s, *a):
                g.cur, g.prefix = s.old
                return s.st.__exit__(*a)
        return _S()

    def on(self, eng, fn, wait=(), inc=None, inck=1):
        waits = []
        for wv in wait:
            s, v = wv if isinstance(wv, tuple) else (wv, None)
            waits.append((s.h, s.n if v is None else v))
        if inc is not None:
            inc.n += inck
        self.q[eng].append((tuple(waits), fn, None if inc is None else (inc.h, inck)))

    def w(self, eng, *sems):
        self.on(eng, None, wait=list(sems))

    def run_phase(self, name=None):
        qs = self.q
        self.q = {k: [] for k in self.ENG}
        self.nblk += 1
        if name is not None:
            name = self.prefix + name
        with self.nc.Block(name) as b:
            for engname in self.ENG:
                items = qs[engname]
                if not items:
                    continue

                dr = engname in ("vector", "scalar", "gpsimd")

                def body(e, items=tuple(items), dr=dr):
                    for waits, fn, inc in items:
                        for (h, v) in waits:
                            e.wait_ge(h, v)
                        instr = fn(e) if fn is not None else None
                        if inc is not None:
                            h, k = inc
                            (instr if instr is not None else e.nop()).then_inc(h, k)
                        if dr and fn is not None:
                            e.drain()
                getattr(b, engname)(body)


def host_consts(core):
    pos = 2048.0 * (core % 4) + np.arange(TOK, dtype=np.float64)
    inv_freq = 1.0 / (10000.0 ** (np.arange(0, 32, 2, dtype=np.float64) / 32.0))
    ang = pos[:, None] * inv_freq[None, :]
    C = np.ones((128, TOK), np.float64)
    S = np.zeros((128, TOK), np.float64)
    for p in range(128):
        q = p % 64
        if q < 32:
            C[p] = np.cos(ang[:, q // 2])
            S[p] = np.sin(ang[:, q // 2])
    P = np.zeros((128, 128), np.float64)
    for base in (0, 64):
        for i in range(16):
            P[base + 2 * i, base + 2 * i + 1] = -1.0
            P[base + 2 * i + 1, base + 2 * i] = 1.0
    u = np.arange(1920)[None, :]
    p = np.arange(128)[:, None]
    Tb = (p <= u - 896).astype(np.float64)
    wsel = np.zeros((8, 1), np.float32)
    wsel[:core] = 1.0
    Sx = S.copy()
    Sx[0::2] *= -1.0
    return {
        "ropeC": np.ascontiguousarray(C.astype(BF)),
        "ropeS": np.ascontiguousarray(S.astype(BF)),
        "ropeSx": np.ascontiguousarray(Sx.astype(BF)),
        "PT": np.ascontiguousarray(P.T.astype(BF)),
        "Tbase": np.ascontiguousarray(Tb.astype(BF)),
        "id8": np.eye(8, dtype=np.float32),
        "id128": np.eye(128, dtype=np.float32),
        "id1": np.ones((1, 1), np.float32),
        "wsel": wsel,
    }


CONST_SPECS = {
    "ropeC": ([128, TOK], BF16), "ropeS": ([128, TOK], BF16),
    "ropeSx": ([128, TOK], BF16),
    "PT": ([128, 128], BF16), "Tbase": ([128, 1920], BF16),
    "id8": ([8, 8], F32), "id128": ([128, 128], F32), "id1": ([1, 1], F32),
    "wsel": ([8, 1], F32),
}


def decl_in(nc, name, shape, dt=F32):
    return nc.dram_tensor(name, shape, dt, kind="ExternalInput").ap()


def decl_out(nc, name, shape, dt=F32):
    return nc.dram_tensor(name, shape, dt, kind="ExternalOutput").ap()


def const_decls_subset(nc, names):
    return {n: decl_in(nc, n, *CONST_SPECS[n]) for n in names}


def alloc_common(g, cd, const_names):
    cm = {}
    cm["ones128_bf"] = g.sb("ones128_bf", [128, 1], BF16)
    cm["ones128_f32"] = g.sb("ones128_f32", [128, 1], F32)
    cm["onesrow_f32"] = g.sb("onesrow_f32", [1, 128], F32)
    cm["onesrow_bf"] = g.sb("onesrow_bf", [1, 512], BF16)
    cm["ones8_f32"] = g.sb("ones8_f32", [8, 1], F32)

    cm["eps_ln"] = g.sb("eps_ln", [1, 1], F32)

    def ms(e):
        e.memset(cm["eps_ln"][:], LNEPS)
        e.memset(cm["ones128_bf"][:], 1.0)
        e.memset(cm["ones128_f32"][:], 1.0)
        e.memset(cm["onesrow_f32"][:], 1.0)
        e.memset(cm["onesrow_bf"][:], 1.0)
        return e.memset(cm["ones8_f32"][:], 1.0)
    g.on("vector", ms, inc=g.sem["ve"])
    for n in const_names:
        shape, dt = CONST_SPECS[n]
        t = g.sb(n + "_sb", shape, dt)
        cm[n + "_sb"] = t
        g.on("sync", lambda e, t=t, ap=cd[n]: e.dma_start(out=t[:], in_=ap),
             inc=g.sem["qs"], inck=16)
    return cm


def load_xT(g, xT, x_in):
    g.on("sync", lambda e: e.dma_start(
        out=xT[:].rearrange("p (a c) -> p a c", a=DT),
        in_=x_in.rearrange("(a p) c -> p a c", p=128)), inc=g.sem["qs"], inck=16)


def store_xT(g, xT, x_out):
    g.w("gpsimd", g.sem["ve"], g.sem["se"])
    g.on("gpsimd", lambda e: e.dma_start(
        out=x_out.rearrange("(a p) c -> p a c", p=128),
        in_=xT[:].rearrange("p (a c) -> p a c", a=DT)), inc=g.sem["qg"], inck=16)
    g.w("gpsimd", g.sem["qg"])
    g.run_phase("store_x")


def emit_casts(g, casts, sem=None):
    s = g.sem["qg"] if sem is None else sem
    for dst, src in casts:
        rows = dst.shape[0]
        nsp = 8 if rows % 8 == 0 and rows >= 8 else 1
        rs = rows // nsp
        for i in range(nsp):
            g.on("gpsimd", lambda e, dst=dst, src=src, i=i, rs=rs: e.dma_start(
                out=dst[rs * i: rs * (i + 1)], in_=src[rs * i: rs * (i + 1)]),
                inc=s, inck=16)


def alloc_ln(g, sq=None, xbf=None):
    b = {}
    b["xbf"] = g.sb("ln_xbf", [128, DT * CH], BF16) if xbf is None else xbf
    b["sq"] = g.sb("ln_sq", [128, DT * CH], BF16) if sq is None else sq
    b["m_sb"] = g.sb("ln_m", [1, CH], F32)
    b["v_sb"] = g.sb("ln_v", [1, CH], F32)
    b["r_sb"] = g.sb("ln_r", [1, CH], F32)
    b["mr_sb"] = b["t1_sb"] = g.sb("ln_t1", [1, CH], F32)
    b["u_sb"] = g.sb("ln_u", [128, CH], F32)
    return b


def emit_ln_phase(g, cm, b, xT, hT, chunks=None, router=None, hT_off=None, name="ln", steps=6):
    nc = g.nc
    se, pe, ve = g.sem["se"], g.sem["pe"], g.sem["ve"]
    if chunks is None:
        chunks = list(range(NCH))
    chunks = list(chunks)
    NC = len(chunks)
    if hT_off is None:
        HTOK, hT_off = TOK, 0
    else:
        HTOK = NC * CH
    with ExitStack() as ps:
        msum_ps = ps.enter_context(g.ps(name + "_msum", [1, CH], F32))
        s2_ps = ps.enter_context(g.ps(name + "_s2", [1, CH], F32))
        rb_ps = ps.enter_context(g.ps(name + "_rb", [128, CH], F32))
        mrb_ps = ps.enter_context(g.ps(name + "_mrb", [128, CH], F32))
        lg_ps = ps.enter_context(g.ps(name + "_lg", [8, TOK], F32)) \
            if router is not None else None

        reds_pe = {}
        stats_ve = {}
        mr_ve = {}
        bcast_pe = {}
        hcomp_ve = {}
        rstd_se = {}
        rb8_se = {}

        def e_cast(ci, ch):
            def cast(e):
                for d in range(DT):
                    e.activation(out=b["xbf"][:, CH * d: CH * (d + 1)],
                                 in_=xT[:, TOK * d + CH * ch: TOK * d + CH * (ch + 1)], func=AF.Copy)
                for d in range(DT):
                    last = e.activation(out=b["sq"][:, CH * d: CH * (d + 1)],
                                        in_=xT[:, TOK * d + CH * ch: TOK * d + CH * (ch + 1)], func=AF.Square)
                return last
            g.on("scalar", cast, wait=[(pe, reds_pe[ci - 1])] if ci >= 1 else [(pe, pe.n)], inc=se)

        def e_reds(ci, ch):
            def reds(e):
                for d in range(DT):
                    e.matmul(msum_ps[:], cm["ones128_bf"][:],
                             b["xbf"][:, CH * d: CH * (d + 1)], start=(d == 0), stop=(d == DT - 1))
                for d in range(DT):
                    last = e.matmul(s2_ps[:], cm["ones128_bf"][:],
                                    b["sq"][:, CH * d: CH * (d + 1)], start=(d == 0), stop=(d == DT - 1))
                if router is not None:
                    for d in range(DT):
                        last = e.matmul(lg_ps[:, CH * ch: CH * (ch + 1)],
                                        router["wg_sb"][:, 8 * d: 8 * (d + 1)],
                                        xT[:, TOK * d + CH * ch: TOK * d + CH * (ch + 1)],
                                        start=(d == 0), stop=False)
                return last
            w = [se] + ([(ve, stats_ve[ci - 1])] if ci >= 1 else [])
            g.on("tensor", reds, wait=w, inc=pe)
            reds_pe[ci] = pe.n

        def e_stats(ci, ch):
            def stats(e):
                e.tensor_scalar(out=b["m_sb"][:], in0=msum_ps[:], scalar1=1.0 / D, scalar2=None, op0=ALU.mult)
                e.drain()
                e.tensor_scalar(out=b["v_sb"][:], in0=s2_ps[:], scalar1=1.0 / D, scalar2=None, op0=ALU.mult)
                e.drain()
                e.tensor_tensor(out=b["t1_sb"][:], in0=b["m_sb"][:], in1=b["m_sb"][:], op=ALU.mult)
                e.drain()
                return e.tensor_tensor(out=b["v_sb"][:], in0=b["v_sb"][:], in1=b["t1_sb"][:], op=ALU.subtract)
            w = [(pe, reds_pe[ci])] + ([(se, rstd_se[ci - 1])] if ci >= 1 else [])
            g.on("vector", stats, wait=w, inc=ve)
            stats_ve[ci] = ve.n

        def e_rstd(ci, ch):
            def rstd(e):
                e.activation(out=b["t1_sb"][:], in_=b["v_sb"][:], func=AF.Ln, bias=cm["eps_ln"][:])
                return e.activation(out=b["r_sb"][:], in_=b["t1_sb"][:], func=AF.Exp, scale=-0.5)
            g.on("scalar", rstd, wait=[(ve, stats_ve[ci])], inc=se)
            rstd_se[ci] = se.n

        def e_mr(ci, ch):
            g.on("vector", lambda e: e.tensor_tensor(out=b["mr_sb"][:], in0=b["m_sb"][:],
                                                     in1=b["r_sb"][:], op=ALU.mult),
                 wait=[(se, rstd_se[ci])], inc=ve)
            mr_ve[ci] = ve.n

        def e_bcast(ci, ch):
            def bcast(e):
                e.matmul(rb_ps[:], cm["onesrow_f32"][:], b["r_sb"][:], start=True, stop=True)
                last = e.matmul(mrb_ps[:], cm["onesrow_f32"][:], b["mr_sb"][:], start=True, stop=True)
                if router is not None:
                    last = e.matmul(lg_ps[:, CH * ch: CH * (ch + 1)],
                                    router["ngsum_sb"][:], b["m_sb"][:], start=False, stop=True)
                return last
            w = [(ve, mr_ve[ci])]
            if ci >= 1:
                w.append((ve, hcomp_ve[ci - 1]))
                if router is not None:
                    w.append((se, rb8_se[ci - 1]))
            g.on("tensor", bcast, wait=w, inc=pe)
            bcast_pe[ci] = pe.n

        def e_h(ci, ch):
            if router is not None:
                g.on("scalar", lambda e: e.copy(out=router["rb8_sb"][:], in_=rb_ps[0:8, :]),
                     wait=[(pe, bcast_pe[ci])], inc=se)
                rb8_se[ci] = se.n
            def hcomp(e):
                for d in range(DT):
                    e.tensor_tensor(out=b["u_sb"][:],
                                    in0=xT[:, TOK * d + CH * ch: TOK * d + CH * (ch + 1)],
                                    in1=rb_ps[:], op=ALU.mult)
                    e.drain()
                    last = e.tensor_tensor(
                        out=hT[:, HTOK * d + CH * ch - hT_off: HTOK * d + CH * (ch + 1) - hT_off],
                        in0=b["u_sb"][:], in1=mrb_ps[:], op=ALU.subtract)
                    e.drain()
                return last
            w = [(pe, bcast_pe[ci])] + ([(se, rb8_se[ci])] if router is not None else [])
            g.on("vector", hcomp, wait=w, inc=ve)
            if router is not None:
                g.on("vector", lambda e: e.tensor_tensor(
                    out=router["lg_sb"][:, CH * ch: CH * (ch + 1)],
                    in0=lg_ps[:, CH * ch: CH * (ch + 1)],
                    in1=router["rb8_sb"][:], op=ALU.mult), inc=ve)
            hcomp_ve[ci] = ve.n

        stages = [e_cast, e_reds, e_stats, e_rstd, e_mr, e_bcast, e_h]
        for ci in range(NC + 1):
            if ci < NC:
                e_cast(ci, chunks[ci])
            if ci >= 1:
                cj = ci - 1
                e_stats(cj, chunks[cj])
                e_rstd(cj, chunks[cj])
                e_mr(cj, chunks[cj])
                e_bcast(cj, chunks[cj])
                e_h(cj, chunks[cj])
            if ci < NC:
                e_reds(ci, chunks[ci])
        g.run_phase(name)


def alloc_gating(g):
    gt = {}
    gt["ltm_sb"] = g.sb("gt_ltm_sb", [128, 128], F32)
    gt["mx_sb"] = g.sb("gt_mx", [128, 16], F32)
    gt["msk_tm"] = g.sb("gt_msk_tm", [128, 128], F32)
    gt["mskT_sb"] = g.sb("gt_mskT_sb", [8, TOK], F32)
    gt["zero8"] = g.sb("gt_zero8", [8, 1], F32)
    gt["cumT_sb"] = g.sb("gt_cumT", [8, TOK], F32)
    gt["tot_sb"] = g.sb("gt_tot", [8, 8], F32)
    gt["eT_sb"] = g.sb("gt_eT", [8, TOK], F32)
    gt["sT_sb"] = g.sb("gt_sT_sb", [1, TOK], F32)
    gt["rs_sb"] = gt["sT_sb"]
    gt["nin_sb"] = gt["eT_sb"]
    gt["gateT_sb"] = g.sb("gt_gateT", [1, TOK], F32)
    gt["offs_sb"] = g.sb("gt_offs_sb", [1, 8], F32)
    gt["offsT_sb"] = g.sb("gt_offsT_sb", [8, 1], F32)
    gt["k8_sb"] = gt["cumT_sb"]
    gt["factorT_sb"] = gt["gateT_sb"]
    gt["fB_sb"] = g.sb("gt_fB_sb", [128, TOK], F32)
    return gt


def emit_gating(g, cm, lg_sb, gt, totals_mode, tot_io):
    nc = g.nc
    se, pe, ve = g.sem["se"], g.sem["pe"], g.sem["ve"]
    qg, cc = g.sem["qg"], g.sem["cc"]

    with g.ps("gt_ltm", [128, 128], F32) as ltm_ps, \
         g.ps("gt_mskT", [8, TOK], F32) as mskT_ps:
        def ttm(e):
            for j in range(16):
                last = e.transpose(ltm_ps[:, 8 * j: 8 * (j + 1)],
                                   lg_sb[:, 128 * j: 128 * (j + 1)], cm["id8_sb"][:])
            return last
        g.on("tensor", ttm, wait=[ve], inc=pe)
        g.on("scalar", lambda e: e.copy(out=gt["ltm_sb"][:], in_=ltm_ps[:]),
             wait=[pe], inc=se)
        def masks(e):
            for j in range(16):
                e.tensor_reduce(out=gt["mx_sb"][:, j: j + 1],
                                in_=gt["ltm_sb"][:, 8 * j: 8 * (j + 1)], axis=AX.X, op=ALU.max)
            e.drain()
            for j in range(16):
                last = e.tensor_scalar(out=gt["msk_tm"][:, 8 * j: 8 * (j + 1)],
                                       in0=gt["ltm_sb"][:, 8 * j: 8 * (j + 1)],
                                       scalar1=gt["mx_sb"][:, j: j + 1], scalar2=None, op0=ALU.is_ge)
            return last
        g.on("vector", masks, wait=[se], inc=ve)
        def tfm(e):
            for j in range(16):
                last = e.transpose(mskT_ps[:, 128 * j: 128 * (j + 1)],
                                   gt["msk_tm"][:, 8 * j: 8 * (j + 1)], cm["id128_sb"][:])
            return last
        g.on("tensor", tfm, wait=[ve], inc=pe)
        g.on("scalar", lambda e: e.copy(out=gt["mskT_sb"][:], in_=mskT_ps[:]),
             wait=[pe], inc=se)
        g.on("vector", lambda e: e.memset(gt["zero8"][:], 0.0))
        g.on("vector", lambda e: e.tensor_tensor_scan(
            out=gt["cumT_sb"][:], data0=gt["mskT_sb"][:], data1=gt["mskT_sb"][:],
            initial=gt["zero8"][:], op0=ALU.add, op1=ALU.bypass),
            wait=[se], inc=ve)
        g.run_phase("gate_mask")

    if totals_mode == "out":
        g.w("gpsimd", ve)
        g.on("gpsimd", lambda e: e.dma_start(
            out=tot_io["tot_out_ext"], in_=gt["cumT_sb"][:, TOK - 1: TOK]),
            inc=qg, inck=16)
        g.w("gpsimd", qg)
        g.run_phase("gate_tot")
        return

    if totals_mode == "cc":
        g.w("gpsimd", ve)
        g.on("gpsimd", lambda e: e.dma_start(
            out=tot_io["tot_in"].rearrange("a e -> e a"),
            in_=gt["cumT_sb"][:, TOK - 1: TOK]), inc=qg, inck=16)
        g.w("gpsimd", qg)
        g.on("gpsimd", lambda e: e.collective_compute(
            "AllGather", ALU.bypass, replica_groups=[list(range(NCORE))],
            ins=[tot_io["tot_in"]], outs=[tot_io["tot_all"]]), inc=cc)
        g.w("gpsimd", cc)
        g.on("gpsimd", lambda e: e.dma_start(out=gt["tot_sb"][:], in_=tot_io["tot_all"]),
             inc=qg, inck=16)
    else:
        g.on("gpsimd", lambda e: e.dma_start(out=gt["tot_sb"][:], in_=tot_io["tot_all"]),
             inc=qg, inck=16)

    with g.ps("gt_sT", [1, TOK], F32) as sT_ps:
        g.on("scalar", lambda e: e.activation(out=gt["eT_sb"][:], in_=lg_sb[:], func=AF.Exp),
             wait=[ve], inc=se)
        def smm(e):
            for c in range(NCH):
                last = e.matmul(sT_ps[:, CH * c: CH * (c + 1)], cm["ones8_f32"][:],
                                gt["eT_sb"][:, CH * c: CH * (c + 1)], start=True, stop=True)
            return last
        g.on("tensor", smm, wait=[se], inc=pe)
        g.on("scalar", lambda e: e.copy(out=gt["sT_sb"][:], in_=sT_ps[:]),
             wait=[pe], inc=se)
        g.on("vector", lambda e: e.reciprocal(out=gt["rs_sb"][:], in_=gt["sT_sb"][:]),
             wait=[se], inc=ve)
        g.on("vector", lambda e: e.tensor_tensor(out=gt["nin_sb"][:], in0=gt["eT_sb"][:],
                                                 in1=gt["mskT_sb"][:], op=ALU.mult), inc=ve)
        g.run_phase("gate_probs")

    with g.ps("gt_num", [1, TOK], F32) as num_ps:
        def nmm(e):
            for c in range(NCH):
                last = e.matmul(num_ps[:, CH * c: CH * (c + 1)], cm["ones8_f32"][:],
                                gt["nin_sb"][:, CH * c: CH * (c + 1)], start=True, stop=True)
            return last
        g.on("tensor", nmm, wait=[ve], inc=pe)
        g.on("vector", lambda e: e.tensor_tensor(out=gt["gateT_sb"][:], in0=num_ps[:],
                                                 in1=gt["rs_sb"][:], op=ALU.mult),
             wait=[pe], inc=ve)
        g.run_phase("gate_gate")

    with g.ps("gt_offs", [1, 8], F32) as offs_ps, \
         g.ps("gt_offsT", [8, 1], F32) as offsT_ps:
        g.on("tensor", lambda e: e.matmul(offs_ps[:], cm["wsel_sb"][:], gt["tot_sb"][:],
                                          start=True, stop=True), wait=[qg], inc=pe)
        g.on("scalar", lambda e: e.copy(out=gt["offs_sb"][:], in_=offs_ps[:]),
             wait=[pe], inc=se)
        g.on("tensor", lambda e: e.transpose(offsT_ps[:], gt["offs_sb"][:], cm["id1_sb"][:]),
             wait=[se], inc=pe)
        g.on("scalar", lambda e: e.copy(out=gt["offsT_sb"][:], in_=offsT_ps[:]),
             wait=[pe], inc=se)
        g.on("vector", lambda e: e.tensor_scalar(
            out=gt["k8_sb"][:], in0=gt["cumT_sb"][:], scalar1=gt["offsT_sb"][:],
            scalar2=CAP, op0=ALU.add, op1=ALU.is_le), wait=[se], inc=ve)
        g.on("vector", lambda e: e.tensor_tensor(out=gt["k8_sb"][:], in0=gt["k8_sb"][:],
                                                 in1=gt["mskT_sb"][:], op=ALU.mult), inc=ve)
        g.run_phase("gate_keep")

    with g.ps("gt_keep", [1, TOK], F32) as keep_ps, \
         g.ps("gt_fB", [128, TOK], F32) as fB_ps:
        def kmm(e):
            for c in range(NCH):
                last = e.matmul(keep_ps[:, CH * c: CH * (c + 1)], cm["ones8_f32"][:],
                                gt["k8_sb"][:, CH * c: CH * (c + 1)], start=True, stop=True)
            return last
        g.on("tensor", kmm, wait=[ve], inc=pe)
        g.on("vector", lambda e: e.tensor_tensor(out=gt["factorT_sb"][:], in0=keep_ps[:],
                                                 in1=gt["gateT_sb"][:], op=ALU.mult),
             wait=[pe], inc=ve)
        g.on("vector", lambda e: e.tensor_scalar(out=gt["factorT_sb"][:], in0=gt["factorT_sb"][:],
                                                 scalar1=0.5, scalar2=None, op0=ALU.mult), inc=ve)
        def fmm(e):
            for c in range(NCH):
                last = e.matmul(fB_ps[:, CH * c: CH * (c + 1)], cm["onesrow_f32"][:],
                                gt["factorT_sb"][:, CH * c: CH * (c + 1)], start=True, stop=True)
            return last
        g.on("tensor", fmm, wait=[ve], inc=pe)
        g.on("scalar", lambda e: e.copy(out=gt["fB_sb"][:], in_=fB_ps[:]),
             wait=[pe], inc=se)
        g.run_phase("gate_factor")


def emit_expert(g, cm, xT, hT, fB_sb, we_sb, beV_sb, ex_t):
    nc, pe, ve = g.nc, g.sem["pe"], g.sem["ve"]
    se, qs, qg = g.sem["se"], g.sem["qs"], g.sem["qg"]
    with g.ps("ex_y", [128, 2 * CH], F32) as y_ps:
        g.w("tensor", qs, qg, se, ve)
        evacs = []
        for i, (ch, do) in enumerate([(c, d) for c in range(NCH) for d in range(DT)]):
            par = i % 2
            ysl = y_ps[:, CH * par: CH * (par + 1)]
            def mm(e, ch=ch, do=do, ysl=ysl):
                for di in range(DT):
                    last = e.matmul(ysl, we_sb[:, 1024 * di + 128 * do: 1024 * di + 128 * (do + 1)],
                                    hT[:, TOK * di + CH * ch: TOK * di + CH * (ch + 1)],
                                    start=(di == 0), stop=(di == DT - 1))
                return last
            wt = [(ve, evacs[i - 2])] if i >= 2 else []
            g.on("tensor", mm, wait=wt, inc=pe)
            def evac(e, ch=ch, do=do, ysl=ysl):
                xsl = xT[:, TOK * do + CH * ch: TOK * do + CH * (ch + 1)]
                e.scalar_tensor_tensor(out=ex_t[:], in0=ysl, scalar=beV_sb[:, do: do + 1],
                                       in1=fB_sb[:, CH * ch: CH * (ch + 1)],
                                       op0=ALU.add, op1=ALU.mult)
                e.drain()
                return e.tensor_tensor(out=xsl, in0=xsl, in1=ex_t[:], op=ALU.add)
            g.on("vector", evac, wait=[pe], inc=ve)
            evacs.append(ve.n)
        g.run_phase("expert")


def emit_moe_phase(g, cm, xT, weights, totals_mode, tot_io, after_loads=None):
    nc = g.nc
    qs, qg, pe, ve, se = g.sem["qs"], g.sem["qg"], g.sem["pe"], g.sem["ve"], g.sem["se"]
    apply_mode = totals_mode != "out"

    hT = g.sb("moe_hT", [128, DT * TOK], BF16)
    ln = alloc_ln(g)
    gt = alloc_gating(g)
    ro = {
        "wg_sb": g.sb("moe_wg", [128, 8 * DT], F32),
        "ngsum_sb": g.sb("moe_ngsum", [1, 8], F32),
        "rb8_sb": g.sb("moe_rb8", [8, CH], F32),
        "lg_sb": g.sb("moe_lg", [8, TOK], F32),
    }
    if apply_mode:
        we_sb = g.sb("moe_we", [128, DT * 1024], BF16)
        beV_sb = g.sb("moe_beV", [128, 8], F32)
        ex_t = g.sb("moe_ex_t", [128, CH], F32)

    g.on("gpsimd", lambda e: e.dma_start(
        out=ro["wg_sb"][:].rearrange("p (a c) -> p a c", a=DT),
        in_=weights["wg"].rearrange("(a p) c -> p a c", p=128)), inc=qg, inck=16)
    if apply_mode:
        g.on("gpsimd", lambda e: e.dma_start(
            out=we_sb[:].rearrange("p (a c) -> p a c", a=DT),
            in_=weights["we_bf"].rearrange("(a p) c -> p a c", p=128)), inc=qg, inck=16)
    if after_loads is not None:
        after_loads()

    with g.ps("moe_gsum", [1, 8], F32) as gsum_ps:
        def gsum(e):
            for d in range(DT):
                last = e.matmul(gsum_ps[:], cm["ones128_f32"][:],
                                ro["wg_sb"][:, 8 * d: 8 * (d + 1)], start=(d == 0), stop=(d == DT - 1))
            return last
        g.on("tensor", gsum, wait=[qs, qg, ve], inc=pe)
        g.on("vector", lambda e: e.tensor_scalar(out=ro["ngsum_sb"][:], in0=gsum_ps[:],
                                                 scalar1=-1.0, scalar2=None, op0=ALU.mult),
             wait=[pe], inc=ve)
        g.run_phase("moe_prep")

    emit_ln_phase(g, cm, ln, xT, hT, router=ro, name="moe_ln")
    if apply_mode:
        for do in range(DT):
            g.on("gpsimd", lambda e, do=do: e.dma_start(
                out=beV_sb[:, do: do + 1],
                in_=weights["beh"].rearrange("a (f p) -> (a f) p", p=128)[do: do + 1]
                    .rearrange("f p -> p f")),
                inc=qg, inck=16)
    emit_gating(g, cm, ro["lg_sb"], gt, totals_mode, tot_io)
    if not apply_mode:
        return
    emit_expert(g, cm, xT, hT, gt["fB_sb"], we_sb, beV_sb, ex_t)


def build_moe(totals_mode):
    nc = bass.Bass("TRN2", target_bir_lowering=False)
    x_in = decl_in(nc, "x", [D, TOK])
    wg = decl_in(nc, "wg", [D, 8])
    cd = const_decls_subset(nc, ("id8", "id128", "id1", "wsel"))
    tot_io = {}
    weights = {"wg": wg}
    apply_mode = totals_mode != "out"
    if totals_mode == "out":
        tot_io["tot_out_ext"] = decl_out(nc, "tot", [8, 1])
    else:
        tot_io["tot_all"] = decl_in(nc, "tot_all", [8, 8])
        we = decl_in(nc, "we", [D, D])
        be = decl_in(nc, "be", [1, D])
        x_out = decl_out(nc, "x_out", [D, TOK])
        weights["we_bf"] = nc.dram_tensor("we_bf", [D, D], BF16).ap()
        weights["beh"] = nc.dram_tensor("beh", [1, D], BF16).ap()

    with ExitStack() as ctx:
        g = G(nc, ctx)
        cm = alloc_common(g, cd, ("id8", "id128", "id1", "wsel"))
        xT = g.sb("xT", [128, DT * TOK], F32)
        load_xT(g, xT, x_in)
        if apply_mode:
            emit_casts(g, [(weights["we_bf"], we), (weights["beh"], be)])
        g.w("tensor", g.sem["qs"])
        g.w("scalar", g.sem["qs"])
        g.w("vector", g.sem["qs"])
        g.run_phase("load")
        emit_moe_phase(g, cm, xT, weights, totals_mode, tot_io)
        if apply_mode:
            store_xT(g, xT, x_out)
        else:
            g.run_phase("end")
    return nc


SH_Q = 128 * TOK
SH_K = 128 * TOK
SH_V = TOK * 130
SHARD = SH_Q + SH_K + SH_V
RET = 128 * TOK


def emit_qkv(g, cm, xT, hT, w_bf, sendqk, sendv_d, rope_on, v_own=None,
             on_qk_done=None, w_wait=()):
    nc = g.nc
    qs, qg, pe, ve, se = g.sem["qs"], g.sem["qg"], g.sem["pe"], g.sem["ve"], g.sem["se"]
    sendv = sendqk

    w_sb2 = g.sb("qkv_w", [128, 2 * DT * 1024], BF16)
    o_sb = g.sb("qkv_o", [128, 2 * TOK], BF16)
    z_sb = g.sb("qkv_z", [128, TOK], BF16) if rope_on else None
    if v_own is None:
        v_own = g.sb("qkv_vown", [128, 16 * 1040], BF16)

    wslot = {"wq": 0, "wk": 1, "wv": 0}
    wsems = (g.sem["wa"], g.sem["wb"])

    def load_w(which, wait_extra=()):
        par = wslot[which]
        g.on("sync", lambda e, par=par: e.dma_start(
            out=w_sb2[:, DT * 1024 * par: DT * 1024 * (par + 1)].rearrange("p (a c) -> p a c", a=DT),
            in_=w_bf[which].rearrange("(a p) c -> p a c", p=128)),
            wait=[qg] + list(wait_extra) + list(w_wait), inc=wsems[par], inck=16)
        return wsems[par].n

    rope_prev = [None]
    with g.ps("qkv_q", [128, 2 * CH], F32) as q_ps, \
         g.ps("qkv_zp", [128, TOK], F32) as z_ps:
        for which, off in (("wq", 0), ("wk", SH_Q)):
            if which == "wq":
                wn = load_w("wq")
                kn = load_w("wk")
                wcur = wslot["wq"]
            else:
                wn, wcur = kn, wslot["wk"]
            g.on("tensor", None, wait=[(wsems[wcur], wn)])
            w_sb = w_sb2[:, DT * 1024 * wcur: DT * 1024 * (wcur + 1)]
            evl = []
            for m in range(DT):
                par = m % 2
                ot = o_sb[:, TOK * par: TOK * (par + 1)]
                for ch in range(NCH):
                    i = m * NCH + ch
                    ysl = q_ps[:, CH * (ch % 2): CH * (ch % 2 + 1)]
                    def mm(e, m=m, ch=ch, ysl=ysl):
                        for d in range(DT):
                            last = e.matmul(ysl, w_sb[:, 1024 * d + 128 * m: 1024 * d + 128 * (m + 1)],
                                            hT[:, TOK * d + CH * ch: TOK * d + CH * (ch + 1)],
                                            start=(d == 0), stop=(d == DT - 1))
                        return last
                    g.on("tensor", mm, wait=[(se, evl[i - 2])] if i >= 2 else [se], inc=pe)
                    g.on("scalar", lambda e, ch=ch, ysl=ysl, ot=ot: e.copy(
                        out=ot[:, CH * ch: CH * (ch + 1)], in_=ysl), wait=[pe, qg], inc=se)
                    evl.append(se.n)
                if rope_on:
                    zw = [(se, evl[-1])]
                    def zmm(e, ot=ot):
                        for ch in range(NCH):
                            last = e.matmul(z_ps[:, CH * ch: CH * (ch + 1)],
                                            cm["PT_sb"][:], ot[:, CH * ch: CH * (ch + 1)],
                                            start=True, stop=True)
                        return last
                    g.on("tensor", zmm, wait=zw, inc=pe)
                    zevw = [(pe, pe.n)]
                    if rope_prev[0] is not None:
                        zevw.append((ve, rope_prev[0]))
                    g.on("scalar", lambda e: e.copy(out=z_sb[:], in_=z_ps[:]),
                         wait=zevw, inc=se)
                    zn = se.n
                    def rope(e, ot=ot):
                        e.tensor_tensor(out=z_sb[:], in0=z_sb[:],
                                        in1=cm["ropeS_sb"][:], op=ALU.mult)
                        e.drain()
                        e.tensor_tensor(out=ot[:], in0=ot[:],
                                        in1=cm["ropeC_sb"][:], op=ALU.mult)
                        e.drain()
                        last = e.tensor_tensor(out=ot[:], in0=ot[:], in1=z_sb[:], op=ALU.add)
                        e.drain()
                        return last
                    g.on("vector", rope, wait=[(se, zn)], inc=ve)
                    rope_prev[0] = ve.n
                g.w("gpsimd", se, ve)
                g.on("gpsimd", lambda e, m=m, off=off, ot=ot: e.dma_start(
                    out=sendv[m, off + 0: off + 128 * TOK].rearrange("(p t) -> p t", p=128),
                    in_=ot[:]), inc=qg, inck=16)
            g.run_phase("qkv_" + which)

    if on_qk_done is not None:
        on_qk_done()

    vn = load_w("wv", wait_extra=[pe])
    w_sb = w_sb2[:, 0: DT * 1024]
    g.on("vector", lambda e: e.memset(
        v_own[:].rearrange("p (t f) -> p t f", f=65)[:, :, 64:65], 1.0),
        wait=[(g.sem["pe"], g.sem["pe"].n)], inc=ve)
    with g.ps("qkv_v", [128, 2 * CH], F32) as v_ps:
        g.on("tensor", None, wait=[(wsems[0], vn)])
        g.w("tensor", ve)
        for tt in range(16):
            for c2 in range(2):
                i = tt * 2 + c2
                ysl = v_ps[:, CH * (i % 2): CH * (i % 2 + 1)]
                def vmm(e, tt=tt, c2=c2, ysl=ysl):
                    for d in range(DT):
                        last = e.matmul(ysl, hT[:, TOK * d + 128 * tt: TOK * d + 128 * (tt + 1)],
                                        w_sb[:, 1024 * d + CH * c2: 1024 * d + CH * (c2 + 1)],
                                        start=(d == 0), stop=(d == DT - 1))
                    return last
                g.on("tensor", vmm, wait=[se] if i >= 2 else [], inc=pe)
                st = 1040 * tt + 520 * c2
                def vev(e, st=st, ysl=ysl):
                    last = e.tensor_scalar(
                        out=v_own[:, st: st + 520].rearrange("p (h f) -> p h f", f=65)[:, :, 0:64],
                        in0=ysl.rearrange("p (h f) -> p h f", f=64),
                        scalar1=1.0, scalar2=None, op0=ALU.mult)
                    e.drain()
                    return last
                g.on("vector", vev, wait=[pe], inc=se)
        g.w("sync", se)
        for j in range(NCORE):
            g.on("sync", lambda e, j=j: e.dma_start(
                out=sendv_d[j, 0: SH_V].rearrange("(t p f) -> p t f", p=128, f=130),
                in_=v_own[:].rearrange("p (t f) -> p t f", f=1040)[:, :, 130 * j: 130 * (j + 1)]),
                inc=qs, inck=16)
        g.run_phase("qkv_v")


def emit_flash(g, cm, recvqk, recvv, ret, causal, bg=None, wait_qk=None, wait_v=None):
    nc = g.nc
    qs, qg, pe, ve, se = g.sem["qs"], g.sem["qg"], g.sem["pe"], g.sem["ve"], g.sem["se"]

    qf = g.sb("fl_q", [128, 2 * SEQ], BF16)
    kf = g.sb("fl_k", [128, 2 * SEQ], BF16)
    vf = g.sb("fl_v", [128, 2 * 64 * 130], BF16)
    p_sb = g.sb("fl_p", [128, 3 * 1024], BF16)

    first_s, first_g = [wait_qk] if wait_qk else [], [wait_v] if wait_v else []
    for b in range(2):
        for r4 in range(4):
            r = 4 * b + r4
            g.on("sync", lambda e, b=b, r4=r4, r=r: e.dma_start(
                out=qf[:, SEQ * b + TOK * r4: SEQ * b + TOK * (r4 + 1)],
                in_=recvqk[r, 0: SH_Q].rearrange("(p t) -> p t", p=128)),
                wait=first_s, inc=qs, inck=16)
            first_s = []
            g.on("sync", lambda e, b=b, r4=r4, r=r: e.dma_start(
                out=kf[:, SEQ * b + TOK * r4: SEQ * b + TOK * (r4 + 1)],
                in_=recvqk[r, SH_Q: SH_Q + SH_K].rearrange("(p t) -> p t", p=128)), inc=qs, inck=16)
            g.on("gpsimd", lambda e, b=b, r4=r4, r=r: e.dma_start(
                out=vf[:].rearrange("p (t f) -> p t f", f=130)[:, 64 * b + 16 * r4: 64 * b + 16 * (r4 + 1), :],
                in_=recvv[r, 0: SH_V].rearrange("(t p f) -> p t f", p=128, f=130)),
                wait=first_g, inc=qg, inck=16)
            first_g = []
    g.run_phase("fl_load")
    bw_se, bw_ve = g.sem["se"].n, g.sem["ve"].n

    with g.ps("fl_s", [128, 2048], F32) as s_ps, \
         g.ps("fl_o", [65, 2048], F32) as o_ps:
        oraw = g.sb("fl_oraw", [65, 2 * 2048], BF16)
        bcB2 = g.sb("fl_bcB2", [64, 2048], BF16)
        o_sb2 = g.sb("fl_osb2", [64, 2048], BF16)
        dscr = g.nc.dram_tensor(g.prefix + "fl_dscr", [2, 2048], BF16).ap()
        n0, n1 = g.sem["n0"], g.sem["n1"]
        nrm_ve = [g.sem["ve"].n]
        oev_se = [g.sem["se"].n]
        otts_done = {}
        send_done = {}
        qcidx = [0]
        prev_exp = None
        g.w("tensor", qs, qg, ve)
        for b in range(2):
            for qc in range(8):
                units = []
                for qh in range(2):
                    nk = (8 * qc + 4 * (qh + 1)) if causal else 64
                    for kt in range(nk):
                        units.append((qh, kt, kt == 0, kt == nk - 1))
                NU = len(units)
                exp_done = {}
                s_done = {}

                def emit_av(u):
                    qh, kt, first, last = units[u]
                    psl = p_sb[:, 1024 * (u % 3): 1024 * (u % 3 + 1)]
                    diag = causal and kt >= 8 * qc + 4 * qh
                    if diag:
                        off = 896 - 128 * (kt - 8 * qc) + 512 * qh
                        def dmask(e, psl=psl, off=off):
                            e.tensor_tensor(out=psl[:, 0:512], in0=psl[:, 0:512],
                                            in1=cm["Tbase_sb"][:, off: off + 512], op=ALU.mult)
                            e.drain()
                            last_i = e.tensor_tensor(out=psl[:, 512:1024], in0=psl[:, 512:1024],
                                                     in1=cm["Tbase_sb"][:, off: off + 512], op=ALU.mult)
                            e.drain()
                            return last_i
                        g.on("vector", dmask, wait=[(se, exp_done[u])], inc=ve)
                        avwait = [(ve, ve.n)]
                    else:
                        avwait = [(se, exp_done[u])]
                    if first:
                        avwait.append((se, oev_se[0]))
                    def avmm(e, kt=kt, qh=qh, psl=psl, first=first, last=last, b=b):
                        for h in range(2):
                            last_i = e.matmul(
                                o_ps[:, 1024 * h + 512 * qh: 1024 * h + 512 * (qh + 1)],
                                vf[:, 8320 * b + 130 * kt + 65 * h: 8320 * b + 130 * kt + 65 * (h + 1)],
                                psl[:, 512 * h: 512 * (h + 1)],
                                start=first, stop=last)
                        return last_i
                    g.on("tensor", avmm, wait=avwait, inc=pe)

                for u, (qh, kt, first, last) in enumerate(units):
                    par = u % 2
                    ssl = s_ps[:, 1024 * par: 1024 * (par + 1)]
                    q0 = SEQ * b + 1024 * qc + 512 * qh
                    def smm(e, b=b, kt=kt, q0=q0, ssl=ssl):
                        for h, tp in ((0, (0, 0)), (1, (64, 0))):
                            base = 64 * h
                            last_i = e.matmul(
                                ssl[:, 512 * h: 512 * (h + 1)],
                                kf[base:base + 64, SEQ * b + 128 * kt: SEQ * b + 128 * (kt + 1)],
                                qf[base:base + 64, q0: q0 + 512],
                                start=True, stop=True, tile_position=tp)
                        return last_i
                    if u >= 2:
                        swait = [(se, exp_done[u - 2])]
                    elif prev_exp is None:
                        swait = [(se, bw_se), (ve, bw_ve)]
                    else:
                        swait = [(se, prev_exp[u])]
                    g.on("tensor", smm, wait=swait, inc=pe)
                    s_done[u] = pe.n
                    psl = p_sb[:, 1024 * (u % 3): 1024 * (u % 3 + 1)]
                    g.on("scalar", lambda e, psl=psl, ssl=ssl: e.activation(
                        out=psl, in_=ssl, func=AF.Exp, scale=ATTN_SCALE),
                        wait=[(pe, s_done[u])], inc=se)
                    exp_done[u] = se.n
                    if u >= 1:
                        emit_av(u - 1)
                emit_av(NU - 1)
                prev_exp = (exp_done[NU - 2], exp_done[NU - 1])

                qci = qcidx[0]
                qcidx[0] += 1
                slot = qci % 2
                osl = oraw[:, 2048 * slot: 2048 * (slot + 1)]
                bsl = bcB2[:]
                csl = o_sb2[:]
                oevw = [(pe, pe.n)]
                if qci >= 2:
                    oevw.append((ve, otts_done[qci - 2]))
                g.on("scalar", lambda e, osl=osl: e.copy(out=osl, in_=o_ps[:]),
                     wait=oevw, inc=se)
                oev_se[0] = se.n
                def recips(e, osl=osl):
                    with g.nc.allow_low_precision(reason="softmax denom recip in bf16; denominator sums bf16 terms"):
                        for h in range(2):
                            e.reciprocal(out=osl[64:65, 1024 * h: 1024 * (h + 1)],
                                         in_=osl[64:65, 1024 * h: 1024 * (h + 1)])
                            e.drain()
                    return e.nop()
                g.on("vector", recips, wait=[(se, oev_se[0])], inc=ve)
                rec_ve = ve.n
                g.on("gpsimd", lambda e, slot=slot, osl=osl: e.dma_start(
                    out=dscr[slot: slot + 1, :], in_=osl[64:65, :]),
                    wait=[(ve, rec_ve)], inc=n0, inck=16)
                g.on("gpsimd", lambda e, slot=slot, bsl=bsl: e.dma_start(
                    out=bsl, in_=dscr[slot: slot + 1, :].partition_broadcast(64)),
                    wait=[(n0, g.sem["n0"].n)], inc=n1, inck=16)
                bc_n1 = n1.n
                ottw = [(n1, bc_n1)]
                if qci >= 1:
                    ottw.append((qg, send_done[qci - 1]))
                def otts(e, osl=osl, bsl=bsl, csl=csl):
                    last_i = e.tensor_tensor(out=csl, in0=osl[0:64, :], in1=bsl, op=ALU.mult)
                    e.drain()
                    return last_i
                g.on("vector", otts, wait=ottw, inc=ve)
                otts_done[qci] = ve.n
                nrm_ve[0] = ve.n
                rdest = 4 * b + qc // 2
                col = 1024 * (qc % 2)
                for h in range(2):
                    g.on("gpsimd", lambda e, rdest=rdest, h=h, col=col, csl=csl: e.dma_start(
                        out=ret.rearrange("r (p t) -> r p t", p=128)[rdest, 64 * h: 64 * (h + 1), col: col + 1024],
                        in_=csl[:, 1024 * h: 1024 * (h + 1)]),
                        wait=[(ve, otts_done[qci])], inc=qg, inck=16)
                send_done[qci] = qg.n
                if bg is not None:
                    for _ in range(11):
                        if next(bg, "END") == "END":
                            bg = None
                            break
            g.run_phase("fl_b%d" % b)
        if bg is not None:
            for _ in iter(lambda: next(bg, "END"), "END"):
                pass
        g.w("gpsimd", qg)
        g.run_phase("fl_end")


def emit_outproj(g, cm, xT, ret_recv, wo_bf, bo_bf, w_wait=()):
    nc = g.nc
    qs, pe, ve, se = g.sem["qs"], g.sem["pe"], g.sem["ve"], g.sem["se"]
    if w_wait:
        g.on("sync", None, wait=list(w_wait))
    at_sb = g.sb("pr_at", [128, DT * TOK], BF16)
    wo_sb = g.sb("pr_wo", [128, DT * 1024], BF16)
    boh_sb = g.sb("pr_bo", [1, 1024], BF16)
    for r in range(DT):
        g.on("sync", lambda e, r=r: e.dma_start(
            out=at_sb[:, TOK * r: TOK * (r + 1)],
            in_=ret_recv[r, :].rearrange("(p t) -> p t", p=128)),
            wait=[g.sem["qg"]], inc=qs, inck=16)
    g.on("sync", lambda e: e.dma_start(
        out=wo_sb[:].rearrange("p (a c) -> p a c", a=DT),
        in_=wo_bf.rearrange("(a p) c -> p a c", p=128)), wait=[g.sem["qg"]], inc=qs, inck=16)
    g.on("sync", lambda e: e.dma_start(out=boh_sb[:], in_=bo_bf),
         wait=[g.sem["qg"]], inc=qs, inck=16)
    with g.ps("pr_u", [128, 2 * CH], F32) as u_ps:
        g.w("tensor", qs)
        evacs = []
        for i, (dcol, ch) in enumerate([(dc, c) for dc in range(DT) for c in range(NCH)]):
            ysl = u_ps[:, CH * (i % 2): CH * (i % 2 + 1)]
            def mm(e, dcol=dcol, ch=ch, ysl=ysl):
                e.matmul(ysl, boh_sb[:, 128 * dcol: 128 * (dcol + 1)],
                         cm["onesrow_bf"][:], start=True, stop=False)
                for it in range(DT):
                    last = e.matmul(ysl, wo_sb[:, 1024 * it + 128 * dcol: 1024 * it + 128 * (dcol + 1)],
                                    at_sb[:, TOK * it + CH * ch: TOK * it + CH * (ch + 1)],
                                    start=False, stop=(it == DT - 1))
                return last
            g.on("tensor", mm, wait=[(ve, evacs[i - 2])] if i >= 2 else [], inc=pe)
            def evac(e, dcol=dcol, ch=ch, ysl=ysl):
                xsl = xT[:, TOK * dcol + CH * ch: TOK * dcol + CH * (ch + 1)]
                return e.tensor_tensor(out=xsl, in0=xsl, in1=ysl, op=ALU.add)
            g.on("vector", evac, wait=[pe], inc=ve)
            evacs.append(g.sem["ve"].n)
        g.run_phase("outproj")


def build_qkv(layer):
    nc = bass.Bass("TRN2", target_bir_lowering=False)
    x_in = decl_in(nc, "x", [D, TOK])
    rope_on = layer == 0
    cnames = ("PT", "ropeC", "ropeS") if rope_on else ()
    cd = const_decls_subset(nc, cnames)
    send = decl_out(nc, "send", [NCORE, SHARD], BF16)
    w_bf = {k: nc.dram_tensor("w_" + k, [D, D], BF16).ap() for k in ("wq", "wk", "wv")}
    if layer == 0:
        wq = decl_in(nc, "Wq", [D, D])
        wkv = decl_in(nc, "Wkv", [D, 2 * D])
        casts = [(w_bf["wq"], wq), (w_bf["wk"], wkv[:, 0:D]), (w_bf["wv"], wkv[:, D:2 * D])]
    else:
        casts = [(w_bf[k], decl_in(nc, n, [D, D]))
                 for k, n in (("wq", "Wq"), ("wk", "Wk"), ("wv", "Wv"))]
    with ExitStack() as ctx:
        g = G(nc, ctx)
        cm = alloc_common(g, cd, cnames)
        xT = g.sb("xT", [128, DT * TOK], F32)
        hT = g.sb("at_hT", [128, DT * TOK], BF16)
        load_xT(g, xT, x_in)
        emit_casts(g, casts)
        g.w("tensor", g.sem["qs"])
        g.w("scalar", g.sem["qs"])
        g.w("vector", g.sem["qs"])
        g.run_phase("load")
        vown = g.sb("bq_vown", [128, 16 * 1040], BF16)
        ln = alloc_ln(g, sq=vown[:, 0: DT * CH])
        emit_ln_phase(g, cm, ln, xT, hT, name="at_ln")
        emit_qkv(g, cm, xT, hT, w_bf, send[:, 0: SH_Q + SH_K],
                 send[:, SH_Q + SH_K: SHARD], rope_on, v_own=vown)
    return nc


def build_flash(causal):
    nc = bass.Bass("TRN2", target_bir_lowering=False)
    cnames = ("Tbase",) if causal else ()
    cd = const_decls_subset(nc, cnames)
    recv = decl_in(nc, "recv", [NCORE, SHARD], BF16)
    ret = decl_out(nc, "ret", [NCORE, RET], BF16)
    with ExitStack() as ctx:
        g = G(nc, ctx)
        cm = alloc_common(g, cd, cnames)
        g.run_phase("load")
        emit_flash(g, cm, recv[:, 0: SH_Q + SH_K], recv[:, SH_Q + SH_K: SHARD],
                   ret, causal)
    return nc


def build_proj():
    nc = bass.Bass("TRN2", target_bir_lowering=False)
    x_in = decl_in(nc, "x", [D, TOK])
    ret_recv = decl_in(nc, "ret_recv", [NCORE, RET], BF16)
    wo = decl_in(nc, "Wo", [D, D])
    bo = decl_in(nc, "bo", [1, D])
    x_out = decl_out(nc, "x_out", [D, TOK])
    wo_bf = nc.dram_tensor("wo_bf", [D, D], BF16).ap()
    bo_bf = nc.dram_tensor("bo_bf", [1, D], BF16).ap()
    with ExitStack() as ctx:
        g = G(nc, ctx)
        cm = alloc_common(g, cd := {}, ())
        xT = g.sb("xT", [128, DT * TOK], F32)
        load_xT(g, xT, x_in)
        emit_casts(g, [(wo_bf, wo), (bo_bf, bo)])
        g.w("tensor", g.sem["qs"])
        g.w("vector", g.sem["qs"])
        g.run_phase("load")
        emit_outproj(g, cm, xT, ret_recv, wo_bf, bo_bf)
        store_xT(g, xT, x_out)
    return nc


def run_attn_subblock(xs_parts, layer, wdict):
    consts = get_consts()
    nc1 = build_qkv(layer)
    cnames = ("PT", "ropeC", "ropeS") if layer == 0 else ()
    im = []
    for c in range(NCORE):
        m = {"x": xs_parts[c], **{k: consts[c][k] for k in cnames}}
        if layer == 0:
            m["Wq"] = wdict["Wq"]
            m["Wkv"] = wdict["Wkv"]
        else:
            m["Wq"], m["Wk"], m["Wv"] = wdict["Wq"], wdict["Wk"], wdict["Wv"]
        im.append(m)
    res = run_bass_kernel_spmd(nc1, im, core_ids=list(range(NCORE)))
    sends = [np.asarray(res.results[c]["send"]) for c in range(NCORE)]
    recvs = [np.ascontiguousarray(np.stack([sends[r][c] for r in range(NCORE)]))
             for c in range(NCORE)]
    nc2 = build_flash(causal=(layer == 1))
    cnames2 = ("Tbase",) if layer == 1 else ()
    im = [{"recv": recvs[c], **{k: consts[c][k] for k in cnames2}} for c in range(NCORE)]
    res = run_bass_kernel_spmd(nc2, im, core_ids=list(range(NCORE)))
    rets = [np.asarray(res.results[c]["ret"]) for c in range(NCORE)]
    ret_recvs = [np.ascontiguousarray(np.stack([rets[r][c] for r in range(NCORE)]))
                 for c in range(NCORE)]
    nc3 = build_proj()
    wo = wdict["Wout"] if layer == 0 else wdict["Wo"]
    bo = wdict["bout"] if layer == 0 else wdict["bo"]
    im = [{"x": xs_parts[c], "ret_recv": ret_recvs[c], "Wo": wo,
           "bo": bo.reshape(1, D)} for c in range(NCORE)]
    res = run_bass_kernel_spmd(nc3, im, core_ids=list(range(NCORE)))
    return [np.asarray(res.results[c]["x_out"]) for c in range(NCORE)]


def alloc_quant(g):
    return {"wt": g.sb("qz_wt", [128, 2 * 1024], F32),
            "racc": g.sb("qz_racc", [128, 32], F32),
            "r1": g.sb("qz_r1", [128, 1], F32),
            "rrow": g.sb("qz_rrow", [1, 128], F32),
            "tt1": g.sb("qz_t1", [1, 1], F32),
            "rB": g.sb("qz_rB", [128, 1], F32),
            "a": g.sb("qz_a", [128, 1024], BF16),
            "q": g.sb("qz_q", [128, 2 * 1024], BF16)}


def emit_quant(g, cm, w_in, wq_out, kind, scale_half, name, scr, bufs):
    qs, ve = g.sem["qs"], g.sem["ve"]
    wp = (g.sem["wa"], g.sem["wb"])
    sp = (g.sem["sc"], g.sem["sd"])
    wt, racc = bufs["wt"], bufs["racc"]
    r1, rrow, tt1, rB = bufs["r1"], bufs["rrow"], bufs["tt1"], bufs["rB"]
    a_sb, q_sb = bufs["a"], bufs["q"]
    NG = 32

    def src(dram, i):
        if kind == "w1":
            return dram[128 * i: 128 * (i + 1), :]
        r, c = i // 4, i % 4
        return dram[128 * r: 128 * (r + 1), 1024 * c: 1024 * (c + 1)]

    def wview(par):
        return wt[:, 1024 * par: 1024 * (par + 1)]

    def qview(par):
        return q_sb[:, 1024 * par: 1024 * (par + 1)]

    red_done = {}
    for i in range(NG):
        par = i % 2
        g.on("sync", lambda e, i=i, par=par: e.dma_start(
            out=wview(par), in_=src(w_in, i)),
            wait=[(ve, red_done[i - 2])] if i >= 2 else [ve], inc=wp[par], inck=16)
        ld = wp[par].n
        g.on("vector", lambda e, i=i, par=par: e.tensor_reduce(
            out=racc[:, i: i + 1], in_=wview(par),
            axis=AX.X, op=ALU.add, apply_absolute_value=True),
            wait=[(wp[par], ld)], inc=ve)
        red_done[i] = ve.n
        yield

    qa, gp = g.sem["qa"], g.sem["gp"]
    g.on("vector", lambda e: e.tensor_reduce(out=r1[:], in_=racc[:, 0:NG],
                                             axis=AX.X, op=ALU.add),
         wait=[(ve, red_done[NG - 1])], inc=ve)
    g.on("sync", lambda e: e.dma_start(out=scr.rearrange("a b -> b a"), in_=r1[:]),
         wait=[ve], inc=qa, inck=16)
    g.on("sync", lambda e: e.dma_start(out=rrow[:], in_=scr), wait=[qa], inc=gp, inck=16)
    def gam(e):
        e.tensor_reduce(out=tt1[:], in_=rrow[:], axis=AX.X, op=ALU.add)
        e.drain()
        e.tensor_scalar(out=tt1[:], in0=tt1[:], scalar1=1.0 / (4096.0 * 1024.0),
                        scalar2=QEPS, op0=ALU.mult, op1=ALU.add)
        e.drain()
        return e.reciprocal(out=tt1[:], in_=tt1[:])
    g.on("vector", gam, wait=[gp], inc=ve)
    g.on("sync", lambda e: e.dma_start(out=scr[0:1, 0:1], in_=tt1[:]),
         wait=[ve], inc=qa, inck=16)
    g.on("sync", lambda e: e.dma_start(out=rB[:], in_=scr[0:1, 0:1].partition_broadcast(128)),
         wait=[qa], inc=gp, inck=16)
    rB_ready = gp.n
    yield

    qz_done = {}
    st_done = {}
    for i in range(NG):
        par = i % 2
        g.on("sync", lambda e, i=i, par=par: e.dma_start(
            out=wview(par), in_=src(w_in, i)),
            wait=[(ve, qz_done[i - 2])] if i >= 2 else [(ve, red_done[NG - 1])],
            inc=wp[par], inck=16)
        ld = wp[par].n
        def qz(e, i=i, par=par):
            w = wview(par)
            qsl = qview(par)
            e.tensor_scalar(out=a_sb[:], in0=w, scalar1=rB[:], scalar2=0.5,
                            op0=ALU.mult, op1=ALU.is_ge)
            e.drain()
            e.tensor_scalar(out=qsl, in0=w, scalar1=rB[:], scalar2=-0.5,
                            op0=ALU.mult, op1=ALU.is_le)
            e.drain()
            e.tensor_tensor(out=qsl, in0=a_sb[:], in1=qsl, op=ALU.subtract)
            e.drain()
            if scale_half:
                e.tensor_scalar(out=qsl, in0=qsl, scalar1=0.5, scalar2=None, op0=ALU.mult)
                e.drain()
            return e.nop()
        wz = [(wp[par], ld), (g.sem["gp"], rB_ready),
              (sp[par], st_done[i - 2] if i >= 2 else sp[par].n)]
        g.on("vector", qz, wait=wz, inc=ve)
        qz_done[i] = ve.n
        g.on("sync", lambda e, i=i, par=par: e.dma_start(
            out=src(wq_out, i), in_=qview(par)),
            wait=[(ve, qz_done[i])], inc=sp[par], inck=16)
        st_done[i] = sp[par].n
        yield


def emit_ffn(g, cm, xT, ln, wq1, wq2, b1v_sb, b2v_sb, hT_ch, hff, w1T_sb, w2T_sb):
    nc = g.nc
    qs, qg, pe, ve, se = g.sem["qs"], g.sem["qg"], g.sem["pe"], g.sem["ve"], g.sem["se"]
    wpar_sem = (g.sem["wa"], g.sem["wb"], g.sem["sc"])

    for d in range(DT):
        g.on("sync", lambda e, d=d: e.dma_start_transpose(
            out=w1T_sb[:, 4096 * d: 4096 * (d + 1)], in_=wq1[:, 128 * d: 128 * (d + 1)]),
            wait=[pe, qg], inc=qs, inck=16)
    g.run_phase("ffn_w1t")

    for tch2 in range(2):
        emit_ln_phase(g, cm, ln, xT, hT_ch, chunks=[2 * tch2, 2 * tch2 + 1],
                      hT_off=CH * 2 * tch2, name="ffn_ln%d" % tch2)
        for half in range(2):
            tch = 2 * tch2 + half
            hoff = CH * half
            with g.ps("ffn_h1_%d" % tch, [128, 2 * CH], F32) as h1_ps:
                if half == 0:
                    g.w("tensor", qs, g.sem["ve"])
                relus = []
                for f in range(32):
                    ysl = h1_ps[:, CH * (f % 2): CH * (f % 2 + 1)]
                    def mm1(e, f=f, ysl=ysl, hoff=hoff):
                        for d in range(DT):
                            last = e.matmul(ysl, w1T_sb[:, 4096 * d + 128 * f: 4096 * d + 128 * (f + 1)],
                                            hT_ch[:, 1024 * d + hoff: 1024 * d + hoff + CH],
                                            start=(d == 0), stop=(d == DT - 1))
                        return last
                    g.on("tensor", mm1, wait=[(se, relus[f - 2])] if f >= 2 else [], inc=pe)
                    g.on("scalar", lambda e, f=f, ysl=ysl: e.activation(
                        out=hff[:, CH * f: CH * (f + 1)], in_=ysl, func=AF.Relu,
                        bias=b1v_sb[:, f: f + 1]),
                        wait=[pe], inc=se)
                    relus.append(se.n)
                relu_last = se.n
                g.run_phase("ffn_m1_%d" % tch)
            with g.ps("ffn_u_%d" % tch, [128, DT * CH], F32) as u_ps:
                mmdone = []
                for f in range(32):
                    par = f % 3
                    wsem = wpar_sem[par]
                    g.on("sync", lambda e, f=f, par=par: e.dma_start_transpose(
                        out=w2T_sb[:, 1024 * par: 1024 * (par + 1)],
                        in_=wq2[:, 128 * f: 128 * (f + 1)]),
                        wait=[(pe, mmdone[f - 3])] if f >= 3 else [pe],
                        inc=wsem, inck=16)
                    def mm2(e, f=f, par=par):
                        for dc in range(DT):
                            last = e.matmul(u_ps[:, CH * dc: CH * (dc + 1)],
                                            w2T_sb[:, 1024 * par + 128 * dc: 1024 * par + 128 * (dc + 1)],
                                            hff[:, CH * f: CH * (f + 1)],
                                            start=(f == 0), stop=(f == 31 and dc == DT - 1))
                        return last
                    g.on("tensor", mm2,
                         wait=[wsem, (se, relu_last)] if f == 0 else [wsem], inc=pe)
                    mmdone.append(pe.n)
                def evac(e, tch=tch):
                    for dc in range(DT):
                        xsl = xT[:, TOK * dc + CH * tch: TOK * dc + CH * (tch + 1)]
                        e.scalar_tensor_tensor(out=xsl, in0=u_ps[:, CH * dc: CH * (dc + 1)],
                                               scalar=b2v_sb[:, dc: dc + 1], in1=xsl,
                                               op0=ALU.add, op1=ALU.add)
                        last = e.drain()
                    return last
                g.on("vector", evac, wait=[pe], inc=ve)
                g.run_phase("ffn_m2_%d" % tch)


def emit_w2t_pass(g, wq2, wq2t, tx=None):
    wp = (g.sem["wa"], g.sem["wb"])
    sp = (g.sem["sc"], g.sem["sd"])
    if tx is None:
        tx = g.sb("w2tp_tx", [128, 1024], BF16)
    st_done = {}
    for f in range(32):
        par = f % 2
        g.on("sync", lambda e, f=f: e.dma_start_transpose(
            out=tx[:], in_=wq2[:, 128 * f: 128 * (f + 1)]),
            wait=[(sp[0], st_done[f - 1])] if f >= 1 else [(sp[0], sp[0].n), (sp[1], sp[1].n)],
            inc=wp[par], inck=16)
        ld = wp[par].n
        g.on("sync", lambda e, f=f: e.dma_start(
            out=wq2t[128 * f: 128 * (f + 1), :], in_=tx[:]),
            wait=[(wp[par], ld)], inc=sp[0], inck=16)
        st_done[f] = sp[0].n
    g.run_phase("w2tp")


def build_quant():
    nc = bass.Bass("TRN2", target_bir_lowering=False)
    w1 = decl_in(nc, "W1", [DFF, D])
    w2 = decl_in(nc, "W2", [D, DFF])
    wq1 = decl_out(nc, "wq1", [DFF, D], BF16)
    wq2 = decl_out(nc, "wq2", [D, DFF], BF16)
    scr1 = nc.dram_tensor("scr1", [1, 128], F32).ap()
    scr2 = nc.dram_tensor("scr2", [1, 128], F32).ap()
    with ExitStack() as ctx:
        g = G(nc, ctx)
        cm = alloc_common(g, {}, ())
        g.run_phase("init")
        qb = alloc_quant(g)
        for _ in emit_quant(g, cm, w1, wq1, "w1", False, "q1", scr1, qb):
            pass
        for _ in emit_quant(g, cm, w2, wq2, "w2", True, "q2", scr2, qb):
            pass
        g.w("gpsimd", g.sem["qs"], g.sem["ve"])
        g.run_phase("qdone")
    return nc


def build_ffn():
    nc = bass.Bass("TRN2", target_bir_lowering=False)
    x_in = decl_in(nc, "x", [D, TOK])
    wq1 = decl_in(nc, "wq1", [DFF, D], BF16)
    wq2 = decl_in(nc, "wq2", [D, DFF], BF16)
    b1 = decl_in(nc, "b1", [1, DFF])
    b2 = decl_in(nc, "b2", [1, D])
    x_out = decl_out(nc, "x_out", [D, TOK])
    wq2t = nc.dram_tensor("i_wq2t", [DFF, D], BF16).ap()
    with ExitStack() as ctx:
        g = G(nc, ctx)
        cm = alloc_common(g, {}, ())
        xT = g.sb("xT", [128, DT * TOK], F32)
        b1h_sb = g.sb("ffn_b1h", [1, DFF], BF16)
        b2h_sb = g.sb("ffn_b2h", [1, D], BF16)
        load_xT(g, xT, x_in)
        g.on("gpsimd", lambda e: e.dma_start(out=b1h_sb[:], in_=b1), inc=g.sem["qg"], inck=16)
        g.on("gpsimd", lambda e: e.dma_start(out=b2h_sb[:], in_=b2), inc=g.sem["qg"], inck=16)
        g.w("vector", g.sem["qg"])
        g.on("vector", lambda e: e.tensor_scalar(out=b2h_sb[:], in0=b2h_sb[:], scalar1=0.5,
                                                 scalar2=None, op0=ALU.mult), inc=g.sem["ve"])
        g.w("tensor", g.sem["qs"])
        g.w("scalar", g.sem["qs"])
        g.run_phase("load")
        ln = alloc_ln(g)
        hT_ch = ln["xbf"]
        hff = g.sb("ffn_hff", [128, 32 * CH], BF16)
        w1T_sb = g.sb("ffn_w1T", [128, DT * 4096], BF16)
        w2T_sb = g.sb("ffn_w2T", [128, 3 * 1024], BF16)
        emit_ffn(g, cm, xT, ln, wq1, wq2, b1h_sb, b2h_sb, hT_ch, hff, w1T_sb, w2T_sb)
        store_xT(g, xT, x_out)
    return nc


def run_ffn_subblock(xs_parts, wq1, wq2, b1, b2):
    nc = build_ffn()
    im = [{"x": xs_parts[c], "wq1": wq1, "wq2": wq2,
           "b1": b1.reshape(1, DFF), "b2": b2.reshape(1, D)} for c in range(NCORE)]
    res = run_bass_kernel_spmd(nc, im, core_ids=list(range(NCORE)))
    return [np.asarray(res.results[c]["x_out"]) for c in range(NCORE)]


def run_quant(W1, W2):
    nc = build_quant()
    im = [{"W1": W1, "W2": W2} for _ in range(NCORE)]
    res = run_bass_kernel_spmd(nc, im, core_ids=list(range(NCORE)))
    return np.asarray(res.results[0]["wq1"]), np.asarray(res.results[0]["wq2"])


def shard_x(x):
    xs = x.reshape(NCORE, TOK, D)
    return [np.ascontiguousarray(xs[c].T) for c in range(NCORE)]


def unshard_x(parts):
    x = np.stack([np.asarray(p).T for p in parts], axis=0)
    return np.ascontiguousarray(x.reshape(2, SEQ, D))


_CONSTS = None


def get_consts():
    global _CONSTS
    if _CONSTS is None:
        _CONSTS = [host_consts(c) for c in range(NCORE)]
    return _CONSTS


def run_moe_subblock(xs_parts, wg, we, be):
    consts = get_consts()
    names = ("id8", "id128", "id1", "wsel")
    nc1 = build_moe("out")
    im = [{"x": xs_parts[c], "wg": wg, **{k: consts[c][k] for k in names}}
          for c in range(NCORE)]
    res = run_bass_kernel_spmd(nc1, im, core_ids=list(range(NCORE)))
    tots = np.concatenate([np.asarray(res.results[c]["tot"]).reshape(1, 8)
                           for c in range(NCORE)], axis=0)
    tots = np.ascontiguousarray(tots)
    nc2 = build_moe("ext")
    im = [{"x": xs_parts[c], "wg": wg, "we": we, "be": be.reshape(1, D),
           "tot_all": tots, **{k: consts[c][k] for k in names}}
          for c in range(NCORE)]
    res = run_bass_kernel_spmd(nc2, im, core_ids=list(range(NCORE)))
    return [np.asarray(res.results[c]["x_out"]) for c in range(NCORE)]




def build_fused():
    nc = bass.Bass("TRN2", target_bir_lowering=False)
    cd = const_decls_subset(nc, tuple(CONST_SPECS))
    x_in = decl_in(nc, "x", [D, TOK])
    x_out = decl_out(nc, "x_out", [D, TOK])
    win = {}
    for nm, shape in (("L0_ffn_Wg", [D, 8]), ("L0_ffn_We", [D, D]), ("L0_ffn_be", [1, D]),
                      ("L0_attn_Wq", [D, D]), ("L0_attn_Wkv", [D, 2 * D]),
                      ("L0_attn_Wout", [D, D]), ("L0_attn_bout", [1, D]),
                      ("L1_attn_Wq", [D, D]), ("L1_attn_Wk", [D, D]), ("L1_attn_Wv", [D, D]),
                      ("L1_attn_Wo", [D, D]), ("L1_attn_bo", [1, D]),
                      ("L1_ffn_W1", [DFF, D]), ("L1_ffn_b1", [1, DFF]),
                      ("L1_ffn_W2", [D, DFF]), ("L1_ffn_b2", [1, D])):
        win[nm] = decl_in(nc, nm, shape)
    dram = lambda nm, shape, dt=BF16: nc.dram_tensor(nm, shape, dt).ap()
    we_bf = dram("i_we", [D, D]); beh = dram("i_beh", [1, D])
    a_w = {0: {}, 1: {}}
    for k in ("wq", "wk", "wv"):
        a_w[0][k] = dram("i_a0_" + k, [D, D])
        a_w[1][k] = dram("i_a1_" + k, [D, D])
    wo_bf = {0: dram("i_wo0", [D, D]), 1: dram("i_wo1", [D, D])}
    bo_bf = {0: dram("i_bo0", [1, D]), 1: dram("i_bo1", [1, D])}
    wq1 = dram("i_wq1", [DFF, D]); wq2 = dram("i_wq2", [D, DFF])
    wq2t = dram("i_wq2t", [DFF, D])
    exch = {}
    for L in (0, 1):
        exch[L] = {
            "sendqk": dram("i_sendqk%d" % L, [NCORE, SH_Q + SH_K]),
            "recvqk": dram("i_recvqk%d" % L, [NCORE, SH_Q + SH_K]),
            "sendv": dram("i_sendv%d" % L, [NCORE, SH_V]),
            "recvv": dram("i_recvv%d" % L, [NCORE, SH_V]),
            "rsend": dram("i_rsend%d" % L, [NCORE, RET]),
            "rrecv": dram("i_rrecv%d" % L, [NCORE, RET]),
        }
    scrq = {1: nc.dram_tensor("i_scr1", [1, 128], F32).ap(),
            2: nc.dram_tensor("i_scr2", [1, 128], F32).ap()}
    tots = {}
    for i in (0, 1):
        tots[i] = {"tot_in": nc.dram_tensor("i_totin%d" % i, [1, 8], F32).ap(),
                   "tot_all": nc.dram_tensor("i_totall%d" % i, [8, 8], F32).ap()}

    groups8 = [list(range(NCORE))]

    with ExitStack() as ctx:
        g = G(nc, ctx)
        qs, qg, cc, pe, ve, se = (g.sem[k] for k in ("qs", "qg", "cc", "pe", "ve", "se"))
        cm = alloc_common(g, cd, ("id8", "id128", "id1", "wsel"))
        xT = g.sb("xT", [128, DT * TOK], F32)

        def scope_consts(names):
            cm2 = dict(cm)
            for n in names:
                shape, dt = CONST_SPECS[n]
                t = g.sb(n + "_sb", shape, dt)
                cm2[n + "_sb"] = t
                g.on("sync", lambda e, t=t, ap=cd[n]: e.dma_start(out=t[:], in_=ap),
                     inc=qs, inck=16)
            return cm2
        load_xT(g, xT, x_in)
        emit_casts(g, [(we_bf, win["L0_ffn_We"]), (beh, win["L0_ffn_be"])])
        wc_sem = g.sem["sd"]
        wc_state = {}
        def big_casts():
            emit_casts(g, [
                (a_w[0]["wq"], win["L0_attn_Wq"]),
                (a_w[0]["wk"], win["L0_attn_Wkv"][:, 0:D]),
                (a_w[0]["wv"], win["L0_attn_Wkv"][:, D:2 * D]),
            ], sem=wc_sem)
            wc_state[0] = wc_sem.n
        def rest_casts():
            emit_casts(g, [
                (wo_bf[0], win["L0_attn_Wout"]), (bo_bf[0], win["L0_attn_bout"]),
                (a_w[1]["wq"], win["L1_attn_Wq"]), (a_w[1]["wk"], win["L1_attn_Wk"]),
                (a_w[1]["wv"], win["L1_attn_Wv"]),
                (wo_bf[1], win["L1_attn_Wo"]), (bo_bf[1], win["L1_attn_bo"]),
            ], sem=wc_sem)
            wc_state[1] = wc_sem.n
        g.w("tensor", qs)
        g.w("scalar", qs)
        g.w("vector", qs)
        g.run_phase("load")

        def moe(i):
            with g.subscope("m%d_" % i):
                emit_moe_phase(g, cm, xT, {"wg": win["L0_ffn_Wg"], "we_bf": we_bf,
                                           "beh": beh}, "cc", tots[i],
                               after_loads=(big_casts if i == 0 else None))

        def attn(L):
            ex = exch[L]
            ccval = {}
            def qk_hook(ex=ex):
                g.w("gpsimd", qg)
                def ccqk(e, ex=ex):
                    with allow_ici_all_to_all():
                        return e.collective_compute(
                            "AllToAll", ALU.bypass, replica_groups=groups8,
                            ins=[ex["sendqk"]], outs=[ex["recvqk"]])
                g.on("gpsimd", ccqk, inc=cc)
                ccval["qk"] = cc.n
            with g.subscope("a%dq_" % L):
                cm2 = scope_consts(("PT", "ropeC", "ropeS") if L == 0 else ())
                hT = g.sb("hT", [128, DT * TOK], BF16)
                v_own = g.sb("vown", [128, 16 * 1040], BF16)
                ln = alloc_ln(g, sq=v_own[:, 0: DT * CH])
                emit_ln_phase(g, cm2, ln, xT, hT, name="ln")
                emit_qkv(g, cm2, xT, hT, a_w[L], ex["sendqk"], ex["sendv"],
                         rope_on=(L == 0), v_own=v_own, on_qk_done=qk_hook,
                         w_wait=[(wc_sem, wc_state[min(L, max(wc_state))])])
            def ccv(e, ex=ex):
                with allow_ici_all_to_all():
                    return e.collective_compute(
                        "AllToAll", ALU.bypass, replica_groups=groups8,
                        ins=[ex["sendv"]], outs=[ex["recvv"]])
            g.on("gpsimd", ccv, wait=[(qs, qs.n)], inc=cc)
            ccval["v"] = cc.n
            g.run_phase("a%d_cc1" % L)
            with g.subscope("a%df_" % L):
                cm2 = scope_consts(("Tbase",) if L == 1 else ())
                bg = None
                if L == 0:
                    import itertools
                    qb = alloc_quant(g)
                    bg = itertools.chain(
                        emit_quant(g, cm, win["L1_ffn_W1"], wq1, "w1",
                                   False, "q1", scrq[1], qb),
                        emit_quant(g, cm, win["L1_ffn_W2"], wq2, "w2",
                                   True, "q2", scrq[2], qb))
                emit_flash(g, cm2, ex["recvqk"], ex["recvv"], ex["rsend"],
                           causal=(L == 1), bg=bg,
                           wait_qk=(cc, ccval["qk"]), wait_v=(cc, ccval["v"]))
            g.w("gpsimd", qg)
            def cc2(e, ex=ex):
                with allow_ici_all_to_all():
                    return e.collective_compute(
                        "AllToAll", ALU.bypass, replica_groups=groups8,
                        ins=[ex["rsend"]], outs=[ex["rrecv"]])
            g.on("gpsimd", cc2, inc=cc)
            g.run_phase("a%d_cc2" % L)
            with g.subscope("a%dp_" % L):
                g.w("sync", cc)
                emit_outproj(g, cm, xT, ex["rrecv"], wo_bf[L], bo_bf[L],
                             w_wait=[(wc_sem, wc_state[1])])

        def ffn(i):
            with g.subscope("f%d_" % i):
                b1v_sb = g.sb("b1v", [128, 32], F32)
                b2v_sb = g.sb("b2v", [128, 8], F32)
                for f in range(32):
                    g.on("sync", lambda e, f=f: e.dma_start(
                        out=b1v_sb[:, f: f + 1],
                        in_=win["L1_ffn_b1"].rearrange("a (f p) -> (a f) p", p=128)[f: f + 1]
                            .rearrange("f p -> p f")),
                        inc=qs, inck=16)
                for f in range(DT):
                    g.on("sync", lambda e, f=f: e.dma_start(
                        out=b2v_sb[:, f: f + 1],
                        in_=win["L1_ffn_b2"].rearrange("a (f p) -> (a f) p", p=128)[f: f + 1]
                            .rearrange("f p -> p f")),
                        inc=qs, inck=16)
                g.w("vector", qs)
                g.on("vector", lambda e: e.tensor_scalar(
                    out=b2v_sb[:], in0=b2v_sb[:], scalar1=0.5, scalar2=None, op0=ALU.mult),
                    inc=ve)
                g.run_phase("fb")
                hff = g.sb("hff", [128, 32 * CH], BF16)
                ln = alloc_ln(g, sq=hff[:, 0:DT * CH], xbf=hff[:, DT * CH: 2 * DT * CH])
                hT_ch = g.sb("hTch", [128, DT * 2 * CH], BF16)
                w1T_sb = g.sb("w1T", [128, DT * 4096], BF16)
                w2T_sb = g.sb("w2T", [128, 3 * 1024], BF16)
                emit_ffn(g, cm, xT, ln, wq1, wq2, b1v_sb, b2v_sb, hT_ch, hff,
                         w1T_sb, w2T_sb)

        moe(0)
        rest_casts()
        attn(0)
        moe(1)
        ffn(0)
        attn(1)
        ffn(1)
        store_xT(g, xT, x_out)
    return nc


_FUSED_NC = None


def kernel_fused(**inputs):
    global _FUSED_NC
    f32 = lambda k: np.ascontiguousarray(np.asarray(inputs[k], np.float32))
    x = f32("L0_attn_queries")
    xs = shard_x(x)
    consts = get_consts()
    if _FUSED_NC is None:
        _FUSED_NC = build_fused()
    wkeys = ["L0_ffn_Wg", "L0_ffn_We", "L0_attn_Wq", "L0_attn_Wkv", "L0_attn_Wout",
             "L1_attn_Wq", "L1_attn_Wk", "L1_attn_Wv", "L1_attn_Wo",
             "L1_ffn_W1", "L1_ffn_W2"]
    bkeys = ["L0_ffn_be", "L0_attn_bout", "L1_attn_bo", "L1_ffn_b1", "L1_ffn_b2"]
    im = []
    for c in range(NCORE):
        m = {"x": xs[c], **consts[c]}
        for k in wkeys:
            m[k] = f32(k)
        for k in bkeys:
            m[k] = f32(k).reshape(1, -1)
        im.append(m)
    res = run_bass_kernel_spmd(_FUSED_NC, im, core_ids=list(range(NCORE)))
    return unshard_x([np.asarray(res.results[c]["x_out"]) for c in range(NCORE)])

def kernel(**inputs):
    return kernel_fused(**inputs)


def kernel_unfused(**inputs):

    f32 = lambda k: np.ascontiguousarray(np.asarray(inputs[k], np.float32))
    x = f32("L0_attn_queries")
    xs = shard_x(x)

    moe_w = (f32("L0_ffn_Wg"), f32("L0_ffn_We"), f32("L0_ffn_be"))
    xs = run_moe_subblock(xs, *moe_w)
    xs = run_attn_subblock(xs, 0, {"Wq": f32("L0_attn_Wq"), "Wkv": f32("L0_attn_Wkv"),
                                   "Wout": f32("L0_attn_Wout"), "bout": f32("L0_attn_bout")})
    xs = run_moe_subblock(xs, *moe_w)
    wq1, wq2 = run_quant(f32("L1_ffn_W1"), f32("L1_ffn_W2"))
    b1, b2 = f32("L1_ffn_b1"), f32("L1_ffn_b2")
    xs = run_ffn_subblock(xs, wq1, wq2, b1, b2)
    xs = run_attn_subblock(xs, 1, {"Wq": f32("L1_attn_Wq"), "Wk": f32("L1_attn_Wk"),
                                   "Wv": f32("L1_attn_Wv"), "Wo": f32("L1_attn_Wo"),
                                   "bo": f32("L1_attn_bo")})
    xs = run_ffn_subblock(xs, wq1, wq2, b1, b2)
    return unshard_x(xs)


# version 31
# speedup vs baseline: 1.0025x; 1.0025x over previous
import numpy as np
import ml_dtypes
from contextlib import ExitStack

import concourse.bass as bass
import concourse.mybir as mybir
from concourse.bass_utils import run_bass_kernel_spmd
from concourse.aot_env import allow_ici_all_to_all

F32 = mybir.dt.float32
BF16 = mybir.dt.bfloat16
AF = mybir.ActivationFunctionType
ALU = mybir.AluOpType
AX = mybir.AxisListType
BF = ml_dtypes.bfloat16

NCORE = 8
TOK = 2048
D = 1024
DT = D // 128
CH = 512
NCH = TOK // CH
CAP = 2048.0
LNEPS = 1e-5
QEPS = 1e-5
DFF = 4096
SEQ = 8192
ATTN_SCALE = 0.125


class Sem:
    def __init__(self, ctx, nc, name):
        self.h = ctx.enter_context(nc.semaphore(name))
        self.n = 0


class G:
    ENG = ("sync", "tensor", "vector", "scalar", "gpsimd")

    def __init__(self, nc, ctx):
        self.nc = nc
        self.ctx = ctx
        self.cur = ctx
        self.prefix = ""
        self.q = {k: [] for k in self.ENG}
        self.sem = {name: Sem(ctx, nc, "sem_" + name) for name in
                    ("qs", "qg", "qa", "pe", "ve", "se", "gp", "cc", "wa", "wb", "sc", "sd", "n0", "n1")}
        self.nblk = 0

    def sb(self, name, shape, dt):
        return self.cur.enter_context(self.nc.sbuf_tensor(self.prefix + name, shape, dt))

    def ps(self, name, shape, dt=F32):
        return self.nc.psum_tensor(self.prefix + name, shape, dt)

    def subscope(g, prefix):
        class _S:
            def __enter__(s):
                s.st = ExitStack()
                s.st.__enter__()
                s.old = (g.cur, g.prefix)
                g.cur, g.prefix = s.st, prefix
                for eng in ("sync", "gpsimd"):
                    g.w(eng, g.sem["pe"], g.sem["ve"], g.sem["se"],
                        g.sem["qs"], g.sem["qg"])
                return s
            def __exit__(s, *a):
                g.cur, g.prefix = s.old
                return s.st.__exit__(*a)
        return _S()

    def on(self, eng, fn, wait=(), inc=None, inck=1):
        waits = []
        for wv in wait:
            s, v = wv if isinstance(wv, tuple) else (wv, None)
            waits.append((s.h, s.n if v is None else v))
        if inc is not None:
            inc.n += inck
        self.q[eng].append((tuple(waits), fn, None if inc is None else (inc.h, inck)))

    def w(self, eng, *sems):
        self.on(eng, None, wait=list(sems))

    def run_phase(self, name=None):
        qs = self.q
        self.q = {k: [] for k in self.ENG}
        self.nblk += 1
        if name is not None:
            name = self.prefix + name
        with self.nc.Block(name) as b:
            for engname in self.ENG:
                items = qs[engname]
                if not items:
                    continue

                dr = engname in ("vector", "scalar", "gpsimd")

                def body(e, items=tuple(items), dr=dr):
                    for waits, fn, inc in items:
                        for (h, v) in waits:
                            e.wait_ge(h, v)
                        instr = fn(e) if fn is not None else None
                        if inc is not None:
                            h, k = inc
                            (instr if instr is not None else e.nop()).then_inc(h, k)
                        if dr and fn is not None:
                            e.drain()
                getattr(b, engname)(body)


def host_consts(core):
    pos = 2048.0 * (core % 4) + np.arange(TOK, dtype=np.float64)
    inv_freq = 1.0 / (10000.0 ** (np.arange(0, 32, 2, dtype=np.float64) / 32.0))
    ang = pos[:, None] * inv_freq[None, :]
    C = np.ones((128, TOK), np.float64)
    S = np.zeros((128, TOK), np.float64)
    for p in range(128):
        q = p % 64
        if q < 32:
            C[p] = np.cos(ang[:, q // 2])
            S[p] = np.sin(ang[:, q // 2])
    P = np.zeros((128, 128), np.float64)
    for base in (0, 64):
        for i in range(16):
            P[base + 2 * i, base + 2 * i + 1] = -1.0
            P[base + 2 * i + 1, base + 2 * i] = 1.0
    u = np.arange(1920)[None, :]
    p = np.arange(128)[:, None]
    Tb = (p <= u - 896).astype(np.float64)
    wsel = np.zeros((8, 1), np.float32)
    wsel[:core] = 1.0
    Sx = S.copy()
    Sx[0::2] *= -1.0
    return {
        "ropeC": np.ascontiguousarray(C.astype(BF)),
        "ropeS": np.ascontiguousarray(S.astype(BF)),
        "ropeSx": np.ascontiguousarray(Sx.astype(BF)),
        "PT": np.ascontiguousarray(P.T.astype(BF)),
        "Tbase": np.ascontiguousarray(Tb.astype(BF)),
        "id8": np.eye(8, dtype=np.float32),
        "id128": np.eye(128, dtype=np.float32),
        "id1": np.ones((1, 1), np.float32),
        "wsel": wsel,
    }


CONST_SPECS = {
    "ropeC": ([128, TOK], BF16), "ropeS": ([128, TOK], BF16),
    "ropeSx": ([128, TOK], BF16),
    "PT": ([128, 128], BF16), "Tbase": ([128, 1920], BF16),
    "id8": ([8, 8], F32), "id128": ([128, 128], F32), "id1": ([1, 1], F32),
    "wsel": ([8, 1], F32),
}


def decl_in(nc, name, shape, dt=F32):
    return nc.dram_tensor(name, shape, dt, kind="ExternalInput").ap()


def decl_out(nc, name, shape, dt=F32):
    return nc.dram_tensor(name, shape, dt, kind="ExternalOutput").ap()


def const_decls_subset(nc, names):
    return {n: decl_in(nc, n, *CONST_SPECS[n]) for n in names}


def alloc_common(g, cd, const_names):
    cm = {}
    cm["ones128_bf"] = g.sb("ones128_bf", [128, 1], BF16)
    cm["ones128_f32"] = g.sb("ones128_f32", [128, 1], F32)
    cm["onesrow_f32"] = g.sb("onesrow_f32", [1, 128], F32)
    cm["onesrow_bf"] = g.sb("onesrow_bf", [1, 512], BF16)
    cm["ones8_f32"] = g.sb("ones8_f32", [8, 1], F32)

    cm["eps_ln"] = g.sb("eps_ln", [1, 1], F32)

    def ms(e):
        e.memset(cm["eps_ln"][:], LNEPS)
        e.memset(cm["ones128_bf"][:], 1.0)
        e.memset(cm["ones128_f32"][:], 1.0)
        e.memset(cm["onesrow_f32"][:], 1.0)
        e.memset(cm["onesrow_bf"][:], 1.0)
        return e.memset(cm["ones8_f32"][:], 1.0)
    g.on("vector", ms, inc=g.sem["ve"])
    for n in const_names:
        shape, dt = CONST_SPECS[n]
        t = g.sb(n + "_sb", shape, dt)
        cm[n + "_sb"] = t
        g.on("sync", lambda e, t=t, ap=cd[n]: e.dma_start(out=t[:], in_=ap),
             inc=g.sem["qs"], inck=16)
    return cm


def load_xT(g, xT, x_in):
    g.on("sync", lambda e: e.dma_start(
        out=xT[:].rearrange("p (a c) -> p a c", a=DT),
        in_=x_in.rearrange("(a p) c -> p a c", p=128)), inc=g.sem["qs"], inck=16)


def store_xT(g, xT, x_out):
    g.w("gpsimd", g.sem["ve"], g.sem["se"])
    g.on("gpsimd", lambda e: e.dma_start(
        out=x_out.rearrange("(a p) c -> p a c", p=128),
        in_=xT[:].rearrange("p (a c) -> p a c", a=DT)), inc=g.sem["qg"], inck=16)
    g.w("gpsimd", g.sem["qg"])
    g.run_phase("store_x")


def emit_casts(g, casts, sem=None):
    s = g.sem["qg"] if sem is None else sem
    for dst, src in casts:
        rows = dst.shape[0]
        nsp = 8 if rows % 8 == 0 and rows >= 8 else 1
        rs = rows // nsp
        for i in range(nsp):
            g.on("gpsimd", lambda e, dst=dst, src=src, i=i, rs=rs: e.dma_start(
                out=dst[rs * i: rs * (i + 1)], in_=src[rs * i: rs * (i + 1)]),
                inc=s, inck=16)


def alloc_ln(g, sq=None, xbf=None):
    b = {}
    b["xbf"] = g.sb("ln_xbf", [128, DT * CH], BF16) if xbf is None else xbf
    b["sq"] = g.sb("ln_sq", [128, DT * CH], BF16) if sq is None else sq
    b["m_sb"] = g.sb("ln_m", [1, CH], F32)
    b["v_sb"] = g.sb("ln_v", [1, CH], F32)
    b["r_sb"] = g.sb("ln_r", [1, CH], F32)
    b["mr_sb"] = b["t1_sb"] = g.sb("ln_t1", [1, CH], F32)
    b["u_sb"] = g.sb("ln_u", [128, CH], F32)
    return b


def emit_ln_phase(g, cm, b, xT, hT, chunks=None, router=None, hT_off=None, name="ln", steps=6):
    nc = g.nc
    se, pe, ve = g.sem["se"], g.sem["pe"], g.sem["ve"]
    if chunks is None:
        chunks = list(range(NCH))
    chunks = list(chunks)
    NC = len(chunks)
    if hT_off is None:
        HTOK, hT_off = TOK, 0
    else:
        HTOK = NC * CH
    with ExitStack() as ps:
        msum_ps = ps.enter_context(g.ps(name + "_msum", [1, CH], F32))
        s2_ps = ps.enter_context(g.ps(name + "_s2", [1, CH], F32))
        rb_ps = ps.enter_context(g.ps(name + "_rb", [128, CH], F32))
        mrb_ps = ps.enter_context(g.ps(name + "_mrb", [128, CH], F32))
        lg_ps = ps.enter_context(g.ps(name + "_lg", [8, TOK], F32)) \
            if router is not None else None

        reds_pe = {}
        stats_ve = {}
        mr_ve = {}
        bcast_pe = {}
        hcomp_ve = {}
        rstd_se = {}
        rb8_se = {}

        def e_cast(ci, ch):
            def cast(e):
                for d in range(DT):
                    e.activation(out=b["xbf"][:, CH * d: CH * (d + 1)],
                                 in_=xT[:, TOK * d + CH * ch: TOK * d + CH * (ch + 1)], func=AF.Copy)
                for d in range(DT):
                    last = e.activation(out=b["sq"][:, CH * d: CH * (d + 1)],
                                        in_=xT[:, TOK * d + CH * ch: TOK * d + CH * (ch + 1)], func=AF.Square)
                return last
            g.on("scalar", cast, wait=[(pe, reds_pe[ci - 1])] if ci >= 1 else [(pe, pe.n)], inc=se)

        def e_reds(ci, ch):
            def reds(e):
                for d in range(DT):
                    e.matmul(msum_ps[:], cm["ones128_bf"][:],
                             b["xbf"][:, CH * d: CH * (d + 1)], start=(d == 0), stop=(d == DT - 1))
                for d in range(DT):
                    last = e.matmul(s2_ps[:], cm["ones128_bf"][:],
                                    b["sq"][:, CH * d: CH * (d + 1)], start=(d == 0), stop=(d == DT - 1))
                if router is not None:
                    for d in range(DT):
                        last = e.matmul(lg_ps[:, CH * ch: CH * (ch + 1)],
                                        router["wg_sb"][:, 8 * d: 8 * (d + 1)],
                                        xT[:, TOK * d + CH * ch: TOK * d + CH * (ch + 1)],
                                        start=(d == 0), stop=False)
                return last
            w = [se] + ([(ve, stats_ve[ci - 1])] if ci >= 1 else [])
            g.on("tensor", reds, wait=w, inc=pe)
            reds_pe[ci] = pe.n

        def e_stats(ci, ch):
            def stats(e):
                e.tensor_scalar(out=b["m_sb"][:], in0=msum_ps[:], scalar1=1.0 / D, scalar2=None, op0=ALU.mult)
                e.drain()
                e.tensor_scalar(out=b["v_sb"][:], in0=s2_ps[:], scalar1=1.0 / D, scalar2=None, op0=ALU.mult)
                e.drain()
                e.tensor_tensor(out=b["t1_sb"][:], in0=b["m_sb"][:], in1=b["m_sb"][:], op=ALU.mult)
                e.drain()
                return e.tensor_tensor(out=b["v_sb"][:], in0=b["v_sb"][:], in1=b["t1_sb"][:], op=ALU.subtract)
            w = [(pe, reds_pe[ci])] + ([(se, rstd_se[ci - 1])] if ci >= 1 else [])
            g.on("vector", stats, wait=w, inc=ve)
            stats_ve[ci] = ve.n

        def e_rstd(ci, ch):
            def rstd(e):
                e.activation(out=b["t1_sb"][:], in_=b["v_sb"][:], func=AF.Ln, bias=cm["eps_ln"][:])
                return e.activation(out=b["r_sb"][:], in_=b["t1_sb"][:], func=AF.Exp, scale=-0.5)
            g.on("scalar", rstd, wait=[(ve, stats_ve[ci])], inc=se)
            rstd_se[ci] = se.n

        def e_mr(ci, ch):
            g.on("vector", lambda e: e.tensor_tensor(out=b["mr_sb"][:], in0=b["m_sb"][:],
                                                     in1=b["r_sb"][:], op=ALU.mult),
                 wait=[(se, rstd_se[ci])], inc=ve)
            mr_ve[ci] = ve.n

        def e_bcast(ci, ch):
            def bcast(e):
                e.matmul(rb_ps[:], cm["onesrow_f32"][:], b["r_sb"][:], start=True, stop=True)
                last = e.matmul(mrb_ps[:], cm["onesrow_f32"][:], b["mr_sb"][:], start=True, stop=True)
                if router is not None:
                    last = e.matmul(lg_ps[:, CH * ch: CH * (ch + 1)],
                                    router["ngsum_sb"][:], b["m_sb"][:], start=False, stop=True)
                return last
            w = [(ve, mr_ve[ci])]
            if ci >= 1:
                w.append((ve, hcomp_ve[ci - 1]))
                if router is not None:
                    w.append((se, rb8_se[ci - 1]))
            g.on("tensor", bcast, wait=w, inc=pe)
            bcast_pe[ci] = pe.n

        def e_h(ci, ch):
            if router is not None:
                g.on("scalar", lambda e: e.copy(out=router["rb8_sb"][:], in_=rb_ps[0:8, :]),
                     wait=[(pe, bcast_pe[ci])], inc=se)
                rb8_se[ci] = se.n
            def hcomp(e):
                for d in range(DT):
                    e.tensor_tensor(out=b["u_sb"][:],
                                    in0=xT[:, TOK * d + CH * ch: TOK * d + CH * (ch + 1)],
                                    in1=rb_ps[:], op=ALU.mult)
                    e.drain()
                    last = e.tensor_tensor(
                        out=hT[:, HTOK * d + CH * ch - hT_off: HTOK * d + CH * (ch + 1) - hT_off],
                        in0=b["u_sb"][:], in1=mrb_ps[:], op=ALU.subtract)
                    e.drain()
                return last
            w = [(pe, bcast_pe[ci])] + ([(se, rb8_se[ci])] if router is not None else [])
            g.on("vector", hcomp, wait=w, inc=ve)
            if router is not None:
                g.on("vector", lambda e: e.tensor_tensor(
                    out=router["lg_sb"][:, CH * ch: CH * (ch + 1)],
                    in0=lg_ps[:, CH * ch: CH * (ch + 1)],
                    in1=router["rb8_sb"][:], op=ALU.mult), inc=ve)
            hcomp_ve[ci] = ve.n

        stages = [e_cast, e_reds, e_stats, e_rstd, e_mr, e_bcast, e_h]
        for ci in range(NC + 1):
            if ci < NC:
                e_cast(ci, chunks[ci])
            if ci >= 1:
                cj = ci - 1
                e_stats(cj, chunks[cj])
                e_rstd(cj, chunks[cj])
                e_mr(cj, chunks[cj])
                e_bcast(cj, chunks[cj])
                e_h(cj, chunks[cj])
            if ci < NC:
                e_reds(ci, chunks[ci])
        g.run_phase(name)


def alloc_gating(g):
    gt = {}
    gt["ltm_sb"] = g.sb("gt_ltm_sb", [128, 128], F32)
    gt["mx_sb"] = g.sb("gt_mx", [128, 16], F32)
    gt["msk_tm"] = g.sb("gt_msk_tm", [128, 128], F32)
    gt["mskT_sb"] = g.sb("gt_mskT_sb", [8, TOK], F32)
    gt["zero8"] = g.sb("gt_zero8", [8, 1], F32)
    gt["cumT_sb"] = g.sb("gt_cumT", [8, TOK], F32)
    gt["tot_sb"] = g.sb("gt_tot", [8, 8], F32)
    gt["eT_sb"] = g.sb("gt_eT", [8, TOK], F32)
    gt["sT_sb"] = g.sb("gt_sT_sb", [1, TOK], F32)
    gt["rs_sb"] = gt["sT_sb"]
    gt["nin_sb"] = gt["eT_sb"]
    gt["gateT_sb"] = g.sb("gt_gateT", [1, TOK], F32)
    gt["offs_sb"] = g.sb("gt_offs_sb", [1, 8], F32)
    gt["offsT_sb"] = g.sb("gt_offsT_sb", [8, 1], F32)
    gt["k8_sb"] = gt["cumT_sb"]
    gt["factorT_sb"] = gt["gateT_sb"]
    gt["fB_sb"] = g.sb("gt_fB_sb", [128, TOK], F32)
    return gt


def emit_gating(g, cm, lg_sb, gt, totals_mode, tot_io):
    nc = g.nc
    se, pe, ve = g.sem["se"], g.sem["pe"], g.sem["ve"]
    qg, cc = g.sem["qg"], g.sem["cc"]

    with g.ps("gt_ltm", [128, 128], F32) as ltm_ps, \
         g.ps("gt_mskT", [8, TOK], F32) as mskT_ps:
        def ttm(e):
            for j in range(16):
                last = e.transpose(ltm_ps[:, 8 * j: 8 * (j + 1)],
                                   lg_sb[:, 128 * j: 128 * (j + 1)], cm["id8_sb"][:])
            return last
        g.on("tensor", ttm, wait=[ve], inc=pe)
        g.on("scalar", lambda e: e.copy(out=gt["ltm_sb"][:], in_=ltm_ps[:]),
             wait=[pe], inc=se)
        def masks(e):
            for j in range(16):
                e.tensor_reduce(out=gt["mx_sb"][:, j: j + 1],
                                in_=gt["ltm_sb"][:, 8 * j: 8 * (j + 1)], axis=AX.X, op=ALU.max)
            e.drain()
            for j in range(16):
                last = e.tensor_scalar(out=gt["msk_tm"][:, 8 * j: 8 * (j + 1)],
                                       in0=gt["ltm_sb"][:, 8 * j: 8 * (j + 1)],
                                       scalar1=gt["mx_sb"][:, j: j + 1], scalar2=None, op0=ALU.is_ge)
            return last
        g.on("vector", masks, wait=[se], inc=ve)
        def tfm(e):
            for j in range(16):
                last = e.transpose(mskT_ps[:, 128 * j: 128 * (j + 1)],
                                   gt["msk_tm"][:, 8 * j: 8 * (j + 1)], cm["id128_sb"][:])
            return last
        g.on("tensor", tfm, wait=[ve], inc=pe)
        g.on("scalar", lambda e: e.copy(out=gt["mskT_sb"][:], in_=mskT_ps[:]),
             wait=[pe], inc=se)
        g.on("vector", lambda e: e.memset(gt["zero8"][:], 0.0))
        g.on("vector", lambda e: e.tensor_tensor_scan(
            out=gt["cumT_sb"][:], data0=gt["mskT_sb"][:], data1=gt["mskT_sb"][:],
            initial=gt["zero8"][:], op0=ALU.add, op1=ALU.bypass),
            wait=[se], inc=ve)
        g.run_phase("gate_mask")

    if totals_mode == "out":
        g.w("gpsimd", ve)
        g.on("gpsimd", lambda e: e.dma_start(
            out=tot_io["tot_out_ext"], in_=gt["cumT_sb"][:, TOK - 1: TOK]),
            inc=qg, inck=16)
        g.w("gpsimd", qg)
        g.run_phase("gate_tot")
        return

    if totals_mode == "cc":
        g.w("gpsimd", ve)
        g.on("gpsimd", lambda e: e.dma_start(
            out=tot_io["tot_in"].rearrange("a e -> e a"),
            in_=gt["cumT_sb"][:, TOK - 1: TOK]), inc=qg, inck=16)
        g.w("gpsimd", qg)
        g.on("gpsimd", lambda e: e.collective_compute(
            "AllGather", ALU.bypass, replica_groups=[list(range(NCORE))],
            ins=[tot_io["tot_in"]], outs=[tot_io["tot_all"]]), inc=cc)
        g.w("gpsimd", cc)
        g.on("gpsimd", lambda e: e.dma_start(out=gt["tot_sb"][:], in_=tot_io["tot_all"]),
             inc=qg, inck=16)
    else:
        g.on("gpsimd", lambda e: e.dma_start(out=gt["tot_sb"][:], in_=tot_io["tot_all"]),
             inc=qg, inck=16)

    with g.ps("gt_sT", [1, TOK], F32) as sT_ps:
        g.on("scalar", lambda e: e.activation(out=gt["eT_sb"][:], in_=lg_sb[:], func=AF.Exp),
             wait=[ve], inc=se)
        def smm(e):
            for c in range(NCH):
                last = e.matmul(sT_ps[:, CH * c: CH * (c + 1)], cm["ones8_f32"][:],
                                gt["eT_sb"][:, CH * c: CH * (c + 1)], start=True, stop=True)
            return last
        g.on("tensor", smm, wait=[se], inc=pe)
        g.on("scalar", lambda e: e.copy(out=gt["sT_sb"][:], in_=sT_ps[:]),
             wait=[pe], inc=se)
        g.on("vector", lambda e: e.reciprocal(out=gt["rs_sb"][:], in_=gt["sT_sb"][:]),
             wait=[se], inc=ve)
        g.on("vector", lambda e: e.tensor_tensor(out=gt["nin_sb"][:], in0=gt["eT_sb"][:],
                                                 in1=gt["mskT_sb"][:], op=ALU.mult), inc=ve)
        g.run_phase("gate_probs")

    with g.ps("gt_num", [1, TOK], F32) as num_ps:
        def nmm(e):
            for c in range(NCH):
                last = e.matmul(num_ps[:, CH * c: CH * (c + 1)], cm["ones8_f32"][:],
                                gt["nin_sb"][:, CH * c: CH * (c + 1)], start=True, stop=True)
            return last
        g.on("tensor", nmm, wait=[ve], inc=pe)
        g.on("vector", lambda e: e.tensor_tensor(out=gt["gateT_sb"][:], in0=num_ps[:],
                                                 in1=gt["rs_sb"][:], op=ALU.mult),
             wait=[pe], inc=ve)
        g.run_phase("gate_gate")

    with g.ps("gt_offs", [1, 8], F32) as offs_ps, \
         g.ps("gt_offsT", [8, 1], F32) as offsT_ps:
        g.on("tensor", lambda e: e.matmul(offs_ps[:], cm["wsel_sb"][:], gt["tot_sb"][:],
                                          start=True, stop=True), wait=[qg], inc=pe)
        g.on("scalar", lambda e: e.copy(out=gt["offs_sb"][:], in_=offs_ps[:]),
             wait=[pe], inc=se)
        g.on("tensor", lambda e: e.transpose(offsT_ps[:], gt["offs_sb"][:], cm["id1_sb"][:]),
             wait=[se], inc=pe)
        g.on("scalar", lambda e: e.copy(out=gt["offsT_sb"][:], in_=offsT_ps[:]),
             wait=[pe], inc=se)
        g.on("vector", lambda e: e.tensor_scalar(
            out=gt["k8_sb"][:], in0=gt["cumT_sb"][:], scalar1=gt["offsT_sb"][:],
            scalar2=CAP, op0=ALU.add, op1=ALU.is_le), wait=[se], inc=ve)
        g.on("vector", lambda e: e.tensor_tensor(out=gt["k8_sb"][:], in0=gt["k8_sb"][:],
                                                 in1=gt["mskT_sb"][:], op=ALU.mult), inc=ve)
        g.run_phase("gate_keep")

    with g.ps("gt_keep", [1, TOK], F32) as keep_ps, \
         g.ps("gt_fB", [128, TOK], F32) as fB_ps:
        def kmm(e):
            for c in range(NCH):
                last = e.matmul(keep_ps[:, CH * c: CH * (c + 1)], cm["ones8_f32"][:],
                                gt["k8_sb"][:, CH * c: CH * (c + 1)], start=True, stop=True)
            return last
        g.on("tensor", kmm, wait=[ve], inc=pe)
        g.on("vector", lambda e: e.tensor_tensor(out=gt["factorT_sb"][:], in0=keep_ps[:],
                                                 in1=gt["gateT_sb"][:], op=ALU.mult),
             wait=[pe], inc=ve)
        g.on("vector", lambda e: e.tensor_scalar(out=gt["factorT_sb"][:], in0=gt["factorT_sb"][:],
                                                 scalar1=0.5, scalar2=None, op0=ALU.mult), inc=ve)
        def fmm(e):
            for c in range(NCH):
                last = e.matmul(fB_ps[:, CH * c: CH * (c + 1)], cm["onesrow_f32"][:],
                                gt["factorT_sb"][:, CH * c: CH * (c + 1)], start=True, stop=True)
            return last
        g.on("tensor", fmm, wait=[ve], inc=pe)
        g.on("scalar", lambda e: e.copy(out=gt["fB_sb"][:], in_=fB_ps[:]),
             wait=[pe], inc=se)
        g.run_phase("gate_factor")


def emit_expert(g, cm, xT, hT, fB_sb, we_sb, beV_sb, ex_t):
    nc, pe, ve = g.nc, g.sem["pe"], g.sem["ve"]
    se, qs, qg = g.sem["se"], g.sem["qs"], g.sem["qg"]
    with g.ps("ex_y", [128, 2 * CH], F32) as y_ps:
        g.w("tensor", qs, qg, se, ve)
        evacs = []
        for i, (ch, do) in enumerate([(c, d) for c in range(NCH) for d in range(DT)]):
            par = i % 2
            ysl = y_ps[:, CH * par: CH * (par + 1)]
            def mm(e, ch=ch, do=do, ysl=ysl):
                for di in range(DT):
                    last = e.matmul(ysl, we_sb[:, 1024 * di + 128 * do: 1024 * di + 128 * (do + 1)],
                                    hT[:, TOK * di + CH * ch: TOK * di + CH * (ch + 1)],
                                    start=(di == 0), stop=(di == DT - 1))
                return last
            wt = [(ve, evacs[i - 2])] if i >= 2 else []
            g.on("tensor", mm, wait=wt, inc=pe)
            def evac(e, ch=ch, do=do, ysl=ysl):
                xsl = xT[:, TOK * do + CH * ch: TOK * do + CH * (ch + 1)]
                e.scalar_tensor_tensor(out=ex_t[:], in0=ysl, scalar=beV_sb[:, do: do + 1],
                                       in1=fB_sb[:, CH * ch: CH * (ch + 1)],
                                       op0=ALU.add, op1=ALU.mult)
                e.drain()
                return e.tensor_tensor(out=xsl, in0=xsl, in1=ex_t[:], op=ALU.add)
            g.on("vector", evac, wait=[pe], inc=ve)
            evacs.append(ve.n)
        g.run_phase("expert")


def emit_moe_phase(g, cm, xT, weights, totals_mode, tot_io, after_loads=None):
    nc = g.nc
    qs, qg, pe, ve, se = g.sem["qs"], g.sem["qg"], g.sem["pe"], g.sem["ve"], g.sem["se"]
    apply_mode = totals_mode != "out"

    hT = g.sb("moe_hT", [128, DT * TOK], BF16)
    ln = alloc_ln(g)
    gt = alloc_gating(g)
    ro = {
        "wg_sb": g.sb("moe_wg", [128, 8 * DT], F32),
        "ngsum_sb": g.sb("moe_ngsum", [1, 8], F32),
        "rb8_sb": g.sb("moe_rb8", [8, CH], F32),
        "lg_sb": g.sb("moe_lg", [8, TOK], F32),
    }
    if apply_mode:
        we_sb = g.sb("moe_we", [128, DT * 1024], BF16)
        beV_sb = g.sb("moe_beV", [128, 8], F32)
        ex_t = g.sb("moe_ex_t", [128, CH], F32)

    g.on("gpsimd", lambda e: e.dma_start(
        out=ro["wg_sb"][:].rearrange("p (a c) -> p a c", a=DT),
        in_=weights["wg"].rearrange("(a p) c -> p a c", p=128)), inc=qg, inck=16)
    if apply_mode:
        g.on("gpsimd", lambda e: e.dma_start(
            out=we_sb[:].rearrange("p (a c) -> p a c", a=DT),
            in_=weights["we_bf"].rearrange("(a p) c -> p a c", p=128)), inc=qg, inck=16)
    if after_loads is not None:
        after_loads()

    with g.ps("moe_gsum", [1, 8], F32) as gsum_ps:
        def gsum(e):
            for d in range(DT):
                last = e.matmul(gsum_ps[:], cm["ones128_f32"][:],
                                ro["wg_sb"][:, 8 * d: 8 * (d + 1)], start=(d == 0), stop=(d == DT - 1))
            return last
        g.on("tensor", gsum, wait=[qs, qg, ve], inc=pe)
        g.on("vector", lambda e: e.tensor_scalar(out=ro["ngsum_sb"][:], in0=gsum_ps[:],
                                                 scalar1=-1.0, scalar2=None, op0=ALU.mult),
             wait=[pe], inc=ve)
        g.run_phase("moe_prep")

    emit_ln_phase(g, cm, ln, xT, hT, router=ro, name="moe_ln")
    if apply_mode:
        for do in range(DT):
            g.on("gpsimd", lambda e, do=do: e.dma_start(
                out=beV_sb[:, do: do + 1],
                in_=weights["beh"].rearrange("a (f p) -> (a f) p", p=128)[do: do + 1]
                    .rearrange("f p -> p f")),
                inc=qg, inck=16)
    emit_gating(g, cm, ro["lg_sb"], gt, totals_mode, tot_io)
    if not apply_mode:
        return
    emit_expert(g, cm, xT, hT, gt["fB_sb"], we_sb, beV_sb, ex_t)


def build_moe(totals_mode):
    nc = bass.Bass("TRN2", target_bir_lowering=False)
    x_in = decl_in(nc, "x", [D, TOK])
    wg = decl_in(nc, "wg", [D, 8])
    cd = const_decls_subset(nc, ("id8", "id128", "id1", "wsel"))
    tot_io = {}
    weights = {"wg": wg}
    apply_mode = totals_mode != "out"
    if totals_mode == "out":
        tot_io["tot_out_ext"] = decl_out(nc, "tot", [8, 1])
    else:
        tot_io["tot_all"] = decl_in(nc, "tot_all", [8, 8])
        we = decl_in(nc, "we", [D, D])
        be = decl_in(nc, "be", [1, D])
        x_out = decl_out(nc, "x_out", [D, TOK])
        weights["we_bf"] = nc.dram_tensor("we_bf", [D, D], BF16).ap()
        weights["beh"] = nc.dram_tensor("beh", [1, D], BF16).ap()

    with ExitStack() as ctx:
        g = G(nc, ctx)
        cm = alloc_common(g, cd, ("id8", "id128", "id1", "wsel"))
        xT = g.sb("xT", [128, DT * TOK], F32)
        load_xT(g, xT, x_in)
        if apply_mode:
            emit_casts(g, [(weights["we_bf"], we), (weights["beh"], be)])
        g.w("tensor", g.sem["qs"])
        g.w("scalar", g.sem["qs"])
        g.w("vector", g.sem["qs"])
        g.run_phase("load")
        emit_moe_phase(g, cm, xT, weights, totals_mode, tot_io)
        if apply_mode:
            store_xT(g, xT, x_out)
        else:
            g.run_phase("end")
    return nc


SH_Q = 128 * TOK
SH_K = 128 * TOK
SH_V = TOK * 130
SHARD = SH_Q + SH_K + SH_V
RET = 128 * TOK


def emit_qkv(g, cm, xT, hT, w_bf, sendqk, sendv_d, rope_on, v_own=None,
             on_qk_done=None, w_wait=()):
    nc = g.nc
    qs, qg, pe, ve, se = g.sem["qs"], g.sem["qg"], g.sem["pe"], g.sem["ve"], g.sem["se"]
    sendv = sendqk

    w_sb2 = g.sb("qkv_w", [128, 2 * DT * 1024], BF16)
    o_sb = g.sb("qkv_o", [128, 2 * TOK], BF16)
    z_sb = g.sb("qkv_z", [128, TOK], BF16) if rope_on else None
    if v_own is None:
        v_own = g.sb("qkv_vown", [128, 16 * 1040], BF16)

    wslot = {"wq": 0, "wk": 1, "wv": 0}
    wsems = (g.sem["wa"], g.sem["wb"])

    def load_w(which, wait_extra=()):
        par = wslot[which]
        g.on("sync", lambda e, par=par: e.dma_start(
            out=w_sb2[:, DT * 1024 * par: DT * 1024 * (par + 1)].rearrange("p (a c) -> p a c", a=DT),
            in_=w_bf[which].rearrange("(a p) c -> p a c", p=128)),
            wait=[qg] + list(wait_extra) + list(w_wait), inc=wsems[par], inck=16)
        return wsems[par].n

    rope_prev = [None]
    with g.ps("qkv_q", [128, 2 * CH], F32) as q_ps, \
         g.ps("qkv_zp", [128, TOK], F32) as z_ps:
        for which, off in (("wq", 0), ("wk", SH_Q)):
            if which == "wq":
                wn = load_w("wq")
                kn = load_w("wk")
                wcur = wslot["wq"]
            else:
                wn, wcur = kn, wslot["wk"]
            g.on("tensor", None, wait=[(wsems[wcur], wn)])
            w_sb = w_sb2[:, DT * 1024 * wcur: DT * 1024 * (wcur + 1)]
            evl = []
            for m in range(DT):
                par = m % 2
                ot = o_sb[:, TOK * par: TOK * (par + 1)]
                for ch in range(NCH):
                    i = m * NCH + ch
                    ysl = q_ps[:, CH * (ch % 2): CH * (ch % 2 + 1)]
                    def mm(e, m=m, ch=ch, ysl=ysl):
                        for d in range(DT):
                            last = e.matmul(ysl, w_sb[:, 1024 * d + 128 * m: 1024 * d + 128 * (m + 1)],
                                            hT[:, TOK * d + CH * ch: TOK * d + CH * (ch + 1)],
                                            start=(d == 0), stop=(d == DT - 1))
                        return last
                    g.on("tensor", mm, wait=[(se, evl[i - 2])] if i >= 2 else [se], inc=pe)
                    g.on("scalar", lambda e, ch=ch, ysl=ysl, ot=ot: e.copy(
                        out=ot[:, CH * ch: CH * (ch + 1)], in_=ysl), wait=[pe, qg], inc=se)
                    evl.append(se.n)
                if rope_on:
                    zw = [(se, evl[-1])]
                    if rope_prev[0] is not None:
                        zw.append((ve, rope_prev[0]))
                    def zmm(e, ot=ot):
                        for ch in range(NCH):
                            last = e.matmul(z_ps[:, CH * ch: CH * (ch + 1)],
                                            cm["PT_sb"][:], ot[:, CH * ch: CH * (ch + 1)],
                                            start=True, stop=True)
                        return last
                    g.on("tensor", zmm, wait=zw, inc=pe)
                    zn = pe.n
                    def rope(e, ot=ot):
                        e.tensor_tensor(out=z_sb[:], in0=z_ps[:],
                                        in1=cm["ropeS_sb"][:], op=ALU.mult)
                        e.drain()
                        e.tensor_tensor(out=ot[:], in0=ot[:],
                                        in1=cm["ropeC_sb"][:], op=ALU.mult)
                        e.drain()
                        last = e.tensor_tensor(out=ot[:], in0=ot[:], in1=z_sb[:], op=ALU.add)
                        e.drain()
                        return last
                    g.on("vector", rope, wait=[(pe, zn)], inc=ve)
                    rope_prev[0] = ve.n
                g.w("gpsimd", se, ve)
                g.on("gpsimd", lambda e, m=m, off=off, ot=ot: e.dma_start(
                    out=sendv[m, off + 0: off + 128 * TOK].rearrange("(p t) -> p t", p=128),
                    in_=ot[:]), inc=qg, inck=16)
            g.run_phase("qkv_" + which)

    if on_qk_done is not None:
        on_qk_done()

    vn = load_w("wv", wait_extra=[pe])
    w_sb = w_sb2[:, 0: DT * 1024]
    g.on("vector", lambda e: e.memset(
        v_own[:].rearrange("p (t f) -> p t f", f=65)[:, :, 64:65], 1.0),
        wait=[(g.sem["pe"], g.sem["pe"].n)], inc=ve)
    with g.ps("qkv_v", [128, 2 * CH], F32) as v_ps:
        g.on("tensor", None, wait=[(wsems[0], vn)])
        g.w("tensor", ve)
        for tt in range(16):
            for c2 in range(2):
                i = tt * 2 + c2
                ysl = v_ps[:, CH * (i % 2): CH * (i % 2 + 1)]
                def vmm(e, tt=tt, c2=c2, ysl=ysl):
                    for d in range(DT):
                        last = e.matmul(ysl, hT[:, TOK * d + 128 * tt: TOK * d + 128 * (tt + 1)],
                                        w_sb[:, 1024 * d + CH * c2: 1024 * d + CH * (c2 + 1)],
                                        start=(d == 0), stop=(d == DT - 1))
                    return last
                g.on("tensor", vmm, wait=[se] if i >= 2 else [], inc=pe)
                st = 1040 * tt + 520 * c2
                def vev(e, st=st, ysl=ysl):
                    last = e.tensor_scalar(
                        out=v_own[:, st: st + 520].rearrange("p (h f) -> p h f", f=65)[:, :, 0:64],
                        in0=ysl.rearrange("p (h f) -> p h f", f=64),
                        scalar1=1.0, scalar2=None, op0=ALU.mult)
                    e.drain()
                    return last
                g.on("vector", vev, wait=[pe], inc=se)
        g.w("sync", se)
        for j in range(NCORE):
            g.on("sync", lambda e, j=j: e.dma_start(
                out=sendv_d[j, 0: SH_V].rearrange("(t p f) -> p t f", p=128, f=130),
                in_=v_own[:].rearrange("p (t f) -> p t f", f=1040)[:, :, 130 * j: 130 * (j + 1)]),
                inc=qs, inck=16)
        g.run_phase("qkv_v")


def emit_flash(g, cm, recvqk, recvv, ret, causal, bg=None, wait_qk=None, wait_v=None):
    nc = g.nc
    qs, qg, pe, ve, se = g.sem["qs"], g.sem["qg"], g.sem["pe"], g.sem["ve"], g.sem["se"]

    qf = g.sb("fl_q", [128, 2 * SEQ], BF16)
    kf = g.sb("fl_k", [128, 2 * SEQ], BF16)
    vf = g.sb("fl_v", [128, 2 * 64 * 130], BF16)
    p_sb = g.sb("fl_p", [128, 3 * 1024], BF16)

    first_s, first_g = [wait_qk] if wait_qk else [], [wait_v] if wait_v else []
    for b in range(2):
        for r4 in range(4):
            r = 4 * b + r4
            g.on("sync", lambda e, b=b, r4=r4, r=r: e.dma_start(
                out=qf[:, SEQ * b + TOK * r4: SEQ * b + TOK * (r4 + 1)],
                in_=recvqk[r, 0: SH_Q].rearrange("(p t) -> p t", p=128)),
                wait=first_s, inc=qs, inck=16)
            first_s = []
            g.on("sync", lambda e, b=b, r4=r4, r=r: e.dma_start(
                out=kf[:, SEQ * b + TOK * r4: SEQ * b + TOK * (r4 + 1)],
                in_=recvqk[r, SH_Q: SH_Q + SH_K].rearrange("(p t) -> p t", p=128)), inc=qs, inck=16)
            g.on("gpsimd", lambda e, b=b, r4=r4, r=r: e.dma_start(
                out=vf[:].rearrange("p (t f) -> p t f", f=130)[:, 64 * b + 16 * r4: 64 * b + 16 * (r4 + 1), :],
                in_=recvv[r, 0: SH_V].rearrange("(t p f) -> p t f", p=128, f=130)),
                wait=first_g, inc=qg, inck=16)
            first_g = []
    g.run_phase("fl_load")
    bw_se, bw_ve = g.sem["se"].n, g.sem["ve"].n

    with g.ps("fl_s", [128, 2048], F32) as s_ps, \
         g.ps("fl_o", [65, 2048], F32) as o_ps:
        oraw = g.sb("fl_oraw", [65, 2 * 2048], BF16)
        bcB2 = g.sb("fl_bcB2", [64, 2048], BF16)
        o_sb2 = g.sb("fl_osb2", [64, 2048], BF16)
        dscr = g.nc.dram_tensor(g.prefix + "fl_dscr", [2, 2048], BF16).ap()
        n0, n1 = g.sem["n0"], g.sem["n1"]
        nrm_ve = [g.sem["ve"].n]
        oev_se = [g.sem["se"].n]
        otts_done = {}
        send_done = {}
        qcidx = [0]
        prev_exp = None
        g.w("tensor", qs, qg, ve)
        for b in range(2):
            for qc in range(8):
                units = []
                for qh in range(2):
                    nk = (8 * qc + 4 * (qh + 1)) if causal else 64
                    for kt in range(nk):
                        units.append((qh, kt, kt == 0, kt == nk - 1))
                NU = len(units)
                exp_done = {}
                s_done = {}

                def emit_av(u):
                    qh, kt, first, last = units[u]
                    psl = p_sb[:, 1024 * (u % 3): 1024 * (u % 3 + 1)]
                    diag = causal and kt >= 8 * qc + 4 * qh
                    if diag:
                        off = 896 - 128 * (kt - 8 * qc) + 512 * qh
                        def dmask(e, psl=psl, off=off):
                            e.tensor_tensor(out=psl[:, 0:512], in0=psl[:, 0:512],
                                            in1=cm["Tbase_sb"][:, off: off + 512], op=ALU.mult)
                            e.drain()
                            last_i = e.tensor_tensor(out=psl[:, 512:1024], in0=psl[:, 512:1024],
                                                     in1=cm["Tbase_sb"][:, off: off + 512], op=ALU.mult)
                            e.drain()
                            return last_i
                        g.on("vector", dmask, wait=[(se, exp_done[u])], inc=ve)
                        avwait = [(ve, ve.n)]
                    else:
                        avwait = [(se, exp_done[u])]
                    if first:
                        avwait.append((se, oev_se[0]))
                    def avmm(e, kt=kt, qh=qh, psl=psl, first=first, last=last, b=b):
                        for h in range(2):
                            last_i = e.matmul(
                                o_ps[:, 1024 * h + 512 * qh: 1024 * h + 512 * (qh + 1)],
                                vf[:, 8320 * b + 130 * kt + 65 * h: 8320 * b + 130 * kt + 65 * (h + 1)],
                                psl[:, 512 * h: 512 * (h + 1)],
                                start=first, stop=last)
                        return last_i
                    g.on("tensor", avmm, wait=avwait, inc=pe)

                for u, (qh, kt, first, last) in enumerate(units):
                    par = u % 2
                    ssl = s_ps[:, 1024 * par: 1024 * (par + 1)]
                    q0 = SEQ * b + 1024 * qc + 512 * qh
                    def smm(e, b=b, kt=kt, q0=q0, ssl=ssl):
                        for h, tp in ((0, (0, 0)), (1, (64, 0))):
                            base = 64 * h
                            last_i = e.matmul(
                                ssl[:, 512 * h: 512 * (h + 1)],
                                kf[base:base + 64, SEQ * b + 128 * kt: SEQ * b + 128 * (kt + 1)],
                                qf[base:base + 64, q0: q0 + 512],
                                start=True, stop=True, tile_position=tp)
                        return last_i
                    if u >= 2:
                        swait = [(se, exp_done[u - 2])]
                    elif prev_exp is None:
                        swait = [(se, bw_se), (ve, bw_ve)]
                    else:
                        swait = [(se, prev_exp[u])]
                    g.on("tensor", smm, wait=swait, inc=pe)
                    s_done[u] = pe.n
                    psl = p_sb[:, 1024 * (u % 3): 1024 * (u % 3 + 1)]
                    g.on("scalar", lambda e, psl=psl, ssl=ssl: e.activation(
                        out=psl, in_=ssl, func=AF.Exp, scale=ATTN_SCALE),
                        wait=[(pe, s_done[u])], inc=se)
                    exp_done[u] = se.n
                    if u >= 1:
                        emit_av(u - 1)
                emit_av(NU - 1)
                prev_exp = (exp_done[NU - 2], exp_done[NU - 1])

                qci = qcidx[0]
                qcidx[0] += 1
                slot = qci % 2
                osl = oraw[:, 2048 * slot: 2048 * (slot + 1)]
                bsl = bcB2[:]
                csl = o_sb2[:]
                oevw = [(pe, pe.n)]
                if qci >= 2:
                    oevw.append((ve, otts_done[qci - 2]))
                g.on("scalar", lambda e, osl=osl: e.copy(out=osl, in_=o_ps[:]),
                     wait=oevw, inc=se)
                oev_se[0] = se.n
                def recips(e, osl=osl):
                    with g.nc.allow_low_precision(reason="softmax denom recip in bf16; denominator sums bf16 terms"):
                        for h in range(2):
                            e.reciprocal(out=osl[64:65, 1024 * h: 1024 * (h + 1)],
                                         in_=osl[64:65, 1024 * h: 1024 * (h + 1)])
                            e.drain()
                    return e.nop()
                g.on("vector", recips, wait=[(se, oev_se[0])], inc=ve)
                rec_ve = ve.n
                g.on("gpsimd", lambda e, slot=slot, osl=osl: e.dma_start(
                    out=dscr[slot: slot + 1, :], in_=osl[64:65, :]),
                    wait=[(ve, rec_ve)], inc=n0, inck=16)
                g.on("gpsimd", lambda e, slot=slot, bsl=bsl: e.dma_start(
                    out=bsl, in_=dscr[slot: slot + 1, :].partition_broadcast(64)),
                    wait=[(n0, g.sem["n0"].n)], inc=n1, inck=16)
                bc_n1 = n1.n
                ottw = [(n1, bc_n1)]
                if qci >= 1:
                    ottw.append((qg, send_done[qci - 1]))
                def otts(e, osl=osl, bsl=bsl, csl=csl):
                    last_i = e.tensor_tensor(out=csl, in0=osl[0:64, :], in1=bsl, op=ALU.mult)
                    e.drain()
                    return last_i
                g.on("vector", otts, wait=ottw, inc=ve)
                otts_done[qci] = ve.n
                nrm_ve[0] = ve.n
                rdest = 4 * b + qc // 2
                col = 1024 * (qc % 2)
                for h in range(2):
                    g.on("gpsimd", lambda e, rdest=rdest, h=h, col=col, csl=csl: e.dma_start(
                        out=ret.rearrange("r (p t) -> r p t", p=128)[rdest, 64 * h: 64 * (h + 1), col: col + 1024],
                        in_=csl[:, 1024 * h: 1024 * (h + 1)]),
                        wait=[(ve, otts_done[qci])], inc=qg, inck=16)
                send_done[qci] = qg.n
                if bg is not None:
                    for _ in range(11):
                        if next(bg, "END") == "END":
                            bg = None
                            break
            g.run_phase("fl_b%d" % b)
        if bg is not None:
            for _ in iter(lambda: next(bg, "END"), "END"):
                pass
        g.w("gpsimd", qg)
        g.run_phase("fl_end")


def emit_outproj(g, cm, xT, ret_recv, wo_bf, bo_bf, w_wait=()):
    nc = g.nc
    qs, pe, ve, se = g.sem["qs"], g.sem["pe"], g.sem["ve"], g.sem["se"]
    if w_wait:
        g.on("sync", None, wait=list(w_wait))
    at_sb = g.sb("pr_at", [128, DT * TOK], BF16)
    wo_sb = g.sb("pr_wo", [128, DT * 1024], BF16)
    boh_sb = g.sb("pr_bo", [1, 1024], BF16)
    for r in range(DT):
        g.on("sync", lambda e, r=r: e.dma_start(
            out=at_sb[:, TOK * r: TOK * (r + 1)],
            in_=ret_recv[r, :].rearrange("(p t) -> p t", p=128)),
            wait=[g.sem["qg"]], inc=qs, inck=16)
    g.on("sync", lambda e: e.dma_start(
        out=wo_sb[:].rearrange("p (a c) -> p a c", a=DT),
        in_=wo_bf.rearrange("(a p) c -> p a c", p=128)), wait=[g.sem["qg"]], inc=qs, inck=16)
    g.on("sync", lambda e: e.dma_start(out=boh_sb[:], in_=bo_bf),
         wait=[g.sem["qg"]], inc=qs, inck=16)
    with g.ps("pr_u", [128, 2 * CH], F32) as u_ps:
        g.w("tensor", qs)
        evacs = []
        for i, (dcol, ch) in enumerate([(dc, c) for dc in range(DT) for c in range(NCH)]):
            ysl = u_ps[:, CH * (i % 2): CH * (i % 2 + 1)]
            def mm(e, dcol=dcol, ch=ch, ysl=ysl):
                e.matmul(ysl, boh_sb[:, 128 * dcol: 128 * (dcol + 1)],
                         cm["onesrow_bf"][:], start=True, stop=False)
                for it in range(DT):
                    last = e.matmul(ysl, wo_sb[:, 1024 * it + 128 * dcol: 1024 * it + 128 * (dcol + 1)],
                                    at_sb[:, TOK * it + CH * ch: TOK * it + CH * (ch + 1)],
                                    start=False, stop=(it == DT - 1))
                return last
            g.on("tensor", mm, wait=[(ve, evacs[i - 2])] if i >= 2 else [], inc=pe)
            def evac(e, dcol=dcol, ch=ch, ysl=ysl):
                xsl = xT[:, TOK * dcol + CH * ch: TOK * dcol + CH * (ch + 1)]
                return e.tensor_tensor(out=xsl, in0=xsl, in1=ysl, op=ALU.add)
            g.on("vector", evac, wait=[pe], inc=ve)
            evacs.append(g.sem["ve"].n)
        g.run_phase("outproj")


def build_qkv(layer):
    nc = bass.Bass("TRN2", target_bir_lowering=False)
    x_in = decl_in(nc, "x", [D, TOK])
    rope_on = layer == 0
    cnames = ("PT", "ropeC", "ropeS") if rope_on else ()
    cd = const_decls_subset(nc, cnames)
    send = decl_out(nc, "send", [NCORE, SHARD], BF16)
    w_bf = {k: nc.dram_tensor("w_" + k, [D, D], BF16).ap() for k in ("wq", "wk", "wv")}
    if layer == 0:
        wq = decl_in(nc, "Wq", [D, D])
        wkv = decl_in(nc, "Wkv", [D, 2 * D])
        casts = [(w_bf["wq"], wq), (w_bf["wk"], wkv[:, 0:D]), (w_bf["wv"], wkv[:, D:2 * D])]
    else:
        casts = [(w_bf[k], decl_in(nc, n, [D, D]))
                 for k, n in (("wq", "Wq"), ("wk", "Wk"), ("wv", "Wv"))]
    with ExitStack() as ctx:
        g = G(nc, ctx)
        cm = alloc_common(g, cd, cnames)
        xT = g.sb("xT", [128, DT * TOK], F32)
        hT = g.sb("at_hT", [128, DT * TOK], BF16)
        load_xT(g, xT, x_in)
        emit_casts(g, casts)
        g.w("tensor", g.sem["qs"])
        g.w("scalar", g.sem["qs"])
        g.w("vector", g.sem["qs"])
        g.run_phase("load")
        vown = g.sb("bq_vown", [128, 16 * 1040], BF16)
        ln = alloc_ln(g, sq=vown[:, 0: DT * CH])
        emit_ln_phase(g, cm, ln, xT, hT, name="at_ln")
        emit_qkv(g, cm, xT, hT, w_bf, send[:, 0: SH_Q + SH_K],
                 send[:, SH_Q + SH_K: SHARD], rope_on, v_own=vown)
    return nc


def build_flash(causal):
    nc = bass.Bass("TRN2", target_bir_lowering=False)
    cnames = ("Tbase",) if causal else ()
    cd = const_decls_subset(nc, cnames)
    recv = decl_in(nc, "recv", [NCORE, SHARD], BF16)
    ret = decl_out(nc, "ret", [NCORE, RET], BF16)
    with ExitStack() as ctx:
        g = G(nc, ctx)
        cm = alloc_common(g, cd, cnames)
        g.run_phase("load")
        emit_flash(g, cm, recv[:, 0: SH_Q + SH_K], recv[:, SH_Q + SH_K: SHARD],
                   ret, causal)
    return nc


def build_proj():
    nc = bass.Bass("TRN2", target_bir_lowering=False)
    x_in = decl_in(nc, "x", [D, TOK])
    ret_recv = decl_in(nc, "ret_recv", [NCORE, RET], BF16)
    wo = decl_in(nc, "Wo", [D, D])
    bo = decl_in(nc, "bo", [1, D])
    x_out = decl_out(nc, "x_out", [D, TOK])
    wo_bf = nc.dram_tensor("wo_bf", [D, D], BF16).ap()
    bo_bf = nc.dram_tensor("bo_bf", [1, D], BF16).ap()
    with ExitStack() as ctx:
        g = G(nc, ctx)
        cm = alloc_common(g, cd := {}, ())
        xT = g.sb("xT", [128, DT * TOK], F32)
        load_xT(g, xT, x_in)
        emit_casts(g, [(wo_bf, wo), (bo_bf, bo)])
        g.w("tensor", g.sem["qs"])
        g.w("vector", g.sem["qs"])
        g.run_phase("load")
        emit_outproj(g, cm, xT, ret_recv, wo_bf, bo_bf)
        store_xT(g, xT, x_out)
    return nc


def run_attn_subblock(xs_parts, layer, wdict):
    consts = get_consts()
    nc1 = build_qkv(layer)
    cnames = ("PT", "ropeC", "ropeS") if layer == 0 else ()
    im = []
    for c in range(NCORE):
        m = {"x": xs_parts[c], **{k: consts[c][k] for k in cnames}}
        if layer == 0:
            m["Wq"] = wdict["Wq"]
            m["Wkv"] = wdict["Wkv"]
        else:
            m["Wq"], m["Wk"], m["Wv"] = wdict["Wq"], wdict["Wk"], wdict["Wv"]
        im.append(m)
    res = run_bass_kernel_spmd(nc1, im, core_ids=list(range(NCORE)))
    sends = [np.asarray(res.results[c]["send"]) for c in range(NCORE)]
    recvs = [np.ascontiguousarray(np.stack([sends[r][c] for r in range(NCORE)]))
             for c in range(NCORE)]
    nc2 = build_flash(causal=(layer == 1))
    cnames2 = ("Tbase",) if layer == 1 else ()
    im = [{"recv": recvs[c], **{k: consts[c][k] for k in cnames2}} for c in range(NCORE)]
    res = run_bass_kernel_spmd(nc2, im, core_ids=list(range(NCORE)))
    rets = [np.asarray(res.results[c]["ret"]) for c in range(NCORE)]
    ret_recvs = [np.ascontiguousarray(np.stack([rets[r][c] for r in range(NCORE)]))
                 for c in range(NCORE)]
    nc3 = build_proj()
    wo = wdict["Wout"] if layer == 0 else wdict["Wo"]
    bo = wdict["bout"] if layer == 0 else wdict["bo"]
    im = [{"x": xs_parts[c], "ret_recv": ret_recvs[c], "Wo": wo,
           "bo": bo.reshape(1, D)} for c in range(NCORE)]
    res = run_bass_kernel_spmd(nc3, im, core_ids=list(range(NCORE)))
    return [np.asarray(res.results[c]["x_out"]) for c in range(NCORE)]


def alloc_quant(g):
    return {"wt": g.sb("qz_wt", [128, 2 * 1024], F32),
            "racc": g.sb("qz_racc", [128, 32], F32),
            "r1": g.sb("qz_r1", [128, 1], F32),
            "rrow": g.sb("qz_rrow", [1, 128], F32),
            "tt1": g.sb("qz_t1", [1, 1], F32),
            "rB": g.sb("qz_rB", [128, 1], F32),
            "a": g.sb("qz_a", [128, 1024], BF16),
            "q": g.sb("qz_q", [128, 2 * 1024], BF16)}


def emit_quant(g, cm, w_in, wq_out, kind, scale_half, name, scr, bufs):
    qs, ve = g.sem["qs"], g.sem["ve"]
    wp = (g.sem["wa"], g.sem["wb"])
    sp = (g.sem["sc"], g.sem["sd"])
    wt, racc = bufs["wt"], bufs["racc"]
    r1, rrow, tt1, rB = bufs["r1"], bufs["rrow"], bufs["tt1"], bufs["rB"]
    a_sb, q_sb = bufs["a"], bufs["q"]
    NG = 32

    def src(dram, i):
        if kind == "w1":
            return dram[128 * i: 128 * (i + 1), :]
        r, c = i // 4, i % 4
        return dram[128 * r: 128 * (r + 1), 1024 * c: 1024 * (c + 1)]

    def wview(par):
        return wt[:, 1024 * par: 1024 * (par + 1)]

    def qview(par):
        return q_sb[:, 1024 * par: 1024 * (par + 1)]

    red_done = {}
    for i in range(NG):
        par = i % 2
        g.on("sync", lambda e, i=i, par=par: e.dma_start(
            out=wview(par), in_=src(w_in, i)),
            wait=[(ve, red_done[i - 2])] if i >= 2 else [ve], inc=wp[par], inck=16)
        ld = wp[par].n
        g.on("vector", lambda e, i=i, par=par: e.tensor_reduce(
            out=racc[:, i: i + 1], in_=wview(par),
            axis=AX.X, op=ALU.add, apply_absolute_value=True),
            wait=[(wp[par], ld)], inc=ve)
        red_done[i] = ve.n
        yield

    qa, gp = g.sem["qa"], g.sem["gp"]
    g.on("vector", lambda e: e.tensor_reduce(out=r1[:], in_=racc[:, 0:NG],
                                             axis=AX.X, op=ALU.add),
         wait=[(ve, red_done[NG - 1])], inc=ve)
    g.on("sync", lambda e: e.dma_start(out=scr.rearrange("a b -> b a"), in_=r1[:]),
         wait=[ve], inc=qa, inck=16)
    g.on("sync", lambda e: e.dma_start(out=rrow[:], in_=scr), wait=[qa], inc=gp, inck=16)
    def gam(e):
        e.tensor_reduce(out=tt1[:], in_=rrow[:], axis=AX.X, op=ALU.add)
        e.drain()
        e.tensor_scalar(out=tt1[:], in0=tt1[:], scalar1=1.0 / (4096.0 * 1024.0),
                        scalar2=QEPS, op0=ALU.mult, op1=ALU.add)
        e.drain()
        return e.reciprocal(out=tt1[:], in_=tt1[:])
    g.on("vector", gam, wait=[gp], inc=ve)
    g.on("sync", lambda e: e.dma_start(out=scr[0:1, 0:1], in_=tt1[:]),
         wait=[ve], inc=qa, inck=16)
    g.on("sync", lambda e: e.dma_start(out=rB[:], in_=scr[0:1, 0:1].partition_broadcast(128)),
         wait=[qa], inc=gp, inck=16)
    rB_ready = gp.n
    yield

    qz_done = {}
    st_done = {}
    for i in range(NG):
        par = i % 2
        g.on("sync", lambda e, i=i, par=par: e.dma_start(
            out=wview(par), in_=src(w_in, i)),
            wait=[(ve, qz_done[i - 2])] if i >= 2 else [(ve, red_done[NG - 1])],
            inc=wp[par], inck=16)
        ld = wp[par].n
        def qz(e, i=i, par=par):
            w = wview(par)
            qsl = qview(par)
            e.tensor_scalar(out=a_sb[:], in0=w, scalar1=rB[:], scalar2=0.5,
                            op0=ALU.mult, op1=ALU.is_ge)
            e.drain()
            e.tensor_scalar(out=qsl, in0=w, scalar1=rB[:], scalar2=-0.5,
                            op0=ALU.mult, op1=ALU.is_le)
            e.drain()
            e.tensor_tensor(out=qsl, in0=a_sb[:], in1=qsl, op=ALU.subtract)
            e.drain()
            if scale_half:
                e.tensor_scalar(out=qsl, in0=qsl, scalar1=0.5, scalar2=None, op0=ALU.mult)
                e.drain()
            return e.nop()
        wz = [(wp[par], ld), (g.sem["gp"], rB_ready),
              (sp[par], st_done[i - 2] if i >= 2 else sp[par].n)]
        g.on("vector", qz, wait=wz, inc=ve)
        qz_done[i] = ve.n
        g.on("sync", lambda e, i=i, par=par: e.dma_start(
            out=src(wq_out, i), in_=qview(par)),
            wait=[(ve, qz_done[i])], inc=sp[par], inck=16)
        st_done[i] = sp[par].n
        yield


def emit_ffn(g, cm, xT, ln, wq1, wq2, b1v_sb, b2v_sb, hT_ch, hff, w1T_sb, w2T_sb):
    nc = g.nc
    qs, qg, pe, ve, se = g.sem["qs"], g.sem["qg"], g.sem["pe"], g.sem["ve"], g.sem["se"]
    wpar_sem = (g.sem["wa"], g.sem["wb"], g.sem["sc"])

    for tch2 in range(2):
        emit_ln_phase(g, cm, ln, xT, hT_ch, chunks=[2 * tch2, 2 * tch2 + 1],
                      hT_off=CH * 2 * tch2, name="ffn_ln%d" % tch2)
        if tch2 == 0:
            for d in range(DT):
                g.on("sync", lambda e, d=d: e.dma_start_transpose(
                    out=w1T_sb[:, 4096 * d: 4096 * (d + 1)],
                    in_=wq1[:, 128 * d: 128 * (d + 1)]),
                    wait=[pe, qg] if d == 0 else [], inc=qs, inck=16)
        for half in range(2):
            tch = 2 * tch2 + half
            hoff = CH * half
            with g.ps("ffn_h1_%d" % tch, [128, 2 * CH], F32) as h1_ps:
                if half == 0:
                    g.w("tensor", qs, g.sem["ve"])
                relus = []
                for f in range(32):
                    ysl = h1_ps[:, CH * (f % 2): CH * (f % 2 + 1)]
                    def mm1(e, f=f, ysl=ysl, hoff=hoff):
                        for d in range(DT):
                            last = e.matmul(ysl, w1T_sb[:, 4096 * d + 128 * f: 4096 * d + 128 * (f + 1)],
                                            hT_ch[:, 1024 * d + hoff: 1024 * d + hoff + CH],
                                            start=(d == 0), stop=(d == DT - 1))
                        return last
                    g.on("tensor", mm1, wait=[(se, relus[f - 2])] if f >= 2 else [], inc=pe)
                    g.on("scalar", lambda e, f=f, ysl=ysl: e.activation(
                        out=hff[:, CH * f: CH * (f + 1)], in_=ysl, func=AF.Relu,
                        bias=b1v_sb[:, f: f + 1]),
                        wait=[pe], inc=se)
                    relus.append(se.n)
                relu_last = se.n
                g.run_phase("ffn_m1_%d" % tch)
            with g.ps("ffn_u_%d" % tch, [128, DT * CH], F32) as u_ps:
                mmdone = []
                for f in range(32):
                    par = f % 3
                    wsem = wpar_sem[par]
                    g.on("sync", lambda e, f=f, par=par: e.dma_start_transpose(
                        out=w2T_sb[:, 1024 * par: 1024 * (par + 1)],
                        in_=wq2[:, 128 * f: 128 * (f + 1)]),
                        wait=[(pe, mmdone[f - 3])] if f >= 3 else [pe],
                        inc=wsem, inck=16)
                    def mm2(e, f=f, par=par):
                        for dc in range(DT):
                            last = e.matmul(u_ps[:, CH * dc: CH * (dc + 1)],
                                            w2T_sb[:, 1024 * par + 128 * dc: 1024 * par + 128 * (dc + 1)],
                                            hff[:, CH * f: CH * (f + 1)],
                                            start=(f == 0), stop=(f == 31 and dc == DT - 1))
                        return last
                    g.on("tensor", mm2,
                         wait=[wsem, (se, relu_last)] if f == 0 else [wsem], inc=pe)
                    mmdone.append(pe.n)
                def evac(e, tch=tch):
                    for dc in range(DT):
                        xsl = xT[:, TOK * dc + CH * tch: TOK * dc + CH * (tch + 1)]
                        e.scalar_tensor_tensor(out=xsl, in0=u_ps[:, CH * dc: CH * (dc + 1)],
                                               scalar=b2v_sb[:, dc: dc + 1], in1=xsl,
                                               op0=ALU.add, op1=ALU.add)
                        last = e.drain()
                    return last
                g.on("vector", evac, wait=[pe], inc=ve)
                g.run_phase("ffn_m2_%d" % tch)


def emit_w2t_pass(g, wq2, wq2t, tx=None):
    wp = (g.sem["wa"], g.sem["wb"])
    sp = (g.sem["sc"], g.sem["sd"])
    if tx is None:
        tx = g.sb("w2tp_tx", [128, 1024], BF16)
    st_done = {}
    for f in range(32):
        par = f % 2
        g.on("sync", lambda e, f=f: e.dma_start_transpose(
            out=tx[:], in_=wq2[:, 128 * f: 128 * (f + 1)]),
            wait=[(sp[0], st_done[f - 1])] if f >= 1 else [(sp[0], sp[0].n), (sp[1], sp[1].n)],
            inc=wp[par], inck=16)
        ld = wp[par].n
        g.on("sync", lambda e, f=f: e.dma_start(
            out=wq2t[128 * f: 128 * (f + 1), :], in_=tx[:]),
            wait=[(wp[par], ld)], inc=sp[0], inck=16)
        st_done[f] = sp[0].n
    g.run_phase("w2tp")


def build_quant():
    nc = bass.Bass("TRN2", target_bir_lowering=False)
    w1 = decl_in(nc, "W1", [DFF, D])
    w2 = decl_in(nc, "W2", [D, DFF])
    wq1 = decl_out(nc, "wq1", [DFF, D], BF16)
    wq2 = decl_out(nc, "wq2", [D, DFF], BF16)
    scr1 = nc.dram_tensor("scr1", [1, 128], F32).ap()
    scr2 = nc.dram_tensor("scr2", [1, 128], F32).ap()
    with ExitStack() as ctx:
        g = G(nc, ctx)
        cm = alloc_common(g, {}, ())
        g.run_phase("init")
        qb = alloc_quant(g)
        for _ in emit_quant(g, cm, w1, wq1, "w1", False, "q1", scr1, qb):
            pass
        for _ in emit_quant(g, cm, w2, wq2, "w2", True, "q2", scr2, qb):
            pass
        g.w("gpsimd", g.sem["qs"], g.sem["ve"])
        g.run_phase("qdone")
    return nc


def build_ffn():
    nc = bass.Bass("TRN2", target_bir_lowering=False)
    x_in = decl_in(nc, "x", [D, TOK])
    wq1 = decl_in(nc, "wq1", [DFF, D], BF16)
    wq2 = decl_in(nc, "wq2", [D, DFF], BF16)
    b1 = decl_in(nc, "b1", [1, DFF])
    b2 = decl_in(nc, "b2", [1, D])
    x_out = decl_out(nc, "x_out", [D, TOK])
    wq2t = nc.dram_tensor("i_wq2t", [DFF, D], BF16).ap()
    with ExitStack() as ctx:
        g = G(nc, ctx)
        cm = alloc_common(g, {}, ())
        xT = g.sb("xT", [128, DT * TOK], F32)
        b1h_sb = g.sb("ffn_b1h", [1, DFF], BF16)
        b2h_sb = g.sb("ffn_b2h", [1, D], BF16)
        load_xT(g, xT, x_in)
        g.on("gpsimd", lambda e: e.dma_start(out=b1h_sb[:], in_=b1), inc=g.sem["qg"], inck=16)
        g.on("gpsimd", lambda e: e.dma_start(out=b2h_sb[:], in_=b2), inc=g.sem["qg"], inck=16)
        g.w("vector", g.sem["qg"])
        g.on("vector", lambda e: e.tensor_scalar(out=b2h_sb[:], in0=b2h_sb[:], scalar1=0.5,
                                                 scalar2=None, op0=ALU.mult), inc=g.sem["ve"])
        g.w("tensor", g.sem["qs"])
        g.w("scalar", g.sem["qs"])
        g.run_phase("load")
        ln = alloc_ln(g)
        hT_ch = ln["xbf"]
        hff = g.sb("ffn_hff", [128, 32 * CH], BF16)
        w1T_sb = g.sb("ffn_w1T", [128, DT * 4096], BF16)
        w2T_sb = g.sb("ffn_w2T", [128, 3 * 1024], BF16)
        emit_ffn(g, cm, xT, ln, wq1, wq2, b1h_sb, b2h_sb, hT_ch, hff, w1T_sb, w2T_sb)
        store_xT(g, xT, x_out)
    return nc


def run_ffn_subblock(xs_parts, wq1, wq2, b1, b2):
    nc = build_ffn()
    im = [{"x": xs_parts[c], "wq1": wq1, "wq2": wq2,
           "b1": b1.reshape(1, DFF), "b2": b2.reshape(1, D)} for c in range(NCORE)]
    res = run_bass_kernel_spmd(nc, im, core_ids=list(range(NCORE)))
    return [np.asarray(res.results[c]["x_out"]) for c in range(NCORE)]


def run_quant(W1, W2):
    nc = build_quant()
    im = [{"W1": W1, "W2": W2} for _ in range(NCORE)]
    res = run_bass_kernel_spmd(nc, im, core_ids=list(range(NCORE)))
    return np.asarray(res.results[0]["wq1"]), np.asarray(res.results[0]["wq2"])


def shard_x(x):
    xs = x.reshape(NCORE, TOK, D)
    return [np.ascontiguousarray(xs[c].T) for c in range(NCORE)]


def unshard_x(parts):
    x = np.stack([np.asarray(p).T for p in parts], axis=0)
    return np.ascontiguousarray(x.reshape(2, SEQ, D))


_CONSTS = None


def get_consts():
    global _CONSTS
    if _CONSTS is None:
        _CONSTS = [host_consts(c) for c in range(NCORE)]
    return _CONSTS


def run_moe_subblock(xs_parts, wg, we, be):
    consts = get_consts()
    names = ("id8", "id128", "id1", "wsel")
    nc1 = build_moe("out")
    im = [{"x": xs_parts[c], "wg": wg, **{k: consts[c][k] for k in names}}
          for c in range(NCORE)]
    res = run_bass_kernel_spmd(nc1, im, core_ids=list(range(NCORE)))
    tots = np.concatenate([np.asarray(res.results[c]["tot"]).reshape(1, 8)
                           for c in range(NCORE)], axis=0)
    tots = np.ascontiguousarray(tots)
    nc2 = build_moe("ext")
    im = [{"x": xs_parts[c], "wg": wg, "we": we, "be": be.reshape(1, D),
           "tot_all": tots, **{k: consts[c][k] for k in names}}
          for c in range(NCORE)]
    res = run_bass_kernel_spmd(nc2, im, core_ids=list(range(NCORE)))
    return [np.asarray(res.results[c]["x_out"]) for c in range(NCORE)]




def build_fused():
    nc = bass.Bass("TRN2", target_bir_lowering=False)
    cd = const_decls_subset(nc, tuple(CONST_SPECS))
    x_in = decl_in(nc, "x", [D, TOK])
    x_out = decl_out(nc, "x_out", [D, TOK])
    win = {}
    for nm, shape in (("L0_ffn_Wg", [D, 8]), ("L0_ffn_We", [D, D]), ("L0_ffn_be", [1, D]),
                      ("L0_attn_Wq", [D, D]), ("L0_attn_Wkv", [D, 2 * D]),
                      ("L0_attn_Wout", [D, D]), ("L0_attn_bout", [1, D]),
                      ("L1_attn_Wq", [D, D]), ("L1_attn_Wk", [D, D]), ("L1_attn_Wv", [D, D]),
                      ("L1_attn_Wo", [D, D]), ("L1_attn_bo", [1, D]),
                      ("L1_ffn_W1", [DFF, D]), ("L1_ffn_b1", [1, DFF]),
                      ("L1_ffn_W2", [D, DFF]), ("L1_ffn_b2", [1, D])):
        win[nm] = decl_in(nc, nm, shape)
    dram = lambda nm, shape, dt=BF16: nc.dram_tensor(nm, shape, dt).ap()
    we_bf = dram("i_we", [D, D]); beh = dram("i_beh", [1, D])
    a_w = {0: {}, 1: {}}
    for k in ("wq", "wk", "wv"):
        a_w[0][k] = dram("i_a0_" + k, [D, D])
        a_w[1][k] = dram("i_a1_" + k, [D, D])
    wo_bf = {0: dram("i_wo0", [D, D]), 1: dram("i_wo1", [D, D])}
    bo_bf = {0: dram("i_bo0", [1, D]), 1: dram("i_bo1", [1, D])}
    wq1 = dram("i_wq1", [DFF, D]); wq2 = dram("i_wq2", [D, DFF])
    wq2t = dram("i_wq2t", [DFF, D])
    exch = {}
    for L in (0, 1):
        exch[L] = {
            "sendqk": dram("i_sendqk%d" % L, [NCORE, SH_Q + SH_K]),
            "recvqk": dram("i_recvqk%d" % L, [NCORE, SH_Q + SH_K]),
            "sendv": dram("i_sendv%d" % L, [NCORE, SH_V]),
            "recvv": dram("i_recvv%d" % L, [NCORE, SH_V]),
            "rsend": dram("i_rsend%d" % L, [NCORE, RET]),
            "rrecv": dram("i_rrecv%d" % L, [NCORE, RET]),
        }
    scrq = {1: nc.dram_tensor("i_scr1", [1, 128], F32).ap(),
            2: nc.dram_tensor("i_scr2", [1, 128], F32).ap()}
    tots = {}
    for i in (0, 1):
        tots[i] = {"tot_in": nc.dram_tensor("i_totin%d" % i, [1, 8], F32).ap(),
                   "tot_all": nc.dram_tensor("i_totall%d" % i, [8, 8], F32).ap()}

    groups8 = [list(range(NCORE))]

    with ExitStack() as ctx:
        g = G(nc, ctx)
        qs, qg, cc, pe, ve, se = (g.sem[k] for k in ("qs", "qg", "cc", "pe", "ve", "se"))
        cm = alloc_common(g, cd, ("id8", "id128", "id1", "wsel"))
        xT = g.sb("xT", [128, DT * TOK], F32)

        def scope_consts(names):
            cm2 = dict(cm)
            for n in names:
                shape, dt = CONST_SPECS[n]
                t = g.sb(n + "_sb", shape, dt)
                cm2[n + "_sb"] = t
                g.on("sync", lambda e, t=t, ap=cd[n]: e.dma_start(out=t[:], in_=ap),
                     inc=qs, inck=16)
            return cm2
        load_xT(g, xT, x_in)
        emit_casts(g, [(we_bf, win["L0_ffn_We"]), (beh, win["L0_ffn_be"])])
        wc_sem = g.sem["sd"]
        wc_state = {}
        def big_casts():
            emit_casts(g, [
                (a_w[0]["wq"], win["L0_attn_Wq"]),
                (a_w[0]["wk"], win["L0_attn_Wkv"][:, 0:D]),
                (a_w[0]["wv"], win["L0_attn_Wkv"][:, D:2 * D]),
            ], sem=wc_sem)
            wc_state[0] = wc_sem.n
        def rest_casts():
            emit_casts(g, [
                (wo_bf[0], win["L0_attn_Wout"]), (bo_bf[0], win["L0_attn_bout"]),
                (a_w[1]["wq"], win["L1_attn_Wq"]), (a_w[1]["wk"], win["L1_attn_Wk"]),
                (a_w[1]["wv"], win["L1_attn_Wv"]),
                (wo_bf[1], win["L1_attn_Wo"]), (bo_bf[1], win["L1_attn_bo"]),
            ], sem=wc_sem)
            wc_state[1] = wc_sem.n
        g.w("tensor", qs)
        g.w("scalar", qs)
        g.w("vector", qs)
        g.run_phase("load")

        def moe(i):
            with g.subscope("m%d_" % i):
                emit_moe_phase(g, cm, xT, {"wg": win["L0_ffn_Wg"], "we_bf": we_bf,
                                           "beh": beh}, "cc", tots[i],
                               after_loads=(big_casts if i == 0 else None))

        def attn(L):
            ex = exch[L]
            ccval = {}
            def qk_hook(ex=ex):
                g.w("gpsimd", qg)
                def ccqk(e, ex=ex):
                    with allow_ici_all_to_all():
                        return e.collective_compute(
                            "AllToAll", ALU.bypass, replica_groups=groups8,
                            ins=[ex["sendqk"]], outs=[ex["recvqk"]])
                g.on("gpsimd", ccqk, inc=cc)
                ccval["qk"] = cc.n
            with g.subscope("a%dq_" % L):
                cm2 = scope_consts(("PT", "ropeC", "ropeS") if L == 0 else ())
                hT = g.sb("hT", [128, DT * TOK], BF16)
                v_own = g.sb("vown", [128, 16 * 1040], BF16)
                ln = alloc_ln(g, sq=v_own[:, 0: DT * CH])
                emit_ln_phase(g, cm2, ln, xT, hT, name="ln")
                emit_qkv(g, cm2, xT, hT, a_w[L], ex["sendqk"], ex["sendv"],
                         rope_on=(L == 0), v_own=v_own, on_qk_done=qk_hook,
                         w_wait=[(wc_sem, wc_state[min(L, max(wc_state))])])
            def ccv(e, ex=ex):
                with allow_ici_all_to_all():
                    return e.collective_compute(
                        "AllToAll", ALU.bypass, replica_groups=groups8,
                        ins=[ex["sendv"]], outs=[ex["recvv"]])
            g.on("gpsimd", ccv, wait=[(qs, qs.n)], inc=cc)
            ccval["v"] = cc.n
            g.run_phase("a%d_cc1" % L)
            with g.subscope("a%df_" % L):
                cm2 = scope_consts(("Tbase",) if L == 1 else ())
                bg = None
                if L == 0:
                    import itertools
                    qb = alloc_quant(g)
                    bg = itertools.chain(
                        emit_quant(g, cm, win["L1_ffn_W1"], wq1, "w1",
                                   False, "q1", scrq[1], qb),
                        emit_quant(g, cm, win["L1_ffn_W2"], wq2, "w2",
                                   True, "q2", scrq[2], qb))
                emit_flash(g, cm2, ex["recvqk"], ex["recvv"], ex["rsend"],
                           causal=(L == 1), bg=bg,
                           wait_qk=(cc, ccval["qk"]), wait_v=(cc, ccval["v"]))
            g.w("gpsimd", qg)
            def cc2(e, ex=ex):
                with allow_ici_all_to_all():
                    return e.collective_compute(
                        "AllToAll", ALU.bypass, replica_groups=groups8,
                        ins=[ex["rsend"]], outs=[ex["rrecv"]])
            g.on("gpsimd", cc2, inc=cc)
            g.run_phase("a%d_cc2" % L)
            with g.subscope("a%dp_" % L):
                g.w("sync", cc)
                emit_outproj(g, cm, xT, ex["rrecv"], wo_bf[L], bo_bf[L],
                             w_wait=[(wc_sem, wc_state[1])])

        def ffn(i):
            with g.subscope("f%d_" % i):
                b1v_sb = g.sb("b1v", [128, 32], F32)
                b2v_sb = g.sb("b2v", [128, 8], F32)
                for f in range(32):
                    g.on("sync", lambda e, f=f: e.dma_start(
                        out=b1v_sb[:, f: f + 1],
                        in_=win["L1_ffn_b1"].rearrange("a (f p) -> (a f) p", p=128)[f: f + 1]
                            .rearrange("f p -> p f")),
                        inc=qs, inck=16)
                for f in range(DT):
                    g.on("sync", lambda e, f=f: e.dma_start(
                        out=b2v_sb[:, f: f + 1],
                        in_=win["L1_ffn_b2"].rearrange("a (f p) -> (a f) p", p=128)[f: f + 1]
                            .rearrange("f p -> p f")),
                        inc=qs, inck=16)
                g.w("vector", qs)
                g.on("vector", lambda e: e.tensor_scalar(
                    out=b2v_sb[:], in0=b2v_sb[:], scalar1=0.5, scalar2=None, op0=ALU.mult),
                    inc=ve)
                g.run_phase("fb")
                hff = g.sb("hff", [128, 32 * CH], BF16)
                ln = alloc_ln(g, sq=hff[:, 0:DT * CH], xbf=hff[:, DT * CH: 2 * DT * CH])
                hT_ch = g.sb("hTch", [128, DT * 2 * CH], BF16)
                w1T_sb = g.sb("w1T", [128, DT * 4096], BF16)
                w2T_sb = g.sb("w2T", [128, 3 * 1024], BF16)
                emit_ffn(g, cm, xT, ln, wq1, wq2, b1v_sb, b2v_sb, hT_ch, hff,
                         w1T_sb, w2T_sb)

        moe(0)
        rest_casts()
        attn(0)
        moe(1)
        ffn(0)
        attn(1)
        ffn(1)
        store_xT(g, xT, x_out)
    return nc


_FUSED_NC = None


def kernel_fused(**inputs):
    global _FUSED_NC
    f32 = lambda k: np.ascontiguousarray(np.asarray(inputs[k], np.float32))
    x = f32("L0_attn_queries")
    xs = shard_x(x)
    consts = get_consts()
    if _FUSED_NC is None:
        _FUSED_NC = build_fused()
    wkeys = ["L0_ffn_Wg", "L0_ffn_We", "L0_attn_Wq", "L0_attn_Wkv", "L0_attn_Wout",
             "L1_attn_Wq", "L1_attn_Wk", "L1_attn_Wv", "L1_attn_Wo",
             "L1_ffn_W1", "L1_ffn_W2"]
    bkeys = ["L0_ffn_be", "L0_attn_bout", "L1_attn_bo", "L1_ffn_b1", "L1_ffn_b2"]
    im = []
    for c in range(NCORE):
        m = {"x": xs[c], **consts[c]}
        for k in wkeys:
            m[k] = f32(k)
        for k in bkeys:
            m[k] = f32(k).reshape(1, -1)
        im.append(m)
    res = run_bass_kernel_spmd(_FUSED_NC, im, core_ids=list(range(NCORE)))
    return unshard_x([np.asarray(res.results[c]["x_out"]) for c in range(NCORE)])

def kernel(**inputs):
    return kernel_fused(**inputs)


def kernel_unfused(**inputs):

    f32 = lambda k: np.ascontiguousarray(np.asarray(inputs[k], np.float32))
    x = f32("L0_attn_queries")
    xs = shard_x(x)

    moe_w = (f32("L0_ffn_Wg"), f32("L0_ffn_We"), f32("L0_ffn_be"))
    xs = run_moe_subblock(xs, *moe_w)
    xs = run_attn_subblock(xs, 0, {"Wq": f32("L0_attn_Wq"), "Wkv": f32("L0_attn_Wkv"),
                                   "Wout": f32("L0_attn_Wout"), "bout": f32("L0_attn_bout")})
    xs = run_moe_subblock(xs, *moe_w)
    wq1, wq2 = run_quant(f32("L1_ffn_W1"), f32("L1_ffn_W2"))
    b1, b2 = f32("L1_ffn_b1"), f32("L1_ffn_b2")
    xs = run_ffn_subblock(xs, wq1, wq2, b1, b2)
    xs = run_attn_subblock(xs, 1, {"Wq": f32("L1_attn_Wq"), "Wk": f32("L1_attn_Wk"),
                                   "Wv": f32("L1_attn_Wv"), "Wo": f32("L1_attn_Wo"),
                                   "bo": f32("L1_attn_bo")})
    xs = run_ffn_subblock(xs, wq1, wq2, b1, b2)
    return unshard_x(xs)


# version 32
# speedup vs baseline: 1.1399x; 1.1371x over previous
import numpy as np
import ml_dtypes
from contextlib import ExitStack

import concourse.bass as bass
import concourse.mybir as mybir
from concourse.bass_utils import run_bass_kernel_spmd
from concourse.aot_env import allow_ici_all_to_all

F32 = mybir.dt.float32
BF16 = mybir.dt.bfloat16
AF = mybir.ActivationFunctionType
ALU = mybir.AluOpType
AX = mybir.AxisListType
BF = ml_dtypes.bfloat16

NCORE = 8
TOK = 2048
D = 1024
DT = D // 128
CH = 512
NCH = TOK // CH
CAP = 2048.0
LNEPS = 1e-5
QEPS = 1e-5
DFF = 4096
SEQ = 8192
ATTN_SCALE = 0.125


class Sem:
    def __init__(self, ctx, nc, name):
        self.h = ctx.enter_context(nc.semaphore(name))
        self.n = 0


class G:
    ENG = ("sync", "tensor", "vector", "scalar", "gpsimd")

    def __init__(self, nc, ctx):
        self.nc = nc
        self.ctx = ctx
        self.cur = ctx
        self.prefix = ""
        self.q = {k: [] for k in self.ENG}
        self.sem = {name: Sem(ctx, nc, "sem_" + name) for name in
                    ("qs", "qg", "qa", "pe", "ve", "se", "gp", "cc", "wa", "wb", "sc", "sd", "n0", "n1")}
        self.nblk = 0

    def sb(self, name, shape, dt):
        return self.cur.enter_context(self.nc.sbuf_tensor(self.prefix + name, shape, dt))

    def ps(self, name, shape, dt=F32):
        return self.nc.psum_tensor(self.prefix + name, shape, dt)

    def subscope(g, prefix):
        class _S:
            def __enter__(s):
                s.st = ExitStack()
                s.st.__enter__()
                s.old = (g.cur, g.prefix)
                g.cur, g.prefix = s.st, prefix
                for eng in ("sync", "gpsimd"):
                    g.w(eng, g.sem["pe"], g.sem["ve"], g.sem["se"],
                        g.sem["qs"], g.sem["qg"])
                return s
            def __exit__(s, *a):
                g.cur, g.prefix = s.old
                return s.st.__exit__(*a)
        return _S()

    def on(self, eng, fn, wait=(), inc=None, inck=1):
        waits = []
        for wv in wait:
            s, v = wv if isinstance(wv, tuple) else (wv, None)
            waits.append((s.h, s.n if v is None else v))
        if inc is not None:
            inc.n += inck
        self.q[eng].append((tuple(waits), fn, None if inc is None else (inc.h, inck)))

    def w(self, eng, *sems):
        self.on(eng, None, wait=list(sems))

    def run_phase(self, name=None):
        qs = self.q
        self.q = {k: [] for k in self.ENG}
        self.nblk += 1
        if name is not None:
            name = self.prefix + name
        with self.nc.Block(name) as b:
            for engname in self.ENG:
                items = qs[engname]
                if not items:
                    continue

                dr = engname in ("vector", "scalar", "gpsimd")

                def body(e, items=tuple(items), dr=dr):
                    for waits, fn, inc in items:
                        for (h, v) in waits:
                            e.wait_ge(h, v)
                        instr = fn(e) if fn is not None else None
                        if inc is not None:
                            h, k = inc
                            (instr if instr is not None else e.nop()).then_inc(h, k)
                        if dr and fn is not None:
                            e.drain()
                getattr(b, engname)(body)


def host_consts(core):
    pos = 2048.0 * (core % 4) + np.arange(TOK, dtype=np.float64)
    inv_freq = 1.0 / (10000.0 ** (np.arange(0, 32, 2, dtype=np.float64) / 32.0))
    ang = pos[:, None] * inv_freq[None, :]
    C = np.ones((128, TOK), np.float64)
    S = np.zeros((128, TOK), np.float64)
    for p in range(128):
        q = p % 64
        if q < 32:
            C[p] = np.cos(ang[:, q // 2])
            S[p] = np.sin(ang[:, q // 2])
    P = np.zeros((128, 128), np.float64)
    for base in (0, 64):
        for i in range(16):
            P[base + 2 * i, base + 2 * i + 1] = -1.0
            P[base + 2 * i + 1, base + 2 * i] = 1.0
    u = np.arange(1920)[None, :]
    p = np.arange(128)[:, None]
    Tb = (p <= u - 896).astype(np.float64)
    wsel = np.zeros((8, 1), np.float32)
    wsel[:core] = 1.0
    Sx = S.copy()
    Sx[0::2] *= -1.0
    return {
        "ropeC": np.ascontiguousarray(C.astype(BF)),
        "ropeS": np.ascontiguousarray(S.astype(BF)),
        "ropeSx": np.ascontiguousarray(Sx.astype(BF)),
        "PT": np.ascontiguousarray(P.T.astype(BF)),
        "Tbase": np.ascontiguousarray(Tb.astype(BF)),
        "id8": np.eye(8, dtype=np.float32),
        "id128": np.eye(128, dtype=np.float32),
        "id1": np.ones((1, 1), np.float32),
        "wsel": wsel,
    }


CONST_SPECS = {
    "ropeC": ([128, TOK], BF16), "ropeS": ([128, TOK], BF16),
    "ropeSx": ([128, TOK], BF16),
    "PT": ([128, 128], BF16), "Tbase": ([128, 1920], BF16),
    "id8": ([8, 8], F32), "id128": ([128, 128], F32), "id1": ([1, 1], F32),
    "wsel": ([8, 1], F32),
}


def decl_in(nc, name, shape, dt=F32):
    return nc.dram_tensor(name, shape, dt, kind="ExternalInput").ap()


def decl_out(nc, name, shape, dt=F32):
    return nc.dram_tensor(name, shape, dt, kind="ExternalOutput").ap()


def const_decls_subset(nc, names):
    return {n: decl_in(nc, n, *CONST_SPECS[n]) for n in names}


def alloc_common(g, cd, const_names):
    cm = {}
    cm["ones128_bf"] = g.sb("ones128_bf", [128, 1], BF16)
    cm["ones128_f32"] = g.sb("ones128_f32", [128, 1], F32)
    cm["onesrow_f32"] = g.sb("onesrow_f32", [1, 128], F32)
    cm["onesrow_bf"] = g.sb("onesrow_bf", [1, 512], BF16)
    cm["ones8_f32"] = g.sb("ones8_f32", [8, 1], F32)

    cm["eps_ln"] = g.sb("eps_ln", [1, 1], F32)

    def ms(e):
        e.memset(cm["eps_ln"][:], LNEPS)
        e.memset(cm["ones128_bf"][:], 1.0)
        e.memset(cm["ones128_f32"][:], 1.0)
        e.memset(cm["onesrow_f32"][:], 1.0)
        e.memset(cm["onesrow_bf"][:], 1.0)
        return e.memset(cm["ones8_f32"][:], 1.0)
    g.on("vector", ms, inc=g.sem["ve"])
    for n in const_names:
        shape, dt = CONST_SPECS[n]
        t = g.sb(n + "_sb", shape, dt)
        cm[n + "_sb"] = t
        g.on("sync", lambda e, t=t, ap=cd[n]: e.dma_start(out=t[:], in_=ap),
             inc=g.sem["qs"], inck=16)
    return cm


def load_xT(g, xT, x_in):
    g.on("sync", lambda e: e.dma_start(
        out=xT[:].rearrange("p (a c) -> p a c", a=DT),
        in_=x_in.rearrange("(a p) c -> p a c", p=128)), inc=g.sem["qs"], inck=16)


def store_xT(g, xT, x_out):
    g.w("gpsimd", g.sem["ve"], g.sem["se"])
    g.on("gpsimd", lambda e: e.dma_start(
        out=x_out.rearrange("(a p) c -> p a c", p=128),
        in_=xT[:].rearrange("p (a c) -> p a c", a=DT)), inc=g.sem["qg"], inck=16)
    g.w("gpsimd", g.sem["qg"])
    g.run_phase("store_x")


def emit_casts(g, casts, sem=None):
    s = g.sem["qg"] if sem is None else sem
    for dst, src in casts:
        rows = dst.shape[0]
        nsp = 8 if rows % 8 == 0 and rows >= 8 else 1
        rs = rows // nsp
        for i in range(nsp):
            g.on("gpsimd", lambda e, dst=dst, src=src, i=i, rs=rs: e.dma_start(
                out=dst[rs * i: rs * (i + 1)], in_=src[rs * i: rs * (i + 1)]),
                inc=s, inck=16)


def alloc_ln(g, sq=None, xbf=None):
    b = {}
    b["xbf"] = g.sb("ln_xbf", [128, DT * CH], BF16) if xbf is None else xbf
    b["sq"] = g.sb("ln_sq", [128, DT * CH], BF16) if sq is None else sq
    b["m_sb"] = g.sb("ln_m", [1, CH], F32)
    b["v_sb"] = g.sb("ln_v", [1, CH], F32)
    b["r_sb"] = g.sb("ln_r", [1, CH], F32)
    b["mr_sb"] = b["t1_sb"] = g.sb("ln_t1", [1, CH], F32)
    b["u_sb"] = g.sb("ln_u", [128, CH], F32)
    return b


def emit_ln_phase(g, cm, b, xT, hT, chunks=None, router=None, hT_off=None, name="ln", steps=6):
    nc = g.nc
    se, pe, ve = g.sem["se"], g.sem["pe"], g.sem["ve"]
    if chunks is None:
        chunks = list(range(NCH))
    chunks = list(chunks)
    NC = len(chunks)
    if hT_off is None:
        HTOK, hT_off = TOK, 0
    else:
        HTOK = NC * CH
    with ExitStack() as ps:
        msum_ps = ps.enter_context(g.ps(name + "_msum", [1, CH], F32))
        s2_ps = ps.enter_context(g.ps(name + "_s2", [1, CH], F32))
        rb_ps = ps.enter_context(g.ps(name + "_rb", [128, CH], F32))
        mrb_ps = ps.enter_context(g.ps(name + "_mrb", [128, CH], F32))
        lg_ps = ps.enter_context(g.ps(name + "_lg", [8, TOK], F32)) \
            if router is not None else None

        reds_pe = {}
        stats_ve = {}
        mr_ve = {}
        bcast_pe = {}
        hcomp_ve = {}
        rstd_se = {}
        rb8_se = {}

        def e_cast(ci, ch):
            def cast(e):
                for d in range(DT):
                    e.activation(out=b["xbf"][:, CH * d: CH * (d + 1)],
                                 in_=xT[:, TOK * d + CH * ch: TOK * d + CH * (ch + 1)], func=AF.Copy)
                for d in range(DT):
                    last = e.activation(out=b["sq"][:, CH * d: CH * (d + 1)],
                                        in_=xT[:, TOK * d + CH * ch: TOK * d + CH * (ch + 1)], func=AF.Square)
                return last
            g.on("scalar", cast, wait=[(pe, reds_pe[ci - 1])] if ci >= 1 else [(pe, pe.n)], inc=se)

        def e_reds(ci, ch):
            def reds(e):
                for d in range(DT):
                    e.matmul(msum_ps[:], cm["ones128_bf"][:],
                             b["xbf"][:, CH * d: CH * (d + 1)], start=(d == 0), stop=(d == DT - 1))
                for d in range(DT):
                    last = e.matmul(s2_ps[:], cm["ones128_bf"][:],
                                    b["sq"][:, CH * d: CH * (d + 1)], start=(d == 0), stop=(d == DT - 1))
                if router is not None:
                    for d in range(DT):
                        last = e.matmul(lg_ps[:, CH * ch: CH * (ch + 1)],
                                        router["wg_sb"][:, 8 * d: 8 * (d + 1)],
                                        xT[:, TOK * d + CH * ch: TOK * d + CH * (ch + 1)],
                                        start=(d == 0), stop=False)
                return last
            w = [se] + ([(ve, stats_ve[ci - 1])] if ci >= 1 else [])
            g.on("tensor", reds, wait=w, inc=pe)
            reds_pe[ci] = pe.n

        def e_stats(ci, ch):
            def stats(e):
                e.tensor_scalar(out=b["m_sb"][:], in0=msum_ps[:], scalar1=1.0 / D, scalar2=None, op0=ALU.mult)
                e.drain()
                e.tensor_scalar(out=b["v_sb"][:], in0=s2_ps[:], scalar1=1.0 / D, scalar2=None, op0=ALU.mult)
                e.drain()
                e.tensor_tensor(out=b["t1_sb"][:], in0=b["m_sb"][:], in1=b["m_sb"][:], op=ALU.mult)
                e.drain()
                return e.tensor_tensor(out=b["v_sb"][:], in0=b["v_sb"][:], in1=b["t1_sb"][:], op=ALU.subtract)
            w = [(pe, reds_pe[ci])] + ([(se, rstd_se[ci - 1])] if ci >= 1 else [])
            g.on("vector", stats, wait=w, inc=ve)
            stats_ve[ci] = ve.n

        def e_rstd(ci, ch):
            def rstd(e):
                e.activation(out=b["t1_sb"][:], in_=b["v_sb"][:], func=AF.Ln, bias=cm["eps_ln"][:])
                return e.activation(out=b["r_sb"][:], in_=b["t1_sb"][:], func=AF.Exp, scale=-0.5)
            g.on("scalar", rstd, wait=[(ve, stats_ve[ci])], inc=se)
            rstd_se[ci] = se.n

        def e_mr(ci, ch):
            g.on("vector", lambda e: e.tensor_tensor(out=b["mr_sb"][:], in0=b["m_sb"][:],
                                                     in1=b["r_sb"][:], op=ALU.mult),
                 wait=[(se, rstd_se[ci])], inc=ve)
            mr_ve[ci] = ve.n

        def e_bcast(ci, ch):
            def bcast(e):
                e.matmul(rb_ps[:], cm["onesrow_f32"][:], b["r_sb"][:], start=True, stop=True)
                last = e.matmul(mrb_ps[:], cm["onesrow_f32"][:], b["mr_sb"][:], start=True, stop=True)
                if router is not None:
                    last = e.matmul(lg_ps[:, CH * ch: CH * (ch + 1)],
                                    router["ngsum_sb"][:], b["m_sb"][:], start=False, stop=True)
                return last
            w = [(ve, mr_ve[ci])]
            if ci >= 1:
                w.append((ve, hcomp_ve[ci - 1]))
                if router is not None:
                    w.append((se, rb8_se[ci - 1]))
            g.on("tensor", bcast, wait=w, inc=pe)
            bcast_pe[ci] = pe.n

        def e_h(ci, ch):
            if router is not None:
                g.on("scalar", lambda e: e.copy(out=router["rb8_sb"][:], in_=rb_ps[0:8, :]),
                     wait=[(pe, bcast_pe[ci])], inc=se)
                rb8_se[ci] = se.n
            def hcomp(e):
                for d in range(DT):
                    e.tensor_tensor(out=b["u_sb"][:],
                                    in0=xT[:, TOK * d + CH * ch: TOK * d + CH * (ch + 1)],
                                    in1=rb_ps[:], op=ALU.mult)
                    e.drain()
                    last = e.tensor_tensor(
                        out=hT[:, HTOK * d + CH * ch - hT_off: HTOK * d + CH * (ch + 1) - hT_off],
                        in0=b["u_sb"][:], in1=mrb_ps[:], op=ALU.subtract)
                    e.drain()
                return last
            w = [(pe, bcast_pe[ci])] + ([(se, rb8_se[ci])] if router is not None else [])
            g.on("vector", hcomp, wait=w, inc=ve)
            if router is not None:
                g.on("vector", lambda e: e.tensor_tensor(
                    out=router["lg_sb"][:, CH * ch: CH * (ch + 1)],
                    in0=lg_ps[:, CH * ch: CH * (ch + 1)],
                    in1=router["rb8_sb"][:], op=ALU.mult), inc=ve)
            hcomp_ve[ci] = ve.n

        stages = [e_cast, e_reds, e_stats, e_rstd, e_mr, e_bcast, e_h]
        for ci in range(NC + 1):
            if ci < NC:
                e_cast(ci, chunks[ci])
            if ci >= 1:
                cj = ci - 1
                e_stats(cj, chunks[cj])
                e_rstd(cj, chunks[cj])
                e_mr(cj, chunks[cj])
                e_bcast(cj, chunks[cj])
                e_h(cj, chunks[cj])
            if ci < NC:
                e_reds(ci, chunks[ci])
        g.run_phase(name)


def alloc_gating(g):
    gt = {}
    gt["ltm_sb"] = g.sb("gt_ltm_sb", [128, 128], F32)
    gt["mx_sb"] = g.sb("gt_mx", [128, 16], F32)
    gt["msk_tm"] = g.sb("gt_msk_tm", [128, 128], F32)
    gt["mskT_sb"] = g.sb("gt_mskT_sb", [8, TOK], F32)
    gt["zero8"] = g.sb("gt_zero8", [8, 1], F32)
    gt["cumT_sb"] = g.sb("gt_cumT", [8, TOK], F32)
    gt["tot_sb"] = g.sb("gt_tot", [8, 8], F32)
    gt["eT_sb"] = g.sb("gt_eT", [8, TOK], F32)
    gt["sT_sb"] = g.sb("gt_sT_sb", [1, TOK], F32)
    gt["rs_sb"] = gt["sT_sb"]
    gt["nin_sb"] = gt["eT_sb"]
    gt["gateT_sb"] = g.sb("gt_gateT", [1, TOK], F32)
    gt["offs_sb"] = g.sb("gt_offs_sb", [1, 8], F32)
    gt["offsT_sb"] = g.sb("gt_offsT_sb", [8, 1], F32)
    gt["k8_sb"] = gt["cumT_sb"]
    gt["factorT_sb"] = gt["gateT_sb"]
    gt["fB_sb"] = g.sb("gt_fB_sb", [128, TOK], F32)
    return gt


def emit_gating(g, cm, lg_sb, gt, totals_mode, tot_io):
    nc = g.nc
    se, pe, ve = g.sem["se"], g.sem["pe"], g.sem["ve"]
    qg, cc = g.sem["qg"], g.sem["cc"]

    with g.ps("gt_ltm", [128, 128], F32) as ltm_ps, \
         g.ps("gt_mskT", [8, TOK], F32) as mskT_ps:
        def ttm(e):
            for j in range(16):
                last = e.transpose(ltm_ps[:, 8 * j: 8 * (j + 1)],
                                   lg_sb[:, 128 * j: 128 * (j + 1)], cm["id8_sb"][:])
            return last
        g.on("tensor", ttm, wait=[ve], inc=pe)
        g.on("scalar", lambda e: e.copy(out=gt["ltm_sb"][:], in_=ltm_ps[:]),
             wait=[pe], inc=se)
        def masks(e):
            for j in range(16):
                e.tensor_reduce(out=gt["mx_sb"][:, j: j + 1],
                                in_=gt["ltm_sb"][:, 8 * j: 8 * (j + 1)], axis=AX.X, op=ALU.max)
            e.drain()
            for j in range(16):
                last = e.tensor_scalar(out=gt["msk_tm"][:, 8 * j: 8 * (j + 1)],
                                       in0=gt["ltm_sb"][:, 8 * j: 8 * (j + 1)],
                                       scalar1=gt["mx_sb"][:, j: j + 1], scalar2=None, op0=ALU.is_ge)
            return last
        g.on("vector", masks, wait=[se], inc=ve)
        def tfm(e):
            for j in range(16):
                last = e.transpose(mskT_ps[:, 128 * j: 128 * (j + 1)],
                                   gt["msk_tm"][:, 8 * j: 8 * (j + 1)], cm["id128_sb"][:])
            return last
        g.on("tensor", tfm, wait=[ve], inc=pe)
        g.on("scalar", lambda e: e.copy(out=gt["mskT_sb"][:], in_=mskT_ps[:]),
             wait=[pe], inc=se)
        g.on("vector", lambda e: e.memset(gt["zero8"][:], 0.0))
        g.on("vector", lambda e: e.tensor_tensor_scan(
            out=gt["cumT_sb"][:], data0=gt["mskT_sb"][:], data1=gt["mskT_sb"][:],
            initial=gt["zero8"][:], op0=ALU.add, op1=ALU.bypass),
            wait=[se], inc=ve)
        g.run_phase("gate_mask")

    if totals_mode == "out":
        g.w("gpsimd", ve)
        g.on("gpsimd", lambda e: e.dma_start(
            out=tot_io["tot_out_ext"], in_=gt["cumT_sb"][:, TOK - 1: TOK]),
            inc=qg, inck=16)
        g.w("gpsimd", qg)
        g.run_phase("gate_tot")
        return

    if totals_mode == "cc":
        g.w("gpsimd", ve)
        g.on("gpsimd", lambda e: e.dma_start(
            out=tot_io["tot_in"].rearrange("a e -> e a"),
            in_=gt["cumT_sb"][:, TOK - 1: TOK]), inc=qg, inck=16)
        g.w("gpsimd", qg)
        g.on("gpsimd", lambda e: e.collective_compute(
            "AllGather", ALU.bypass, replica_groups=[list(range(NCORE))],
            ins=[tot_io["tot_in"]], outs=[tot_io["tot_all"]]), inc=cc)
        g.w("gpsimd", cc)
        g.on("gpsimd", lambda e: e.dma_start(out=gt["tot_sb"][:], in_=tot_io["tot_all"]),
             inc=qg, inck=16)
    else:
        g.on("gpsimd", lambda e: e.dma_start(out=gt["tot_sb"][:], in_=tot_io["tot_all"]),
             inc=qg, inck=16)

    with g.ps("gt_sT", [1, TOK], F32) as sT_ps:
        g.on("scalar", lambda e: e.activation(out=gt["eT_sb"][:], in_=lg_sb[:], func=AF.Exp),
             wait=[ve], inc=se)
        def smm(e):
            for c in range(NCH):
                last = e.matmul(sT_ps[:, CH * c: CH * (c + 1)], cm["ones8_f32"][:],
                                gt["eT_sb"][:, CH * c: CH * (c + 1)], start=True, stop=True)
            return last
        g.on("tensor", smm, wait=[se], inc=pe)
        g.on("scalar", lambda e: e.copy(out=gt["sT_sb"][:], in_=sT_ps[:]),
             wait=[pe], inc=se)
        g.on("vector", lambda e: e.reciprocal(out=gt["rs_sb"][:], in_=gt["sT_sb"][:]),
             wait=[se], inc=ve)
        g.on("vector", lambda e: e.tensor_tensor(out=gt["nin_sb"][:], in0=gt["eT_sb"][:],
                                                 in1=gt["mskT_sb"][:], op=ALU.mult), inc=ve)
        g.run_phase("gate_probs")

    with g.ps("gt_num", [1, TOK], F32) as num_ps:
        def nmm(e):
            for c in range(NCH):
                last = e.matmul(num_ps[:, CH * c: CH * (c + 1)], cm["ones8_f32"][:],
                                gt["nin_sb"][:, CH * c: CH * (c + 1)], start=True, stop=True)
            return last
        g.on("tensor", nmm, wait=[ve], inc=pe)
        g.on("vector", lambda e: e.tensor_tensor(out=gt["gateT_sb"][:], in0=num_ps[:],
                                                 in1=gt["rs_sb"][:], op=ALU.mult),
             wait=[pe], inc=ve)
        g.run_phase("gate_gate")

    with g.ps("gt_offs", [1, 8], F32) as offs_ps, \
         g.ps("gt_offsT", [8, 1], F32) as offsT_ps:
        g.on("tensor", lambda e: e.matmul(offs_ps[:], cm["wsel_sb"][:], gt["tot_sb"][:],
                                          start=True, stop=True), wait=[qg], inc=pe)
        g.on("scalar", lambda e: e.copy(out=gt["offs_sb"][:], in_=offs_ps[:]),
             wait=[pe], inc=se)
        g.on("tensor", lambda e: e.transpose(offsT_ps[:], gt["offs_sb"][:], cm["id1_sb"][:]),
             wait=[se], inc=pe)
        g.on("scalar", lambda e: e.copy(out=gt["offsT_sb"][:], in_=offsT_ps[:]),
             wait=[pe], inc=se)
        g.on("vector", lambda e: e.tensor_scalar(
            out=gt["k8_sb"][:], in0=gt["cumT_sb"][:], scalar1=gt["offsT_sb"][:],
            scalar2=CAP, op0=ALU.add, op1=ALU.is_le), wait=[se], inc=ve)
        g.on("vector", lambda e: e.tensor_tensor(out=gt["k8_sb"][:], in0=gt["k8_sb"][:],
                                                 in1=gt["mskT_sb"][:], op=ALU.mult), inc=ve)
        g.run_phase("gate_keep")

    with g.ps("gt_keep", [1, TOK], F32) as keep_ps, \
         g.ps("gt_fB", [128, TOK], F32) as fB_ps:
        def kmm(e):
            for c in range(NCH):
                last = e.matmul(keep_ps[:, CH * c: CH * (c + 1)], cm["ones8_f32"][:],
                                gt["k8_sb"][:, CH * c: CH * (c + 1)], start=True, stop=True)
            return last
        g.on("tensor", kmm, wait=[ve], inc=pe)
        g.on("vector", lambda e: e.tensor_tensor(out=gt["factorT_sb"][:], in0=keep_ps[:],
                                                 in1=gt["gateT_sb"][:], op=ALU.mult),
             wait=[pe], inc=ve)
        g.on("vector", lambda e: e.tensor_scalar(out=gt["factorT_sb"][:], in0=gt["factorT_sb"][:],
                                                 scalar1=0.5, scalar2=None, op0=ALU.mult), inc=ve)
        def fmm(e):
            for c in range(NCH):
                last = e.matmul(fB_ps[:, CH * c: CH * (c + 1)], cm["onesrow_f32"][:],
                                gt["factorT_sb"][:, CH * c: CH * (c + 1)], start=True, stop=True)
            return last
        g.on("tensor", fmm, wait=[ve], inc=pe)
        g.on("scalar", lambda e: e.copy(out=gt["fB_sb"][:], in_=fB_ps[:]),
             wait=[pe], inc=se)
        g.run_phase("gate_factor")


def emit_expert(g, cm, xT, hT, fB_sb, we_sb, beV_sb, ex_t):
    nc, pe, ve = g.nc, g.sem["pe"], g.sem["ve"]
    se, qs, qg = g.sem["se"], g.sem["qs"], g.sem["qg"]
    with g.ps("ex_y", [128, 2 * CH], F32) as y_ps:
        g.w("tensor", qs, qg, se, ve)
        evacs = []
        for i, (ch, do) in enumerate([(c, d) for c in range(NCH) for d in range(DT)]):
            par = i % 2
            ysl = y_ps[:, CH * par: CH * (par + 1)]
            def mm(e, ch=ch, do=do, ysl=ysl):
                for di in range(DT):
                    last = e.matmul(ysl, we_sb[:, 1024 * di + 128 * do: 1024 * di + 128 * (do + 1)],
                                    hT[:, TOK * di + CH * ch: TOK * di + CH * (ch + 1)],
                                    start=(di == 0), stop=(di == DT - 1))
                return last
            wt = [(ve, evacs[i - 2])] if i >= 2 else []
            g.on("tensor", mm, wait=wt, inc=pe)
            def evac(e, ch=ch, do=do, ysl=ysl):
                xsl = xT[:, TOK * do + CH * ch: TOK * do + CH * (ch + 1)]
                e.scalar_tensor_tensor(out=ex_t[:], in0=ysl, scalar=beV_sb[:, do: do + 1],
                                       in1=fB_sb[:, CH * ch: CH * (ch + 1)],
                                       op0=ALU.add, op1=ALU.mult)
                e.drain()
                return e.tensor_tensor(out=xsl, in0=xsl, in1=ex_t[:], op=ALU.add)
            g.on("vector", evac, wait=[pe], inc=ve)
            evacs.append(ve.n)
        g.run_phase("expert")


def emit_moe_phase(g, cm, xT, weights, totals_mode, tot_io, after_loads=None):
    nc = g.nc
    qs, qg, pe, ve, se = g.sem["qs"], g.sem["qg"], g.sem["pe"], g.sem["ve"], g.sem["se"]
    apply_mode = totals_mode != "out"

    hT = g.sb("moe_hT", [128, DT * TOK], BF16)
    ln = alloc_ln(g)
    gt = alloc_gating(g)
    ro = {
        "wg_sb": g.sb("moe_wg", [128, 8 * DT], F32),
        "ngsum_sb": g.sb("moe_ngsum", [1, 8], F32),
        "rb8_sb": g.sb("moe_rb8", [8, CH], F32),
        "lg_sb": g.sb("moe_lg", [8, TOK], F32),
    }
    if apply_mode:
        we_sb = g.sb("moe_we", [128, DT * 1024], BF16)
        beV_sb = g.sb("moe_beV", [128, 8], F32)
        ex_t = g.sb("moe_ex_t", [128, CH], F32)

    g.on("gpsimd", lambda e: e.dma_start(
        out=ro["wg_sb"][:].rearrange("p (a c) -> p a c", a=DT),
        in_=weights["wg"].rearrange("(a p) c -> p a c", p=128)), inc=qg, inck=16)
    if apply_mode:
        g.on("gpsimd", lambda e: e.dma_start(
            out=we_sb[:].rearrange("p (a c) -> p a c", a=DT),
            in_=weights["we_bf"].rearrange("(a p) c -> p a c", p=128)), inc=qg, inck=16)
    if after_loads is not None:
        after_loads()

    with g.ps("moe_gsum", [1, 8], F32) as gsum_ps:
        def gsum(e):
            for d in range(DT):
                last = e.matmul(gsum_ps[:], cm["ones128_f32"][:],
                                ro["wg_sb"][:, 8 * d: 8 * (d + 1)], start=(d == 0), stop=(d == DT - 1))
            return last
        g.on("tensor", gsum, wait=[qs, qg, ve], inc=pe)
        g.on("vector", lambda e: e.tensor_scalar(out=ro["ngsum_sb"][:], in0=gsum_ps[:],
                                                 scalar1=-1.0, scalar2=None, op0=ALU.mult),
             wait=[pe], inc=ve)
        g.run_phase("moe_prep")

    emit_ln_phase(g, cm, ln, xT, hT, router=ro, name="moe_ln")
    if apply_mode:
        for do in range(DT):
            g.on("gpsimd", lambda e, do=do: e.dma_start(
                out=beV_sb[:, do: do + 1],
                in_=weights["beh"].rearrange("a (f p) -> (a f) p", p=128)[do: do + 1]
                    .rearrange("f p -> p f")),
                inc=qg, inck=16)
    emit_gating(g, cm, ro["lg_sb"], gt, totals_mode, tot_io)
    if not apply_mode:
        return
    emit_expert(g, cm, xT, hT, gt["fB_sb"], we_sb, beV_sb, ex_t)


def build_moe(totals_mode):
    nc = bass.Bass("TRN2", target_bir_lowering=False)
    x_in = decl_in(nc, "x", [D, TOK])
    wg = decl_in(nc, "wg", [D, 8])
    cd = const_decls_subset(nc, ("id8", "id128", "id1", "wsel"))
    tot_io = {}
    weights = {"wg": wg}
    apply_mode = totals_mode != "out"
    if totals_mode == "out":
        tot_io["tot_out_ext"] = decl_out(nc, "tot", [8, 1])
    else:
        tot_io["tot_all"] = decl_in(nc, "tot_all", [8, 8])
        we = decl_in(nc, "we", [D, D])
        be = decl_in(nc, "be", [1, D])
        x_out = decl_out(nc, "x_out", [D, TOK])
        weights["we_bf"] = nc.dram_tensor("we_bf", [D, D], BF16).ap()
        weights["beh"] = nc.dram_tensor("beh", [1, D], BF16).ap()

    with ExitStack() as ctx:
        g = G(nc, ctx)
        cm = alloc_common(g, cd, ("id8", "id128", "id1", "wsel"))
        xT = g.sb("xT", [128, DT * TOK], F32)
        load_xT(g, xT, x_in)
        if apply_mode:
            emit_casts(g, [(weights["we_bf"], we), (weights["beh"], be)])
        g.w("tensor", g.sem["qs"])
        g.w("scalar", g.sem["qs"])
        g.w("vector", g.sem["qs"])
        g.run_phase("load")
        emit_moe_phase(g, cm, xT, weights, totals_mode, tot_io)
        if apply_mode:
            store_xT(g, xT, x_out)
        else:
            g.run_phase("end")
    return nc


SH_Q = 128 * TOK
SH_K = 128 * TOK
SH_V = TOK * 130
SHARD = SH_Q + SH_K + SH_V
RET = 128 * TOK


def emit_qkv(g, cm, xT, hT, w_bf, sendqk, sendv_d, rope_on, v_own=None,
             on_qk_done=None, w_wait=()):
    nc = g.nc
    qs, qg, pe, ve, se = g.sem["qs"], g.sem["qg"], g.sem["pe"], g.sem["ve"], g.sem["se"]
    sendv = sendqk

    w_sb2 = g.sb("qkv_w", [128, 2 * DT * 1024], BF16)
    o_sb = g.sb("qkv_o", [128, 2 * TOK], BF16)
    z_sb = g.sb("qkv_z", [128, TOK], BF16) if rope_on else None
    if v_own is None:
        v_own = g.sb("qkv_vown", [128, 16 * 1040], BF16)

    wslot = {"wq": 0, "wk": 1, "wv": 0}
    wsems = (g.sem["wa"], g.sem["wb"])

    def load_w(which, wait_extra=()):
        par = wslot[which]
        g.on("sync", lambda e, par=par: e.dma_start(
            out=w_sb2[:, DT * 1024 * par: DT * 1024 * (par + 1)].rearrange("p (a c) -> p a c", a=DT),
            in_=w_bf[which].rearrange("(a p) c -> p a c", p=128)),
            wait=[qg] + list(wait_extra) + list(w_wait), inc=wsems[par], inck=16)
        return wsems[par].n

    rope_prev = [None]
    with g.ps("qkv_q", [128, 2 * CH], F32) as q_ps, \
         g.ps("qkv_zp", [128, TOK], F32) as z_ps:
        for which, off in (("wq", 0), ("wk", SH_Q)):
            if which == "wq":
                wn = load_w("wq")
                kn = load_w("wk")
                wcur = wslot["wq"]
            else:
                wn, wcur = kn, wslot["wk"]
            g.on("tensor", None, wait=[(wsems[wcur], wn)])
            w_sb = w_sb2[:, DT * 1024 * wcur: DT * 1024 * (wcur + 1)]
            evl = []
            for m in range(DT):
                par = m % 2
                ot = o_sb[:, TOK * par: TOK * (par + 1)]
                for ch in range(NCH):
                    i = m * NCH + ch
                    ysl = q_ps[:, CH * (ch % 2): CH * (ch % 2 + 1)]
                    def mm(e, m=m, ch=ch, ysl=ysl):
                        for d in range(DT):
                            last = e.matmul(ysl, w_sb[:, 1024 * d + 128 * m: 1024 * d + 128 * (m + 1)],
                                            hT[:, TOK * d + CH * ch: TOK * d + CH * (ch + 1)],
                                            start=(d == 0), stop=(d == DT - 1))
                        return last
                    g.on("tensor", mm, wait=[(se, evl[i - 2])] if i >= 2 else [se], inc=pe)
                    g.on("scalar", lambda e, ch=ch, ysl=ysl, ot=ot: e.copy(
                        out=ot[:, CH * ch: CH * (ch + 1)], in_=ysl), wait=[pe, qg], inc=se)
                    evl.append(se.n)
                if rope_on:
                    zw = [(se, evl[-1])]
                    if rope_prev[0] is not None:
                        zw.append((ve, rope_prev[0]))
                    def zmm(e, ot=ot):
                        for ch in range(NCH):
                            last = e.matmul(z_ps[:, CH * ch: CH * (ch + 1)],
                                            cm["PT_sb"][:], ot[:, CH * ch: CH * (ch + 1)],
                                            start=True, stop=True)
                        return last
                    g.on("tensor", zmm, wait=zw, inc=pe)
                    zn = pe.n
                    def rope(e, ot=ot):
                        e.tensor_tensor(out=z_sb[:], in0=z_ps[:],
                                        in1=cm["ropeS_sb"][:], op=ALU.mult)
                        e.drain()
                        e.tensor_tensor(out=ot[:], in0=ot[:],
                                        in1=cm["ropeC_sb"][:], op=ALU.mult)
                        e.drain()
                        last = e.tensor_tensor(out=ot[:], in0=ot[:], in1=z_sb[:], op=ALU.add)
                        e.drain()
                        return last
                    g.on("vector", rope, wait=[(pe, zn)], inc=ve)
                    rope_prev[0] = ve.n
                g.w("gpsimd", se, ve)
                g.on("gpsimd", lambda e, m=m, off=off, ot=ot: e.dma_start(
                    out=sendv[m, off + 0: off + 128 * TOK].rearrange("(p t) -> p t", p=128),
                    in_=ot[:]), inc=qg, inck=16)
            g.run_phase("qkv_" + which)

    if on_qk_done is not None:
        on_qk_done()

    vn = load_w("wv", wait_extra=[pe])
    w_sb = w_sb2[:, 0: DT * 1024]
    g.on("vector", lambda e: e.memset(
        v_own[:].rearrange("p (t f) -> p t f", f=65)[:, :, 64:65], 1.0),
        wait=[(g.sem["pe"], g.sem["pe"].n)], inc=ve)
    with g.ps("qkv_v", [128, 2 * CH], F32) as v_ps:
        g.on("tensor", None, wait=[(wsems[0], vn)])
        g.w("tensor", ve)
        for tt in range(16):
            for c2 in range(2):
                i = tt * 2 + c2
                ysl = v_ps[:, CH * (i % 2): CH * (i % 2 + 1)]
                def vmm(e, tt=tt, c2=c2, ysl=ysl):
                    for d in range(DT):
                        last = e.matmul(ysl, hT[:, TOK * d + 128 * tt: TOK * d + 128 * (tt + 1)],
                                        w_sb[:, 1024 * d + CH * c2: 1024 * d + CH * (c2 + 1)],
                                        start=(d == 0), stop=(d == DT - 1))
                    return last
                g.on("tensor", vmm, wait=[se] if i >= 2 else [], inc=pe)
                st = 1040 * tt + 520 * c2
                def vev(e, st=st, ysl=ysl):
                    last = e.tensor_scalar(
                        out=v_own[:, st: st + 520].rearrange("p (h f) -> p h f", f=65)[:, :, 0:64],
                        in0=ysl.rearrange("p (h f) -> p h f", f=64),
                        scalar1=1.0, scalar2=None, op0=ALU.mult)
                    e.drain()
                    return last
                g.on("vector", vev, wait=[pe], inc=se)
        g.w("sync", se)
        for j in range(NCORE):
            g.on("sync", lambda e, j=j: e.dma_start(
                out=sendv_d[j, 0: SH_V].rearrange("(t p f) -> p t f", p=128, f=130),
                in_=v_own[:].rearrange("p (t f) -> p t f", f=1040)[:, :, 130 * j: 130 * (j + 1)]),
                inc=qs, inck=16)
        g.run_phase("qkv_v")


def emit_flash(g, cm, recvqk, recvv, ret, causal, bg=None, wait_qk=None, wait_v=None):
    nc = g.nc
    qs, qg, pe, ve, se = g.sem["qs"], g.sem["qg"], g.sem["pe"], g.sem["ve"], g.sem["se"]

    qf = g.sb("fl_q", [128, 2 * SEQ], BF16)
    kf = g.sb("fl_k", [128, 2 * SEQ], BF16)
    vf = g.sb("fl_v", [128, 2 * 64 * 130], BF16)
    p_sb = g.sb("fl_p", [128, 3 * 1024], BF16)

    qa = g.sem["qa"]
    first_s = [wait_qk] if wait_qk else []
    first_g = [wait_v] if wait_v else []
    first_a = [(qs, qs.n), (qg, qg.n), (g.sem["pe"], g.sem["pe"].n),
               (g.sem["ve"], g.sem["ve"].n)] + ([wait_qk] if wait_qk else [])
    for b in range(2):
        for r4 in range(4):
            r = 4 * b + r4
            g.on("sync", lambda e, b=b, r4=r4, r=r: e.dma_start(
                out=qf[:, SEQ * b + TOK * r4: SEQ * b + TOK * (r4 + 1)],
                in_=recvqk[r, 0: SH_Q].rearrange("(p t) -> p t", p=128)),
                wait=first_s, inc=qs, inck=16)
            first_s = []
            g.on("scalar", lambda e, b=b, r4=r4, r=r: e.dma_start(
                out=kf[:, SEQ * b + TOK * r4: SEQ * b + TOK * (r4 + 1)],
                in_=recvqk[r, SH_Q: SH_Q + SH_K].rearrange("(p t) -> p t", p=128)),
                wait=first_a, inc=qa, inck=16)
            first_a = []
            g.on("gpsimd", lambda e, b=b, r4=r4, r=r: e.dma_start(
                out=vf[:].rearrange("p (t f) -> p t f", f=130)[:, 64 * b + 16 * r4: 64 * b + 16 * (r4 + 1), :],
                in_=recvv[r, 0: SH_V].rearrange("(t p f) -> p t f", p=128, f=130)),
                wait=first_g, inc=qg, inck=16)
            first_g = []
    g.run_phase("fl_load")
    bw_se, bw_ve = g.sem["se"].n, g.sem["ve"].n

    with g.ps("fl_s", [128, 2048], F32) as s_ps, \
         g.ps("fl_o", [65, 2048], F32) as o_ps:
        oraw = g.sb("fl_oraw", [65, 2 * 2048], BF16)
        bcB2 = g.sb("fl_bcB2", [64, 2048], BF16)
        o_sb2 = g.sb("fl_osb2", [64, 2048], BF16)
        dscr = g.nc.dram_tensor(g.prefix + "fl_dscr", [2, 2048], BF16).ap()
        n0, n1 = g.sem["n0"], g.sem["n1"]
        nrm_ve = [g.sem["ve"].n]
        oev_se = [g.sem["se"].n]
        otts_done = {}
        send_done = {}
        qcidx = [0]
        prev_exp = None
        g.w("tensor", qs, qg, ve, qa)
        for b in range(2):
            for qc in range(8):
                units = []
                for qh in range(2):
                    nk = (8 * qc + 4 * (qh + 1)) if causal else 64
                    for kt in range(nk):
                        units.append((qh, kt, kt == 0, kt == nk - 1))
                NU = len(units)
                exp_done = {}
                s_done = {}

                def emit_av(u):
                    qh, kt, first, last = units[u]
                    psl = p_sb[:, 1024 * (u % 3): 1024 * (u % 3 + 1)]
                    diag = causal and kt >= 8 * qc + 4 * qh
                    if diag:
                        off = 896 - 128 * (kt - 8 * qc) + 512 * qh
                        def dmask(e, psl=psl, off=off):
                            e.tensor_tensor(out=psl[:, 0:512], in0=psl[:, 0:512],
                                            in1=cm["Tbase_sb"][:, off: off + 512], op=ALU.mult)
                            e.drain()
                            last_i = e.tensor_tensor(out=psl[:, 512:1024], in0=psl[:, 512:1024],
                                                     in1=cm["Tbase_sb"][:, off: off + 512], op=ALU.mult)
                            e.drain()
                            return last_i
                        g.on("vector", dmask, wait=[(se, exp_done[u])], inc=ve)
                        avwait = [(ve, ve.n)]
                    else:
                        avwait = [(se, exp_done[u])]
                    if first:
                        avwait.append((se, oev_se[0]))
                    def avmm(e, kt=kt, qh=qh, psl=psl, first=first, last=last, b=b):
                        for h in range(2):
                            last_i = e.matmul(
                                o_ps[:, 1024 * h + 512 * qh: 1024 * h + 512 * (qh + 1)],
                                vf[:, 8320 * b + 130 * kt + 65 * h: 8320 * b + 130 * kt + 65 * (h + 1)],
                                psl[:, 512 * h: 512 * (h + 1)],
                                start=first, stop=last)
                        return last_i
                    g.on("tensor", avmm, wait=avwait, inc=pe)

                for u, (qh, kt, first, last) in enumerate(units):
                    par = u % 2
                    ssl = s_ps[:, 1024 * par: 1024 * (par + 1)]
                    q0 = SEQ * b + 1024 * qc + 512 * qh
                    def smm(e, b=b, kt=kt, q0=q0, ssl=ssl):
                        for h, tp in ((0, (0, 0)), (1, (64, 0))):
                            base = 64 * h
                            last_i = e.matmul(
                                ssl[:, 512 * h: 512 * (h + 1)],
                                kf[base:base + 64, SEQ * b + 128 * kt: SEQ * b + 128 * (kt + 1)],
                                qf[base:base + 64, q0: q0 + 512],
                                start=True, stop=True, tile_position=tp)
                        return last_i
                    if u >= 2:
                        swait = [(se, exp_done[u - 2])]
                    elif prev_exp is None:
                        swait = [(se, bw_se), (ve, bw_ve)]
                    else:
                        swait = [(se, prev_exp[u])]
                    g.on("tensor", smm, wait=swait, inc=pe)
                    s_done[u] = pe.n
                    psl = p_sb[:, 1024 * (u % 3): 1024 * (u % 3 + 1)]
                    g.on("scalar", lambda e, psl=psl, ssl=ssl: e.activation(
                        out=psl, in_=ssl, func=AF.Exp, scale=ATTN_SCALE),
                        wait=[(pe, s_done[u])], inc=se)
                    exp_done[u] = se.n
                    if u >= 1:
                        emit_av(u - 1)
                emit_av(NU - 1)
                prev_exp = (exp_done[NU - 2], exp_done[NU - 1])

                qci = qcidx[0]
                qcidx[0] += 1
                slot = qci % 2
                osl = oraw[:, 2048 * slot: 2048 * (slot + 1)]
                bsl = bcB2[:]
                csl = o_sb2[:]
                oevw = [(pe, pe.n)]
                if qci >= 2:
                    oevw.append((ve, otts_done[qci - 2]))
                g.on("scalar", lambda e, osl=osl: e.copy(out=osl, in_=o_ps[:]),
                     wait=oevw, inc=se)
                oev_se[0] = se.n
                def recips(e, osl=osl):
                    with g.nc.allow_low_precision(reason="softmax denom recip in bf16; denominator sums bf16 terms"):
                        for h in range(2):
                            e.reciprocal(out=osl[64:65, 1024 * h: 1024 * (h + 1)],
                                         in_=osl[64:65, 1024 * h: 1024 * (h + 1)])
                            e.drain()
                    return e.nop()
                g.on("vector", recips, wait=[(se, oev_se[0])], inc=ve)
                rec_ve = ve.n
                g.on("gpsimd", lambda e, slot=slot, osl=osl: e.dma_start(
                    out=dscr[slot: slot + 1, :], in_=osl[64:65, :]),
                    wait=[(ve, rec_ve)], inc=n0, inck=16)
                g.on("gpsimd", lambda e, slot=slot, bsl=bsl: e.dma_start(
                    out=bsl, in_=dscr[slot: slot + 1, :].partition_broadcast(64)),
                    wait=[(n0, g.sem["n0"].n)], inc=n1, inck=16)
                bc_n1 = n1.n
                ottw = [(n1, bc_n1)]
                if qci >= 1:
                    ottw.append((qg, send_done[qci - 1]))
                def otts(e, osl=osl, bsl=bsl, csl=csl):
                    last_i = e.tensor_tensor(out=csl, in0=osl[0:64, :], in1=bsl, op=ALU.mult)
                    e.drain()
                    return last_i
                g.on("vector", otts, wait=ottw, inc=ve)
                otts_done[qci] = ve.n
                nrm_ve[0] = ve.n
                rdest = 4 * b + qc // 2
                col = 1024 * (qc % 2)
                for h in range(2):
                    g.on("gpsimd", lambda e, rdest=rdest, h=h, col=col, csl=csl: e.dma_start(
                        out=ret.rearrange("r (p t) -> r p t", p=128)[rdest, 64 * h: 64 * (h + 1), col: col + 1024],
                        in_=csl[:, 1024 * h: 1024 * (h + 1)]),
                        wait=[(ve, otts_done[qci])], inc=qg, inck=16)
                send_done[qci] = qg.n
                if bg is not None:
                    for _ in range(11):
                        if next(bg, "END") == "END":
                            bg = None
                            break
            g.run_phase("fl_b%d" % b)
        if bg is not None:
            for _ in iter(lambda: next(bg, "END"), "END"):
                pass
        g.w("gpsimd", qg)
        g.run_phase("fl_end")


def emit_outproj(g, cm, xT, ret_recv, wo_bf, bo_bf, w_wait=()):
    nc = g.nc
    qs, pe, ve, se = g.sem["qs"], g.sem["pe"], g.sem["ve"], g.sem["se"]
    if w_wait:
        g.on("sync", None, wait=list(w_wait))
    at_sb = g.sb("pr_at", [128, DT * TOK], BF16)
    wo_sb = g.sb("pr_wo", [128, DT * 1024], BF16)
    boh_sb = g.sb("pr_bo", [1, 1024], BF16)
    for r in range(DT):
        g.on("sync", lambda e, r=r: e.dma_start(
            out=at_sb[:, TOK * r: TOK * (r + 1)],
            in_=ret_recv[r, :].rearrange("(p t) -> p t", p=128)),
            wait=[g.sem["qg"]], inc=qs, inck=16)
    g.on("sync", lambda e: e.dma_start(
        out=wo_sb[:].rearrange("p (a c) -> p a c", a=DT),
        in_=wo_bf.rearrange("(a p) c -> p a c", p=128)), wait=[g.sem["qg"]], inc=qs, inck=16)
    g.on("sync", lambda e: e.dma_start(out=boh_sb[:], in_=bo_bf),
         wait=[g.sem["qg"]], inc=qs, inck=16)
    with g.ps("pr_u", [128, 2 * CH], F32) as u_ps:
        g.w("tensor", qs)
        evacs = []
        for i, (dcol, ch) in enumerate([(dc, c) for dc in range(DT) for c in range(NCH)]):
            ysl = u_ps[:, CH * (i % 2): CH * (i % 2 + 1)]
            def mm(e, dcol=dcol, ch=ch, ysl=ysl):
                e.matmul(ysl, boh_sb[:, 128 * dcol: 128 * (dcol + 1)],
                         cm["onesrow_bf"][:], start=True, stop=False)
                for it in range(DT):
                    last = e.matmul(ysl, wo_sb[:, 1024 * it + 128 * dcol: 1024 * it + 128 * (dcol + 1)],
                                    at_sb[:, TOK * it + CH * ch: TOK * it + CH * (ch + 1)],
                                    start=False, stop=(it == DT - 1))
                return last
            g.on("tensor", mm, wait=[(ve, evacs[i - 2])] if i >= 2 else [], inc=pe)
            def evac(e, dcol=dcol, ch=ch, ysl=ysl):
                xsl = xT[:, TOK * dcol + CH * ch: TOK * dcol + CH * (ch + 1)]
                return e.tensor_tensor(out=xsl, in0=xsl, in1=ysl, op=ALU.add)
            g.on("vector", evac, wait=[pe], inc=ve)
            evacs.append(g.sem["ve"].n)
        g.run_phase("outproj")


def build_qkv(layer):
    nc = bass.Bass("TRN2", target_bir_lowering=False)
    x_in = decl_in(nc, "x", [D, TOK])
    rope_on = layer == 0
    cnames = ("PT", "ropeC", "ropeS") if rope_on else ()
    cd = const_decls_subset(nc, cnames)
    send = decl_out(nc, "send", [NCORE, SHARD], BF16)
    w_bf = {k: nc.dram_tensor("w_" + k, [D, D], BF16).ap() for k in ("wq", "wk", "wv")}
    if layer == 0:
        wq = decl_in(nc, "Wq", [D, D])
        wkv = decl_in(nc, "Wkv", [D, 2 * D])
        casts = [(w_bf["wq"], wq), (w_bf["wk"], wkv[:, 0:D]), (w_bf["wv"], wkv[:, D:2 * D])]
    else:
        casts = [(w_bf[k], decl_in(nc, n, [D, D]))
                 for k, n in (("wq", "Wq"), ("wk", "Wk"), ("wv", "Wv"))]
    with ExitStack() as ctx:
        g = G(nc, ctx)
        cm = alloc_common(g, cd, cnames)
        xT = g.sb("xT", [128, DT * TOK], F32)
        hT = g.sb("at_hT", [128, DT * TOK], BF16)
        load_xT(g, xT, x_in)
        emit_casts(g, casts)
        g.w("tensor", g.sem["qs"])
        g.w("scalar", g.sem["qs"])
        g.w("vector", g.sem["qs"])
        g.run_phase("load")
        vown = g.sb("bq_vown", [128, 16 * 1040], BF16)
        ln = alloc_ln(g, sq=vown[:, 0: DT * CH])
        emit_ln_phase(g, cm, ln, xT, hT, name="at_ln")
        emit_qkv(g, cm, xT, hT, w_bf, send[:, 0: SH_Q + SH_K],
                 send[:, SH_Q + SH_K: SHARD], rope_on, v_own=vown)
    return nc


def build_flash(causal):
    nc = bass.Bass("TRN2", target_bir_lowering=False)
    cnames = ("Tbase",) if causal else ()
    cd = const_decls_subset(nc, cnames)
    recv = decl_in(nc, "recv", [NCORE, SHARD], BF16)
    ret = decl_out(nc, "ret", [NCORE, RET], BF16)
    with ExitStack() as ctx:
        g = G(nc, ctx)
        cm = alloc_common(g, cd, cnames)
        g.run_phase("load")
        emit_flash(g, cm, recv[:, 0: SH_Q + SH_K], recv[:, SH_Q + SH_K: SHARD],
                   ret, causal)
    return nc


def build_proj():
    nc = bass.Bass("TRN2", target_bir_lowering=False)
    x_in = decl_in(nc, "x", [D, TOK])
    ret_recv = decl_in(nc, "ret_recv", [NCORE, RET], BF16)
    wo = decl_in(nc, "Wo", [D, D])
    bo = decl_in(nc, "bo", [1, D])
    x_out = decl_out(nc, "x_out", [D, TOK])
    wo_bf = nc.dram_tensor("wo_bf", [D, D], BF16).ap()
    bo_bf = nc.dram_tensor("bo_bf", [1, D], BF16).ap()
    with ExitStack() as ctx:
        g = G(nc, ctx)
        cm = alloc_common(g, cd := {}, ())
        xT = g.sb("xT", [128, DT * TOK], F32)
        load_xT(g, xT, x_in)
        emit_casts(g, [(wo_bf, wo), (bo_bf, bo)])
        g.w("tensor", g.sem["qs"])
        g.w("vector", g.sem["qs"])
        g.run_phase("load")
        emit_outproj(g, cm, xT, ret_recv, wo_bf, bo_bf)
        store_xT(g, xT, x_out)
    return nc


def run_attn_subblock(xs_parts, layer, wdict):
    consts = get_consts()
    nc1 = build_qkv(layer)
    cnames = ("PT", "ropeC", "ropeS") if layer == 0 else ()
    im = []
    for c in range(NCORE):
        m = {"x": xs_parts[c], **{k: consts[c][k] for k in cnames}}
        if layer == 0:
            m["Wq"] = wdict["Wq"]
            m["Wkv"] = wdict["Wkv"]
        else:
            m["Wq"], m["Wk"], m["Wv"] = wdict["Wq"], wdict["Wk"], wdict["Wv"]
        im.append(m)
    res = run_bass_kernel_spmd(nc1, im, core_ids=list(range(NCORE)))
    sends = [np.asarray(res.results[c]["send"]) for c in range(NCORE)]
    recvs = [np.ascontiguousarray(np.stack([sends[r][c] for r in range(NCORE)]))
             for c in range(NCORE)]
    nc2 = build_flash(causal=(layer == 1))
    cnames2 = ("Tbase",) if layer == 1 else ()
    im = [{"recv": recvs[c], **{k: consts[c][k] for k in cnames2}} for c in range(NCORE)]
    res = run_bass_kernel_spmd(nc2, im, core_ids=list(range(NCORE)))
    rets = [np.asarray(res.results[c]["ret"]) for c in range(NCORE)]
    ret_recvs = [np.ascontiguousarray(np.stack([rets[r][c] for r in range(NCORE)]))
                 for c in range(NCORE)]
    nc3 = build_proj()
    wo = wdict["Wout"] if layer == 0 else wdict["Wo"]
    bo = wdict["bout"] if layer == 0 else wdict["bo"]
    im = [{"x": xs_parts[c], "ret_recv": ret_recvs[c], "Wo": wo,
           "bo": bo.reshape(1, D)} for c in range(NCORE)]
    res = run_bass_kernel_spmd(nc3, im, core_ids=list(range(NCORE)))
    return [np.asarray(res.results[c]["x_out"]) for c in range(NCORE)]


def alloc_quant(g):
    return {"wt": g.sb("qz_wt", [128, 2 * 1024], F32),
            "racc": g.sb("qz_racc", [128, 32], F32),
            "r1": g.sb("qz_r1", [128, 1], F32),
            "rrow": g.sb("qz_rrow", [1, 128], F32),
            "tt1": g.sb("qz_t1", [1, 1], F32),
            "rB": g.sb("qz_rB", [128, 1], F32),
            "a": g.sb("qz_a", [128, 1024], BF16),
            "q": g.sb("qz_q", [128, 2 * 1024], BF16)}


def emit_quant(g, cm, w_in, wq_out, kind, scale_half, name, scr, bufs):
    qs, ve = g.sem["qs"], g.sem["ve"]
    wp = (g.sem["wa"], g.sem["wb"])
    sp = (g.sem["sc"], g.sem["sd"])
    wt, racc = bufs["wt"], bufs["racc"]
    r1, rrow, tt1, rB = bufs["r1"], bufs["rrow"], bufs["tt1"], bufs["rB"]
    a_sb, q_sb = bufs["a"], bufs["q"]
    NG = 32

    def src(dram, i):
        if kind == "w1":
            return dram[128 * i: 128 * (i + 1), :]
        r, c = i // 4, i % 4
        return dram[128 * r: 128 * (r + 1), 1024 * c: 1024 * (c + 1)]

    def wview(par):
        return wt[:, 1024 * par: 1024 * (par + 1)]

    def qview(par):
        return q_sb[:, 1024 * par: 1024 * (par + 1)]

    red_done = {}
    for i in range(NG):
        par = i % 2
        g.on("sync", lambda e, i=i, par=par: e.dma_start(
            out=wview(par), in_=src(w_in, i)),
            wait=[(ve, red_done[i - 2])] if i >= 2 else [ve], inc=wp[par], inck=16)
        ld = wp[par].n
        g.on("vector", lambda e, i=i, par=par: e.tensor_reduce(
            out=racc[:, i: i + 1], in_=wview(par),
            axis=AX.X, op=ALU.add, apply_absolute_value=True),
            wait=[(wp[par], ld)], inc=ve)
        red_done[i] = ve.n
        yield

    qa, gp = g.sem["qa"], g.sem["gp"]
    g.on("vector", lambda e: e.tensor_reduce(out=r1[:], in_=racc[:, 0:NG],
                                             axis=AX.X, op=ALU.add),
         wait=[(ve, red_done[NG - 1])], inc=ve)
    g.on("sync", lambda e: e.dma_start(out=scr.rearrange("a b -> b a"), in_=r1[:]),
         wait=[ve], inc=qa, inck=16)
    g.on("sync", lambda e: e.dma_start(out=rrow[:], in_=scr), wait=[qa], inc=gp, inck=16)
    def gam(e):
        e.tensor_reduce(out=tt1[:], in_=rrow[:], axis=AX.X, op=ALU.add)
        e.drain()
        e.tensor_scalar(out=tt1[:], in0=tt1[:], scalar1=1.0 / (4096.0 * 1024.0),
                        scalar2=QEPS, op0=ALU.mult, op1=ALU.add)
        e.drain()
        return e.reciprocal(out=tt1[:], in_=tt1[:])
    g.on("vector", gam, wait=[gp], inc=ve)
    g.on("sync", lambda e: e.dma_start(out=scr[0:1, 0:1], in_=tt1[:]),
         wait=[ve], inc=qa, inck=16)
    g.on("sync", lambda e: e.dma_start(out=rB[:], in_=scr[0:1, 0:1].partition_broadcast(128)),
         wait=[qa], inc=gp, inck=16)
    rB_ready = gp.n
    yield

    qz_done = {}
    st_done = {}
    for i in range(NG):
        par = i % 2
        g.on("sync", lambda e, i=i, par=par: e.dma_start(
            out=wview(par), in_=src(w_in, i)),
            wait=[(ve, qz_done[i - 2])] if i >= 2 else [(ve, red_done[NG - 1])],
            inc=wp[par], inck=16)
        ld = wp[par].n
        def qz(e, i=i, par=par):
            w = wview(par)
            qsl = qview(par)
            e.tensor_scalar(out=a_sb[:], in0=w, scalar1=rB[:], scalar2=0.5,
                            op0=ALU.mult, op1=ALU.is_ge)
            e.drain()
            e.tensor_scalar(out=qsl, in0=w, scalar1=rB[:], scalar2=-0.5,
                            op0=ALU.mult, op1=ALU.is_le)
            e.drain()
            e.tensor_tensor(out=qsl, in0=a_sb[:], in1=qsl, op=ALU.subtract)
            e.drain()
            if scale_half:
                e.tensor_scalar(out=qsl, in0=qsl, scalar1=0.5, scalar2=None, op0=ALU.mult)
                e.drain()
            return e.nop()
        wz = [(wp[par], ld), (g.sem["gp"], rB_ready),
              (sp[par], st_done[i - 2] if i >= 2 else sp[par].n)]
        g.on("vector", qz, wait=wz, inc=ve)
        qz_done[i] = ve.n
        g.on("sync", lambda e, i=i, par=par: e.dma_start(
            out=src(wq_out, i), in_=qview(par)),
            wait=[(ve, qz_done[i])], inc=sp[par], inck=16)
        st_done[i] = sp[par].n
        yield


def emit_ffn(g, cm, xT, ln, wq1, wq2, b1v_sb, b2v_sb, hT_ch, hff, w1T_sb, w2T_sb):
    nc = g.nc
    qs, qg, pe, ve, se = g.sem["qs"], g.sem["qg"], g.sem["pe"], g.sem["ve"], g.sem["se"]
    wpar_sem = (g.sem["wa"], g.sem["wb"], g.sem["sc"])

    for tch2 in range(2):
        emit_ln_phase(g, cm, ln, xT, hT_ch, chunks=[2 * tch2, 2 * tch2 + 1],
                      hT_off=CH * 2 * tch2, name="ffn_ln%d" % tch2)
        if tch2 == 0:
            for d in range(DT):
                g.on("sync", lambda e, d=d: e.dma_start_transpose(
                    out=w1T_sb[:, 4096 * d: 4096 * (d + 1)],
                    in_=wq1[:, 128 * d: 128 * (d + 1)]),
                    wait=[pe, qg] if d == 0 else [], inc=qs, inck=16)
        for half in range(2):
            tch = 2 * tch2 + half
            hoff = CH * half
            with g.ps("ffn_h1_%d" % tch, [128, 2 * CH], F32) as h1_ps:
                if half == 0:
                    g.w("tensor", qs, g.sem["ve"])
                relus = []
                for f in range(32):
                    ysl = h1_ps[:, CH * (f % 2): CH * (f % 2 + 1)]
                    def mm1(e, f=f, ysl=ysl, hoff=hoff):
                        for d in range(DT):
                            last = e.matmul(ysl, w1T_sb[:, 4096 * d + 128 * f: 4096 * d + 128 * (f + 1)],
                                            hT_ch[:, 1024 * d + hoff: 1024 * d + hoff + CH],
                                            start=(d == 0), stop=(d == DT - 1))
                        return last
                    g.on("tensor", mm1, wait=[(se, relus[f - 2])] if f >= 2 else [], inc=pe)
                    g.on("scalar", lambda e, f=f, ysl=ysl: e.activation(
                        out=hff[:, CH * f: CH * (f + 1)], in_=ysl, func=AF.Relu,
                        bias=b1v_sb[:, f: f + 1]),
                        wait=[pe], inc=se)
                    relus.append(se.n)
                relu_last = se.n
                g.run_phase("ffn_m1_%d" % tch)
            with g.ps("ffn_u_%d" % tch, [128, DT * CH], F32) as u_ps:
                mmdone = []
                for f in range(32):
                    par = f % 3
                    wsem = wpar_sem[par]
                    g.on("sync", lambda e, f=f, par=par: e.dma_start_transpose(
                        out=w2T_sb[:, 1024 * par: 1024 * (par + 1)],
                        in_=wq2[:, 128 * f: 128 * (f + 1)]),
                        wait=[(pe, mmdone[f - 3])] if f >= 3 else [pe],
                        inc=wsem, inck=16)
                    def mm2(e, f=f, par=par):
                        for dc in range(DT):
                            last = e.matmul(u_ps[:, CH * dc: CH * (dc + 1)],
                                            w2T_sb[:, 1024 * par + 128 * dc: 1024 * par + 128 * (dc + 1)],
                                            hff[:, CH * f: CH * (f + 1)],
                                            start=(f == 0), stop=(f == 31 and dc == DT - 1))
                        return last
                    g.on("tensor", mm2,
                         wait=[wsem, (se, relu_last)] if f == 0 else [wsem], inc=pe)
                    mmdone.append(pe.n)
                def evac(e, tch=tch):
                    for dc in range(DT):
                        xsl = xT[:, TOK * dc + CH * tch: TOK * dc + CH * (tch + 1)]
                        e.scalar_tensor_tensor(out=xsl, in0=u_ps[:, CH * dc: CH * (dc + 1)],
                                               scalar=b2v_sb[:, dc: dc + 1], in1=xsl,
                                               op0=ALU.add, op1=ALU.add)
                        last = e.drain()
                    return last
                g.on("vector", evac, wait=[pe], inc=ve)
                g.run_phase("ffn_m2_%d" % tch)


def emit_w2t_pass(g, wq2, wq2t, tx=None):
    wp = (g.sem["wa"], g.sem["wb"])
    sp = (g.sem["sc"], g.sem["sd"])
    if tx is None:
        tx = g.sb("w2tp_tx", [128, 1024], BF16)
    st_done = {}
    for f in range(32):
        par = f % 2
        g.on("sync", lambda e, f=f: e.dma_start_transpose(
            out=tx[:], in_=wq2[:, 128 * f: 128 * (f + 1)]),
            wait=[(sp[0], st_done[f - 1])] if f >= 1 else [(sp[0], sp[0].n), (sp[1], sp[1].n)],
            inc=wp[par], inck=16)
        ld = wp[par].n
        g.on("sync", lambda e, f=f: e.dma_start(
            out=wq2t[128 * f: 128 * (f + 1), :], in_=tx[:]),
            wait=[(wp[par], ld)], inc=sp[0], inck=16)
        st_done[f] = sp[0].n
    g.run_phase("w2tp")


def build_quant():
    nc = bass.Bass("TRN2", target_bir_lowering=False)
    w1 = decl_in(nc, "W1", [DFF, D])
    w2 = decl_in(nc, "W2", [D, DFF])
    wq1 = decl_out(nc, "wq1", [DFF, D], BF16)
    wq2 = decl_out(nc, "wq2", [D, DFF], BF16)
    scr1 = nc.dram_tensor("scr1", [1, 128], F32).ap()
    scr2 = nc.dram_tensor("scr2", [1, 128], F32).ap()
    with ExitStack() as ctx:
        g = G(nc, ctx)
        cm = alloc_common(g, {}, ())
        g.run_phase("init")
        qb = alloc_quant(g)
        for _ in emit_quant(g, cm, w1, wq1, "w1", False, "q1", scr1, qb):
            pass
        for _ in emit_quant(g, cm, w2, wq2, "w2", True, "q2", scr2, qb):
            pass
        g.w("gpsimd", g.sem["qs"], g.sem["ve"])
        g.run_phase("qdone")
    return nc


def build_ffn():
    nc = bass.Bass("TRN2", target_bir_lowering=False)
    x_in = decl_in(nc, "x", [D, TOK])
    wq1 = decl_in(nc, "wq1", [DFF, D], BF16)
    wq2 = decl_in(nc, "wq2", [D, DFF], BF16)
    b1 = decl_in(nc, "b1", [1, DFF])
    b2 = decl_in(nc, "b2", [1, D])
    x_out = decl_out(nc, "x_out", [D, TOK])
    wq2t = nc.dram_tensor("i_wq2t", [DFF, D], BF16).ap()
    with ExitStack() as ctx:
        g = G(nc, ctx)
        cm = alloc_common(g, {}, ())
        xT = g.sb("xT", [128, DT * TOK], F32)
        b1h_sb = g.sb("ffn_b1h", [1, DFF], BF16)
        b2h_sb = g.sb("ffn_b2h", [1, D], BF16)
        load_xT(g, xT, x_in)
        g.on("gpsimd", lambda e: e.dma_start(out=b1h_sb[:], in_=b1), inc=g.sem["qg"], inck=16)
        g.on("gpsimd", lambda e: e.dma_start(out=b2h_sb[:], in_=b2), inc=g.sem["qg"], inck=16)
        g.w("vector", g.sem["qg"])
        g.on("vector", lambda e: e.tensor_scalar(out=b2h_sb[:], in0=b2h_sb[:], scalar1=0.5,
                                                 scalar2=None, op0=ALU.mult), inc=g.sem["ve"])
        g.w("tensor", g.sem["qs"])
        g.w("scalar", g.sem["qs"])
        g.run_phase("load")
        ln = alloc_ln(g)
        hT_ch = ln["xbf"]
        hff = g.sb("ffn_hff", [128, 32 * CH], BF16)
        w1T_sb = g.sb("ffn_w1T", [128, DT * 4096], BF16)
        w2T_sb = g.sb("ffn_w2T", [128, 3 * 1024], BF16)
        emit_ffn(g, cm, xT, ln, wq1, wq2, b1h_sb, b2h_sb, hT_ch, hff, w1T_sb, w2T_sb)
        store_xT(g, xT, x_out)
    return nc


def run_ffn_subblock(xs_parts, wq1, wq2, b1, b2):
    nc = build_ffn()
    im = [{"x": xs_parts[c], "wq1": wq1, "wq2": wq2,
           "b1": b1.reshape(1, DFF), "b2": b2.reshape(1, D)} for c in range(NCORE)]
    res = run_bass_kernel_spmd(nc, im, core_ids=list(range(NCORE)))
    return [np.asarray(res.results[c]["x_out"]) for c in range(NCORE)]


def run_quant(W1, W2):
    nc = build_quant()
    im = [{"W1": W1, "W2": W2} for _ in range(NCORE)]
    res = run_bass_kernel_spmd(nc, im, core_ids=list(range(NCORE)))
    return np.asarray(res.results[0]["wq1"]), np.asarray(res.results[0]["wq2"])


def shard_x(x):
    xs = x.reshape(NCORE, TOK, D)
    return [np.ascontiguousarray(xs[c].T) for c in range(NCORE)]


def unshard_x(parts):
    x = np.stack([np.asarray(p).T for p in parts], axis=0)
    return np.ascontiguousarray(x.reshape(2, SEQ, D))


_CONSTS = None


def get_consts():
    global _CONSTS
    if _CONSTS is None:
        _CONSTS = [host_consts(c) for c in range(NCORE)]
    return _CONSTS


def run_moe_subblock(xs_parts, wg, we, be):
    consts = get_consts()
    names = ("id8", "id128", "id1", "wsel")
    nc1 = build_moe("out")
    im = [{"x": xs_parts[c], "wg": wg, **{k: consts[c][k] for k in names}}
          for c in range(NCORE)]
    res = run_bass_kernel_spmd(nc1, im, core_ids=list(range(NCORE)))
    tots = np.concatenate([np.asarray(res.results[c]["tot"]).reshape(1, 8)
                           for c in range(NCORE)], axis=0)
    tots = np.ascontiguousarray(tots)
    nc2 = build_moe("ext")
    im = [{"x": xs_parts[c], "wg": wg, "we": we, "be": be.reshape(1, D),
           "tot_all": tots, **{k: consts[c][k] for k in names}}
          for c in range(NCORE)]
    res = run_bass_kernel_spmd(nc2, im, core_ids=list(range(NCORE)))
    return [np.asarray(res.results[c]["x_out"]) for c in range(NCORE)]




def build_fused():
    nc = bass.Bass("TRN2", target_bir_lowering=False)
    cd = const_decls_subset(nc, tuple(CONST_SPECS))
    x_in = decl_in(nc, "x", [D, TOK])
    x_out = decl_out(nc, "x_out", [D, TOK])
    win = {}
    for nm, shape in (("L0_ffn_Wg", [D, 8]), ("L0_ffn_We", [D, D]), ("L0_ffn_be", [1, D]),
                      ("L0_attn_Wq", [D, D]), ("L0_attn_Wkv", [D, 2 * D]),
                      ("L0_attn_Wout", [D, D]), ("L0_attn_bout", [1, D]),
                      ("L1_attn_Wq", [D, D]), ("L1_attn_Wk", [D, D]), ("L1_attn_Wv", [D, D]),
                      ("L1_attn_Wo", [D, D]), ("L1_attn_bo", [1, D]),
                      ("L1_ffn_W1", [DFF, D]), ("L1_ffn_b1", [1, DFF]),
                      ("L1_ffn_W2", [D, DFF]), ("L1_ffn_b2", [1, D])):
        win[nm] = decl_in(nc, nm, shape)
    dram = lambda nm, shape, dt=BF16: nc.dram_tensor(nm, shape, dt).ap()
    we_bf = dram("i_we", [D, D]); beh = dram("i_beh", [1, D])
    a_w = {0: {}, 1: {}}
    for k in ("wq", "wk", "wv"):
        a_w[0][k] = dram("i_a0_" + k, [D, D])
        a_w[1][k] = dram("i_a1_" + k, [D, D])
    wo_bf = {0: dram("i_wo0", [D, D]), 1: dram("i_wo1", [D, D])}
    bo_bf = {0: dram("i_bo0", [1, D]), 1: dram("i_bo1", [1, D])}
    wq1 = dram("i_wq1", [DFF, D]); wq2 = dram("i_wq2", [D, DFF])
    wq2t = dram("i_wq2t", [DFF, D])
    exch = {}
    for L in (0, 1):
        exch[L] = {
            "sendqk": dram("i_sendqk%d" % L, [NCORE, SH_Q + SH_K]),
            "recvqk": dram("i_recvqk%d" % L, [NCORE, SH_Q + SH_K]),
            "sendv": dram("i_sendv%d" % L, [NCORE, SH_V]),
            "recvv": dram("i_recvv%d" % L, [NCORE, SH_V]),
            "rsend": dram("i_rsend%d" % L, [NCORE, RET]),
            "rrecv": dram("i_rrecv%d" % L, [NCORE, RET]),
        }
    scrq = {1: nc.dram_tensor("i_scr1", [1, 128], F32).ap(),
            2: nc.dram_tensor("i_scr2", [1, 128], F32).ap()}
    tots = {}
    for i in (0, 1):
        tots[i] = {"tot_in": nc.dram_tensor("i_totin%d" % i, [1, 8], F32).ap(),
                   "tot_all": nc.dram_tensor("i_totall%d" % i, [8, 8], F32).ap()}

    groups8 = [list(range(NCORE))]

    with ExitStack() as ctx:
        g = G(nc, ctx)
        qs, qg, cc, pe, ve, se = (g.sem[k] for k in ("qs", "qg", "cc", "pe", "ve", "se"))
        cm = alloc_common(g, cd, ("id8", "id128", "id1", "wsel"))
        xT = g.sb("xT", [128, DT * TOK], F32)

        def scope_consts(names):
            cm2 = dict(cm)
            for n in names:
                shape, dt = CONST_SPECS[n]
                t = g.sb(n + "_sb", shape, dt)
                cm2[n + "_sb"] = t
                g.on("sync", lambda e, t=t, ap=cd[n]: e.dma_start(out=t[:], in_=ap),
                     inc=qs, inck=16)
            return cm2
        load_xT(g, xT, x_in)
        emit_casts(g, [(we_bf, win["L0_ffn_We"]), (beh, win["L0_ffn_be"])])
        wc_sem = g.sem["sd"]
        wc_state = {}
        def big_casts():
            emit_casts(g, [
                (a_w[0]["wq"], win["L0_attn_Wq"]),
                (a_w[0]["wk"], win["L0_attn_Wkv"][:, 0:D]),
                (a_w[0]["wv"], win["L0_attn_Wkv"][:, D:2 * D]),
            ], sem=wc_sem)
            wc_state[0] = wc_sem.n
        def rest_casts():
            emit_casts(g, [
                (wo_bf[0], win["L0_attn_Wout"]), (bo_bf[0], win["L0_attn_bout"]),
                (a_w[1]["wq"], win["L1_attn_Wq"]), (a_w[1]["wk"], win["L1_attn_Wk"]),
                (a_w[1]["wv"], win["L1_attn_Wv"]),
                (wo_bf[1], win["L1_attn_Wo"]), (bo_bf[1], win["L1_attn_bo"]),
            ], sem=wc_sem)
            wc_state[1] = wc_sem.n
        g.w("tensor", qs)
        g.w("scalar", qs)
        g.w("vector", qs)
        g.run_phase("load")

        def moe(i):
            with g.subscope("m%d_" % i):
                emit_moe_phase(g, cm, xT, {"wg": win["L0_ffn_Wg"], "we_bf": we_bf,
                                           "beh": beh}, "cc", tots[i],
                               after_loads=(big_casts if i == 0 else None))

        def attn(L):
            ex = exch[L]
            ccval = {}
            def qk_hook(ex=ex):
                g.w("gpsimd", qg)
                def ccqk(e, ex=ex):
                    with allow_ici_all_to_all():
                        return e.collective_compute(
                            "AllToAll", ALU.bypass, replica_groups=groups8,
                            ins=[ex["sendqk"]], outs=[ex["recvqk"]])
                g.on("gpsimd", ccqk, inc=cc)
                ccval["qk"] = cc.n
            with g.subscope("a%dq_" % L):
                cm2 = scope_consts(("PT", "ropeC", "ropeS") if L == 0 else ())
                hT = g.sb("hT", [128, DT * TOK], BF16)
                v_own = g.sb("vown", [128, 16 * 1040], BF16)
                ln = alloc_ln(g, sq=v_own[:, 0: DT * CH])
                emit_ln_phase(g, cm2, ln, xT, hT, name="ln")
                emit_qkv(g, cm2, xT, hT, a_w[L], ex["sendqk"], ex["sendv"],
                         rope_on=(L == 0), v_own=v_own, on_qk_done=qk_hook,
                         w_wait=[(wc_sem, wc_state[min(L, max(wc_state))])])
            def ccv(e, ex=ex):
                with allow_ici_all_to_all():
                    return e.collective_compute(
                        "AllToAll", ALU.bypass, replica_groups=groups8,
                        ins=[ex["sendv"]], outs=[ex["recvv"]])
            g.on("gpsimd", ccv, wait=[(qs, qs.n)], inc=cc)
            ccval["v"] = cc.n
            g.run_phase("a%d_cc1" % L)
            with g.subscope("a%df_" % L):
                cm2 = scope_consts(("Tbase",) if L == 1 else ())
                bg = None
                if L == 0:
                    import itertools
                    qb = alloc_quant(g)
                    bg = itertools.chain(
                        emit_quant(g, cm, win["L1_ffn_W1"], wq1, "w1",
                                   False, "q1", scrq[1], qb),
                        emit_quant(g, cm, win["L1_ffn_W2"], wq2, "w2",
                                   True, "q2", scrq[2], qb))
                emit_flash(g, cm2, ex["recvqk"], ex["recvv"], ex["rsend"],
                           causal=(L == 1), bg=bg,
                           wait_qk=(cc, ccval["qk"]), wait_v=(cc, ccval["v"]))
            g.w("gpsimd", qg)
            def cc2(e, ex=ex):
                with allow_ici_all_to_all():
                    return e.collective_compute(
                        "AllToAll", ALU.bypass, replica_groups=groups8,
                        ins=[ex["rsend"]], outs=[ex["rrecv"]])
            g.on("gpsimd", cc2, inc=cc)
            g.run_phase("a%d_cc2" % L)
            with g.subscope("a%dp_" % L):
                g.w("sync", cc)
                emit_outproj(g, cm, xT, ex["rrecv"], wo_bf[L], bo_bf[L],
                             w_wait=[(wc_sem, wc_state[1])])

        def ffn(i):
            with g.subscope("f%d_" % i):
                b1v_sb = g.sb("b1v", [128, 32], F32)
                b2v_sb = g.sb("b2v", [128, 8], F32)
                for f in range(32):
                    g.on("sync", lambda e, f=f: e.dma_start(
                        out=b1v_sb[:, f: f + 1],
                        in_=win["L1_ffn_b1"].rearrange("a (f p) -> (a f) p", p=128)[f: f + 1]
                            .rearrange("f p -> p f")),
                        inc=qs, inck=16)
                for f in range(DT):
                    g.on("sync", lambda e, f=f: e.dma_start(
                        out=b2v_sb[:, f: f + 1],
                        in_=win["L1_ffn_b2"].rearrange("a (f p) -> (a f) p", p=128)[f: f + 1]
                            .rearrange("f p -> p f")),
                        inc=qs, inck=16)
                g.w("vector", qs)
                g.on("vector", lambda e: e.tensor_scalar(
                    out=b2v_sb[:], in0=b2v_sb[:], scalar1=0.5, scalar2=None, op0=ALU.mult),
                    inc=ve)
                g.run_phase("fb")
                hff = g.sb("hff", [128, 32 * CH], BF16)
                ln = alloc_ln(g, sq=hff[:, 0:DT * CH], xbf=hff[:, DT * CH: 2 * DT * CH])
                hT_ch = g.sb("hTch", [128, DT * 2 * CH], BF16)
                w1T_sb = g.sb("w1T", [128, DT * 4096], BF16)
                w2T_sb = g.sb("w2T", [128, 3 * 1024], BF16)
                emit_ffn(g, cm, xT, ln, wq1, wq2, b1v_sb, b2v_sb, hT_ch, hff,
                         w1T_sb, w2T_sb)

        moe(0)
        rest_casts()
        attn(0)
        moe(1)
        ffn(0)
        attn(1)
        ffn(1)
        store_xT(g, xT, x_out)
    return nc


_FUSED_NC = None


def kernel_fused(**inputs):
    global _FUSED_NC
    f32 = lambda k: np.ascontiguousarray(np.asarray(inputs[k], np.float32))
    x = f32("L0_attn_queries")
    xs = shard_x(x)
    consts = get_consts()
    if _FUSED_NC is None:
        _FUSED_NC = build_fused()
    wkeys = ["L0_ffn_Wg", "L0_ffn_We", "L0_attn_Wq", "L0_attn_Wkv", "L0_attn_Wout",
             "L1_attn_Wq", "L1_attn_Wk", "L1_attn_Wv", "L1_attn_Wo",
             "L1_ffn_W1", "L1_ffn_W2"]
    bkeys = ["L0_ffn_be", "L0_attn_bout", "L1_attn_bo", "L1_ffn_b1", "L1_ffn_b2"]
    im = []
    for c in range(NCORE):
        m = {"x": xs[c], **consts[c]}
        for k in wkeys:
            m[k] = f32(k)
        for k in bkeys:
            m[k] = f32(k).reshape(1, -1)
        im.append(m)
    res = run_bass_kernel_spmd(_FUSED_NC, im, core_ids=list(range(NCORE)))
    return unshard_x([np.asarray(res.results[c]["x_out"]) for c in range(NCORE)])

def kernel(**inputs):
    return kernel_fused(**inputs)


def kernel_unfused(**inputs):

    f32 = lambda k: np.ascontiguousarray(np.asarray(inputs[k], np.float32))
    x = f32("L0_attn_queries")
    xs = shard_x(x)

    moe_w = (f32("L0_ffn_Wg"), f32("L0_ffn_We"), f32("L0_ffn_be"))
    xs = run_moe_subblock(xs, *moe_w)
    xs = run_attn_subblock(xs, 0, {"Wq": f32("L0_attn_Wq"), "Wkv": f32("L0_attn_Wkv"),
                                   "Wout": f32("L0_attn_Wout"), "bout": f32("L0_attn_bout")})
    xs = run_moe_subblock(xs, *moe_w)
    wq1, wq2 = run_quant(f32("L1_ffn_W1"), f32("L1_ffn_W2"))
    b1, b2 = f32("L1_ffn_b1"), f32("L1_ffn_b2")
    xs = run_ffn_subblock(xs, wq1, wq2, b1, b2)
    xs = run_attn_subblock(xs, 1, {"Wq": f32("L1_attn_Wq"), "Wk": f32("L1_attn_Wk"),
                                   "Wv": f32("L1_attn_Wv"), "Wo": f32("L1_attn_Wo"),
                                   "bo": f32("L1_attn_bo")})
    xs = run_ffn_subblock(xs, wq1, wq2, b1, b2)
    return unshard_x(xs)


# version 33
# speedup vs baseline: 1.1466x; 1.0059x over previous
import numpy as np
import ml_dtypes
from contextlib import ExitStack

import concourse.bass as bass
import concourse.mybir as mybir
from concourse.bass_utils import run_bass_kernel_spmd
from concourse.aot_env import allow_ici_all_to_all

F32 = mybir.dt.float32
BF16 = mybir.dt.bfloat16
AF = mybir.ActivationFunctionType
ALU = mybir.AluOpType
AX = mybir.AxisListType
BF = ml_dtypes.bfloat16

NCORE = 8
TOK = 2048
D = 1024
DT = D // 128
CH = 512
NCH = TOK // CH
CAP = 2048.0
LNEPS = 1e-5
QEPS = 1e-5
DFF = 4096
SEQ = 8192
ATTN_SCALE = 0.125


class Sem:
    def __init__(self, ctx, nc, name):
        self.h = ctx.enter_context(nc.semaphore(name))
        self.n = 0


class G:
    ENG = ("sync", "tensor", "vector", "scalar", "gpsimd")

    def __init__(self, nc, ctx):
        self.nc = nc
        self.ctx = ctx
        self.cur = ctx
        self.prefix = ""
        self.q = {k: [] for k in self.ENG}
        self.sem = {name: Sem(ctx, nc, "sem_" + name) for name in
                    ("qs", "qg", "qa", "pe", "ve", "se", "gp", "cc", "wa", "wb", "sc", "sd", "n0", "n1")}
        self.nblk = 0

    def sb(self, name, shape, dt):
        return self.cur.enter_context(self.nc.sbuf_tensor(self.prefix + name, shape, dt))

    def ps(self, name, shape, dt=F32):
        return self.nc.psum_tensor(self.prefix + name, shape, dt)

    def subscope(g, prefix):
        class _S:
            def __enter__(s):
                s.st = ExitStack()
                s.st.__enter__()
                s.old = (g.cur, g.prefix)
                g.cur, g.prefix = s.st, prefix
                for eng in ("sync", "gpsimd"):
                    g.w(eng, g.sem["pe"], g.sem["ve"], g.sem["se"],
                        g.sem["qs"], g.sem["qg"])
                return s
            def __exit__(s, *a):
                g.cur, g.prefix = s.old
                return s.st.__exit__(*a)
        return _S()

    def on(self, eng, fn, wait=(), inc=None, inck=1):
        waits = []
        for wv in wait:
            s, v = wv if isinstance(wv, tuple) else (wv, None)
            waits.append((s.h, s.n if v is None else v))
        if inc is not None:
            inc.n += inck
        self.q[eng].append((tuple(waits), fn, None if inc is None else (inc.h, inck)))

    def w(self, eng, *sems):
        self.on(eng, None, wait=list(sems))

    def run_phase(self, name=None):
        qs = self.q
        self.q = {k: [] for k in self.ENG}
        self.nblk += 1
        if name is not None:
            name = self.prefix + name
        with self.nc.Block(name) as b:
            for engname in self.ENG:
                items = qs[engname]
                if not items:
                    continue

                dr = engname in ("vector", "scalar", "gpsimd")

                def body(e, items=tuple(items), dr=dr):
                    for waits, fn, inc in items:
                        for (h, v) in waits:
                            e.wait_ge(h, v)
                        instr = fn(e) if fn is not None else None
                        if inc is not None:
                            h, k = inc
                            (instr if instr is not None else e.nop()).then_inc(h, k)
                        if dr and fn is not None:
                            e.drain()
                getattr(b, engname)(body)


def host_consts(core):
    pos = 2048.0 * (core % 4) + np.arange(TOK, dtype=np.float64)
    inv_freq = 1.0 / (10000.0 ** (np.arange(0, 32, 2, dtype=np.float64) / 32.0))
    ang = pos[:, None] * inv_freq[None, :]
    C = np.ones((128, TOK), np.float64)
    S = np.zeros((128, TOK), np.float64)
    for p in range(128):
        q = p % 64
        if q < 32:
            C[p] = np.cos(ang[:, q // 2])
            S[p] = np.sin(ang[:, q // 2])
    P = np.zeros((128, 128), np.float64)
    for base in (0, 64):
        for i in range(16):
            P[base + 2 * i, base + 2 * i + 1] = -1.0
            P[base + 2 * i + 1, base + 2 * i] = 1.0
    u = np.arange(1920)[None, :]
    p = np.arange(128)[:, None]
    Tb = (p <= u - 896).astype(np.float64)
    wsel = np.zeros((8, 1), np.float32)
    wsel[:core] = 1.0
    Sx = S.copy()
    Sx[0::2] *= -1.0
    return {
        "ropeC": np.ascontiguousarray(C.astype(BF)),
        "ropeS": np.ascontiguousarray(S.astype(BF)),
        "ropeSx": np.ascontiguousarray(Sx.astype(BF)),
        "PT": np.ascontiguousarray(P.T.astype(BF)),
        "Tbase": np.ascontiguousarray(Tb.astype(BF)),
        "id8": np.eye(8, dtype=np.float32),
        "id128": np.eye(128, dtype=np.float32),
        "id1": np.ones((1, 1), np.float32),
        "wsel": wsel,
    }


CONST_SPECS = {
    "ropeC": ([128, TOK], BF16), "ropeS": ([128, TOK], BF16),
    "ropeSx": ([128, TOK], BF16),
    "PT": ([128, 128], BF16), "Tbase": ([128, 1920], BF16),
    "id8": ([8, 8], F32), "id128": ([128, 128], F32), "id1": ([1, 1], F32),
    "wsel": ([8, 1], F32),
}


def decl_in(nc, name, shape, dt=F32):
    return nc.dram_tensor(name, shape, dt, kind="ExternalInput").ap()


def decl_out(nc, name, shape, dt=F32):
    return nc.dram_tensor(name, shape, dt, kind="ExternalOutput").ap()


def const_decls_subset(nc, names):
    return {n: decl_in(nc, n, *CONST_SPECS[n]) for n in names}


def alloc_common(g, cd, const_names):
    cm = {}
    cm["ones128_bf"] = g.sb("ones128_bf", [128, 1], BF16)
    cm["ones128_f32"] = g.sb("ones128_f32", [128, 1], F32)
    cm["onesrow_f32"] = g.sb("onesrow_f32", [1, 128], F32)
    cm["onesrow_bf"] = g.sb("onesrow_bf", [1, 512], BF16)
    cm["ones8_f32"] = g.sb("ones8_f32", [8, 1], F32)

    cm["eps_ln"] = g.sb("eps_ln", [1, 1], F32)

    def ms(e):
        e.memset(cm["eps_ln"][:], LNEPS)
        e.memset(cm["ones128_bf"][:], 1.0)
        e.memset(cm["ones128_f32"][:], 1.0)
        e.memset(cm["onesrow_f32"][:], 1.0)
        e.memset(cm["onesrow_bf"][:], 1.0)
        return e.memset(cm["ones8_f32"][:], 1.0)
    g.on("vector", ms, inc=g.sem["ve"])
    for n in const_names:
        shape, dt = CONST_SPECS[n]
        t = g.sb(n + "_sb", shape, dt)
        cm[n + "_sb"] = t
        g.on("sync", lambda e, t=t, ap=cd[n]: e.dma_start(out=t[:], in_=ap),
             inc=g.sem["qs"], inck=16)
    return cm


def load_xT(g, xT, x_in):
    g.on("sync", lambda e: e.dma_start(
        out=xT[:].rearrange("p (a c) -> p a c", a=DT),
        in_=x_in.rearrange("(a p) c -> p a c", p=128)), inc=g.sem["qs"], inck=16)


def store_xT(g, xT, x_out):
    g.w("gpsimd", g.sem["ve"], g.sem["se"])
    g.on("gpsimd", lambda e: e.dma_start(
        out=x_out.rearrange("(a p) c -> p a c", p=128),
        in_=xT[:].rearrange("p (a c) -> p a c", a=DT)), inc=g.sem["qg"], inck=16)
    g.w("gpsimd", g.sem["qg"])
    g.run_phase("store_x")


def emit_casts(g, casts, sem=None):
    s = g.sem["qg"] if sem is None else sem
    for dst, src in casts:
        rows = dst.shape[0]
        nsp = 8 if rows % 8 == 0 and rows >= 8 else 1
        rs = rows // nsp
        for i in range(nsp):
            g.on("gpsimd", lambda e, dst=dst, src=src, i=i, rs=rs: e.dma_start(
                out=dst[rs * i: rs * (i + 1)], in_=src[rs * i: rs * (i + 1)]),
                inc=s, inck=16)


def alloc_ln(g, sq=None, xbf=None):
    b = {}
    b["xbf"] = g.sb("ln_xbf", [128, DT * CH], BF16) if xbf is None else xbf
    b["sq"] = g.sb("ln_sq", [128, DT * CH], BF16) if sq is None else sq
    b["m_sb"] = g.sb("ln_m", [1, CH], F32)
    b["v_sb"] = g.sb("ln_v", [1, CH], F32)
    b["r_sb"] = g.sb("ln_r", [1, CH], F32)
    b["mr_sb"] = b["t1_sb"] = g.sb("ln_t1", [1, CH], F32)
    b["u_sb"] = g.sb("ln_u", [128, CH], F32)
    return b


def emit_ln_phase(g, cm, b, xT, hT, chunks=None, router=None, hT_off=None, name="ln", steps=6):
    nc = g.nc
    se, pe, ve = g.sem["se"], g.sem["pe"], g.sem["ve"]
    if chunks is None:
        chunks = list(range(NCH))
    chunks = list(chunks)
    NC = len(chunks)
    if hT_off is None:
        HTOK, hT_off = TOK, 0
    else:
        HTOK = NC * CH
    with ExitStack() as ps:
        msum_ps = ps.enter_context(g.ps(name + "_msum", [1, CH], F32))
        s2_ps = ps.enter_context(g.ps(name + "_s2", [1, CH], F32))
        rb_ps = ps.enter_context(g.ps(name + "_rb", [128, CH], F32))
        mrb_ps = ps.enter_context(g.ps(name + "_mrb", [128, CH], F32))
        lg_ps = ps.enter_context(g.ps(name + "_lg", [8, TOK], F32)) \
            if router is not None else None

        reds_pe = {}
        stats_ve = {}
        mr_ve = {}
        bcast_pe = {}
        hcomp_ve = {}
        rstd_se = {}
        rb8_se = {}

        def e_cast(ci, ch):
            def cast(e):
                for d in range(DT):
                    e.activation(out=b["xbf"][:, CH * d: CH * (d + 1)],
                                 in_=xT[:, TOK * d + CH * ch: TOK * d + CH * (ch + 1)], func=AF.Copy)
                for d in range(DT):
                    last = e.activation(out=b["sq"][:, CH * d: CH * (d + 1)],
                                        in_=xT[:, TOK * d + CH * ch: TOK * d + CH * (ch + 1)], func=AF.Square)
                return last
            g.on("scalar", cast, wait=[(pe, reds_pe[ci - 1])] if ci >= 1 else [(pe, pe.n)], inc=se)

        def e_reds(ci, ch):
            def reds(e):
                for d in range(DT):
                    e.matmul(msum_ps[:], cm["ones128_bf"][:],
                             b["xbf"][:, CH * d: CH * (d + 1)], start=(d == 0), stop=(d == DT - 1))
                for d in range(DT):
                    last = e.matmul(s2_ps[:], cm["ones128_bf"][:],
                                    b["sq"][:, CH * d: CH * (d + 1)], start=(d == 0), stop=(d == DT - 1))
                if router is not None:
                    for d in range(DT):
                        last = e.matmul(lg_ps[:, CH * ch: CH * (ch + 1)],
                                        router["wg_sb"][:, 8 * d: 8 * (d + 1)],
                                        xT[:, TOK * d + CH * ch: TOK * d + CH * (ch + 1)],
                                        start=(d == 0), stop=False)
                return last
            w = [se] + ([(ve, stats_ve[ci - 1])] if ci >= 1 else [])
            g.on("tensor", reds, wait=w, inc=pe)
            reds_pe[ci] = pe.n

        def e_stats(ci, ch):
            def stats(e):
                e.tensor_scalar(out=b["m_sb"][:], in0=msum_ps[:], scalar1=1.0 / D, scalar2=None, op0=ALU.mult)
                e.drain()
                e.tensor_scalar(out=b["v_sb"][:], in0=s2_ps[:], scalar1=1.0 / D, scalar2=None, op0=ALU.mult)
                e.drain()
                e.tensor_tensor(out=b["t1_sb"][:], in0=b["m_sb"][:], in1=b["m_sb"][:], op=ALU.mult)
                e.drain()
                return e.tensor_tensor(out=b["v_sb"][:], in0=b["v_sb"][:], in1=b["t1_sb"][:], op=ALU.subtract)
            w = [(pe, reds_pe[ci])] + ([(se, rstd_se[ci - 1])] if ci >= 1 else [])
            g.on("vector", stats, wait=w, inc=ve)
            stats_ve[ci] = ve.n

        def e_rstd(ci, ch):
            def rstd(e):
                e.activation(out=b["t1_sb"][:], in_=b["v_sb"][:], func=AF.Ln, bias=cm["eps_ln"][:])
                return e.activation(out=b["r_sb"][:], in_=b["t1_sb"][:], func=AF.Exp, scale=-0.5)
            g.on("scalar", rstd, wait=[(ve, stats_ve[ci])], inc=se)
            rstd_se[ci] = se.n

        def e_mr(ci, ch):
            g.on("vector", lambda e: e.tensor_tensor(out=b["mr_sb"][:], in0=b["m_sb"][:],
                                                     in1=b["r_sb"][:], op=ALU.mult),
                 wait=[(se, rstd_se[ci])], inc=ve)
            mr_ve[ci] = ve.n

        def e_bcast(ci, ch):
            def bcast(e):
                e.matmul(rb_ps[:], cm["onesrow_f32"][:], b["r_sb"][:], start=True, stop=True)
                last = e.matmul(mrb_ps[:], cm["onesrow_f32"][:], b["mr_sb"][:], start=True, stop=True)
                if router is not None:
                    last = e.matmul(lg_ps[:, CH * ch: CH * (ch + 1)],
                                    router["ngsum_sb"][:], b["m_sb"][:], start=False, stop=True)
                return last
            w = [(ve, mr_ve[ci])]
            if ci >= 1:
                w.append((ve, hcomp_ve[ci - 1]))
                if router is not None:
                    w.append((se, rb8_se[ci - 1]))
            g.on("tensor", bcast, wait=w, inc=pe)
            bcast_pe[ci] = pe.n

        def e_h(ci, ch):
            if router is not None:
                g.on("scalar", lambda e: e.copy(out=router["rb8_sb"][:], in_=rb_ps[0:8, :]),
                     wait=[(pe, bcast_pe[ci])], inc=se)
                rb8_se[ci] = se.n
            def hcomp(e):
                for d in range(DT):
                    e.tensor_tensor(out=b["u_sb"][:],
                                    in0=xT[:, TOK * d + CH * ch: TOK * d + CH * (ch + 1)],
                                    in1=rb_ps[:], op=ALU.mult)
                    e.drain()
                    last = e.tensor_tensor(
                        out=hT[:, HTOK * d + CH * ch - hT_off: HTOK * d + CH * (ch + 1) - hT_off],
                        in0=b["u_sb"][:], in1=mrb_ps[:], op=ALU.subtract)
                    e.drain()
                return last
            w = [(pe, bcast_pe[ci])] + ([(se, rb8_se[ci])] if router is not None else [])
            g.on("vector", hcomp, wait=w, inc=ve)
            if router is not None:
                g.on("vector", lambda e: e.tensor_tensor(
                    out=router["lg_sb"][:, CH * ch: CH * (ch + 1)],
                    in0=lg_ps[:, CH * ch: CH * (ch + 1)],
                    in1=router["rb8_sb"][:], op=ALU.mult), inc=ve)
            hcomp_ve[ci] = ve.n

        stages = [e_cast, e_reds, e_stats, e_rstd, e_mr, e_bcast, e_h]
        for ci in range(NC + 1):
            if ci < NC:
                e_cast(ci, chunks[ci])
            if ci >= 1:
                cj = ci - 1
                e_stats(cj, chunks[cj])
                e_rstd(cj, chunks[cj])
                e_mr(cj, chunks[cj])
                e_bcast(cj, chunks[cj])
                e_h(cj, chunks[cj])
            if ci < NC:
                e_reds(ci, chunks[ci])
        g.run_phase(name)


def alloc_gating(g):
    gt = {}
    gt["ltm_sb"] = g.sb("gt_ltm_sb", [128, 128], F32)
    gt["mx_sb"] = g.sb("gt_mx", [128, 16], F32)
    gt["msk_tm"] = g.sb("gt_msk_tm", [128, 128], F32)
    gt["mskT_sb"] = g.sb("gt_mskT_sb", [8, TOK], F32)
    gt["zero8"] = g.sb("gt_zero8", [8, 1], F32)
    gt["cumT_sb"] = g.sb("gt_cumT", [8, TOK], F32)
    gt["tot_sb"] = g.sb("gt_tot", [8, 8], F32)
    gt["eT_sb"] = g.sb("gt_eT", [8, TOK], F32)
    gt["sT_sb"] = g.sb("gt_sT_sb", [1, TOK], F32)
    gt["rs_sb"] = gt["sT_sb"]
    gt["nin_sb"] = gt["eT_sb"]
    gt["gateT_sb"] = g.sb("gt_gateT", [1, TOK], F32)
    gt["offs_sb"] = g.sb("gt_offs_sb", [1, 8], F32)
    gt["offsT_sb"] = g.sb("gt_offsT_sb", [8, 1], F32)
    gt["k8_sb"] = gt["cumT_sb"]
    gt["factorT_sb"] = gt["gateT_sb"]
    gt["fB_sb"] = g.sb("gt_fB_sb", [128, TOK], F32)
    return gt


def emit_gating(g, cm, lg_sb, gt, totals_mode, tot_io):
    nc = g.nc
    se, pe, ve = g.sem["se"], g.sem["pe"], g.sem["ve"]
    qg, cc = g.sem["qg"], g.sem["cc"]

    with g.ps("gt_ltm", [128, 128], F32) as ltm_ps, \
         g.ps("gt_mskT", [8, TOK], F32) as mskT_ps:
        def ttm(e):
            for j in range(16):
                last = e.transpose(ltm_ps[:, 8 * j: 8 * (j + 1)],
                                   lg_sb[:, 128 * j: 128 * (j + 1)], cm["id8_sb"][:])
            return last
        g.on("tensor", ttm, wait=[ve], inc=pe)
        g.on("scalar", lambda e: e.copy(out=gt["ltm_sb"][:], in_=ltm_ps[:]),
             wait=[pe], inc=se)
        def masks(e):
            for j in range(16):
                e.tensor_reduce(out=gt["mx_sb"][:, j: j + 1],
                                in_=gt["ltm_sb"][:, 8 * j: 8 * (j + 1)], axis=AX.X, op=ALU.max)
            e.drain()
            for j in range(16):
                last = e.tensor_scalar(out=gt["msk_tm"][:, 8 * j: 8 * (j + 1)],
                                       in0=gt["ltm_sb"][:, 8 * j: 8 * (j + 1)],
                                       scalar1=gt["mx_sb"][:, j: j + 1], scalar2=None, op0=ALU.is_ge)
            return last
        g.on("vector", masks, wait=[se], inc=ve)
        def tfm(e):
            for j in range(16):
                last = e.transpose(mskT_ps[:, 128 * j: 128 * (j + 1)],
                                   gt["msk_tm"][:, 8 * j: 8 * (j + 1)], cm["id128_sb"][:])
            return last
        g.on("tensor", tfm, wait=[ve], inc=pe)
        g.on("scalar", lambda e: e.copy(out=gt["mskT_sb"][:], in_=mskT_ps[:]),
             wait=[pe], inc=se)
        g.on("vector", lambda e: e.memset(gt["zero8"][:], 0.0))
        g.on("vector", lambda e: e.tensor_tensor_scan(
            out=gt["cumT_sb"][:], data0=gt["mskT_sb"][:], data1=gt["mskT_sb"][:],
            initial=gt["zero8"][:], op0=ALU.add, op1=ALU.bypass),
            wait=[se], inc=ve)
        g.run_phase("gate_mask")

    if totals_mode == "out":
        g.w("gpsimd", ve)
        g.on("gpsimd", lambda e: e.dma_start(
            out=tot_io["tot_out_ext"], in_=gt["cumT_sb"][:, TOK - 1: TOK]),
            inc=qg, inck=16)
        g.w("gpsimd", qg)
        g.run_phase("gate_tot")
        return

    if totals_mode == "cc":
        g.w("gpsimd", ve)
        g.on("gpsimd", lambda e: e.dma_start(
            out=tot_io["tot_in"].rearrange("a e -> e a"),
            in_=gt["cumT_sb"][:, TOK - 1: TOK]), inc=qg, inck=16)
        g.w("gpsimd", qg)
        g.on("gpsimd", lambda e: e.collective_compute(
            "AllGather", ALU.bypass, replica_groups=[list(range(NCORE))],
            ins=[tot_io["tot_in"]], outs=[tot_io["tot_all"]]), inc=cc)
        g.w("gpsimd", cc)
        g.on("gpsimd", lambda e: e.dma_start(out=gt["tot_sb"][:], in_=tot_io["tot_all"]),
             inc=qg, inck=16)
    else:
        g.on("gpsimd", lambda e: e.dma_start(out=gt["tot_sb"][:], in_=tot_io["tot_all"]),
             inc=qg, inck=16)

    with g.ps("gt_sT", [1, TOK], F32) as sT_ps:
        g.on("scalar", lambda e: e.activation(out=gt["eT_sb"][:], in_=lg_sb[:], func=AF.Exp),
             wait=[ve], inc=se)
        def smm(e):
            for c in range(NCH):
                last = e.matmul(sT_ps[:, CH * c: CH * (c + 1)], cm["ones8_f32"][:],
                                gt["eT_sb"][:, CH * c: CH * (c + 1)], start=True, stop=True)
            return last
        g.on("tensor", smm, wait=[se], inc=pe)
        g.on("scalar", lambda e: e.copy(out=gt["sT_sb"][:], in_=sT_ps[:]),
             wait=[pe], inc=se)
        g.on("vector", lambda e: e.reciprocal(out=gt["rs_sb"][:], in_=gt["sT_sb"][:]),
             wait=[se], inc=ve)
        g.on("vector", lambda e: e.tensor_tensor(out=gt["nin_sb"][:], in0=gt["eT_sb"][:],
                                                 in1=gt["mskT_sb"][:], op=ALU.mult), inc=ve)
        g.run_phase("gate_probs")

    with g.ps("gt_num", [1, TOK], F32) as num_ps:
        def nmm(e):
            for c in range(NCH):
                last = e.matmul(num_ps[:, CH * c: CH * (c + 1)], cm["ones8_f32"][:],
                                gt["nin_sb"][:, CH * c: CH * (c + 1)], start=True, stop=True)
            return last
        g.on("tensor", nmm, wait=[ve], inc=pe)
        g.on("vector", lambda e: e.tensor_tensor(out=gt["gateT_sb"][:], in0=num_ps[:],
                                                 in1=gt["rs_sb"][:], op=ALU.mult),
             wait=[pe], inc=ve)
        g.run_phase("gate_gate")

    with g.ps("gt_offs", [1, 8], F32) as offs_ps, \
         g.ps("gt_offsT", [8, 1], F32) as offsT_ps:
        g.on("tensor", lambda e: e.matmul(offs_ps[:], cm["wsel_sb"][:], gt["tot_sb"][:],
                                          start=True, stop=True), wait=[qg], inc=pe)
        g.on("scalar", lambda e: e.copy(out=gt["offs_sb"][:], in_=offs_ps[:]),
             wait=[pe], inc=se)
        g.on("tensor", lambda e: e.transpose(offsT_ps[:], gt["offs_sb"][:], cm["id1_sb"][:]),
             wait=[se], inc=pe)
        g.on("scalar", lambda e: e.copy(out=gt["offsT_sb"][:], in_=offsT_ps[:]),
             wait=[pe], inc=se)
        g.on("vector", lambda e: e.tensor_scalar(
            out=gt["k8_sb"][:], in0=gt["cumT_sb"][:], scalar1=gt["offsT_sb"][:],
            scalar2=CAP, op0=ALU.add, op1=ALU.is_le), wait=[se], inc=ve)
        g.on("vector", lambda e: e.tensor_tensor(out=gt["k8_sb"][:], in0=gt["k8_sb"][:],
                                                 in1=gt["mskT_sb"][:], op=ALU.mult), inc=ve)
        g.run_phase("gate_keep")

    with g.ps("gt_keep", [1, TOK], F32) as keep_ps, \
         g.ps("gt_fB", [128, TOK], F32) as fB_ps:
        def kmm(e):
            for c in range(NCH):
                last = e.matmul(keep_ps[:, CH * c: CH * (c + 1)], cm["ones8_f32"][:],
                                gt["k8_sb"][:, CH * c: CH * (c + 1)], start=True, stop=True)
            return last
        g.on("tensor", kmm, wait=[ve], inc=pe)
        g.on("vector", lambda e: e.tensor_tensor(out=gt["factorT_sb"][:], in0=keep_ps[:],
                                                 in1=gt["gateT_sb"][:], op=ALU.mult),
             wait=[pe], inc=ve)
        g.on("vector", lambda e: e.tensor_scalar(out=gt["factorT_sb"][:], in0=gt["factorT_sb"][:],
                                                 scalar1=0.5, scalar2=None, op0=ALU.mult), inc=ve)
        def fmm(e):
            for c in range(NCH):
                last = e.matmul(fB_ps[:, CH * c: CH * (c + 1)], cm["onesrow_f32"][:],
                                gt["factorT_sb"][:, CH * c: CH * (c + 1)], start=True, stop=True)
            return last
        g.on("tensor", fmm, wait=[ve], inc=pe)
        g.on("scalar", lambda e: e.copy(out=gt["fB_sb"][:], in_=fB_ps[:]),
             wait=[pe], inc=se)
        g.run_phase("gate_factor")


def emit_expert(g, cm, xT, hT, fB_sb, we_sb, beV_sb, ex_t):
    nc, pe, ve = g.nc, g.sem["pe"], g.sem["ve"]
    se, qs, qg = g.sem["se"], g.sem["qs"], g.sem["qg"]
    with g.ps("ex_y", [128, 2 * CH], F32) as y_ps:
        g.w("tensor", qs, qg, se, ve)
        evacs = []
        for i, (ch, do) in enumerate([(c, d) for c in range(NCH) for d in range(DT)]):
            par = i % 2
            ysl = y_ps[:, CH * par: CH * (par + 1)]
            def mm(e, ch=ch, do=do, ysl=ysl):
                for di in range(DT):
                    last = e.matmul(ysl, we_sb[:, 1024 * di + 128 * do: 1024 * di + 128 * (do + 1)],
                                    hT[:, TOK * di + CH * ch: TOK * di + CH * (ch + 1)],
                                    start=(di == 0), stop=(di == DT - 1))
                return last
            wt = [(ve, evacs[i - 2])] if i >= 2 else []
            g.on("tensor", mm, wait=wt, inc=pe)
            def evac(e, ch=ch, do=do, ysl=ysl):
                xsl = xT[:, TOK * do + CH * ch: TOK * do + CH * (ch + 1)]
                e.scalar_tensor_tensor(out=ex_t[:], in0=ysl, scalar=beV_sb[:, do: do + 1],
                                       in1=fB_sb[:, CH * ch: CH * (ch + 1)],
                                       op0=ALU.add, op1=ALU.mult)
                e.drain()
                return e.tensor_tensor(out=xsl, in0=xsl, in1=ex_t[:], op=ALU.add)
            g.on("vector", evac, wait=[pe], inc=ve)
            evacs.append(ve.n)
        g.run_phase("expert")


def emit_moe_phase(g, cm, xT, weights, totals_mode, tot_io, after_loads=None):
    nc = g.nc
    qs, qg, pe, ve, se = g.sem["qs"], g.sem["qg"], g.sem["pe"], g.sem["ve"], g.sem["se"]
    apply_mode = totals_mode != "out"

    hT = g.sb("moe_hT", [128, DT * TOK], BF16)
    ln = alloc_ln(g)
    gt = alloc_gating(g)
    ro = {
        "wg_sb": g.sb("moe_wg", [128, 8 * DT], F32),
        "ngsum_sb": g.sb("moe_ngsum", [1, 8], F32),
        "rb8_sb": g.sb("moe_rb8", [8, CH], F32),
        "lg_sb": g.sb("moe_lg", [8, TOK], F32),
    }
    if apply_mode:
        we_sb = g.sb("moe_we", [128, DT * 1024], BF16)
        beV_sb = g.sb("moe_beV", [128, 8], F32)
        ex_t = g.sb("moe_ex_t", [128, CH], F32)

    g.on("gpsimd", lambda e: e.dma_start(
        out=ro["wg_sb"][:].rearrange("p (a c) -> p a c", a=DT),
        in_=weights["wg"].rearrange("(a p) c -> p a c", p=128)), inc=qg, inck=16)
    if apply_mode:
        g.on("gpsimd", lambda e: e.dma_start(
            out=we_sb[:].rearrange("p (a c) -> p a c", a=DT),
            in_=weights["we_bf"].rearrange("(a p) c -> p a c", p=128)), inc=qg, inck=16)
    if after_loads is not None:
        after_loads()

    with g.ps("moe_gsum", [1, 8], F32) as gsum_ps:
        def gsum(e):
            for d in range(DT):
                last = e.matmul(gsum_ps[:], cm["ones128_f32"][:],
                                ro["wg_sb"][:, 8 * d: 8 * (d + 1)], start=(d == 0), stop=(d == DT - 1))
            return last
        g.on("tensor", gsum, wait=[qs, qg, ve], inc=pe)
        g.on("vector", lambda e: e.tensor_scalar(out=ro["ngsum_sb"][:], in0=gsum_ps[:],
                                                 scalar1=-1.0, scalar2=None, op0=ALU.mult),
             wait=[pe], inc=ve)
        g.run_phase("moe_prep")

    emit_ln_phase(g, cm, ln, xT, hT, router=ro, name="moe_ln")
    if apply_mode:
        for do in range(DT):
            g.on("gpsimd", lambda e, do=do: e.dma_start(
                out=beV_sb[:, do: do + 1],
                in_=weights["beh"].rearrange("a (f p) -> (a f) p", p=128)[do: do + 1]
                    .rearrange("f p -> p f")),
                inc=qg, inck=16)
    emit_gating(g, cm, ro["lg_sb"], gt, totals_mode, tot_io)
    if not apply_mode:
        return
    emit_expert(g, cm, xT, hT, gt["fB_sb"], we_sb, beV_sb, ex_t)


def build_moe(totals_mode):
    nc = bass.Bass("TRN2", target_bir_lowering=False)
    x_in = decl_in(nc, "x", [D, TOK])
    wg = decl_in(nc, "wg", [D, 8])
    cd = const_decls_subset(nc, ("id8", "id128", "id1", "wsel"))
    tot_io = {}
    weights = {"wg": wg}
    apply_mode = totals_mode != "out"
    if totals_mode == "out":
        tot_io["tot_out_ext"] = decl_out(nc, "tot", [8, 1])
    else:
        tot_io["tot_all"] = decl_in(nc, "tot_all", [8, 8])
        we = decl_in(nc, "we", [D, D])
        be = decl_in(nc, "be", [1, D])
        x_out = decl_out(nc, "x_out", [D, TOK])
        weights["we_bf"] = nc.dram_tensor("we_bf", [D, D], BF16).ap()
        weights["beh"] = nc.dram_tensor("beh", [1, D], BF16).ap()

    with ExitStack() as ctx:
        g = G(nc, ctx)
        cm = alloc_common(g, cd, ("id8", "id128", "id1", "wsel"))
        xT = g.sb("xT", [128, DT * TOK], F32)
        load_xT(g, xT, x_in)
        if apply_mode:
            emit_casts(g, [(weights["we_bf"], we), (weights["beh"], be)])
        g.w("tensor", g.sem["qs"])
        g.w("scalar", g.sem["qs"])
        g.w("vector", g.sem["qs"])
        g.run_phase("load")
        emit_moe_phase(g, cm, xT, weights, totals_mode, tot_io)
        if apply_mode:
            store_xT(g, xT, x_out)
        else:
            g.run_phase("end")
    return nc


SH_Q = 128 * TOK
SH_K = 128 * TOK
SH_V = TOK * 130
SHARD = SH_Q + SH_K + SH_V
RET = 128 * TOK


def emit_qkv(g, cm, xT, hT, w_bf, sendqk, sendv_d, rope_on, v_own=None,
             on_qk_done=None, w_wait=()):
    nc = g.nc
    qs, qg, pe, ve, se = g.sem["qs"], g.sem["qg"], g.sem["pe"], g.sem["ve"], g.sem["se"]
    sendv = sendqk

    w_sb2 = g.sb("qkv_w", [128, 2 * DT * 1024], BF16)
    o_sb = g.sb("qkv_o", [128, 2 * TOK], BF16)
    z_sb = g.sb("qkv_z", [128, TOK], BF16) if rope_on else None
    if v_own is None:
        v_own = g.sb("qkv_vown", [128, 16 * 1040], BF16)

    wslot = {"wq": 0, "wk": 1, "wv": 0}
    wsems = (g.sem["wa"], g.sem["wb"])

    def load_w(which, wait_extra=()):
        par = wslot[which]
        g.on("sync", lambda e, par=par: e.dma_start(
            out=w_sb2[:, DT * 1024 * par: DT * 1024 * (par + 1)].rearrange("p (a c) -> p a c", a=DT),
            in_=w_bf[which].rearrange("(a p) c -> p a c", p=128)),
            wait=[qg] + list(wait_extra) + list(w_wait), inc=wsems[par], inck=16)
        return wsems[par].n

    rope_prev = [None]
    with g.ps("qkv_q", [128, 2 * CH], F32) as q_ps, \
         g.ps("qkv_zp", [128, TOK], F32) as z_ps:
        for which, off in (("wq", 0), ("wk", SH_Q)):
            if which == "wq":
                wn = load_w("wq")
                kn = load_w("wk")
                wcur = wslot["wq"]
            else:
                wn, wcur = kn, wslot["wk"]
            g.on("tensor", None, wait=[(wsems[wcur], wn)])
            w_sb = w_sb2[:, DT * 1024 * wcur: DT * 1024 * (wcur + 1)]
            evl = []
            for m in range(DT):
                par = m % 2
                ot = o_sb[:, TOK * par: TOK * (par + 1)]
                for ch in range(NCH):
                    i = m * NCH + ch
                    ysl = q_ps[:, CH * (ch % 2): CH * (ch % 2 + 1)]
                    def mm(e, m=m, ch=ch, ysl=ysl):
                        for d in range(DT):
                            last = e.matmul(ysl, w_sb[:, 1024 * d + 128 * m: 1024 * d + 128 * (m + 1)],
                                            hT[:, TOK * d + CH * ch: TOK * d + CH * (ch + 1)],
                                            start=(d == 0), stop=(d == DT - 1))
                        return last
                    g.on("tensor", mm, wait=[(se, evl[i - 2])] if i >= 2 else [se], inc=pe)
                    g.on("scalar", lambda e, ch=ch, ysl=ysl, ot=ot: e.copy(
                        out=ot[:, CH * ch: CH * (ch + 1)], in_=ysl), wait=[pe, qg], inc=se)
                    evl.append(se.n)
                if rope_on:
                    zw = [(se, evl[-1])]
                    if rope_prev[0] is not None:
                        zw.append((ve, rope_prev[0]))
                    def zmm(e, ot=ot):
                        for ch in range(NCH):
                            last = e.matmul(z_ps[:, CH * ch: CH * (ch + 1)],
                                            cm["PT_sb"][:], ot[:, CH * ch: CH * (ch + 1)],
                                            start=True, stop=True)
                        return last
                    g.on("tensor", zmm, wait=zw, inc=pe)
                    zn = pe.n
                    def rope(e, ot=ot):
                        e.tensor_tensor(out=z_sb[:], in0=z_ps[:],
                                        in1=cm["ropeS_sb"][:], op=ALU.mult)
                        e.drain()
                        e.tensor_tensor(out=ot[:], in0=ot[:],
                                        in1=cm["ropeC_sb"][:], op=ALU.mult)
                        e.drain()
                        last = e.tensor_tensor(out=ot[:], in0=ot[:], in1=z_sb[:], op=ALU.add)
                        e.drain()
                        return last
                    g.on("vector", rope, wait=[(pe, zn)], inc=ve)
                    rope_prev[0] = ve.n
                g.w("gpsimd", se, ve)
                g.on("gpsimd", lambda e, m=m, off=off, ot=ot: e.dma_start(
                    out=sendv[m, off + 0: off + 128 * TOK].rearrange("(p t) -> p t", p=128),
                    in_=ot[:]), inc=qg, inck=16)
            g.run_phase("qkv_" + which)

    if on_qk_done is not None:
        on_qk_done()

    vn = load_w("wv", wait_extra=[pe])
    w_sb = w_sb2[:, 0: DT * 1024]
    g.on("vector", lambda e: e.memset(
        v_own[:].rearrange("p (t f) -> p t f", f=65)[:, :, 64:65], 1.0),
        wait=[(g.sem["pe"], g.sem["pe"].n)], inc=ve)
    with g.ps("qkv_v", [128, 2 * CH], F32) as v_ps:
        g.on("tensor", None, wait=[(wsems[0], vn)])
        g.w("tensor", ve)
        for tt in range(16):
            for c2 in range(2):
                i = tt * 2 + c2
                ysl = v_ps[:, CH * (i % 2): CH * (i % 2 + 1)]
                def vmm(e, tt=tt, c2=c2, ysl=ysl):
                    for d in range(DT):
                        last = e.matmul(ysl, hT[:, TOK * d + 128 * tt: TOK * d + 128 * (tt + 1)],
                                        w_sb[:, 1024 * d + CH * c2: 1024 * d + CH * (c2 + 1)],
                                        start=(d == 0), stop=(d == DT - 1))
                    return last
                g.on("tensor", vmm, wait=[se] if i >= 2 else [], inc=pe)
                st = 1040 * tt + 520 * c2
                def vev(e, st=st, ysl=ysl):
                    last = e.tensor_scalar(
                        out=v_own[:, st: st + 520].rearrange("p (h f) -> p h f", f=65)[:, :, 0:64],
                        in0=ysl.rearrange("p (h f) -> p h f", f=64),
                        scalar1=1.0, scalar2=None, op0=ALU.mult)
                    e.drain()
                    return last
                g.on("vector", vev, wait=[pe], inc=se)
        g.w("sync", se)
        for j in range(NCORE):
            g.on("sync", lambda e, j=j: e.dma_start(
                out=sendv_d[j, 0: SH_V].rearrange("(t p f) -> p t f", p=128, f=130),
                in_=v_own[:].rearrange("p (t f) -> p t f", f=1040)[:, :, 130 * j: 130 * (j + 1)]),
                inc=qs, inck=16)
        g.run_phase("qkv_v")


def emit_flash(g, cm, recvqk, recvv, ret, causal, bg=None, wait_qk=None, wait_v=None):
    nc = g.nc
    qs, qg, pe, ve, se = g.sem["qs"], g.sem["qg"], g.sem["pe"], g.sem["ve"], g.sem["se"]

    qf = g.sb("fl_q", [128, 2 * SEQ], BF16)
    kf = g.sb("fl_k", [128, 2 * SEQ], BF16)
    vf = g.sb("fl_v", [128, 2 * 64 * 130], BF16)
    p_sb = g.sb("fl_p", [128, 3 * 1024], BF16)

    qa = g.sem["qa"]
    first_s = [wait_qk] if wait_qk else []
    first_g = [wait_v] if wait_v else []
    first_a = [(qs, qs.n), (qg, qg.n), (g.sem["pe"], g.sem["pe"].n),
               (g.sem["ve"], g.sem["ve"].n)] + ([wait_qk] if wait_qk else [])
    qf_val, kf_val, vf_val = {}, {}, {}
    for b in range(2):
        for r4 in range(4):
            r = 4 * b + r4
            g.on("sync", lambda e, b=b, r4=r4, r=r: e.dma_start(
                out=qf[:, SEQ * b + TOK * r4: SEQ * b + TOK * (r4 + 1)],
                in_=recvqk[r, 0: SH_Q].rearrange("(p t) -> p t", p=128)),
                wait=first_s, inc=qs, inck=16)
            first_s = []
            qf_val[(b, r4)] = qs.n
            g.on("scalar", lambda e, b=b, r4=r4, r=r: e.dma_start(
                out=kf[:, SEQ * b + TOK * r4: SEQ * b + TOK * (r4 + 1)],
                in_=recvqk[r, SH_Q: SH_Q + SH_K].rearrange("(p t) -> p t", p=128)),
                wait=first_a, inc=qa, inck=16)
            first_a = []
            kf_val[(b, r4)] = qa.n
            g.on("gpsimd", lambda e, b=b, r4=r4, r=r: e.dma_start(
                out=vf[:].rearrange("p (t f) -> p t f", f=130)[:, 64 * b + 16 * r4: 64 * b + 16 * (r4 + 1), :],
                in_=recvv[r, 0: SH_V].rearrange("(t p f) -> p t f", p=128, f=130)),
                wait=first_g, inc=qg, inck=16)
            first_g = []
            vf_val[(b, r4)] = qg.n
    g.run_phase("fl_load")
    bw_se, bw_ve = g.sem["se"].n, g.sem["ve"].n

    with g.ps("fl_s", [128, 2048], F32) as s_ps, \
         g.ps("fl_o", [65, 2048], F32) as o_ps:
        oraw = g.sb("fl_oraw", [65, 2 * 2048], BF16)
        bcB2 = g.sb("fl_bcB2", [64, 2048], BF16)
        o_sb2 = g.sb("fl_osb2", [64, 2048], BF16)
        dscr = g.nc.dram_tensor(g.prefix + "fl_dscr", [2, 2048], BF16).ap()
        n0, n1 = g.sem["n0"], g.sem["n1"]
        nrm_ve = [g.sem["ve"].n]
        oev_se = [g.sem["se"].n]
        otts_done = {}
        send_done = {}
        qcidx = [0]
        prev_exp = None
        g.w("tensor", ve)
        kf_seen, vf_seen = set(), set()
        for b in range(2):
            for qc in range(8):
                units = []
                for qh in range(2):
                    nk = (8 * qc + 4 * (qh + 1)) if causal else 64
                    for kt in range(nk):
                        units.append((qh, kt, kt == 0, kt == nk - 1))
                NU = len(units)
                exp_done = {}
                s_done = {}

                def emit_av(u):
                    qh, kt, first, last = units[u]
                    psl = p_sb[:, 1024 * (u % 3): 1024 * (u % 3 + 1)]
                    diag = causal and kt >= 8 * qc + 4 * qh
                    if diag:
                        off = 896 - 128 * (kt - 8 * qc) + 512 * qh
                        def dmask(e, psl=psl, off=off):
                            e.tensor_tensor(out=psl[:, 0:512], in0=psl[:, 0:512],
                                            in1=cm["Tbase_sb"][:, off: off + 512], op=ALU.mult)
                            e.drain()
                            last_i = e.tensor_tensor(out=psl[:, 512:1024], in0=psl[:, 512:1024],
                                                     in1=cm["Tbase_sb"][:, off: off + 512], op=ALU.mult)
                            e.drain()
                            return last_i
                        g.on("vector", dmask, wait=[(se, exp_done[u])], inc=ve)
                        avwait = [(ve, ve.n)]
                    else:
                        avwait = [(se, exp_done[u])]
                    if first:
                        avwait.append((se, oev_se[0]))
                    if (b, kt // 16) not in vf_seen:
                        vf_seen.add((b, kt // 16))
                        avwait.append((qg, vf_val[(b, kt // 16)]))
                    def avmm(e, kt=kt, qh=qh, psl=psl, first=first, last=last, b=b):
                        for h in range(2):
                            last_i = e.matmul(
                                o_ps[:, 1024 * h + 512 * qh: 1024 * h + 512 * (qh + 1)],
                                vf[:, 8320 * b + 130 * kt + 65 * h: 8320 * b + 130 * kt + 65 * (h + 1)],
                                psl[:, 512 * h: 512 * (h + 1)],
                                start=first, stop=last)
                        return last_i
                    g.on("tensor", avmm, wait=avwait, inc=pe)

                for u, (qh, kt, first, last) in enumerate(units):
                    par = u % 2
                    ssl = s_ps[:, 1024 * par: 1024 * (par + 1)]
                    q0 = SEQ * b + 1024 * qc + 512 * qh
                    def smm(e, b=b, kt=kt, q0=q0, ssl=ssl):
                        for h, tp in ((0, (0, 0)), (1, (64, 0))):
                            base = 64 * h
                            last_i = e.matmul(
                                ssl[:, 512 * h: 512 * (h + 1)],
                                kf[base:base + 64, SEQ * b + 128 * kt: SEQ * b + 128 * (kt + 1)],
                                qf[base:base + 64, q0: q0 + 512],
                                start=True, stop=True, tile_position=tp)
                        return last_i
                    if u >= 2:
                        swait = [(se, exp_done[u - 2])]
                    elif prev_exp is None:
                        swait = [(se, bw_se), (ve, bw_ve)]
                    else:
                        swait = [(se, prev_exp[u])]
                    if u < 2:
                        swait.append((qs, qf_val[(b, qc // 2)]))
                    if (b, kt // 16) not in kf_seen:
                        kf_seen.add((b, kt // 16))
                        swait.append((qa, kf_val[(b, kt // 16)]))
                    g.on("tensor", smm, wait=swait, inc=pe)
                    s_done[u] = pe.n
                    psl = p_sb[:, 1024 * (u % 3): 1024 * (u % 3 + 1)]
                    g.on("scalar", lambda e, psl=psl, ssl=ssl: e.activation(
                        out=psl, in_=ssl, func=AF.Exp, scale=ATTN_SCALE),
                        wait=[(pe, s_done[u])], inc=se)
                    exp_done[u] = se.n
                    if u >= 1:
                        emit_av(u - 1)
                emit_av(NU - 1)
                prev_exp = (exp_done[NU - 2], exp_done[NU - 1])

                qci = qcidx[0]
                qcidx[0] += 1
                slot = qci % 2
                osl = oraw[:, 2048 * slot: 2048 * (slot + 1)]
                bsl = bcB2[:]
                csl = o_sb2[:]
                oevw = [(pe, pe.n)]
                if qci >= 2:
                    oevw.append((ve, otts_done[qci - 2]))
                g.on("scalar", lambda e, osl=osl: e.copy(out=osl, in_=o_ps[:]),
                     wait=oevw, inc=se)
                oev_se[0] = se.n
                def recips(e, osl=osl):
                    with g.nc.allow_low_precision(reason="softmax denom recip in bf16; denominator sums bf16 terms"):
                        for h in range(2):
                            e.reciprocal(out=osl[64:65, 1024 * h: 1024 * (h + 1)],
                                         in_=osl[64:65, 1024 * h: 1024 * (h + 1)])
                            e.drain()
                    return e.nop()
                g.on("vector", recips, wait=[(se, oev_se[0])], inc=ve)
                rec_ve = ve.n
                g.on("gpsimd", lambda e, slot=slot, osl=osl: e.dma_start(
                    out=dscr[slot: slot + 1, :], in_=osl[64:65, :]),
                    wait=[(ve, rec_ve)], inc=n0, inck=16)
                g.on("gpsimd", lambda e, slot=slot, bsl=bsl: e.dma_start(
                    out=bsl, in_=dscr[slot: slot + 1, :].partition_broadcast(64)),
                    wait=[(n0, g.sem["n0"].n)], inc=n1, inck=16)
                bc_n1 = n1.n
                ottw = [(n1, bc_n1)]
                if qci >= 1:
                    ottw.append((qg, send_done[qci - 1]))
                def otts(e, osl=osl, bsl=bsl, csl=csl):
                    last_i = e.tensor_tensor(out=csl, in0=osl[0:64, :], in1=bsl, op=ALU.mult)
                    e.drain()
                    return last_i
                g.on("vector", otts, wait=ottw, inc=ve)
                otts_done[qci] = ve.n
                nrm_ve[0] = ve.n
                rdest = 4 * b + qc // 2
                col = 1024 * (qc % 2)
                for h in range(2):
                    g.on("gpsimd", lambda e, rdest=rdest, h=h, col=col, csl=csl: e.dma_start(
                        out=ret.rearrange("r (p t) -> r p t", p=128)[rdest, 64 * h: 64 * (h + 1), col: col + 1024],
                        in_=csl[:, 1024 * h: 1024 * (h + 1)]),
                        wait=[(ve, otts_done[qci])], inc=qg, inck=16)
                send_done[qci] = qg.n
                if bg is not None:
                    for _ in range(11):
                        if next(bg, "END") == "END":
                            bg = None
                            break
            g.run_phase("fl_b%d" % b)
        if bg is not None:
            for _ in iter(lambda: next(bg, "END"), "END"):
                pass
        g.w("gpsimd", qg)
        g.run_phase("fl_end")


def emit_outproj(g, cm, xT, ret_recv, wo_bf, bo_bf, w_wait=()):
    nc = g.nc
    qs, pe, ve, se = g.sem["qs"], g.sem["pe"], g.sem["ve"], g.sem["se"]
    if w_wait:
        g.on("sync", None, wait=list(w_wait))
    at_sb = g.sb("pr_at", [128, DT * TOK], BF16)
    wo_sb = g.sb("pr_wo", [128, DT * 1024], BF16)
    boh_sb = g.sb("pr_bo", [1, 1024], BF16)
    for r in range(DT):
        g.on("sync", lambda e, r=r: e.dma_start(
            out=at_sb[:, TOK * r: TOK * (r + 1)],
            in_=ret_recv[r, :].rearrange("(p t) -> p t", p=128)),
            wait=[g.sem["qg"]], inc=qs, inck=16)
    g.on("sync", lambda e: e.dma_start(
        out=wo_sb[:].rearrange("p (a c) -> p a c", a=DT),
        in_=wo_bf.rearrange("(a p) c -> p a c", p=128)), wait=[g.sem["qg"]], inc=qs, inck=16)
    g.on("sync", lambda e: e.dma_start(out=boh_sb[:], in_=bo_bf),
         wait=[g.sem["qg"]], inc=qs, inck=16)
    with g.ps("pr_u", [128, 2 * CH], F32) as u_ps:
        g.w("tensor", qs)
        evacs = []
        for i, (dcol, ch) in enumerate([(dc, c) for dc in range(DT) for c in range(NCH)]):
            ysl = u_ps[:, CH * (i % 2): CH * (i % 2 + 1)]
            def mm(e, dcol=dcol, ch=ch, ysl=ysl):
                e.matmul(ysl, boh_sb[:, 128 * dcol: 128 * (dcol + 1)],
                         cm["onesrow_bf"][:], start=True, stop=False)
                for it in range(DT):
                    last = e.matmul(ysl, wo_sb[:, 1024 * it + 128 * dcol: 1024 * it + 128 * (dcol + 1)],
                                    at_sb[:, TOK * it + CH * ch: TOK * it + CH * (ch + 1)],
                                    start=False, stop=(it == DT - 1))
                return last
            g.on("tensor", mm, wait=[(ve, evacs[i - 2])] if i >= 2 else [], inc=pe)
            def evac(e, dcol=dcol, ch=ch, ysl=ysl):
                xsl = xT[:, TOK * dcol + CH * ch: TOK * dcol + CH * (ch + 1)]
                return e.tensor_tensor(out=xsl, in0=xsl, in1=ysl, op=ALU.add)
            g.on("vector", evac, wait=[pe], inc=ve)
            evacs.append(g.sem["ve"].n)
        g.run_phase("outproj")


def build_qkv(layer):
    nc = bass.Bass("TRN2", target_bir_lowering=False)
    x_in = decl_in(nc, "x", [D, TOK])
    rope_on = layer == 0
    cnames = ("PT", "ropeC", "ropeS") if rope_on else ()
    cd = const_decls_subset(nc, cnames)
    send = decl_out(nc, "send", [NCORE, SHARD], BF16)
    w_bf = {k: nc.dram_tensor("w_" + k, [D, D], BF16).ap() for k in ("wq", "wk", "wv")}
    if layer == 0:
        wq = decl_in(nc, "Wq", [D, D])
        wkv = decl_in(nc, "Wkv", [D, 2 * D])
        casts = [(w_bf["wq"], wq), (w_bf["wk"], wkv[:, 0:D]), (w_bf["wv"], wkv[:, D:2 * D])]
    else:
        casts = [(w_bf[k], decl_in(nc, n, [D, D]))
                 for k, n in (("wq", "Wq"), ("wk", "Wk"), ("wv", "Wv"))]
    with ExitStack() as ctx:
        g = G(nc, ctx)
        cm = alloc_common(g, cd, cnames)
        xT = g.sb("xT", [128, DT * TOK], F32)
        hT = g.sb("at_hT", [128, DT * TOK], BF16)
        load_xT(g, xT, x_in)
        emit_casts(g, casts)
        g.w("tensor", g.sem["qs"])
        g.w("scalar", g.sem["qs"])
        g.w("vector", g.sem["qs"])
        g.run_phase("load")
        vown = g.sb("bq_vown", [128, 16 * 1040], BF16)
        ln = alloc_ln(g, sq=vown[:, 0: DT * CH])
        emit_ln_phase(g, cm, ln, xT, hT, name="at_ln")
        emit_qkv(g, cm, xT, hT, w_bf, send[:, 0: SH_Q + SH_K],
                 send[:, SH_Q + SH_K: SHARD], rope_on, v_own=vown)
    return nc


def build_flash(causal):
    nc = bass.Bass("TRN2", target_bir_lowering=False)
    cnames = ("Tbase",) if causal else ()
    cd = const_decls_subset(nc, cnames)
    recv = decl_in(nc, "recv", [NCORE, SHARD], BF16)
    ret = decl_out(nc, "ret", [NCORE, RET], BF16)
    with ExitStack() as ctx:
        g = G(nc, ctx)
        cm = alloc_common(g, cd, cnames)
        g.run_phase("load")
        emit_flash(g, cm, recv[:, 0: SH_Q + SH_K], recv[:, SH_Q + SH_K: SHARD],
                   ret, causal)
    return nc


def build_proj():
    nc = bass.Bass("TRN2", target_bir_lowering=False)
    x_in = decl_in(nc, "x", [D, TOK])
    ret_recv = decl_in(nc, "ret_recv", [NCORE, RET], BF16)
    wo = decl_in(nc, "Wo", [D, D])
    bo = decl_in(nc, "bo", [1, D])
    x_out = decl_out(nc, "x_out", [D, TOK])
    wo_bf = nc.dram_tensor("wo_bf", [D, D], BF16).ap()
    bo_bf = nc.dram_tensor("bo_bf", [1, D], BF16).ap()
    with ExitStack() as ctx:
        g = G(nc, ctx)
        cm = alloc_common(g, cd := {}, ())
        xT = g.sb("xT", [128, DT * TOK], F32)
        load_xT(g, xT, x_in)
        emit_casts(g, [(wo_bf, wo), (bo_bf, bo)])
        g.w("tensor", g.sem["qs"])
        g.w("vector", g.sem["qs"])
        g.run_phase("load")
        emit_outproj(g, cm, xT, ret_recv, wo_bf, bo_bf)
        store_xT(g, xT, x_out)
    return nc


def run_attn_subblock(xs_parts, layer, wdict):
    consts = get_consts()
    nc1 = build_qkv(layer)
    cnames = ("PT", "ropeC", "ropeS") if layer == 0 else ()
    im = []
    for c in range(NCORE):
        m = {"x": xs_parts[c], **{k: consts[c][k] for k in cnames}}
        if layer == 0:
            m["Wq"] = wdict["Wq"]
            m["Wkv"] = wdict["Wkv"]
        else:
            m["Wq"], m["Wk"], m["Wv"] = wdict["Wq"], wdict["Wk"], wdict["Wv"]
        im.append(m)
    res = run_bass_kernel_spmd(nc1, im, core_ids=list(range(NCORE)))
    sends = [np.asarray(res.results[c]["send"]) for c in range(NCORE)]
    recvs = [np.ascontiguousarray(np.stack([sends[r][c] for r in range(NCORE)]))
             for c in range(NCORE)]
    nc2 = build_flash(causal=(layer == 1))
    cnames2 = ("Tbase",) if layer == 1 else ()
    im = [{"recv": recvs[c], **{k: consts[c][k] for k in cnames2}} for c in range(NCORE)]
    res = run_bass_kernel_spmd(nc2, im, core_ids=list(range(NCORE)))
    rets = [np.asarray(res.results[c]["ret"]) for c in range(NCORE)]
    ret_recvs = [np.ascontiguousarray(np.stack([rets[r][c] for r in range(NCORE)]))
                 for c in range(NCORE)]
    nc3 = build_proj()
    wo = wdict["Wout"] if layer == 0 else wdict["Wo"]
    bo = wdict["bout"] if layer == 0 else wdict["bo"]
    im = [{"x": xs_parts[c], "ret_recv": ret_recvs[c], "Wo": wo,
           "bo": bo.reshape(1, D)} for c in range(NCORE)]
    res = run_bass_kernel_spmd(nc3, im, core_ids=list(range(NCORE)))
    return [np.asarray(res.results[c]["x_out"]) for c in range(NCORE)]


def alloc_quant(g):
    return {"wt": g.sb("qz_wt", [128, 2 * 1024], F32),
            "racc": g.sb("qz_racc", [128, 32], F32),
            "r1": g.sb("qz_r1", [128, 1], F32),
            "rrow": g.sb("qz_rrow", [1, 128], F32),
            "tt1": g.sb("qz_t1", [1, 1], F32),
            "rB": g.sb("qz_rB", [128, 1], F32),
            "a": g.sb("qz_a", [128, 1024], BF16),
            "q": g.sb("qz_q", [128, 2 * 1024], BF16)}


def emit_quant(g, cm, w_in, wq_out, kind, scale_half, name, scr, bufs):
    qs, ve = g.sem["qs"], g.sem["ve"]
    wp = (g.sem["wa"], g.sem["wb"])
    sp = (g.sem["sc"], g.sem["sd"])
    wt, racc = bufs["wt"], bufs["racc"]
    r1, rrow, tt1, rB = bufs["r1"], bufs["rrow"], bufs["tt1"], bufs["rB"]
    a_sb, q_sb = bufs["a"], bufs["q"]
    NG = 32

    def src(dram, i):
        if kind == "w1":
            return dram[128 * i: 128 * (i + 1), :]
        r, c = i // 4, i % 4
        return dram[128 * r: 128 * (r + 1), 1024 * c: 1024 * (c + 1)]

    def wview(par):
        return wt[:, 1024 * par: 1024 * (par + 1)]

    def qview(par):
        return q_sb[:, 1024 * par: 1024 * (par + 1)]

    red_done = {}
    for i in range(NG):
        par = i % 2
        g.on("sync", lambda e, i=i, par=par: e.dma_start(
            out=wview(par), in_=src(w_in, i)),
            wait=[(ve, red_done[i - 2])] if i >= 2 else [ve], inc=wp[par], inck=16)
        ld = wp[par].n
        g.on("vector", lambda e, i=i, par=par: e.tensor_reduce(
            out=racc[:, i: i + 1], in_=wview(par),
            axis=AX.X, op=ALU.add, apply_absolute_value=True),
            wait=[(wp[par], ld)], inc=ve)
        red_done[i] = ve.n
        yield

    qa, gp = g.sem["qa"], g.sem["gp"]
    g.on("vector", lambda e: e.tensor_reduce(out=r1[:], in_=racc[:, 0:NG],
                                             axis=AX.X, op=ALU.add),
         wait=[(ve, red_done[NG - 1])], inc=ve)
    g.on("sync", lambda e: e.dma_start(out=scr.rearrange("a b -> b a"), in_=r1[:]),
         wait=[ve], inc=qa, inck=16)
    g.on("sync", lambda e: e.dma_start(out=rrow[:], in_=scr), wait=[qa], inc=gp, inck=16)
    def gam(e):
        e.tensor_reduce(out=tt1[:], in_=rrow[:], axis=AX.X, op=ALU.add)
        e.drain()
        e.tensor_scalar(out=tt1[:], in0=tt1[:], scalar1=1.0 / (4096.0 * 1024.0),
                        scalar2=QEPS, op0=ALU.mult, op1=ALU.add)
        e.drain()
        return e.reciprocal(out=tt1[:], in_=tt1[:])
    g.on("vector", gam, wait=[gp], inc=ve)
    g.on("sync", lambda e: e.dma_start(out=scr[0:1, 0:1], in_=tt1[:]),
         wait=[ve], inc=qa, inck=16)
    g.on("sync", lambda e: e.dma_start(out=rB[:], in_=scr[0:1, 0:1].partition_broadcast(128)),
         wait=[qa], inc=gp, inck=16)
    rB_ready = gp.n
    yield

    qz_done = {}
    st_done = {}
    for i in range(NG):
        par = i % 2
        g.on("sync", lambda e, i=i, par=par: e.dma_start(
            out=wview(par), in_=src(w_in, i)),
            wait=[(ve, qz_done[i - 2])] if i >= 2 else [(ve, red_done[NG - 1])],
            inc=wp[par], inck=16)
        ld = wp[par].n
        def qz(e, i=i, par=par):
            w = wview(par)
            qsl = qview(par)
            e.tensor_scalar(out=a_sb[:], in0=w, scalar1=rB[:], scalar2=0.5,
                            op0=ALU.mult, op1=ALU.is_ge)
            e.drain()
            e.tensor_scalar(out=qsl, in0=w, scalar1=rB[:], scalar2=-0.5,
                            op0=ALU.mult, op1=ALU.is_le)
            e.drain()
            e.tensor_tensor(out=qsl, in0=a_sb[:], in1=qsl, op=ALU.subtract)
            e.drain()
            if scale_half:
                e.tensor_scalar(out=qsl, in0=qsl, scalar1=0.5, scalar2=None, op0=ALU.mult)
                e.drain()
            return e.nop()
        wz = [(wp[par], ld), (g.sem["gp"], rB_ready),
              (sp[par], st_done[i - 2] if i >= 2 else sp[par].n)]
        g.on("vector", qz, wait=wz, inc=ve)
        qz_done[i] = ve.n
        g.on("sync", lambda e, i=i, par=par: e.dma_start(
            out=src(wq_out, i), in_=qview(par)),
            wait=[(ve, qz_done[i])], inc=sp[par], inck=16)
        st_done[i] = sp[par].n
        yield


def emit_ffn(g, cm, xT, ln, wq1, wq2, b1v_sb, b2v_sb, hT_ch, hff, w1T_sb, w2T_sb):
    nc = g.nc
    qs, qg, pe, ve, se = g.sem["qs"], g.sem["qg"], g.sem["pe"], g.sem["ve"], g.sem["se"]
    wpar_sem = (g.sem["wa"], g.sem["wb"], g.sem["sc"])

    for tch2 in range(2):
        emit_ln_phase(g, cm, ln, xT, hT_ch, chunks=[2 * tch2, 2 * tch2 + 1],
                      hT_off=CH * 2 * tch2, name="ffn_ln%d" % tch2)
        if tch2 == 0:
            for d in range(DT):
                g.on("sync", lambda e, d=d: e.dma_start_transpose(
                    out=w1T_sb[:, 4096 * d: 4096 * (d + 1)],
                    in_=wq1[:, 128 * d: 128 * (d + 1)]),
                    wait=[pe, qg] if d == 0 else [], inc=qs, inck=16)
        for half in range(2):
            tch = 2 * tch2 + half
            hoff = CH * half
            with g.ps("ffn_h1_%d" % tch, [128, 2 * CH], F32) as h1_ps:
                if half == 0:
                    g.w("tensor", qs, g.sem["ve"])
                relus = []
                for f in range(32):
                    ysl = h1_ps[:, CH * (f % 2): CH * (f % 2 + 1)]
                    def mm1(e, f=f, ysl=ysl, hoff=hoff):
                        for d in range(DT):
                            last = e.matmul(ysl, w1T_sb[:, 4096 * d + 128 * f: 4096 * d + 128 * (f + 1)],
                                            hT_ch[:, 1024 * d + hoff: 1024 * d + hoff + CH],
                                            start=(d == 0), stop=(d == DT - 1))
                        return last
                    g.on("tensor", mm1, wait=[(se, relus[f - 2])] if f >= 2 else [], inc=pe)
                    g.on("scalar", lambda e, f=f, ysl=ysl: e.activation(
                        out=hff[:, CH * f: CH * (f + 1)], in_=ysl, func=AF.Relu,
                        bias=b1v_sb[:, f: f + 1]),
                        wait=[pe], inc=se)
                    relus.append(se.n)
                relu_last = se.n
                g.run_phase("ffn_m1_%d" % tch)
            with g.ps("ffn_u_%d" % tch, [128, DT * CH], F32) as u_ps:
                mmdone = []
                for f in range(32):
                    par = f % 3
                    wsem = wpar_sem[par]
                    g.on("sync", lambda e, f=f, par=par: e.dma_start_transpose(
                        out=w2T_sb[:, 1024 * par: 1024 * (par + 1)],
                        in_=wq2[:, 128 * f: 128 * (f + 1)]),
                        wait=[(pe, mmdone[f - 3])] if f >= 3 else [pe],
                        inc=wsem, inck=16)
                    def mm2(e, f=f, par=par):
                        for dc in range(DT):
                            last = e.matmul(u_ps[:, CH * dc: CH * (dc + 1)],
                                            w2T_sb[:, 1024 * par + 128 * dc: 1024 * par + 128 * (dc + 1)],
                                            hff[:, CH * f: CH * (f + 1)],
                                            start=(f == 0), stop=(f == 31 and dc == DT - 1))
                        return last
                    g.on("tensor", mm2,
                         wait=[wsem, (se, relu_last)] if f == 0 else [wsem], inc=pe)
                    mmdone.append(pe.n)
                def evac(e, tch=tch):
                    for dc in range(DT):
                        xsl = xT[:, TOK * dc + CH * tch: TOK * dc + CH * (tch + 1)]
                        e.scalar_tensor_tensor(out=xsl, in0=u_ps[:, CH * dc: CH * (dc + 1)],
                                               scalar=b2v_sb[:, dc: dc + 1], in1=xsl,
                                               op0=ALU.add, op1=ALU.add)
                        last = e.drain()
                    return last
                g.on("vector", evac, wait=[pe], inc=ve)
                g.run_phase("ffn_m2_%d" % tch)


def emit_w2t_pass(g, wq2, wq2t, tx=None):
    wp = (g.sem["wa"], g.sem["wb"])
    sp = (g.sem["sc"], g.sem["sd"])
    if tx is None:
        tx = g.sb("w2tp_tx", [128, 1024], BF16)
    st_done = {}
    for f in range(32):
        par = f % 2
        g.on("sync", lambda e, f=f: e.dma_start_transpose(
            out=tx[:], in_=wq2[:, 128 * f: 128 * (f + 1)]),
            wait=[(sp[0], st_done[f - 1])] if f >= 1 else [(sp[0], sp[0].n), (sp[1], sp[1].n)],
            inc=wp[par], inck=16)
        ld = wp[par].n
        g.on("sync", lambda e, f=f: e.dma_start(
            out=wq2t[128 * f: 128 * (f + 1), :], in_=tx[:]),
            wait=[(wp[par], ld)], inc=sp[0], inck=16)
        st_done[f] = sp[0].n
    g.run_phase("w2tp")


def build_quant():
    nc = bass.Bass("TRN2", target_bir_lowering=False)
    w1 = decl_in(nc, "W1", [DFF, D])
    w2 = decl_in(nc, "W2", [D, DFF])
    wq1 = decl_out(nc, "wq1", [DFF, D], BF16)
    wq2 = decl_out(nc, "wq2", [D, DFF], BF16)
    scr1 = nc.dram_tensor("scr1", [1, 128], F32).ap()
    scr2 = nc.dram_tensor("scr2", [1, 128], F32).ap()
    with ExitStack() as ctx:
        g = G(nc, ctx)
        cm = alloc_common(g, {}, ())
        g.run_phase("init")
        qb = alloc_quant(g)
        for _ in emit_quant(g, cm, w1, wq1, "w1", False, "q1", scr1, qb):
            pass
        for _ in emit_quant(g, cm, w2, wq2, "w2", True, "q2", scr2, qb):
            pass
        g.w("gpsimd", g.sem["qs"], g.sem["ve"])
        g.run_phase("qdone")
    return nc


def build_ffn():
    nc = bass.Bass("TRN2", target_bir_lowering=False)
    x_in = decl_in(nc, "x", [D, TOK])
    wq1 = decl_in(nc, "wq1", [DFF, D], BF16)
    wq2 = decl_in(nc, "wq2", [D, DFF], BF16)
    b1 = decl_in(nc, "b1", [1, DFF])
    b2 = decl_in(nc, "b2", [1, D])
    x_out = decl_out(nc, "x_out", [D, TOK])
    wq2t = nc.dram_tensor("i_wq2t", [DFF, D], BF16).ap()
    with ExitStack() as ctx:
        g = G(nc, ctx)
        cm = alloc_common(g, {}, ())
        xT = g.sb("xT", [128, DT * TOK], F32)
        b1h_sb = g.sb("ffn_b1h", [1, DFF], BF16)
        b2h_sb = g.sb("ffn_b2h", [1, D], BF16)
        load_xT(g, xT, x_in)
        g.on("gpsimd", lambda e: e.dma_start(out=b1h_sb[:], in_=b1), inc=g.sem["qg"], inck=16)
        g.on("gpsimd", lambda e: e.dma_start(out=b2h_sb[:], in_=b2), inc=g.sem["qg"], inck=16)
        g.w("vector", g.sem["qg"])
        g.on("vector", lambda e: e.tensor_scalar(out=b2h_sb[:], in0=b2h_sb[:], scalar1=0.5,
                                                 scalar2=None, op0=ALU.mult), inc=g.sem["ve"])
        g.w("tensor", g.sem["qs"])
        g.w("scalar", g.sem["qs"])
        g.run_phase("load")
        ln = alloc_ln(g)
        hT_ch = ln["xbf"]
        hff = g.sb("ffn_hff", [128, 32 * CH], BF16)
        w1T_sb = g.sb("ffn_w1T", [128, DT * 4096], BF16)
        w2T_sb = g.sb("ffn_w2T", [128, 3 * 1024], BF16)
        emit_ffn(g, cm, xT, ln, wq1, wq2, b1h_sb, b2h_sb, hT_ch, hff, w1T_sb, w2T_sb)
        store_xT(g, xT, x_out)
    return nc


def run_ffn_subblock(xs_parts, wq1, wq2, b1, b2):
    nc = build_ffn()
    im = [{"x": xs_parts[c], "wq1": wq1, "wq2": wq2,
           "b1": b1.reshape(1, DFF), "b2": b2.reshape(1, D)} for c in range(NCORE)]
    res = run_bass_kernel_spmd(nc, im, core_ids=list(range(NCORE)))
    return [np.asarray(res.results[c]["x_out"]) for c in range(NCORE)]


def run_quant(W1, W2):
    nc = build_quant()
    im = [{"W1": W1, "W2": W2} for _ in range(NCORE)]
    res = run_bass_kernel_spmd(nc, im, core_ids=list(range(NCORE)))
    return np.asarray(res.results[0]["wq1"]), np.asarray(res.results[0]["wq2"])


def shard_x(x):
    xs = x.reshape(NCORE, TOK, D)
    return [np.ascontiguousarray(xs[c].T) for c in range(NCORE)]


def unshard_x(parts):
    x = np.stack([np.asarray(p).T for p in parts], axis=0)
    return np.ascontiguousarray(x.reshape(2, SEQ, D))


_CONSTS = None


def get_consts():
    global _CONSTS
    if _CONSTS is None:
        _CONSTS = [host_consts(c) for c in range(NCORE)]
    return _CONSTS


def run_moe_subblock(xs_parts, wg, we, be):
    consts = get_consts()
    names = ("id8", "id128", "id1", "wsel")
    nc1 = build_moe("out")
    im = [{"x": xs_parts[c], "wg": wg, **{k: consts[c][k] for k in names}}
          for c in range(NCORE)]
    res = run_bass_kernel_spmd(nc1, im, core_ids=list(range(NCORE)))
    tots = np.concatenate([np.asarray(res.results[c]["tot"]).reshape(1, 8)
                           for c in range(NCORE)], axis=0)
    tots = np.ascontiguousarray(tots)
    nc2 = build_moe("ext")
    im = [{"x": xs_parts[c], "wg": wg, "we": we, "be": be.reshape(1, D),
           "tot_all": tots, **{k: consts[c][k] for k in names}}
          for c in range(NCORE)]
    res = run_bass_kernel_spmd(nc2, im, core_ids=list(range(NCORE)))
    return [np.asarray(res.results[c]["x_out"]) for c in range(NCORE)]




def build_fused():
    nc = bass.Bass("TRN2", target_bir_lowering=False)
    cd = const_decls_subset(nc, tuple(CONST_SPECS))
    x_in = decl_in(nc, "x", [D, TOK])
    x_out = decl_out(nc, "x_out", [D, TOK])
    win = {}
    for nm, shape in (("L0_ffn_Wg", [D, 8]), ("L0_ffn_We", [D, D]), ("L0_ffn_be", [1, D]),
                      ("L0_attn_Wq", [D, D]), ("L0_attn_Wkv", [D, 2 * D]),
                      ("L0_attn_Wout", [D, D]), ("L0_attn_bout", [1, D]),
                      ("L1_attn_Wq", [D, D]), ("L1_attn_Wk", [D, D]), ("L1_attn_Wv", [D, D]),
                      ("L1_attn_Wo", [D, D]), ("L1_attn_bo", [1, D]),
                      ("L1_ffn_W1", [DFF, D]), ("L1_ffn_b1", [1, DFF]),
                      ("L1_ffn_W2", [D, DFF]), ("L1_ffn_b2", [1, D])):
        win[nm] = decl_in(nc, nm, shape)
    dram = lambda nm, shape, dt=BF16: nc.dram_tensor(nm, shape, dt).ap()
    we_bf = dram("i_we", [D, D]); beh = dram("i_beh", [1, D])
    a_w = {0: {}, 1: {}}
    for k in ("wq", "wk", "wv"):
        a_w[0][k] = dram("i_a0_" + k, [D, D])
        a_w[1][k] = dram("i_a1_" + k, [D, D])
    wo_bf = {0: dram("i_wo0", [D, D]), 1: dram("i_wo1", [D, D])}
    bo_bf = {0: dram("i_bo0", [1, D]), 1: dram("i_bo1", [1, D])}
    wq1 = dram("i_wq1", [DFF, D]); wq2 = dram("i_wq2", [D, DFF])
    wq2t = dram("i_wq2t", [DFF, D])
    exch = {}
    for L in (0, 1):
        exch[L] = {
            "sendqk": dram("i_sendqk%d" % L, [NCORE, SH_Q + SH_K]),
            "recvqk": dram("i_recvqk%d" % L, [NCORE, SH_Q + SH_K]),
            "sendv": dram("i_sendv%d" % L, [NCORE, SH_V]),
            "recvv": dram("i_recvv%d" % L, [NCORE, SH_V]),
            "rsend": dram("i_rsend%d" % L, [NCORE, RET]),
            "rrecv": dram("i_rrecv%d" % L, [NCORE, RET]),
        }
    scrq = {1: nc.dram_tensor("i_scr1", [1, 128], F32).ap(),
            2: nc.dram_tensor("i_scr2", [1, 128], F32).ap()}
    tots = {}
    for i in (0, 1):
        tots[i] = {"tot_in": nc.dram_tensor("i_totin%d" % i, [1, 8], F32).ap(),
                   "tot_all": nc.dram_tensor("i_totall%d" % i, [8, 8], F32).ap()}

    groups8 = [list(range(NCORE))]

    with ExitStack() as ctx:
        g = G(nc, ctx)
        qs, qg, cc, pe, ve, se = (g.sem[k] for k in ("qs", "qg", "cc", "pe", "ve", "se"))
        cm = alloc_common(g, cd, ("id8", "id128", "id1", "wsel"))
        xT = g.sb("xT", [128, DT * TOK], F32)

        def scope_consts(names):
            cm2 = dict(cm)
            for n in names:
                shape, dt = CONST_SPECS[n]
                t = g.sb(n + "_sb", shape, dt)
                cm2[n + "_sb"] = t
                g.on("sync", lambda e, t=t, ap=cd[n]: e.dma_start(out=t[:], in_=ap),
                     inc=qs, inck=16)
            return cm2
        load_xT(g, xT, x_in)
        emit_casts(g, [(we_bf, win["L0_ffn_We"]), (beh, win["L0_ffn_be"])])
        wc_sem = g.sem["sd"]
        wc_state = {}
        def big_casts():
            emit_casts(g, [
                (a_w[0]["wq"], win["L0_attn_Wq"]),
                (a_w[0]["wk"], win["L0_attn_Wkv"][:, 0:D]),
                (a_w[0]["wv"], win["L0_attn_Wkv"][:, D:2 * D]),
            ], sem=wc_sem)
            wc_state[0] = wc_sem.n
        def rest_casts():
            emit_casts(g, [
                (wo_bf[0], win["L0_attn_Wout"]), (bo_bf[0], win["L0_attn_bout"]),
                (a_w[1]["wq"], win["L1_attn_Wq"]), (a_w[1]["wk"], win["L1_attn_Wk"]),
                (a_w[1]["wv"], win["L1_attn_Wv"]),
                (wo_bf[1], win["L1_attn_Wo"]), (bo_bf[1], win["L1_attn_bo"]),
            ], sem=wc_sem)
            wc_state[1] = wc_sem.n
        g.w("tensor", qs)
        g.w("scalar", qs)
        g.w("vector", qs)
        g.run_phase("load")

        def moe(i):
            with g.subscope("m%d_" % i):
                emit_moe_phase(g, cm, xT, {"wg": win["L0_ffn_Wg"], "we_bf": we_bf,
                                           "beh": beh}, "cc", tots[i],
                               after_loads=(big_casts if i == 0 else None))

        def attn(L):
            ex = exch[L]
            ccval = {}
            def qk_hook(ex=ex):
                g.w("gpsimd", qg)
                def ccqk(e, ex=ex):
                    with allow_ici_all_to_all():
                        return e.collective_compute(
                            "AllToAll", ALU.bypass, replica_groups=groups8,
                            ins=[ex["sendqk"]], outs=[ex["recvqk"]])
                g.on("gpsimd", ccqk, inc=cc)
                ccval["qk"] = cc.n
            with g.subscope("a%dq_" % L):
                cm2 = scope_consts(("PT", "ropeC", "ropeS") if L == 0 else ())
                hT = g.sb("hT", [128, DT * TOK], BF16)
                v_own = g.sb("vown", [128, 16 * 1040], BF16)
                ln = alloc_ln(g, sq=v_own[:, 0: DT * CH])
                emit_ln_phase(g, cm2, ln, xT, hT, name="ln")
                emit_qkv(g, cm2, xT, hT, a_w[L], ex["sendqk"], ex["sendv"],
                         rope_on=(L == 0), v_own=v_own, on_qk_done=qk_hook,
                         w_wait=[(wc_sem, wc_state[min(L, max(wc_state))])])
            def ccv(e, ex=ex):
                with allow_ici_all_to_all():
                    return e.collective_compute(
                        "AllToAll", ALU.bypass, replica_groups=groups8,
                        ins=[ex["sendv"]], outs=[ex["recvv"]])
            g.on("gpsimd", ccv, wait=[(qs, qs.n)], inc=cc)
            ccval["v"] = cc.n
            g.run_phase("a%d_cc1" % L)
            with g.subscope("a%df_" % L):
                cm2 = scope_consts(("Tbase",) if L == 1 else ())
                bg = None
                if L == 0:
                    import itertools
                    qb = alloc_quant(g)
                    bg = itertools.chain(
                        emit_quant(g, cm, win["L1_ffn_W1"], wq1, "w1",
                                   False, "q1", scrq[1], qb),
                        emit_quant(g, cm, win["L1_ffn_W2"], wq2, "w2",
                                   True, "q2", scrq[2], qb))
                emit_flash(g, cm2, ex["recvqk"], ex["recvv"], ex["rsend"],
                           causal=(L == 1), bg=bg,
                           wait_qk=(cc, ccval["qk"]), wait_v=(cc, ccval["v"]))
            g.w("gpsimd", qg)
            def cc2(e, ex=ex):
                with allow_ici_all_to_all():
                    return e.collective_compute(
                        "AllToAll", ALU.bypass, replica_groups=groups8,
                        ins=[ex["rsend"]], outs=[ex["rrecv"]])
            g.on("gpsimd", cc2, inc=cc)
            g.run_phase("a%d_cc2" % L)
            with g.subscope("a%dp_" % L):
                g.w("sync", cc)
                emit_outproj(g, cm, xT, ex["rrecv"], wo_bf[L], bo_bf[L],
                             w_wait=[(wc_sem, wc_state[1])])

        def ffn(i):
            with g.subscope("f%d_" % i):
                b1v_sb = g.sb("b1v", [128, 32], F32)
                b2v_sb = g.sb("b2v", [128, 8], F32)
                for f in range(32):
                    g.on("sync", lambda e, f=f: e.dma_start(
                        out=b1v_sb[:, f: f + 1],
                        in_=win["L1_ffn_b1"].rearrange("a (f p) -> (a f) p", p=128)[f: f + 1]
                            .rearrange("f p -> p f")),
                        inc=qs, inck=16)
                for f in range(DT):
                    g.on("sync", lambda e, f=f: e.dma_start(
                        out=b2v_sb[:, f: f + 1],
                        in_=win["L1_ffn_b2"].rearrange("a (f p) -> (a f) p", p=128)[f: f + 1]
                            .rearrange("f p -> p f")),
                        inc=qs, inck=16)
                g.w("vector", qs)
                g.on("vector", lambda e: e.tensor_scalar(
                    out=b2v_sb[:], in0=b2v_sb[:], scalar1=0.5, scalar2=None, op0=ALU.mult),
                    inc=ve)
                g.run_phase("fb")
                hff = g.sb("hff", [128, 32 * CH], BF16)
                ln = alloc_ln(g, sq=hff[:, 0:DT * CH], xbf=hff[:, DT * CH: 2 * DT * CH])
                hT_ch = g.sb("hTch", [128, DT * 2 * CH], BF16)
                w1T_sb = g.sb("w1T", [128, DT * 4096], BF16)
                w2T_sb = g.sb("w2T", [128, 3 * 1024], BF16)
                emit_ffn(g, cm, xT, ln, wq1, wq2, b1v_sb, b2v_sb, hT_ch, hff,
                         w1T_sb, w2T_sb)

        moe(0)
        rest_casts()
        attn(0)
        moe(1)
        ffn(0)
        attn(1)
        ffn(1)
        store_xT(g, xT, x_out)
    return nc


_FUSED_NC = None


def kernel_fused(**inputs):
    global _FUSED_NC
    f32 = lambda k: np.ascontiguousarray(np.asarray(inputs[k], np.float32))
    x = f32("L0_attn_queries")
    xs = shard_x(x)
    consts = get_consts()
    if _FUSED_NC is None:
        _FUSED_NC = build_fused()
    wkeys = ["L0_ffn_Wg", "L0_ffn_We", "L0_attn_Wq", "L0_attn_Wkv", "L0_attn_Wout",
             "L1_attn_Wq", "L1_attn_Wk", "L1_attn_Wv", "L1_attn_Wo",
             "L1_ffn_W1", "L1_ffn_W2"]
    bkeys = ["L0_ffn_be", "L0_attn_bout", "L1_attn_bo", "L1_ffn_b1", "L1_ffn_b2"]
    im = []
    for c in range(NCORE):
        m = {"x": xs[c], **consts[c]}
        for k in wkeys:
            m[k] = f32(k)
        for k in bkeys:
            m[k] = f32(k).reshape(1, -1)
        im.append(m)
    res = run_bass_kernel_spmd(_FUSED_NC, im, core_ids=list(range(NCORE)))
    return unshard_x([np.asarray(res.results[c]["x_out"]) for c in range(NCORE)])

def kernel(**inputs):
    return kernel_fused(**inputs)


def kernel_unfused(**inputs):

    f32 = lambda k: np.ascontiguousarray(np.asarray(inputs[k], np.float32))
    x = f32("L0_attn_queries")
    xs = shard_x(x)

    moe_w = (f32("L0_ffn_Wg"), f32("L0_ffn_We"), f32("L0_ffn_be"))
    xs = run_moe_subblock(xs, *moe_w)
    xs = run_attn_subblock(xs, 0, {"Wq": f32("L0_attn_Wq"), "Wkv": f32("L0_attn_Wkv"),
                                   "Wout": f32("L0_attn_Wout"), "bout": f32("L0_attn_bout")})
    xs = run_moe_subblock(xs, *moe_w)
    wq1, wq2 = run_quant(f32("L1_ffn_W1"), f32("L1_ffn_W2"))
    b1, b2 = f32("L1_ffn_b1"), f32("L1_ffn_b2")
    xs = run_ffn_subblock(xs, wq1, wq2, b1, b2)
    xs = run_attn_subblock(xs, 1, {"Wq": f32("L1_attn_Wq"), "Wk": f32("L1_attn_Wk"),
                                   "Wv": f32("L1_attn_Wv"), "Wo": f32("L1_attn_Wo"),
                                   "bo": f32("L1_attn_bo")})
    xs = run_ffn_subblock(xs, wq1, wq2, b1, b2)
    return unshard_x(xs)


# version 34
# speedup vs baseline: 1.1467x; 1.0001x over previous
import numpy as np
import ml_dtypes
from contextlib import ExitStack

import concourse.bass as bass
import concourse.mybir as mybir
from concourse.bass_utils import run_bass_kernel_spmd
from concourse.aot_env import allow_ici_all_to_all

F32 = mybir.dt.float32
BF16 = mybir.dt.bfloat16
AF = mybir.ActivationFunctionType
ALU = mybir.AluOpType
AX = mybir.AxisListType
BF = ml_dtypes.bfloat16

NCORE = 8
TOK = 2048
D = 1024
DT = D // 128
CH = 512
NCH = TOK // CH
CAP = 2048.0
LNEPS = 1e-5
QEPS = 1e-5
DFF = 4096
SEQ = 8192
ATTN_SCALE = 0.125


class Sem:
    def __init__(self, ctx, nc, name):
        self.h = ctx.enter_context(nc.semaphore(name))
        self.n = 0


class G:
    ENG = ("sync", "tensor", "vector", "scalar", "gpsimd")

    def __init__(self, nc, ctx):
        self.nc = nc
        self.ctx = ctx
        self.cur = ctx
        self.prefix = ""
        self.q = {k: [] for k in self.ENG}
        self.sem = {name: Sem(ctx, nc, "sem_" + name) for name in
                    ("qs", "qg", "qa", "pe", "ve", "se", "gp", "cc", "wa", "wb", "sc", "sd", "n0", "n1")}
        self.nblk = 0

    def sb(self, name, shape, dt):
        return self.cur.enter_context(self.nc.sbuf_tensor(self.prefix + name, shape, dt))

    def ps(self, name, shape, dt=F32):
        return self.nc.psum_tensor(self.prefix + name, shape, dt)

    def subscope(g, prefix):
        class _S:
            def __enter__(s):
                s.st = ExitStack()
                s.st.__enter__()
                s.old = (g.cur, g.prefix)
                g.cur, g.prefix = s.st, prefix
                for eng in ("sync", "gpsimd"):
                    g.w(eng, g.sem["pe"], g.sem["ve"], g.sem["se"],
                        g.sem["qs"], g.sem["qg"])
                return s
            def __exit__(s, *a):
                g.cur, g.prefix = s.old
                return s.st.__exit__(*a)
        return _S()

    def on(self, eng, fn, wait=(), inc=None, inck=1):
        waits = []
        for wv in wait:
            s, v = wv if isinstance(wv, tuple) else (wv, None)
            waits.append((s.h, s.n if v is None else v))
        if inc is not None:
            inc.n += inck
        self.q[eng].append((tuple(waits), fn, None if inc is None else (inc.h, inck)))

    def w(self, eng, *sems):
        self.on(eng, None, wait=list(sems))

    def run_phase(self, name=None):
        qs = self.q
        self.q = {k: [] for k in self.ENG}
        self.nblk += 1
        if name is not None:
            name = self.prefix + name
        with self.nc.Block(name) as b:
            for engname in self.ENG:
                items = qs[engname]
                if not items:
                    continue

                dr = engname in ("vector", "scalar", "gpsimd")

                def body(e, items=tuple(items), dr=dr):
                    for waits, fn, inc in items:
                        for (h, v) in waits:
                            e.wait_ge(h, v)
                        instr = fn(e) if fn is not None else None
                        if inc is not None:
                            h, k = inc
                            (instr if instr is not None else e.nop()).then_inc(h, k)
                        if dr and fn is not None:
                            e.drain()
                getattr(b, engname)(body)


def host_consts(core):
    pos = 2048.0 * (core % 4) + np.arange(TOK, dtype=np.float64)
    inv_freq = 1.0 / (10000.0 ** (np.arange(0, 32, 2, dtype=np.float64) / 32.0))
    ang = pos[:, None] * inv_freq[None, :]
    C = np.ones((128, TOK), np.float64)
    S = np.zeros((128, TOK), np.float64)
    for p in range(128):
        q = p % 64
        if q < 32:
            C[p] = np.cos(ang[:, q // 2])
            S[p] = np.sin(ang[:, q // 2])
    P = np.zeros((128, 128), np.float64)
    for base in (0, 64):
        for i in range(16):
            P[base + 2 * i, base + 2 * i + 1] = -1.0
            P[base + 2 * i + 1, base + 2 * i] = 1.0
    u = np.arange(1920)[None, :]
    p = np.arange(128)[:, None]
    Tb = (p <= u - 896).astype(np.float64)
    wsel = np.zeros((8, 1), np.float32)
    wsel[:core] = 1.0
    Sx = S.copy()
    Sx[0::2] *= -1.0
    return {
        "ropeC": np.ascontiguousarray(C.astype(BF)),
        "ropeS": np.ascontiguousarray(S.astype(BF)),
        "ropeSx": np.ascontiguousarray(Sx.astype(BF)),
        "PT": np.ascontiguousarray(P.T.astype(BF)),
        "Tbase": np.ascontiguousarray(Tb.astype(BF)),
        "id8": np.eye(8, dtype=np.float32),
        "id128": np.eye(128, dtype=np.float32),
        "id1": np.ones((1, 1), np.float32),
        "wsel": wsel,
    }


CONST_SPECS = {
    "ropeC": ([128, TOK], BF16), "ropeS": ([128, TOK], BF16),
    "ropeSx": ([128, TOK], BF16),
    "PT": ([128, 128], BF16), "Tbase": ([128, 1920], BF16),
    "id8": ([8, 8], F32), "id128": ([128, 128], F32), "id1": ([1, 1], F32),
    "wsel": ([8, 1], F32),
}


def decl_in(nc, name, shape, dt=F32):
    return nc.dram_tensor(name, shape, dt, kind="ExternalInput").ap()


def decl_out(nc, name, shape, dt=F32):
    return nc.dram_tensor(name, shape, dt, kind="ExternalOutput").ap()


def const_decls_subset(nc, names):
    return {n: decl_in(nc, n, *CONST_SPECS[n]) for n in names}


def alloc_common(g, cd, const_names):
    cm = {}
    cm["ones128_bf"] = g.sb("ones128_bf", [128, 1], BF16)
    cm["ones128_f32"] = g.sb("ones128_f32", [128, 1], F32)
    cm["onesrow_f32"] = g.sb("onesrow_f32", [1, 128], F32)
    cm["onesrow_bf"] = g.sb("onesrow_bf", [1, 512], BF16)
    cm["ones8_f32"] = g.sb("ones8_f32", [8, 1], F32)

    cm["eps_ln"] = g.sb("eps_ln", [1, 1], F32)

    def ms(e):
        e.memset(cm["eps_ln"][:], LNEPS)
        e.memset(cm["ones128_bf"][:], 1.0)
        e.memset(cm["ones128_f32"][:], 1.0)
        e.memset(cm["onesrow_f32"][:], 1.0)
        e.memset(cm["onesrow_bf"][:], 1.0)
        return e.memset(cm["ones8_f32"][:], 1.0)
    g.on("vector", ms, inc=g.sem["ve"])
    for n in const_names:
        shape, dt = CONST_SPECS[n]
        t = g.sb(n + "_sb", shape, dt)
        cm[n + "_sb"] = t
        g.on("sync", lambda e, t=t, ap=cd[n]: e.dma_start(out=t[:], in_=ap),
             inc=g.sem["qs"], inck=16)
    return cm


def load_xT(g, xT, x_in):
    g.on("sync", lambda e: e.dma_start(
        out=xT[:].rearrange("p (a c) -> p a c", a=DT),
        in_=x_in.rearrange("(a p) c -> p a c", p=128)), inc=g.sem["qs"], inck=16)


def store_xT(g, xT, x_out):
    g.w("gpsimd", g.sem["ve"], g.sem["se"])
    g.on("gpsimd", lambda e: e.dma_start(
        out=x_out.rearrange("(a p) c -> p a c", p=128),
        in_=xT[:].rearrange("p (a c) -> p a c", a=DT)), inc=g.sem["qg"], inck=16)
    g.w("gpsimd", g.sem["qg"])
    g.run_phase("store_x")


def emit_casts(g, casts, sem=None):
    s = g.sem["qg"] if sem is None else sem
    for dst, src in casts:
        rows = dst.shape[0]
        nsp = 8 if rows % 8 == 0 and rows >= 8 else 1
        rs = rows // nsp
        for i in range(nsp):
            g.on("gpsimd", lambda e, dst=dst, src=src, i=i, rs=rs: e.dma_start(
                out=dst[rs * i: rs * (i + 1)], in_=src[rs * i: rs * (i + 1)]),
                inc=s, inck=16)


def alloc_ln(g, sq=None, xbf=None):
    b = {}
    b["xbf"] = g.sb("ln_xbf", [128, DT * CH], BF16) if xbf is None else xbf
    b["sq"] = g.sb("ln_sq", [128, DT * CH], BF16) if sq is None else sq
    b["m_sb"] = g.sb("ln_m", [1, CH], F32)
    b["v_sb"] = g.sb("ln_v", [1, CH], F32)
    b["r_sb"] = g.sb("ln_r", [1, CH], F32)
    b["mr_sb"] = b["t1_sb"] = g.sb("ln_t1", [1, CH], F32)
    b["u_sb"] = g.sb("ln_u", [128, CH], F32)
    return b


def emit_ln_phase(g, cm, b, xT, hT, chunks=None, router=None, hT_off=None, name="ln", steps=6):
    nc = g.nc
    se, pe, ve = g.sem["se"], g.sem["pe"], g.sem["ve"]
    if chunks is None:
        chunks = list(range(NCH))
    chunks = list(chunks)
    NC = len(chunks)
    if hT_off is None:
        HTOK, hT_off = TOK, 0
    else:
        HTOK = NC * CH
    with ExitStack() as ps:
        msum_ps = ps.enter_context(g.ps(name + "_msum", [1, CH], F32))
        s2_ps = ps.enter_context(g.ps(name + "_s2", [1, CH], F32))
        rb_ps = ps.enter_context(g.ps(name + "_rb", [128, CH], F32))
        mrb_ps = ps.enter_context(g.ps(name + "_mrb", [128, CH], F32))
        lg_ps = ps.enter_context(g.ps(name + "_lg", [8, TOK], F32)) \
            if router is not None else None

        reds_pe = {}
        stats_ve = {}
        mr_ve = {}
        bcast_pe = {}
        hcomp_ve = {}
        rstd_se = {}
        rb8_se = {}

        def e_cast(ci, ch):
            def cast(e):
                for d in range(DT):
                    e.activation(out=b["xbf"][:, CH * d: CH * (d + 1)],
                                 in_=xT[:, TOK * d + CH * ch: TOK * d + CH * (ch + 1)], func=AF.Copy)
                for d in range(DT):
                    last = e.activation(out=b["sq"][:, CH * d: CH * (d + 1)],
                                        in_=xT[:, TOK * d + CH * ch: TOK * d + CH * (ch + 1)], func=AF.Square)
                return last
            g.on("scalar", cast, wait=[(pe, reds_pe[ci - 1])] if ci >= 1 else [(pe, pe.n)], inc=se)

        def e_reds(ci, ch):
            def reds(e):
                for d in range(DT):
                    e.matmul(msum_ps[:], cm["ones128_bf"][:],
                             b["xbf"][:, CH * d: CH * (d + 1)], start=(d == 0), stop=(d == DT - 1))
                for d in range(DT):
                    last = e.matmul(s2_ps[:], cm["ones128_bf"][:],
                                    b["sq"][:, CH * d: CH * (d + 1)], start=(d == 0), stop=(d == DT - 1))
                if router is not None:
                    for d in range(DT):
                        last = e.matmul(lg_ps[:, CH * ch: CH * (ch + 1)],
                                        router["wg_sb"][:, 8 * d: 8 * (d + 1)],
                                        xT[:, TOK * d + CH * ch: TOK * d + CH * (ch + 1)],
                                        start=(d == 0), stop=False)
                return last
            w = [se] + ([(ve, stats_ve[ci - 1])] if ci >= 1 else [])
            g.on("tensor", reds, wait=w, inc=pe)
            reds_pe[ci] = pe.n

        def e_stats(ci, ch):
            def stats(e):
                e.tensor_scalar(out=b["m_sb"][:], in0=msum_ps[:], scalar1=1.0 / D, scalar2=None, op0=ALU.mult)
                e.drain()
                e.tensor_scalar(out=b["v_sb"][:], in0=s2_ps[:], scalar1=1.0 / D, scalar2=None, op0=ALU.mult)
                e.drain()
                e.tensor_tensor(out=b["t1_sb"][:], in0=b["m_sb"][:], in1=b["m_sb"][:], op=ALU.mult)
                e.drain()
                return e.tensor_tensor(out=b["v_sb"][:], in0=b["v_sb"][:], in1=b["t1_sb"][:], op=ALU.subtract)
            w = [(pe, reds_pe[ci])] + ([(se, rstd_se[ci - 1])] if ci >= 1 else [])
            g.on("vector", stats, wait=w, inc=ve)
            stats_ve[ci] = ve.n

        def e_rstd(ci, ch):
            def rstd(e):
                e.activation(out=b["t1_sb"][:], in_=b["v_sb"][:], func=AF.Ln, bias=cm["eps_ln"][:])
                return e.activation(out=b["r_sb"][:], in_=b["t1_sb"][:], func=AF.Exp, scale=-0.5)
            g.on("scalar", rstd, wait=[(ve, stats_ve[ci])], inc=se)
            rstd_se[ci] = se.n

        def e_mr(ci, ch):
            g.on("vector", lambda e: e.tensor_tensor(out=b["mr_sb"][:], in0=b["m_sb"][:],
                                                     in1=b["r_sb"][:], op=ALU.mult),
                 wait=[(se, rstd_se[ci])], inc=ve)
            mr_ve[ci] = ve.n

        def e_bcast(ci, ch):
            def bcast(e):
                e.matmul(rb_ps[:], cm["onesrow_f32"][:], b["r_sb"][:], start=True, stop=True)
                last = e.matmul(mrb_ps[:], cm["onesrow_f32"][:], b["mr_sb"][:], start=True, stop=True)
                if router is not None:
                    last = e.matmul(lg_ps[:, CH * ch: CH * (ch + 1)],
                                    router["ngsum_sb"][:], b["m_sb"][:], start=False, stop=True)
                return last
            w = [(ve, mr_ve[ci])]
            if ci >= 1:
                w.append((ve, hcomp_ve[ci - 1]))
                if router is not None:
                    w.append((se, rb8_se[ci - 1]))
            g.on("tensor", bcast, wait=w, inc=pe)
            bcast_pe[ci] = pe.n

        def e_h(ci, ch):
            if router is not None:
                g.on("scalar", lambda e: e.copy(out=router["rb8_sb"][:], in_=rb_ps[0:8, :]),
                     wait=[(pe, bcast_pe[ci])], inc=se)
                rb8_se[ci] = se.n
            def hcomp(e):
                for d in range(DT):
                    e.tensor_tensor(out=b["u_sb"][:],
                                    in0=xT[:, TOK * d + CH * ch: TOK * d + CH * (ch + 1)],
                                    in1=rb_ps[:], op=ALU.mult)
                    e.drain()
                    last = e.tensor_tensor(
                        out=hT[:, HTOK * d + CH * ch - hT_off: HTOK * d + CH * (ch + 1) - hT_off],
                        in0=b["u_sb"][:], in1=mrb_ps[:], op=ALU.subtract)
                    e.drain()
                return last
            w = [(pe, bcast_pe[ci])] + ([(se, rb8_se[ci])] if router is not None else [])
            g.on("vector", hcomp, wait=w, inc=ve)
            if router is not None:
                g.on("vector", lambda e: e.tensor_tensor(
                    out=router["lg_sb"][:, CH * ch: CH * (ch + 1)],
                    in0=lg_ps[:, CH * ch: CH * (ch + 1)],
                    in1=router["rb8_sb"][:], op=ALU.mult), inc=ve)
            hcomp_ve[ci] = ve.n

        stages = [e_cast, e_reds, e_stats, e_rstd, e_mr, e_bcast, e_h]
        for ci in range(NC + 1):
            if ci < NC:
                e_cast(ci, chunks[ci])
            if ci >= 1:
                cj = ci - 1
                e_stats(cj, chunks[cj])
                e_rstd(cj, chunks[cj])
                e_mr(cj, chunks[cj])
                e_bcast(cj, chunks[cj])
                e_h(cj, chunks[cj])
            if ci < NC:
                e_reds(ci, chunks[ci])
        g.run_phase(name)


def alloc_gating(g):
    gt = {}
    gt["ltm_sb"] = g.sb("gt_ltm_sb", [128, 128], F32)
    gt["mx_sb"] = g.sb("gt_mx", [128, 16], F32)
    gt["msk_tm"] = g.sb("gt_msk_tm", [128, 128], F32)
    gt["mskT_sb"] = g.sb("gt_mskT_sb", [8, TOK], F32)
    gt["zero8"] = g.sb("gt_zero8", [8, 1], F32)
    gt["cumT_sb"] = g.sb("gt_cumT", [8, TOK], F32)
    gt["tot_sb"] = g.sb("gt_tot", [8, 8], F32)
    gt["eT_sb"] = g.sb("gt_eT", [8, TOK], F32)
    gt["sT_sb"] = g.sb("gt_sT_sb", [1, TOK], F32)
    gt["rs_sb"] = gt["sT_sb"]
    gt["nin_sb"] = gt["eT_sb"]
    gt["gateT_sb"] = g.sb("gt_gateT", [1, TOK], F32)
    gt["offs_sb"] = g.sb("gt_offs_sb", [1, 8], F32)
    gt["offsT_sb"] = g.sb("gt_offsT_sb", [8, 1], F32)
    gt["k8_sb"] = gt["cumT_sb"]
    gt["factorT_sb"] = gt["gateT_sb"]
    gt["fB_sb"] = g.sb("gt_fB_sb", [128, TOK], F32)
    return gt


def emit_gating(g, cm, lg_sb, gt, totals_mode, tot_io):
    nc = g.nc
    se, pe, ve = g.sem["se"], g.sem["pe"], g.sem["ve"]
    qg, cc = g.sem["qg"], g.sem["cc"]

    with g.ps("gt_ltm", [128, 128], F32) as ltm_ps, \
         g.ps("gt_mskT", [8, TOK], F32) as mskT_ps:
        def ttm(e):
            for j in range(16):
                last = e.transpose(ltm_ps[:, 8 * j: 8 * (j + 1)],
                                   lg_sb[:, 128 * j: 128 * (j + 1)], cm["id8_sb"][:])
            return last
        g.on("tensor", ttm, wait=[ve], inc=pe)
        g.on("scalar", lambda e: e.copy(out=gt["ltm_sb"][:], in_=ltm_ps[:]),
             wait=[pe], inc=se)
        def masks(e):
            for j in range(16):
                e.tensor_reduce(out=gt["mx_sb"][:, j: j + 1],
                                in_=gt["ltm_sb"][:, 8 * j: 8 * (j + 1)], axis=AX.X, op=ALU.max)
            e.drain()
            for j in range(16):
                last = e.tensor_scalar(out=gt["msk_tm"][:, 8 * j: 8 * (j + 1)],
                                       in0=gt["ltm_sb"][:, 8 * j: 8 * (j + 1)],
                                       scalar1=gt["mx_sb"][:, j: j + 1], scalar2=None, op0=ALU.is_ge)
            return last
        g.on("vector", masks, wait=[se], inc=ve)
        def tfm(e):
            for j in range(16):
                last = e.transpose(mskT_ps[:, 128 * j: 128 * (j + 1)],
                                   gt["msk_tm"][:, 8 * j: 8 * (j + 1)], cm["id128_sb"][:])
            return last
        g.on("tensor", tfm, wait=[ve], inc=pe)
        g.on("scalar", lambda e: e.copy(out=gt["mskT_sb"][:], in_=mskT_ps[:]),
             wait=[pe], inc=se)
        g.on("vector", lambda e: e.memset(gt["zero8"][:], 0.0))
        g.on("vector", lambda e: e.tensor_tensor_scan(
            out=gt["cumT_sb"][:], data0=gt["mskT_sb"][:], data1=gt["mskT_sb"][:],
            initial=gt["zero8"][:], op0=ALU.add, op1=ALU.bypass),
            wait=[se], inc=ve)
        g.run_phase("gate_mask")

    if totals_mode == "out":
        g.w("gpsimd", ve)
        g.on("gpsimd", lambda e: e.dma_start(
            out=tot_io["tot_out_ext"], in_=gt["cumT_sb"][:, TOK - 1: TOK]),
            inc=qg, inck=16)
        g.w("gpsimd", qg)
        g.run_phase("gate_tot")
        return

    if totals_mode == "cc":
        g.w("gpsimd", ve)
        g.on("gpsimd", lambda e: e.dma_start(
            out=tot_io["tot_in"].rearrange("a e -> e a"),
            in_=gt["cumT_sb"][:, TOK - 1: TOK]), inc=qg, inck=16)
        g.w("gpsimd", qg)
        g.on("gpsimd", lambda e: e.collective_compute(
            "AllGather", ALU.bypass, replica_groups=[list(range(NCORE))],
            ins=[tot_io["tot_in"]], outs=[tot_io["tot_all"]]), inc=cc)
        g.w("gpsimd", cc)
        g.on("gpsimd", lambda e: e.dma_start(out=gt["tot_sb"][:], in_=tot_io["tot_all"]),
             inc=qg, inck=16)
    else:
        g.on("gpsimd", lambda e: e.dma_start(out=gt["tot_sb"][:], in_=tot_io["tot_all"]),
             inc=qg, inck=16)

    with g.ps("gt_sT", [1, TOK], F32) as sT_ps:
        g.on("scalar", lambda e: e.activation(out=gt["eT_sb"][:], in_=lg_sb[:], func=AF.Exp),
             wait=[ve], inc=se)
        def smm(e):
            for c in range(NCH):
                last = e.matmul(sT_ps[:, CH * c: CH * (c + 1)], cm["ones8_f32"][:],
                                gt["eT_sb"][:, CH * c: CH * (c + 1)], start=True, stop=True)
            return last
        g.on("tensor", smm, wait=[se], inc=pe)
        g.on("scalar", lambda e: e.copy(out=gt["sT_sb"][:], in_=sT_ps[:]),
             wait=[pe], inc=se)
        g.on("vector", lambda e: e.reciprocal(out=gt["rs_sb"][:], in_=gt["sT_sb"][:]),
             wait=[se], inc=ve)
        g.on("vector", lambda e: e.tensor_tensor(out=gt["nin_sb"][:], in0=gt["eT_sb"][:],
                                                 in1=gt["mskT_sb"][:], op=ALU.mult), inc=ve)
        g.run_phase("gate_probs")

    with g.ps("gt_num", [1, TOK], F32) as num_ps:
        def nmm(e):
            for c in range(NCH):
                last = e.matmul(num_ps[:, CH * c: CH * (c + 1)], cm["ones8_f32"][:],
                                gt["nin_sb"][:, CH * c: CH * (c + 1)], start=True, stop=True)
            return last
        g.on("tensor", nmm, wait=[ve], inc=pe)
        g.on("vector", lambda e: e.tensor_tensor(out=gt["gateT_sb"][:], in0=num_ps[:],
                                                 in1=gt["rs_sb"][:], op=ALU.mult),
             wait=[pe], inc=ve)
        g.run_phase("gate_gate")

    with g.ps("gt_offsT", [8, 1], F32) as offsT_ps:
        g.on("tensor", lambda e: e.matmul(offsT_ps[:], gt["tot_sb"][:], cm["wsel_sb"][:],
                                          start=True, stop=True), wait=[qg], inc=pe)
        g.on("scalar", lambda e: e.copy(out=gt["offsT_sb"][:], in_=offsT_ps[:]),
             wait=[pe], inc=se)
        g.on("vector", lambda e: e.tensor_scalar(
            out=gt["k8_sb"][:], in0=gt["cumT_sb"][:], scalar1=gt["offsT_sb"][:],
            scalar2=CAP, op0=ALU.add, op1=ALU.is_le), wait=[se], inc=ve)
        g.on("vector", lambda e: e.tensor_tensor(out=gt["k8_sb"][:], in0=gt["k8_sb"][:],
                                                 in1=gt["mskT_sb"][:], op=ALU.mult), inc=ve)
        g.run_phase("gate_keep")

    with g.ps("gt_keep", [1, TOK], F32) as keep_ps, \
         g.ps("gt_fB", [128, TOK], F32) as fB_ps:
        def kmm(e):
            for c in range(NCH):
                last = e.matmul(keep_ps[:, CH * c: CH * (c + 1)], cm["ones8_f32"][:],
                                gt["k8_sb"][:, CH * c: CH * (c + 1)], start=True, stop=True)
            return last
        g.on("tensor", kmm, wait=[ve], inc=pe)
        g.on("vector", lambda e: e.tensor_tensor(out=gt["factorT_sb"][:], in0=keep_ps[:],
                                                 in1=gt["gateT_sb"][:], op=ALU.mult),
             wait=[pe], inc=ve)
        g.on("vector", lambda e: e.tensor_scalar(out=gt["factorT_sb"][:], in0=gt["factorT_sb"][:],
                                                 scalar1=0.5, scalar2=None, op0=ALU.mult), inc=ve)
        def fmm(e):
            for c in range(NCH):
                last = e.matmul(fB_ps[:, CH * c: CH * (c + 1)], cm["onesrow_f32"][:],
                                gt["factorT_sb"][:, CH * c: CH * (c + 1)], start=True, stop=True)
            return last
        g.on("tensor", fmm, wait=[ve], inc=pe)
        g.on("scalar", lambda e: e.copy(out=gt["fB_sb"][:], in_=fB_ps[:]),
             wait=[pe], inc=se)
        g.run_phase("gate_factor")


def emit_expert(g, cm, xT, hT, fB_sb, we_sb, beV_sb, ex_t):
    nc, pe, ve = g.nc, g.sem["pe"], g.sem["ve"]
    se, qs, qg = g.sem["se"], g.sem["qs"], g.sem["qg"]
    with g.ps("ex_y", [128, 2 * CH], F32) as y_ps:
        g.w("tensor", qs, qg, se, ve)
        evacs = []
        for i, (ch, do) in enumerate([(c, d) for c in range(NCH) for d in range(DT)]):
            par = i % 2
            ysl = y_ps[:, CH * par: CH * (par + 1)]
            def mm(e, ch=ch, do=do, ysl=ysl):
                for di in range(DT):
                    last = e.matmul(ysl, we_sb[:, 1024 * di + 128 * do: 1024 * di + 128 * (do + 1)],
                                    hT[:, TOK * di + CH * ch: TOK * di + CH * (ch + 1)],
                                    start=(di == 0), stop=(di == DT - 1))
                return last
            wt = [(ve, evacs[i - 2])] if i >= 2 else []
            g.on("tensor", mm, wait=wt, inc=pe)
            def evac(e, ch=ch, do=do, ysl=ysl):
                xsl = xT[:, TOK * do + CH * ch: TOK * do + CH * (ch + 1)]
                e.scalar_tensor_tensor(out=ex_t[:], in0=ysl, scalar=beV_sb[:, do: do + 1],
                                       in1=fB_sb[:, CH * ch: CH * (ch + 1)],
                                       op0=ALU.add, op1=ALU.mult)
                e.drain()
                return e.tensor_tensor(out=xsl, in0=xsl, in1=ex_t[:], op=ALU.add)
            g.on("vector", evac, wait=[pe], inc=ve)
            evacs.append(ve.n)
        g.run_phase("expert")


def emit_moe_phase(g, cm, xT, weights, totals_mode, tot_io, after_loads=None):
    nc = g.nc
    qs, qg, pe, ve, se = g.sem["qs"], g.sem["qg"], g.sem["pe"], g.sem["ve"], g.sem["se"]
    apply_mode = totals_mode != "out"

    hT = g.sb("moe_hT", [128, DT * TOK], BF16)
    ln = alloc_ln(g)
    gt = alloc_gating(g)
    ro = {
        "wg_sb": g.sb("moe_wg", [128, 8 * DT], F32),
        "ngsum_sb": g.sb("moe_ngsum", [1, 8], F32),
        "rb8_sb": g.sb("moe_rb8", [8, CH], F32),
        "lg_sb": g.sb("moe_lg", [8, TOK], F32),
    }
    if apply_mode:
        we_sb = g.sb("moe_we", [128, DT * 1024], BF16)
        beV_sb = g.sb("moe_beV", [128, 8], F32)
        ex_t = g.sb("moe_ex_t", [128, CH], F32)

    g.on("gpsimd", lambda e: e.dma_start(
        out=ro["wg_sb"][:].rearrange("p (a c) -> p a c", a=DT),
        in_=weights["wg"].rearrange("(a p) c -> p a c", p=128)), inc=qg, inck=16)
    if apply_mode:
        g.on("gpsimd", lambda e: e.dma_start(
            out=we_sb[:].rearrange("p (a c) -> p a c", a=DT),
            in_=weights["we_bf"].rearrange("(a p) c -> p a c", p=128)), inc=qg, inck=16)
    if after_loads is not None:
        after_loads()

    with g.ps("moe_gsum", [1, 8], F32) as gsum_ps:
        def gsum(e):
            for d in range(DT):
                last = e.matmul(gsum_ps[:], cm["ones128_f32"][:],
                                ro["wg_sb"][:, 8 * d: 8 * (d + 1)], start=(d == 0), stop=(d == DT - 1))
            return last
        g.on("tensor", gsum, wait=[qs, qg, ve], inc=pe)
        g.on("vector", lambda e: e.tensor_scalar(out=ro["ngsum_sb"][:], in0=gsum_ps[:],
                                                 scalar1=-1.0, scalar2=None, op0=ALU.mult),
             wait=[pe], inc=ve)
        g.run_phase("moe_prep")

    emit_ln_phase(g, cm, ln, xT, hT, router=ro, name="moe_ln")
    if apply_mode:
        for do in range(DT):
            g.on("gpsimd", lambda e, do=do: e.dma_start(
                out=beV_sb[:, do: do + 1],
                in_=weights["beh"].rearrange("a (f p) -> (a f) p", p=128)[do: do + 1]
                    .rearrange("f p -> p f")),
                inc=qg, inck=16)
    emit_gating(g, cm, ro["lg_sb"], gt, totals_mode, tot_io)
    if not apply_mode:
        return
    emit_expert(g, cm, xT, hT, gt["fB_sb"], we_sb, beV_sb, ex_t)


def build_moe(totals_mode):
    nc = bass.Bass("TRN2", target_bir_lowering=False)
    x_in = decl_in(nc, "x", [D, TOK])
    wg = decl_in(nc, "wg", [D, 8])
    cd = const_decls_subset(nc, ("id8", "id128", "id1", "wsel"))
    tot_io = {}
    weights = {"wg": wg}
    apply_mode = totals_mode != "out"
    if totals_mode == "out":
        tot_io["tot_out_ext"] = decl_out(nc, "tot", [8, 1])
    else:
        tot_io["tot_all"] = decl_in(nc, "tot_all", [8, 8])
        we = decl_in(nc, "we", [D, D])
        be = decl_in(nc, "be", [1, D])
        x_out = decl_out(nc, "x_out", [D, TOK])
        weights["we_bf"] = nc.dram_tensor("we_bf", [D, D], BF16).ap()
        weights["beh"] = nc.dram_tensor("beh", [1, D], BF16).ap()

    with ExitStack() as ctx:
        g = G(nc, ctx)
        cm = alloc_common(g, cd, ("id8", "id128", "id1", "wsel"))
        xT = g.sb("xT", [128, DT * TOK], F32)
        load_xT(g, xT, x_in)
        if apply_mode:
            emit_casts(g, [(weights["we_bf"], we), (weights["beh"], be)])
        g.w("tensor", g.sem["qs"])
        g.w("scalar", g.sem["qs"])
        g.w("vector", g.sem["qs"])
        g.run_phase("load")
        emit_moe_phase(g, cm, xT, weights, totals_mode, tot_io)
        if apply_mode:
            store_xT(g, xT, x_out)
        else:
            g.run_phase("end")
    return nc


SH_Q = 128 * TOK
SH_K = 128 * TOK
SH_V = TOK * 130
SHARD = SH_Q + SH_K + SH_V
RET = 128 * TOK


def emit_qkv(g, cm, xT, hT, w_bf, sendqk, sendv_d, rope_on, v_own=None,
             on_qk_done=None, w_wait=()):
    nc = g.nc
    qs, qg, pe, ve, se = g.sem["qs"], g.sem["qg"], g.sem["pe"], g.sem["ve"], g.sem["se"]
    sendv = sendqk

    w_sb2 = g.sb("qkv_w", [128, 2 * DT * 1024], BF16)
    o_sb = g.sb("qkv_o", [128, 2 * TOK], BF16)
    z_sb = g.sb("qkv_z", [128, TOK], BF16) if rope_on else None
    if v_own is None:
        v_own = g.sb("qkv_vown", [128, 16 * 1040], BF16)

    wslot = {"wq": 0, "wk": 1, "wv": 0}
    wsems = (g.sem["wa"], g.sem["wb"])

    def load_w(which, wait_extra=()):
        par = wslot[which]
        g.on("sync", lambda e, par=par: e.dma_start(
            out=w_sb2[:, DT * 1024 * par: DT * 1024 * (par + 1)].rearrange("p (a c) -> p a c", a=DT),
            in_=w_bf[which].rearrange("(a p) c -> p a c", p=128)),
            wait=[qg] + list(wait_extra) + list(w_wait), inc=wsems[par], inck=16)
        return wsems[par].n

    rope_prev = [None]
    with g.ps("qkv_q", [128, 2 * CH], F32) as q_ps, \
         g.ps("qkv_zp", [128, TOK], F32) as z_ps:
        for which, off in (("wq", 0), ("wk", SH_Q)):
            if which == "wq":
                wn = load_w("wq")
                kn = load_w("wk")
                wcur = wslot["wq"]
            else:
                wn, wcur = kn, wslot["wk"]
            g.on("tensor", None, wait=[(wsems[wcur], wn)])
            w_sb = w_sb2[:, DT * 1024 * wcur: DT * 1024 * (wcur + 1)]
            evl = []
            for m in range(DT):
                par = m % 2
                ot = o_sb[:, TOK * par: TOK * (par + 1)]
                for ch in range(NCH):
                    i = m * NCH + ch
                    ysl = q_ps[:, CH * (ch % 2): CH * (ch % 2 + 1)]
                    def mm(e, m=m, ch=ch, ysl=ysl):
                        for d in range(DT):
                            last = e.matmul(ysl, w_sb[:, 1024 * d + 128 * m: 1024 * d + 128 * (m + 1)],
                                            hT[:, TOK * d + CH * ch: TOK * d + CH * (ch + 1)],
                                            start=(d == 0), stop=(d == DT - 1))
                        return last
                    g.on("tensor", mm, wait=[(se, evl[i - 2])] if i >= 2 else [se], inc=pe)
                    g.on("scalar", lambda e, ch=ch, ysl=ysl, ot=ot: e.copy(
                        out=ot[:, CH * ch: CH * (ch + 1)], in_=ysl), wait=[pe, qg], inc=se)
                    evl.append(se.n)
                if rope_on:
                    zw = [(se, evl[-1])]
                    if rope_prev[0] is not None:
                        zw.append((ve, rope_prev[0]))
                    def zmm(e, ot=ot):
                        for ch in range(NCH):
                            last = e.matmul(z_ps[:, CH * ch: CH * (ch + 1)],
                                            cm["PT_sb"][:], ot[:, CH * ch: CH * (ch + 1)],
                                            start=True, stop=True)
                        return last
                    g.on("tensor", zmm, wait=zw, inc=pe)
                    zn = pe.n
                    def rope(e, ot=ot):
                        e.tensor_tensor(out=z_sb[:], in0=z_ps[:],
                                        in1=cm["ropeS_sb"][:], op=ALU.mult)
                        e.drain()
                        e.tensor_tensor(out=ot[:], in0=ot[:],
                                        in1=cm["ropeC_sb"][:], op=ALU.mult)
                        e.drain()
                        last = e.tensor_tensor(out=ot[:], in0=ot[:], in1=z_sb[:], op=ALU.add)
                        e.drain()
                        return last
                    g.on("vector", rope, wait=[(pe, zn)], inc=ve)
                    rope_prev[0] = ve.n
                g.w("gpsimd", se, ve)
                g.on("gpsimd", lambda e, m=m, off=off, ot=ot: e.dma_start(
                    out=sendv[m, off + 0: off + 128 * TOK].rearrange("(p t) -> p t", p=128),
                    in_=ot[:]), inc=qg, inck=16)
            g.run_phase("qkv_" + which)

    if on_qk_done is not None:
        on_qk_done()

    vn = load_w("wv", wait_extra=[pe])
    w_sb = w_sb2[:, 0: DT * 1024]
    g.on("vector", lambda e: e.memset(
        v_own[:].rearrange("p (t f) -> p t f", f=65)[:, :, 64:65], 1.0),
        wait=[(g.sem["pe"], g.sem["pe"].n)], inc=ve)
    with g.ps("qkv_v", [128, 2 * CH], F32) as v_ps:
        g.on("tensor", None, wait=[(wsems[0], vn)])
        g.w("tensor", ve)
        for tt in range(16):
            for c2 in range(2):
                i = tt * 2 + c2
                ysl = v_ps[:, CH * (i % 2): CH * (i % 2 + 1)]
                def vmm(e, tt=tt, c2=c2, ysl=ysl):
                    for d in range(DT):
                        last = e.matmul(ysl, hT[:, TOK * d + 128 * tt: TOK * d + 128 * (tt + 1)],
                                        w_sb[:, 1024 * d + CH * c2: 1024 * d + CH * (c2 + 1)],
                                        start=(d == 0), stop=(d == DT - 1))
                    return last
                g.on("tensor", vmm, wait=[se] if i >= 2 else [], inc=pe)
                st = 1040 * tt + 520 * c2
                def vev(e, st=st, ysl=ysl):
                    last = e.tensor_scalar(
                        out=v_own[:, st: st + 520].rearrange("p (h f) -> p h f", f=65)[:, :, 0:64],
                        in0=ysl.rearrange("p (h f) -> p h f", f=64),
                        scalar1=1.0, scalar2=None, op0=ALU.mult)
                    e.drain()
                    return last
                g.on("vector", vev, wait=[pe], inc=se)
        g.w("sync", se)
        for j in range(NCORE):
            g.on("sync", lambda e, j=j: e.dma_start(
                out=sendv_d[j, 0: SH_V].rearrange("(t p f) -> p t f", p=128, f=130),
                in_=v_own[:].rearrange("p (t f) -> p t f", f=1040)[:, :, 130 * j: 130 * (j + 1)]),
                inc=qs, inck=16)
        g.run_phase("qkv_v")


def emit_flash(g, cm, recvqk, recvv, ret, causal, bg=None, wait_qk=None, wait_v=None):
    nc = g.nc
    qs, qg, pe, ve, se = g.sem["qs"], g.sem["qg"], g.sem["pe"], g.sem["ve"], g.sem["se"]

    qf = g.sb("fl_q", [128, 2 * SEQ], BF16)
    kf = g.sb("fl_k", [128, 2 * SEQ], BF16)
    vf = g.sb("fl_v", [128, 2 * 64 * 130], BF16)
    p_sb = g.sb("fl_p", [128, 3 * 1024], BF16)

    qa = g.sem["qa"]
    first_s = [wait_qk] if wait_qk else []
    first_g = [wait_v] if wait_v else []
    first_a = [(qs, qs.n), (qg, qg.n), (g.sem["pe"], g.sem["pe"].n),
               (g.sem["ve"], g.sem["ve"].n)] + ([wait_qk] if wait_qk else [])
    qf_val, kf_val, vf_val = {}, {}, {}
    for b in range(2):
        for r4 in range(4):
            r = 4 * b + r4
            g.on("sync", lambda e, b=b, r4=r4, r=r: e.dma_start(
                out=qf[:, SEQ * b + TOK * r4: SEQ * b + TOK * (r4 + 1)],
                in_=recvqk[r, 0: SH_Q].rearrange("(p t) -> p t", p=128)),
                wait=first_s, inc=qs, inck=16)
            first_s = []
            qf_val[(b, r4)] = qs.n
            g.on("scalar", lambda e, b=b, r4=r4, r=r: e.dma_start(
                out=kf[:, SEQ * b + TOK * r4: SEQ * b + TOK * (r4 + 1)],
                in_=recvqk[r, SH_Q: SH_Q + SH_K].rearrange("(p t) -> p t", p=128)),
                wait=first_a, inc=qa, inck=16)
            first_a = []
            kf_val[(b, r4)] = qa.n
            g.on("gpsimd", lambda e, b=b, r4=r4, r=r: e.dma_start(
                out=vf[:].rearrange("p (t f) -> p t f", f=130)[:, 64 * b + 16 * r4: 64 * b + 16 * (r4 + 1), :],
                in_=recvv[r, 0: SH_V].rearrange("(t p f) -> p t f", p=128, f=130)),
                wait=first_g, inc=qg, inck=16)
            first_g = []
            vf_val[(b, r4)] = qg.n
    g.run_phase("fl_load")
    bw_se, bw_ve = g.sem["se"].n, g.sem["ve"].n

    with g.ps("fl_s", [128, 2048], F32) as s_ps, \
         g.ps("fl_o", [65, 2048], F32) as o_ps:
        oraw = g.sb("fl_oraw", [65, 2 * 2048], BF16)
        bcB2 = g.sb("fl_bcB2", [64, 2048], BF16)
        o_sb2 = g.sb("fl_osb2", [64, 2048], BF16)
        dscr = g.nc.dram_tensor(g.prefix + "fl_dscr", [2, 2048], BF16).ap()
        n0, n1 = g.sem["n0"], g.sem["n1"]
        nrm_ve = [g.sem["ve"].n]
        oev_se = [g.sem["se"].n]
        otts_done = {}
        send_done = {}
        qcidx = [0]
        prev_exp = None
        g.w("tensor", ve)
        kf_seen, vf_seen = set(), set()
        for b in range(2):
            for qc in range(8):
                units = []
                for qh in range(2):
                    nk = (8 * qc + 4 * (qh + 1)) if causal else 64
                    for kt in range(nk):
                        units.append((qh, kt, kt == 0, kt == nk - 1))
                NU = len(units)
                exp_done = {}
                s_done = {}

                def emit_av(u):
                    qh, kt, first, last = units[u]
                    psl = p_sb[:, 1024 * (u % 3): 1024 * (u % 3 + 1)]
                    diag = causal and kt >= 8 * qc + 4 * qh
                    if diag:
                        off = 896 - 128 * (kt - 8 * qc) + 512 * qh
                        def dmask(e, psl=psl, off=off):
                            e.tensor_tensor(out=psl[:, 0:512], in0=psl[:, 0:512],
                                            in1=cm["Tbase_sb"][:, off: off + 512], op=ALU.mult)
                            e.drain()
                            last_i = e.tensor_tensor(out=psl[:, 512:1024], in0=psl[:, 512:1024],
                                                     in1=cm["Tbase_sb"][:, off: off + 512], op=ALU.mult)
                            e.drain()
                            return last_i
                        g.on("vector", dmask, wait=[(se, exp_done[u])], inc=ve)
                        avwait = [(ve, ve.n)]
                    else:
                        avwait = [(se, exp_done[u])]
                    if first:
                        avwait.append((se, oev_se[0]))
                    if (b, kt // 16) not in vf_seen:
                        vf_seen.add((b, kt // 16))
                        avwait.append((qg, vf_val[(b, kt // 16)]))
                    def avmm(e, kt=kt, qh=qh, psl=psl, first=first, last=last, b=b):
                        for h in range(2):
                            last_i = e.matmul(
                                o_ps[:, 1024 * h + 512 * qh: 1024 * h + 512 * (qh + 1)],
                                vf[:, 8320 * b + 130 * kt + 65 * h: 8320 * b + 130 * kt + 65 * (h + 1)],
                                psl[:, 512 * h: 512 * (h + 1)],
                                start=first, stop=last)
                        return last_i
                    g.on("tensor", avmm, wait=avwait, inc=pe)

                for u, (qh, kt, first, last) in enumerate(units):
                    par = u % 2
                    ssl = s_ps[:, 1024 * par: 1024 * (par + 1)]
                    q0 = SEQ * b + 1024 * qc + 512 * qh
                    def smm(e, b=b, kt=kt, q0=q0, ssl=ssl):
                        for h, tp in ((0, (0, 0)), (1, (64, 0))):
                            base = 64 * h
                            last_i = e.matmul(
                                ssl[:, 512 * h: 512 * (h + 1)],
                                kf[base:base + 64, SEQ * b + 128 * kt: SEQ * b + 128 * (kt + 1)],
                                qf[base:base + 64, q0: q0 + 512],
                                start=True, stop=True, tile_position=tp)
                        return last_i
                    if u >= 2:
                        swait = [(se, exp_done[u - 2])]
                    elif prev_exp is None:
                        swait = [(se, bw_se), (ve, bw_ve)]
                    else:
                        swait = [(se, prev_exp[u])]
                    if u < 2:
                        swait.append((qs, qf_val[(b, qc // 2)]))
                    if (b, kt // 16) not in kf_seen:
                        kf_seen.add((b, kt // 16))
                        swait.append((qa, kf_val[(b, kt // 16)]))
                    g.on("tensor", smm, wait=swait, inc=pe)
                    s_done[u] = pe.n
                    psl = p_sb[:, 1024 * (u % 3): 1024 * (u % 3 + 1)]
                    g.on("scalar", lambda e, psl=psl, ssl=ssl: e.activation(
                        out=psl, in_=ssl, func=AF.Exp, scale=ATTN_SCALE),
                        wait=[(pe, s_done[u])], inc=se)
                    exp_done[u] = se.n
                    if u >= 1:
                        emit_av(u - 1)
                emit_av(NU - 1)
                prev_exp = (exp_done[NU - 2], exp_done[NU - 1])

                qci = qcidx[0]
                qcidx[0] += 1
                slot = qci % 2
                osl = oraw[:, 2048 * slot: 2048 * (slot + 1)]
                bsl = bcB2[:]
                csl = o_sb2[:]
                oevw = [(pe, pe.n)]
                if qci >= 2:
                    oevw.append((ve, otts_done[qci - 2]))
                g.on("scalar", lambda e, osl=osl: e.copy(out=osl, in_=o_ps[:]),
                     wait=oevw, inc=se)
                oev_se[0] = se.n
                def recips(e, osl=osl):
                    with g.nc.allow_low_precision(reason="softmax denom recip in bf16; denominator sums bf16 terms"):
                        for h in range(2):
                            e.reciprocal(out=osl[64:65, 1024 * h: 1024 * (h + 1)],
                                         in_=osl[64:65, 1024 * h: 1024 * (h + 1)])
                            e.drain()
                    return e.nop()
                g.on("vector", recips, wait=[(se, oev_se[0])], inc=ve)
                rec_ve = ve.n
                g.on("gpsimd", lambda e, slot=slot, osl=osl: e.dma_start(
                    out=dscr[slot: slot + 1, :], in_=osl[64:65, :]),
                    wait=[(ve, rec_ve)], inc=n0, inck=16)
                g.on("gpsimd", lambda e, slot=slot, bsl=bsl: e.dma_start(
                    out=bsl, in_=dscr[slot: slot + 1, :].partition_broadcast(64)),
                    wait=[(n0, g.sem["n0"].n)], inc=n1, inck=16)
                bc_n1 = n1.n
                ottw = [(n1, bc_n1)]
                if qci >= 1:
                    ottw.append((qg, send_done[qci - 1]))
                def otts(e, osl=osl, bsl=bsl, csl=csl):
                    last_i = e.tensor_tensor(out=csl, in0=osl[0:64, :], in1=bsl, op=ALU.mult)
                    e.drain()
                    return last_i
                g.on("vector", otts, wait=ottw, inc=ve)
                otts_done[qci] = ve.n
                nrm_ve[0] = ve.n
                rdest = 4 * b + qc // 2
                col = 1024 * (qc % 2)
                for h in range(2):
                    g.on("gpsimd", lambda e, rdest=rdest, h=h, col=col, csl=csl: e.dma_start(
                        out=ret.rearrange("r (p t) -> r p t", p=128)[rdest, 64 * h: 64 * (h + 1), col: col + 1024],
                        in_=csl[:, 1024 * h: 1024 * (h + 1)]),
                        wait=[(ve, otts_done[qci])], inc=qg, inck=16)
                send_done[qci] = qg.n
                if bg is not None:
                    for _ in range(11):
                        if next(bg, "END") == "END":
                            bg = None
                            break
            g.run_phase("fl_b%d" % b)
        if bg is not None:
            for _ in iter(lambda: next(bg, "END"), "END"):
                pass
        g.w("gpsimd", qg)
        g.run_phase("fl_end")


def emit_outproj(g, cm, xT, ret_recv, wo_bf, bo_bf, w_wait=()):
    nc = g.nc
    qs, pe, ve, se = g.sem["qs"], g.sem["pe"], g.sem["ve"], g.sem["se"]
    if w_wait:
        g.on("sync", None, wait=list(w_wait))
    at_sb = g.sb("pr_at", [128, DT * TOK], BF16)
    wo_sb = g.sb("pr_wo", [128, DT * 1024], BF16)
    boh_sb = g.sb("pr_bo", [1, 1024], BF16)
    for r in range(DT):
        g.on("sync", lambda e, r=r: e.dma_start(
            out=at_sb[:, TOK * r: TOK * (r + 1)],
            in_=ret_recv[r, :].rearrange("(p t) -> p t", p=128)),
            wait=[g.sem["qg"]], inc=qs, inck=16)
    g.on("sync", lambda e: e.dma_start(
        out=wo_sb[:].rearrange("p (a c) -> p a c", a=DT),
        in_=wo_bf.rearrange("(a p) c -> p a c", p=128)), wait=[g.sem["qg"]], inc=qs, inck=16)
    g.on("sync", lambda e: e.dma_start(out=boh_sb[:], in_=bo_bf),
         wait=[g.sem["qg"]], inc=qs, inck=16)
    with g.ps("pr_u", [128, 2 * CH], F32) as u_ps:
        g.w("tensor", qs)
        evacs = []
        for i, (dcol, ch) in enumerate([(dc, c) for dc in range(DT) for c in range(NCH)]):
            ysl = u_ps[:, CH * (i % 2): CH * (i % 2 + 1)]
            def mm(e, dcol=dcol, ch=ch, ysl=ysl):
                e.matmul(ysl, boh_sb[:, 128 * dcol: 128 * (dcol + 1)],
                         cm["onesrow_bf"][:], start=True, stop=False)
                for it in range(DT):
                    last = e.matmul(ysl, wo_sb[:, 1024 * it + 128 * dcol: 1024 * it + 128 * (dcol + 1)],
                                    at_sb[:, TOK * it + CH * ch: TOK * it + CH * (ch + 1)],
                                    start=False, stop=(it == DT - 1))
                return last
            g.on("tensor", mm, wait=[(ve, evacs[i - 2])] if i >= 2 else [], inc=pe)
            def evac(e, dcol=dcol, ch=ch, ysl=ysl):
                xsl = xT[:, TOK * dcol + CH * ch: TOK * dcol + CH * (ch + 1)]
                return e.tensor_tensor(out=xsl, in0=xsl, in1=ysl, op=ALU.add)
            g.on("vector", evac, wait=[pe], inc=ve)
            evacs.append(g.sem["ve"].n)
        g.run_phase("outproj")


def build_qkv(layer):
    nc = bass.Bass("TRN2", target_bir_lowering=False)
    x_in = decl_in(nc, "x", [D, TOK])
    rope_on = layer == 0
    cnames = ("PT", "ropeC", "ropeS") if rope_on else ()
    cd = const_decls_subset(nc, cnames)
    send = decl_out(nc, "send", [NCORE, SHARD], BF16)
    w_bf = {k: nc.dram_tensor("w_" + k, [D, D], BF16).ap() for k in ("wq", "wk", "wv")}
    if layer == 0:
        wq = decl_in(nc, "Wq", [D, D])
        wkv = decl_in(nc, "Wkv", [D, 2 * D])
        casts = [(w_bf["wq"], wq), (w_bf["wk"], wkv[:, 0:D]), (w_bf["wv"], wkv[:, D:2 * D])]
    else:
        casts = [(w_bf[k], decl_in(nc, n, [D, D]))
                 for k, n in (("wq", "Wq"), ("wk", "Wk"), ("wv", "Wv"))]
    with ExitStack() as ctx:
        g = G(nc, ctx)
        cm = alloc_common(g, cd, cnames)
        xT = g.sb("xT", [128, DT * TOK], F32)
        hT = g.sb("at_hT", [128, DT * TOK], BF16)
        load_xT(g, xT, x_in)
        emit_casts(g, casts)
        g.w("tensor", g.sem["qs"])
        g.w("scalar", g.sem["qs"])
        g.w("vector", g.sem["qs"])
        g.run_phase("load")
        vown = g.sb("bq_vown", [128, 16 * 1040], BF16)
        ln = alloc_ln(g, sq=vown[:, 0: DT * CH])
        emit_ln_phase(g, cm, ln, xT, hT, name="at_ln")
        emit_qkv(g, cm, xT, hT, w_bf, send[:, 0: SH_Q + SH_K],
                 send[:, SH_Q + SH_K: SHARD], rope_on, v_own=vown)
    return nc


def build_flash(causal):
    nc = bass.Bass("TRN2", target_bir_lowering=False)
    cnames = ("Tbase",) if causal else ()
    cd = const_decls_subset(nc, cnames)
    recv = decl_in(nc, "recv", [NCORE, SHARD], BF16)
    ret = decl_out(nc, "ret", [NCORE, RET], BF16)
    with ExitStack() as ctx:
        g = G(nc, ctx)
        cm = alloc_common(g, cd, cnames)
        g.run_phase("load")
        emit_flash(g, cm, recv[:, 0: SH_Q + SH_K], recv[:, SH_Q + SH_K: SHARD],
                   ret, causal)
    return nc


def build_proj():
    nc = bass.Bass("TRN2", target_bir_lowering=False)
    x_in = decl_in(nc, "x", [D, TOK])
    ret_recv = decl_in(nc, "ret_recv", [NCORE, RET], BF16)
    wo = decl_in(nc, "Wo", [D, D])
    bo = decl_in(nc, "bo", [1, D])
    x_out = decl_out(nc, "x_out", [D, TOK])
    wo_bf = nc.dram_tensor("wo_bf", [D, D], BF16).ap()
    bo_bf = nc.dram_tensor("bo_bf", [1, D], BF16).ap()
    with ExitStack() as ctx:
        g = G(nc, ctx)
        cm = alloc_common(g, cd := {}, ())
        xT = g.sb("xT", [128, DT * TOK], F32)
        load_xT(g, xT, x_in)
        emit_casts(g, [(wo_bf, wo), (bo_bf, bo)])
        g.w("tensor", g.sem["qs"])
        g.w("vector", g.sem["qs"])
        g.run_phase("load")
        emit_outproj(g, cm, xT, ret_recv, wo_bf, bo_bf)
        store_xT(g, xT, x_out)
    return nc


def run_attn_subblock(xs_parts, layer, wdict):
    consts = get_consts()
    nc1 = build_qkv(layer)
    cnames = ("PT", "ropeC", "ropeS") if layer == 0 else ()
    im = []
    for c in range(NCORE):
        m = {"x": xs_parts[c], **{k: consts[c][k] for k in cnames}}
        if layer == 0:
            m["Wq"] = wdict["Wq"]
            m["Wkv"] = wdict["Wkv"]
        else:
            m["Wq"], m["Wk"], m["Wv"] = wdict["Wq"], wdict["Wk"], wdict["Wv"]
        im.append(m)
    res = run_bass_kernel_spmd(nc1, im, core_ids=list(range(NCORE)))
    sends = [np.asarray(res.results[c]["send"]) for c in range(NCORE)]
    recvs = [np.ascontiguousarray(np.stack([sends[r][c] for r in range(NCORE)]))
             for c in range(NCORE)]
    nc2 = build_flash(causal=(layer == 1))
    cnames2 = ("Tbase",) if layer == 1 else ()
    im = [{"recv": recvs[c], **{k: consts[c][k] for k in cnames2}} for c in range(NCORE)]
    res = run_bass_kernel_spmd(nc2, im, core_ids=list(range(NCORE)))
    rets = [np.asarray(res.results[c]["ret"]) for c in range(NCORE)]
    ret_recvs = [np.ascontiguousarray(np.stack([rets[r][c] for r in range(NCORE)]))
                 for c in range(NCORE)]
    nc3 = build_proj()
    wo = wdict["Wout"] if layer == 0 else wdict["Wo"]
    bo = wdict["bout"] if layer == 0 else wdict["bo"]
    im = [{"x": xs_parts[c], "ret_recv": ret_recvs[c], "Wo": wo,
           "bo": bo.reshape(1, D)} for c in range(NCORE)]
    res = run_bass_kernel_spmd(nc3, im, core_ids=list(range(NCORE)))
    return [np.asarray(res.results[c]["x_out"]) for c in range(NCORE)]


def alloc_quant(g):
    return {"wt": g.sb("qz_wt", [128, 2 * 1024], F32),
            "racc": g.sb("qz_racc", [128, 32], F32),
            "r1": g.sb("qz_r1", [128, 1], F32),
            "rrow": g.sb("qz_rrow", [1, 128], F32),
            "tt1": g.sb("qz_t1", [1, 1], F32),
            "rB": g.sb("qz_rB", [128, 1], F32),
            "a": g.sb("qz_a", [128, 1024], BF16),
            "q": g.sb("qz_q", [128, 2 * 1024], BF16)}


def emit_quant(g, cm, w_in, wq_out, kind, scale_half, name, scr, bufs):
    qs, ve = g.sem["qs"], g.sem["ve"]
    wp = (g.sem["wa"], g.sem["wb"])
    sp = (g.sem["sc"], g.sem["sd"])
    wt, racc = bufs["wt"], bufs["racc"]
    r1, rrow, tt1, rB = bufs["r1"], bufs["rrow"], bufs["tt1"], bufs["rB"]
    a_sb, q_sb = bufs["a"], bufs["q"]
    NG = 32

    def src(dram, i):
        if kind == "w1":
            return dram[128 * i: 128 * (i + 1), :]
        r, c = i // 4, i % 4
        return dram[128 * r: 128 * (r + 1), 1024 * c: 1024 * (c + 1)]

    def wview(par):
        return wt[:, 1024 * par: 1024 * (par + 1)]

    def qview(par):
        return q_sb[:, 1024 * par: 1024 * (par + 1)]

    red_done = {}
    for i in range(NG):
        par = i % 2
        g.on("sync", lambda e, i=i, par=par: e.dma_start(
            out=wview(par), in_=src(w_in, i)),
            wait=[(ve, red_done[i - 2])] if i >= 2 else [ve], inc=wp[par], inck=16)
        ld = wp[par].n
        g.on("vector", lambda e, i=i, par=par: e.tensor_reduce(
            out=racc[:, i: i + 1], in_=wview(par),
            axis=AX.X, op=ALU.add, apply_absolute_value=True),
            wait=[(wp[par], ld)], inc=ve)
        red_done[i] = ve.n
        yield

    qa, gp = g.sem["qa"], g.sem["gp"]
    g.on("vector", lambda e: e.tensor_reduce(out=r1[:], in_=racc[:, 0:NG],
                                             axis=AX.X, op=ALU.add),
         wait=[(ve, red_done[NG - 1])], inc=ve)
    g.on("sync", lambda e: e.dma_start(out=scr.rearrange("a b -> b a"), in_=r1[:]),
         wait=[ve], inc=qa, inck=16)
    g.on("sync", lambda e: e.dma_start(out=rrow[:], in_=scr), wait=[qa], inc=gp, inck=16)
    def gam(e):
        e.tensor_reduce(out=tt1[:], in_=rrow[:], axis=AX.X, op=ALU.add)
        e.drain()
        e.tensor_scalar(out=tt1[:], in0=tt1[:], scalar1=1.0 / (4096.0 * 1024.0),
                        scalar2=QEPS, op0=ALU.mult, op1=ALU.add)
        e.drain()
        return e.reciprocal(out=tt1[:], in_=tt1[:])
    g.on("vector", gam, wait=[gp], inc=ve)
    g.on("sync", lambda e: e.dma_start(out=scr[0:1, 0:1], in_=tt1[:]),
         wait=[ve], inc=qa, inck=16)
    g.on("sync", lambda e: e.dma_start(out=rB[:], in_=scr[0:1, 0:1].partition_broadcast(128)),
         wait=[qa], inc=gp, inck=16)
    rB_ready = gp.n
    yield

    qz_done = {}
    st_done = {}
    for i in range(NG):
        par = i % 2
        g.on("sync", lambda e, i=i, par=par: e.dma_start(
            out=wview(par), in_=src(w_in, i)),
            wait=[(ve, qz_done[i - 2])] if i >= 2 else [(ve, red_done[NG - 1])],
            inc=wp[par], inck=16)
        ld = wp[par].n
        def qz(e, i=i, par=par):
            w = wview(par)
            qsl = qview(par)
            e.tensor_scalar(out=a_sb[:], in0=w, scalar1=rB[:], scalar2=0.5,
                            op0=ALU.mult, op1=ALU.is_ge)
            e.drain()
            e.tensor_scalar(out=qsl, in0=w, scalar1=rB[:], scalar2=-0.5,
                            op0=ALU.mult, op1=ALU.is_le)
            e.drain()
            e.tensor_tensor(out=qsl, in0=a_sb[:], in1=qsl, op=ALU.subtract)
            e.drain()
            if scale_half:
                e.tensor_scalar(out=qsl, in0=qsl, scalar1=0.5, scalar2=None, op0=ALU.mult)
                e.drain()
            return e.nop()
        wz = [(wp[par], ld), (g.sem["gp"], rB_ready),
              (sp[par], st_done[i - 2] if i >= 2 else sp[par].n)]
        g.on("vector", qz, wait=wz, inc=ve)
        qz_done[i] = ve.n
        g.on("sync", lambda e, i=i, par=par: e.dma_start(
            out=src(wq_out, i), in_=qview(par)),
            wait=[(ve, qz_done[i])], inc=sp[par], inck=16)
        st_done[i] = sp[par].n
        yield


def emit_ffn(g, cm, xT, ln, wq1, wq2, b1v_sb, b2v_sb, hT_ch, hff, w1T_sb, w2T_sb):
    nc = g.nc
    qs, qg, pe, ve, se = g.sem["qs"], g.sem["qg"], g.sem["pe"], g.sem["ve"], g.sem["se"]
    wpar_sem = (g.sem["wa"], g.sem["wb"], g.sem["sc"])

    for tch2 in range(2):
        emit_ln_phase(g, cm, ln, xT, hT_ch, chunks=[2 * tch2, 2 * tch2 + 1],
                      hT_off=CH * 2 * tch2, name="ffn_ln%d" % tch2)
        if tch2 == 0:
            for d in range(DT):
                g.on("sync", lambda e, d=d: e.dma_start_transpose(
                    out=w1T_sb[:, 4096 * d: 4096 * (d + 1)],
                    in_=wq1[:, 128 * d: 128 * (d + 1)]),
                    wait=[pe, qg] if d == 0 else [], inc=qs, inck=16)
        for half in range(2):
            tch = 2 * tch2 + half
            hoff = CH * half
            with g.ps("ffn_h1_%d" % tch, [128, 2 * CH], F32) as h1_ps:
                if half == 0:
                    g.w("tensor", qs, g.sem["ve"])
                relus = []
                for f in range(32):
                    ysl = h1_ps[:, CH * (f % 2): CH * (f % 2 + 1)]
                    def mm1(e, f=f, ysl=ysl, hoff=hoff):
                        for d in range(DT):
                            last = e.matmul(ysl, w1T_sb[:, 4096 * d + 128 * f: 4096 * d + 128 * (f + 1)],
                                            hT_ch[:, 1024 * d + hoff: 1024 * d + hoff + CH],
                                            start=(d == 0), stop=(d == DT - 1))
                        return last
                    g.on("tensor", mm1, wait=[(se, relus[f - 2])] if f >= 2 else [], inc=pe)
                    g.on("scalar", lambda e, f=f, ysl=ysl: e.activation(
                        out=hff[:, CH * f: CH * (f + 1)], in_=ysl, func=AF.Relu,
                        bias=b1v_sb[:, f: f + 1]),
                        wait=[pe], inc=se)
                    relus.append(se.n)
                relu_last = se.n
                g.run_phase("ffn_m1_%d" % tch)
            with g.ps("ffn_u_%d" % tch, [128, DT * CH], F32) as u_ps:
                mmdone = []
                for f in range(32):
                    par = f % 3
                    wsem = wpar_sem[par]
                    g.on("sync", lambda e, f=f, par=par: e.dma_start_transpose(
                        out=w2T_sb[:, 1024 * par: 1024 * (par + 1)],
                        in_=wq2[:, 128 * f: 128 * (f + 1)]),
                        wait=[(pe, mmdone[f - 3])] if f >= 3 else [pe],
                        inc=wsem, inck=16)
                    def mm2(e, f=f, par=par):
                        for dc in range(DT):
                            last = e.matmul(u_ps[:, CH * dc: CH * (dc + 1)],
                                            w2T_sb[:, 1024 * par + 128 * dc: 1024 * par + 128 * (dc + 1)],
                                            hff[:, CH * f: CH * (f + 1)],
                                            start=(f == 0), stop=(f == 31 and dc == DT - 1))
                        return last
                    g.on("tensor", mm2,
                         wait=[wsem, (se, relu_last)] if f == 0 else [wsem], inc=pe)
                    mmdone.append(pe.n)
                def evac(e, tch=tch):
                    for dc in range(DT):
                        xsl = xT[:, TOK * dc + CH * tch: TOK * dc + CH * (tch + 1)]
                        e.scalar_tensor_tensor(out=xsl, in0=u_ps[:, CH * dc: CH * (dc + 1)],
                                               scalar=b2v_sb[:, dc: dc + 1], in1=xsl,
                                               op0=ALU.add, op1=ALU.add)
                        last = e.drain()
                    return last
                g.on("vector", evac, wait=[pe], inc=ve)
                g.run_phase("ffn_m2_%d" % tch)


def emit_w2t_pass(g, wq2, wq2t, tx=None):
    wp = (g.sem["wa"], g.sem["wb"])
    sp = (g.sem["sc"], g.sem["sd"])
    if tx is None:
        tx = g.sb("w2tp_tx", [128, 1024], BF16)
    st_done = {}
    for f in range(32):
        par = f % 2
        g.on("sync", lambda e, f=f: e.dma_start_transpose(
            out=tx[:], in_=wq2[:, 128 * f: 128 * (f + 1)]),
            wait=[(sp[0], st_done[f - 1])] if f >= 1 else [(sp[0], sp[0].n), (sp[1], sp[1].n)],
            inc=wp[par], inck=16)
        ld = wp[par].n
        g.on("sync", lambda e, f=f: e.dma_start(
            out=wq2t[128 * f: 128 * (f + 1), :], in_=tx[:]),
            wait=[(wp[par], ld)], inc=sp[0], inck=16)
        st_done[f] = sp[0].n
    g.run_phase("w2tp")


def build_quant():
    nc = bass.Bass("TRN2", target_bir_lowering=False)
    w1 = decl_in(nc, "W1", [DFF, D])
    w2 = decl_in(nc, "W2", [D, DFF])
    wq1 = decl_out(nc, "wq1", [DFF, D], BF16)
    wq2 = decl_out(nc, "wq2", [D, DFF], BF16)
    scr1 = nc.dram_tensor("scr1", [1, 128], F32).ap()
    scr2 = nc.dram_tensor("scr2", [1, 128], F32).ap()
    with ExitStack() as ctx:
        g = G(nc, ctx)
        cm = alloc_common(g, {}, ())
        g.run_phase("init")
        qb = alloc_quant(g)
        for _ in emit_quant(g, cm, w1, wq1, "w1", False, "q1", scr1, qb):
            pass
        for _ in emit_quant(g, cm, w2, wq2, "w2", True, "q2", scr2, qb):
            pass
        g.w("gpsimd", g.sem["qs"], g.sem["ve"])
        g.run_phase("qdone")
    return nc


def build_ffn():
    nc = bass.Bass("TRN2", target_bir_lowering=False)
    x_in = decl_in(nc, "x", [D, TOK])
    wq1 = decl_in(nc, "wq1", [DFF, D], BF16)
    wq2 = decl_in(nc, "wq2", [D, DFF], BF16)
    b1 = decl_in(nc, "b1", [1, DFF])
    b2 = decl_in(nc, "b2", [1, D])
    x_out = decl_out(nc, "x_out", [D, TOK])
    wq2t = nc.dram_tensor("i_wq2t", [DFF, D], BF16).ap()
    with ExitStack() as ctx:
        g = G(nc, ctx)
        cm = alloc_common(g, {}, ())
        xT = g.sb("xT", [128, DT * TOK], F32)
        b1h_sb = g.sb("ffn_b1h", [1, DFF], BF16)
        b2h_sb = g.sb("ffn_b2h", [1, D], BF16)
        load_xT(g, xT, x_in)
        g.on("gpsimd", lambda e: e.dma_start(out=b1h_sb[:], in_=b1), inc=g.sem["qg"], inck=16)
        g.on("gpsimd", lambda e: e.dma_start(out=b2h_sb[:], in_=b2), inc=g.sem["qg"], inck=16)
        g.w("vector", g.sem["qg"])
        g.on("vector", lambda e: e.tensor_scalar(out=b2h_sb[:], in0=b2h_sb[:], scalar1=0.5,
                                                 scalar2=None, op0=ALU.mult), inc=g.sem["ve"])
        g.w("tensor", g.sem["qs"])
        g.w("scalar", g.sem["qs"])
        g.run_phase("load")
        ln = alloc_ln(g)
        hT_ch = ln["xbf"]
        hff = g.sb("ffn_hff", [128, 32 * CH], BF16)
        w1T_sb = g.sb("ffn_w1T", [128, DT * 4096], BF16)
        w2T_sb = g.sb("ffn_w2T", [128, 3 * 1024], BF16)
        emit_ffn(g, cm, xT, ln, wq1, wq2, b1h_sb, b2h_sb, hT_ch, hff, w1T_sb, w2T_sb)
        store_xT(g, xT, x_out)
    return nc


def run_ffn_subblock(xs_parts, wq1, wq2, b1, b2):
    nc = build_ffn()
    im = [{"x": xs_parts[c], "wq1": wq1, "wq2": wq2,
           "b1": b1.reshape(1, DFF), "b2": b2.reshape(1, D)} for c in range(NCORE)]
    res = run_bass_kernel_spmd(nc, im, core_ids=list(range(NCORE)))
    return [np.asarray(res.results[c]["x_out"]) for c in range(NCORE)]


def run_quant(W1, W2):
    nc = build_quant()
    im = [{"W1": W1, "W2": W2} for _ in range(NCORE)]
    res = run_bass_kernel_spmd(nc, im, core_ids=list(range(NCORE)))
    return np.asarray(res.results[0]["wq1"]), np.asarray(res.results[0]["wq2"])


def shard_x(x):
    xs = x.reshape(NCORE, TOK, D)
    return [np.ascontiguousarray(xs[c].T) for c in range(NCORE)]


def unshard_x(parts):
    x = np.stack([np.asarray(p).T for p in parts], axis=0)
    return np.ascontiguousarray(x.reshape(2, SEQ, D))


_CONSTS = None


def get_consts():
    global _CONSTS
    if _CONSTS is None:
        _CONSTS = [host_consts(c) for c in range(NCORE)]
    return _CONSTS


def run_moe_subblock(xs_parts, wg, we, be):
    consts = get_consts()
    names = ("id8", "id128", "id1", "wsel")
    nc1 = build_moe("out")
    im = [{"x": xs_parts[c], "wg": wg, **{k: consts[c][k] for k in names}}
          for c in range(NCORE)]
    res = run_bass_kernel_spmd(nc1, im, core_ids=list(range(NCORE)))
    tots = np.concatenate([np.asarray(res.results[c]["tot"]).reshape(1, 8)
                           for c in range(NCORE)], axis=0)
    tots = np.ascontiguousarray(tots)
    nc2 = build_moe("ext")
    im = [{"x": xs_parts[c], "wg": wg, "we": we, "be": be.reshape(1, D),
           "tot_all": tots, **{k: consts[c][k] for k in names}}
          for c in range(NCORE)]
    res = run_bass_kernel_spmd(nc2, im, core_ids=list(range(NCORE)))
    return [np.asarray(res.results[c]["x_out"]) for c in range(NCORE)]




def build_fused():
    nc = bass.Bass("TRN2", target_bir_lowering=False)
    cd = const_decls_subset(nc, tuple(CONST_SPECS))
    x_in = decl_in(nc, "x", [D, TOK])
    x_out = decl_out(nc, "x_out", [D, TOK])
    win = {}
    for nm, shape in (("L0_ffn_Wg", [D, 8]), ("L0_ffn_We", [D, D]), ("L0_ffn_be", [1, D]),
                      ("L0_attn_Wq", [D, D]), ("L0_attn_Wkv", [D, 2 * D]),
                      ("L0_attn_Wout", [D, D]), ("L0_attn_bout", [1, D]),
                      ("L1_attn_Wq", [D, D]), ("L1_attn_Wk", [D, D]), ("L1_attn_Wv", [D, D]),
                      ("L1_attn_Wo", [D, D]), ("L1_attn_bo", [1, D]),
                      ("L1_ffn_W1", [DFF, D]), ("L1_ffn_b1", [1, DFF]),
                      ("L1_ffn_W2", [D, DFF]), ("L1_ffn_b2", [1, D])):
        win[nm] = decl_in(nc, nm, shape)
    dram = lambda nm, shape, dt=BF16: nc.dram_tensor(nm, shape, dt).ap()
    we_bf = dram("i_we", [D, D]); beh = dram("i_beh", [1, D])
    a_w = {0: {}, 1: {}}
    for k in ("wq", "wk", "wv"):
        a_w[0][k] = dram("i_a0_" + k, [D, D])
        a_w[1][k] = dram("i_a1_" + k, [D, D])
    wo_bf = {0: dram("i_wo0", [D, D]), 1: dram("i_wo1", [D, D])}
    bo_bf = {0: dram("i_bo0", [1, D]), 1: dram("i_bo1", [1, D])}
    wq1 = dram("i_wq1", [DFF, D]); wq2 = dram("i_wq2", [D, DFF])
    wq2t = dram("i_wq2t", [DFF, D])
    exch = {}
    for L in (0, 1):
        exch[L] = {
            "sendqk": dram("i_sendqk%d" % L, [NCORE, SH_Q + SH_K]),
            "recvqk": dram("i_recvqk%d" % L, [NCORE, SH_Q + SH_K]),
            "sendv": dram("i_sendv%d" % L, [NCORE, SH_V]),
            "recvv": dram("i_recvv%d" % L, [NCORE, SH_V]),
            "rsend": dram("i_rsend%d" % L, [NCORE, RET]),
            "rrecv": dram("i_rrecv%d" % L, [NCORE, RET]),
        }
    scrq = {1: nc.dram_tensor("i_scr1", [1, 128], F32).ap(),
            2: nc.dram_tensor("i_scr2", [1, 128], F32).ap()}
    tots = {}
    for i in (0, 1):
        tots[i] = {"tot_in": nc.dram_tensor("i_totin%d" % i, [1, 8], F32).ap(),
                   "tot_all": nc.dram_tensor("i_totall%d" % i, [8, 8], F32).ap()}

    groups8 = [list(range(NCORE))]

    with ExitStack() as ctx:
        g = G(nc, ctx)
        qs, qg, cc, pe, ve, se = (g.sem[k] for k in ("qs", "qg", "cc", "pe", "ve", "se"))
        cm = alloc_common(g, cd, ("id8", "id128", "id1", "wsel"))
        xT = g.sb("xT", [128, DT * TOK], F32)

        def scope_consts(names):
            cm2 = dict(cm)
            for n in names:
                shape, dt = CONST_SPECS[n]
                t = g.sb(n + "_sb", shape, dt)
                cm2[n + "_sb"] = t
                g.on("sync", lambda e, t=t, ap=cd[n]: e.dma_start(out=t[:], in_=ap),
                     inc=qs, inck=16)
            return cm2
        load_xT(g, xT, x_in)
        emit_casts(g, [(we_bf, win["L0_ffn_We"]), (beh, win["L0_ffn_be"])])
        wc_sem = g.sem["sd"]
        wc_state = {}
        def big_casts():
            emit_casts(g, [
                (a_w[0]["wq"], win["L0_attn_Wq"]),
                (a_w[0]["wk"], win["L0_attn_Wkv"][:, 0:D]),
                (a_w[0]["wv"], win["L0_attn_Wkv"][:, D:2 * D]),
            ], sem=wc_sem)
            wc_state[0] = wc_sem.n
        def rest_casts():
            emit_casts(g, [
                (wo_bf[0], win["L0_attn_Wout"]), (bo_bf[0], win["L0_attn_bout"]),
                (a_w[1]["wq"], win["L1_attn_Wq"]), (a_w[1]["wk"], win["L1_attn_Wk"]),
                (a_w[1]["wv"], win["L1_attn_Wv"]),
                (wo_bf[1], win["L1_attn_Wo"]), (bo_bf[1], win["L1_attn_bo"]),
            ], sem=wc_sem)
            wc_state[1] = wc_sem.n
        g.w("tensor", qs)
        g.w("scalar", qs)
        g.w("vector", qs)
        g.run_phase("load")

        def moe(i):
            with g.subscope("m%d_" % i):
                emit_moe_phase(g, cm, xT, {"wg": win["L0_ffn_Wg"], "we_bf": we_bf,
                                           "beh": beh}, "cc", tots[i],
                               after_loads=(big_casts if i == 0 else None))

        def attn(L):
            ex = exch[L]
            ccval = {}
            def qk_hook(ex=ex):
                g.w("gpsimd", qg)
                def ccqk(e, ex=ex):
                    with allow_ici_all_to_all():
                        return e.collective_compute(
                            "AllToAll", ALU.bypass, replica_groups=groups8,
                            ins=[ex["sendqk"]], outs=[ex["recvqk"]])
                g.on("gpsimd", ccqk, inc=cc)
                ccval["qk"] = cc.n
            with g.subscope("a%dq_" % L):
                cm2 = scope_consts(("PT", "ropeC", "ropeS") if L == 0 else ())
                hT = g.sb("hT", [128, DT * TOK], BF16)
                v_own = g.sb("vown", [128, 16 * 1040], BF16)
                ln = alloc_ln(g, sq=v_own[:, 0: DT * CH])
                emit_ln_phase(g, cm2, ln, xT, hT, name="ln")
                emit_qkv(g, cm2, xT, hT, a_w[L], ex["sendqk"], ex["sendv"],
                         rope_on=(L == 0), v_own=v_own, on_qk_done=qk_hook,
                         w_wait=[(wc_sem, wc_state[min(L, max(wc_state))])])
            def ccv(e, ex=ex):
                with allow_ici_all_to_all():
                    return e.collective_compute(
                        "AllToAll", ALU.bypass, replica_groups=groups8,
                        ins=[ex["sendv"]], outs=[ex["recvv"]])
            g.on("gpsimd", ccv, wait=[(qs, qs.n)], inc=cc)
            ccval["v"] = cc.n
            g.run_phase("a%d_cc1" % L)
            with g.subscope("a%df_" % L):
                cm2 = scope_consts(("Tbase",) if L == 1 else ())
                bg = None
                if L == 0:
                    import itertools
                    qb = alloc_quant(g)
                    bg = itertools.chain(
                        emit_quant(g, cm, win["L1_ffn_W1"], wq1, "w1",
                                   False, "q1", scrq[1], qb),
                        emit_quant(g, cm, win["L1_ffn_W2"], wq2, "w2",
                                   True, "q2", scrq[2], qb))
                emit_flash(g, cm2, ex["recvqk"], ex["recvv"], ex["rsend"],
                           causal=(L == 1), bg=bg,
                           wait_qk=(cc, ccval["qk"]), wait_v=(cc, ccval["v"]))
            g.w("gpsimd", qg)
            def cc2(e, ex=ex):
                with allow_ici_all_to_all():
                    return e.collective_compute(
                        "AllToAll", ALU.bypass, replica_groups=groups8,
                        ins=[ex["rsend"]], outs=[ex["rrecv"]])
            g.on("gpsimd", cc2, inc=cc)
            g.run_phase("a%d_cc2" % L)
            with g.subscope("a%dp_" % L):
                g.w("sync", cc)
                emit_outproj(g, cm, xT, ex["rrecv"], wo_bf[L], bo_bf[L],
                             w_wait=[(wc_sem, wc_state[1])])

        def ffn(i):
            with g.subscope("f%d_" % i):
                b1v_sb = g.sb("b1v", [128, 32], F32)
                b2v_sb = g.sb("b2v", [128, 8], F32)
                for f in range(32):
                    g.on("sync", lambda e, f=f: e.dma_start(
                        out=b1v_sb[:, f: f + 1],
                        in_=win["L1_ffn_b1"].rearrange("a (f p) -> (a f) p", p=128)[f: f + 1]
                            .rearrange("f p -> p f")),
                        inc=qs, inck=16)
                for f in range(DT):
                    g.on("sync", lambda e, f=f: e.dma_start(
                        out=b2v_sb[:, f: f + 1],
                        in_=win["L1_ffn_b2"].rearrange("a (f p) -> (a f) p", p=128)[f: f + 1]
                            .rearrange("f p -> p f")),
                        inc=qs, inck=16)
                g.w("vector", qs)
                g.on("vector", lambda e: e.tensor_scalar(
                    out=b2v_sb[:], in0=b2v_sb[:], scalar1=0.5, scalar2=None, op0=ALU.mult),
                    inc=ve)
                g.run_phase("fb")
                hff = g.sb("hff", [128, 32 * CH], BF16)
                ln = alloc_ln(g, sq=hff[:, 0:DT * CH], xbf=hff[:, DT * CH: 2 * DT * CH])
                hT_ch = g.sb("hTch", [128, DT * 2 * CH], BF16)
                w1T_sb = g.sb("w1T", [128, DT * 4096], BF16)
                w2T_sb = g.sb("w2T", [128, 3 * 1024], BF16)
                emit_ffn(g, cm, xT, ln, wq1, wq2, b1v_sb, b2v_sb, hT_ch, hff,
                         w1T_sb, w2T_sb)

        moe(0)
        rest_casts()
        attn(0)
        moe(1)
        ffn(0)
        attn(1)
        ffn(1)
        store_xT(g, xT, x_out)
    return nc


_FUSED_NC = None


def kernel_fused(**inputs):
    global _FUSED_NC
    f32 = lambda k: np.ascontiguousarray(np.asarray(inputs[k], np.float32))
    x = f32("L0_attn_queries")
    xs = shard_x(x)
    consts = get_consts()
    if _FUSED_NC is None:
        _FUSED_NC = build_fused()
    wkeys = ["L0_ffn_Wg", "L0_ffn_We", "L0_attn_Wq", "L0_attn_Wkv", "L0_attn_Wout",
             "L1_attn_Wq", "L1_attn_Wk", "L1_attn_Wv", "L1_attn_Wo",
             "L1_ffn_W1", "L1_ffn_W2"]
    bkeys = ["L0_ffn_be", "L0_attn_bout", "L1_attn_bo", "L1_ffn_b1", "L1_ffn_b2"]
    im = []
    for c in range(NCORE):
        m = {"x": xs[c], **consts[c]}
        for k in wkeys:
            m[k] = f32(k)
        for k in bkeys:
            m[k] = f32(k).reshape(1, -1)
        im.append(m)
    res = run_bass_kernel_spmd(_FUSED_NC, im, core_ids=list(range(NCORE)))
    return unshard_x([np.asarray(res.results[c]["x_out"]) for c in range(NCORE)])

def kernel(**inputs):
    return kernel_fused(**inputs)


def kernel_unfused(**inputs):

    f32 = lambda k: np.ascontiguousarray(np.asarray(inputs[k], np.float32))
    x = f32("L0_attn_queries")
    xs = shard_x(x)

    moe_w = (f32("L0_ffn_Wg"), f32("L0_ffn_We"), f32("L0_ffn_be"))
    xs = run_moe_subblock(xs, *moe_w)
    xs = run_attn_subblock(xs, 0, {"Wq": f32("L0_attn_Wq"), "Wkv": f32("L0_attn_Wkv"),
                                   "Wout": f32("L0_attn_Wout"), "bout": f32("L0_attn_bout")})
    xs = run_moe_subblock(xs, *moe_w)
    wq1, wq2 = run_quant(f32("L1_ffn_W1"), f32("L1_ffn_W2"))
    b1, b2 = f32("L1_ffn_b1"), f32("L1_ffn_b2")
    xs = run_ffn_subblock(xs, wq1, wq2, b1, b2)
    xs = run_attn_subblock(xs, 1, {"Wq": f32("L1_attn_Wq"), "Wk": f32("L1_attn_Wk"),
                                   "Wv": f32("L1_attn_Wv"), "Wo": f32("L1_attn_Wo"),
                                   "bo": f32("L1_attn_bo")})
    xs = run_ffn_subblock(xs, wq1, wq2, b1, b2)
    return unshard_x(xs)
